# version 26
# baseline (speedup 1.0000x reference)
import math
from contextlib import ExitStack

import numpy as np

import concourse.bass as bass
import concourse.mybir as mybir
from concourse.bass_utils import run_bass_kernel_spmd

F32 = mybir.dt.float32
BF16 = mybir.dt.bfloat16
AF = mybir.ActivationFunctionType
ALU = mybir.AluOpType
AX = mybir.AxisListType.X

D = 1024
NH = 16
HD = 64
NEXP = 8
EDIM = 2048
EPS = 1e-5
ROPE_BASE = 10000.0
N_CORES = 8
GAMMA1 = 1.0 / 32.0
GAMMA2 = 1.0 / 8.0


def _tok_add(d, tok):
    if tok is None:
        return
    k = id(tok[0])
    old = d.get(k)
    if old is None or old[1] < tok[1]:
        d[k] = tok


def _tok_merge(d, other):
    for t in other.values():
        _tok_add(d, t)


class Buf:

    def __init__(self, n=1, carry=None):
        self.w = [None] * n
        self.r = [dict() for _ in range(n)]
        self.carry = dict(carry) if carry else {}


def _units(spec):
    if isinstance(spec, Buf):
        return spec, [0]
    b, u = spec
    if isinstance(u, (list, tuple, range)):
        return b, list(u)
    return b, [u]


class Slot:

    def __init__(self, kb, name):
        self.sem = kb.es.enter_context(kb.nc.semaphore(name))
        self.rounds = 0
        self.tok = None


class KB:
    def __init__(self, nc, es):
        self.nc = nc
        self.es = es
        self.raw = {"pe": nc.tensor, "dve": nc.vector, "act": nc.scalar,
                    "pool": nc.gpsimd, "sp": nc.sync}
        self.sem = {}
        self.cnt = {}
        self.marks = {}
        for e in self.raw:
            self.sem[e] = es.enter_context(nc.semaphore("sem_" + e))
            self.cnt[e] = 0
            self.marks[e] = {}
        self.tag = "init"
        self.tagmap = {}

    def slot(self, name):
        return Slot(self, name)

    def _wait(self, eng, tok):
        if tok is None:
            return
        sem, val = tok
        wm = self.marks[eng]
        key = id(sem)
        if wm.get(key, -1) >= val:
            return
        self.raw[eng].wait_ge(sem, val)
        wm[key] = val

    def _deps(self, eng, reads, writes):
        for spec in reads:
            b, us = _units(spec)
            for u in us:
                if b.w[u] is None:
                    for t in b.carry.values():
                        self._wait(eng, t)
                self._wait(eng, b.w[u])
        for spec in writes:
            b, us = _units(spec)
            for u in us:
                if b.w[u] is None:
                    for t in b.carry.values():
                        self._wait(eng, t)
                self._wait(eng, b.w[u])
                for t in b.r[u].values():
                    self._wait(eng, t)

    def _record(self, tok, reads, writes):
        for spec in reads:
            b, us = _units(spec)
            for u in us:
                _tok_add(b.r[u], tok)
        for spec in writes:
            b, us = _units(spec)
            for u in us:
                b.w[u] = tok
                b.r[u] = {}

    def op(self, eng, method, *args, reads=(), writes=(), **kw):
        self._deps(eng, reads, writes)
        inst = getattr(self.raw[eng], method)(*args, **kw)
        self.cnt[eng] += 1
        tok = (self.sem[eng], self.cnt[eng])
        inst.then_inc(self.sem[eng], 1)
        self._record(tok, reads, writes)
        return tok

    def mmg(self, items, writes):
        self._deps("pe", [], writes)
        all_reads = []
        last = None
        for it in items:
            self._deps("pe", it.get("reads", ()), [])
            all_reads.extend(it.get("reads", ()))
            kw = {}
            if it.get("tile_position") is not None:
                kw["tile_position"] = it["tile_position"]
            last = self.raw["pe"].matmul(it["out"], it["lhsT"], it["rhs"],
                                         start=it["start"], stop=it["stop"], **kw)
            self.tagmap[last.ins.name] = self.tag
        self.cnt["pe"] += 1
        tok = (self.sem["pe"], self.cnt["pe"])
        last.then_inc(self.sem["pe"], 1)
        self._record(tok, all_reads, writes)
        return tok

    def idma(self, slot, out_ap, out_off, in_ap, in_off, reads=(), writes=(),
             **kw):
        self._deps("pool", reads, writes)
        self._wait("pool", slot.tok)
        inst = self.raw["pool"].indirect_dma_start(out_ap, out_off, in_ap,
                                                   in_off, **kw)
        inst.then_inc(slot.sem, 16)
        slot.rounds += 1
        tok = (slot.sem, 16 * slot.rounds)
        slot.tok = tok
        self._record(tok, reads, writes)
        return tok

    def dma(self, slot, out_ap, in_ap, reads=(), writes=(), eng="sp",
            transpose=False):
        self._deps(eng, reads, writes)
        self._wait(eng, slot.tok)
        if transpose:
            inst = self.raw[eng].dma_start_transpose(out_ap, in_ap)
        else:
            inst = self.raw[eng].dma_start(out_ap, in_ap)
        inst.then_inc(slot.sem, 16)
        slot.rounds += 1
        tok = (slot.sem, 16 * slot.rounds)
        slot.tok = tok
        self._record(tok, reads, writes)
        return tok


class Arena:

    def __init__(self, nc, base, cap):
        self.nc = nc
        self.base = base
        self.free_list = [(base, cap, {})]
        self.live = {}
        self.uid = 0

    def alloc(self, name, shape, dtype, units=1, carry=None):
        per_part = int(np.prod(shape[1:])) * mybir.dt.size(dtype)
        per_part = (per_part + 31) & ~31
        self.uid += 1
        nm = f"{name}_u{self.uid}"
        for i, (off, size, toks) in enumerate(self.free_list):
            if size >= per_part:
                h = self.nc.alloc_sbuf_tensor_at(nm, list(shape), dtype, offset=off)
                if size == per_part:
                    self.free_list.pop(i)
                else:
                    self.free_list[i] = (off + per_part, size - per_part,
                                         dict(toks))
                cd = dict(toks)
                if carry:
                    _tok_merge(cd, carry)
                buf = Buf(units, carry=cd)
                self.live[id(h)] = (off, per_part, buf)
                return h, buf
        raise MemoryError(f"arena OOM for {name}: need {per_part}B/p free="
                          f"{[(o, s) for o, s, _ in self.free_list]}")

    def freez(self, h):
        off, size, buf = self.live.pop(id(h))
        toks = dict(buf.carry)
        for t in buf.w:
            _tok_add(toks, t)
        for rl in buf.r:
            _tok_merge(toks, rl)
        self.free_list.append((off, size, toks))
        self.free_list.sort(key=lambda s: s[0])
        out = []
        for seg in self.free_list:
            if out and out[-1][0] + out[-1][1] == seg[0]:
                merged = dict(out[-1][2])
                _tok_merge(merged, seg[2])
                out[-1] = (out[-1][0], out[-1][1] + seg[1], merged)
            else:
                out.append((seg[0], seg[1], dict(seg[2])))
        self.free_list = out
        return toks


class Prog:
    def __init__(self, T=2048, BPC=2, debug=False):
        self.T = T
        self.BPC = BPC
        self.debug = debug
        self.NT = T // 128
        self.nc = bass.Bass("TRN2", target_bir_lowering=False)

    def sb(self, name, shape, dtype):
        return self.es.enter_context(self.nc.sbuf_tensor(name, list(shape), dtype))

    def PB(self, i, lo, hi, p0=0, p1=128):
        h, bf = self.pb[i]
        return h[p0:p1, lo:hi], (bf, list(range(lo // 512, (hi + 511) // 512)))

    def build(self):
        nc, T, NT, BPC = self.nc, self.T, self.NT, self.BPC
        self.TQ = min(512, T)
        self.NTQ = T // self.TQ

        self.x_d = nc.dram_tensor("x", [BPC, T, D], F32, kind="ExternalInput").ap()
        wnames = ["wv_hm", "fwq", "fwk", "fwv", "fwo"]
        self.w_d = {n: nc.dram_tensor(n, [D, D], F32, kind="ExternalInput").ap()
                    for n in wnames}
        for n in ["wqb", "wkb"]:
            self.w_d[n] = nc.dram_tensor(n, [D, 128], F32,
                                         kind="ExternalInput").ap()
        self.rw_d = nc.dram_tensor("rw", [D, NEXP], F32, kind="ExternalInput").ap()
        self.fc1_d = nc.dram_tensor("fc1", [NEXP, D, EDIM], F32, kind="ExternalInput").ap()
        self.fc2_d = nc.dram_tensor("fc2", [NEXP, EDIM, D], F32, kind="ExternalInput").ap()
        self.y_d = nc.dram_tensor("y", [BPC, T, D], F32, kind="ExternalOutput").ap()

        self.scr_xn = [nc.dram_tensor(f"scr_xn{b}", [T, D], BF16).ap() for b in range(BPC)]
        self.scr_hn = [nc.dram_tensor(f"scr_hn{b}", [T, D], BF16).ap() for b in range(BPC)]
        self.scr_Hn = [nc.dram_tensor(f"scr_Hn{b}", [T + 1, D], BF16).ap() for b in range(BPC)]
        self.scr_cw = [nc.dram_tensor(f"scr_cw{b}", [T + 1, NEXP], F32).ap() for b in range(BPC)]
        self.scr_idx = [nc.dram_tensor(f"scr_idx{b}", [max(1, T // 1024) * 8 * 384], mybir.dt.int32).ap() for b in range(BPC)]
        self.scr_gth = [nc.dram_tensor(f"scr_gth{s}", [384, D], BF16).ap() for s in range(2)]
        self.scr_acc = [nc.dram_tensor(f"scr_acc{b}", [T, D], F32).ap() for b in range(BPC)]
        self.scr_moe = [nc.dram_tensor(f"scr_moe{b}", [T + 1, D], F32).ap() for b in range(BPC)]
        self.scr_moe2 = [nc.dram_tensor(f"scr_moe2_{b}", [T + 1, D], F32).ap() for b in range(BPC)]
        self.scr_tab = nc.dram_tensor("scr_tab", [4, 2, 128, T], F32).ap()
        self.scr_w = {n: nc.dram_tensor("scrb_" + n, [D, D], BF16).ap() for n in wnames}
        for n in ["wqb", "wkb"]:
            self.scr_w[n] = nc.dram_tensor("scrb_" + n, [D, 128], BF16).ap()
        self.scr_rw = nc.dram_tensor("scrb_rw", [D, NEXP], BF16).ap()
        self.scr_fc1 = nc.dram_tensor("scrb_fc1", [NEXP, D, EDIM], BF16).ap()
        self.scr_fc2 = nc.dram_tensor("scrb_fc2", [NEXP, EDIM, D], BF16).ap()
        if self.debug:
            self.dbg_y = nc.dram_tensor("dbg_y", [BPC, T, D], F32, kind="ExternalOutput").ap()
            self.dbg_h = nc.dram_tensor("dbg_h", [BPC, T, D], F32, kind="ExternalOutput").ap()
            self.dbg_c = nc.dram_tensor("dbg_c", [BPC, T, NEXP], F32, kind="ExternalOutput").ap()
            self.dbg_moe = nc.dram_tensor("dbg_moe", [BPC, T, D], F32, kind="ExternalOutput").ap()

        self.es = ExitStack()
        es = self.es
        kb = KB(nc, es)
        self.kb = kb
        self._pre3 = None

        self.xstage = self.sb("xstage", [128, 2, D], F32)
        self.nstage = self.sb("nstage", [128, 2, D], BF16)
        self.n2stage = self.sb("n2stage", [128, 2, D], BF16)
        self.sqs = self.sb("sqs", [128, 2, D], F32)
        self.st_m = [self.sb(f"st_m{b}", [128, NT], F32) for b in range(BPC)]
        self.st_rstd = [self.sb(f"st_rstd{b}", [128, NT], F32) for b in range(BPC)]
        self.st_b1 = [self.sb(f"st_b1{b}", [128, NT], F32) for b in range(BPC)]
        self.st_s2 = [self.sb(f"st_s2{b}", [128, NT], F32) for b in range(BPC)]
        self.st_s3 = [self.sb(f"st_s3{b}", [128, NT], F32) for b in range(BPC)]
        self.st_t = self.sb("st_t", [128, 2, 8], F32)
        self.cw = [self.sb(f"cw{b}", [128, NT, NEXP], F32) for b in range(BPC)]
        self.lgt = self.sb("lgt", [128, 2, NEXP], F32)
        self.lgt2 = self.sb("lgt2", [128, 2, NEXP], F32)
        self.lgt3 = self.sb("lgt3", [128, 2, NEXP], F32)
        self.epsv = self.sb("epsv", [128, 1], F32)
        self.pi2 = self.sb("pi2", [128, 1], F32)
        self.ones64 = self.sb("ones64", [1, 64], F32)
        self.iotv = self.sb("iotv", [128, 4], F32)
        self.invf = self.sb("invf", [128, 1], F32)
        self.rrow = self.sb("rrow", [1, 2, self.TQ], F32)
        self.rinv = self.sb("rinv", [1, 2, self.TQ], F32)
        self.ysc = self.sb("ysc", [64, 2, self.TQ], F32)
        self.rwb = self.sb("rwb", [128, 8, NEXP], BF16)
        self.trimask = self.sb("trimask", [128, 128], BF16)
        self.ustrict = self.sb("ustrict", [128, 128], BF16)
        self.erow = self.sb("erow", [128, NEXP], F32)
        self.ones128f = self.sb("ones128f", [1, 128], F32)
        self.onescol = self.sb("onescol", [128, 1], BF16)
        self.senti = self.sb("senti", [128, 24], mybir.dt.int32)
        self.zrow = self.sb("zrow", [128, D], F32)

        self.B_xstage = Buf(2)
        self.B_nstage = Buf(2)
        self.B_n2stage = Buf(2)
        self.B_sqs = Buf(2)
        self.B_stats = [Buf(NT) for _ in range(BPC)]
        self.B_s3 = [Buf(NT) for _ in range(BPC)]
        self.B_stt = Buf(2)
        self.B_cw = [Buf(NT) for _ in range(BPC)]
        self.B_lgt = Buf(2)
        self.B_lgt2 = Buf(2)
        self.B_lgt3 = Buf(2)
        self.B_const = Buf(1)
        self.B_rope = Buf(2)
        self.B_rwb = Buf(1)
        self.B_in = Buf(1)
        self.B_scr_xn = [Buf(NT) for _ in range(BPC)]
        self.B_scr_hn = [Buf(NT) for _ in range(BPC)]
        self.B_scr_Hn = [Buf(NT + 1) for _ in range(BPC)]
        self.B_scr_acc = [Buf(NT) for _ in range(BPC)]
        self.B_scr_moe = [Buf(NEXP) for _ in range(BPC)]
        self.B_scr_moe2 = [Buf(NEXP) for _ in range(BPC)]
        self.B_scrg = [Buf(1), Buf(1)]
        self.B_scr_cw = [Buf(NT + 1) for _ in range(BPC)]
        self.B_scr_idx = [Buf(max(1, T // 1024)) for _ in range(BPC)]
        self.B_scr_tab = Buf(4)
        self.B_scr_w = {n: Buf(8) for n in wnames}
        self.B_scr_w["wqb"] = Buf(8)
        self.B_scr_w["wkb"] = Buf(8)
        self.B_scr_rw8 = Buf(8)
        self.B_scr_fc1 = Buf(NEXP * 8)
        self.B_scr_fc2 = Buf(NEXP * 16)

        self.s_ldx = [kb.slot("s_ldx0"), kb.slot("s_ldx1")]
        self.s_stn = [kb.slot("s_stn0"), kb.slot("s_stn1")]
        self.s_stn2 = [kb.slot("s_stn2a"), kb.slot("s_stn2b")]
        self.s_stn3 = [kb.slot("s_stn3a"), kb.slot("s_stn3b")]
        self.s_ldt = [kb.slot(f"s_ldt{i}") for i in range(4)]
        self.s_ldw = [kb.slot("s_ldw0"), kb.slot("s_ldw1")]
        self.s_stw = [kb.slot("s_stw0"), kb.slot("s_stw1")]
        self.s_acc = [kb.slot("s_acc0"), kb.slot("s_acc1")]
        self.s_acc2 = [kb.slot("s_acc2a"), kb.slot("s_acc2b")]
        self.s_out = [kb.slot("s_out0"), kb.slot("s_out1")]
        self.s_tab = kb.slot("s_tab")
        self.s_fc1 = [kb.slot(f"s_fc1_{i}") for i in range(4)]
        self.s_fc2 = [kb.slot(f"s_fc2_{i}") for i in range(4)]
        self.s_pc = [kb.slot(f"s_pc_{i}") for i in range(9)]
        self.s_ix = [kb.slot(f"s_ix_{i}") for i in range(2)]
        self.s_gg = [kb.slot(f"s_gg_{i}") for i in range(3)]
        self.s_sca = [kb.slot(f"s_sca_{i}") for i in range(2)]
        self.s_zz = kb.slot("s_zz")
        self.s_dbg = kb.slot("s_dbg")

        self.rg_idx = nc.gpsimd.alloc_register("bc_idx")
        nc.gpsimd.reg_mov(self.rg_idx, max(1, self.T // 1024) * 3072 - 1)
        self.rg_tok = nc.gpsimd.alloc_register("bc_tok")
        nc.gpsimd.reg_mov(self.rg_tok, self.T)

        self.pb = []
        for i in range(4):
            h = es.enter_context(nc.psum_tensor(f"pb{i}", [128, 1024], F32))
            self.pb.append((h, Buf(2)))

        static_end = nc.SBUF_PARTITION_SIZE_BYTES - nc.sbuf_bytes_remaining
        self.AR = Arena(nc, static_end + 64, nc.sbuf_top - static_end - 128)

        with es:
            self.p0_consts()
            self.p0_weights()
            self.p1(0)
            self.p0_tables()
            for b in range(self.BPC):
                self.batch_front(b)
                self.p7(b)
                if b + 1 < self.BPC:
                    self.p3(b + 1)
                self.p8(b)
            for sl in self.s_out:
                if sl.tok is not None:
                    kb.raw["sp"].wait_ge(sl.sem, 16 * sl.rounds)
        return nc

    def p0_consts(self):
        self.kb.tag = 'p0t'
        kb = self.kb
        kb.op("pool", "memset", self.epsv[:], EPS, writes=[self.B_const])
        kb.op("pool", "memset", self.pi2[:], float(np.pi / 2), writes=[self.B_const])
        kb.op("pool", "memset", self.ones64[:], 1.0, writes=[self.B_const])
        kb.op("pool", "memset", self.trimask[:], 1.0, writes=[self.B_const])
        kb.op("pool", "affine_select", self.trimask[:], self.trimask[:],
              [[1, 128]], ALU.is_ge, 0.0, base=0, channel_multiplier=-1,
              writes=[self.B_const])
        kb.op("pool", "memset", self.ustrict[:], 1.0, writes=[self.B_const])
        kb.op("pool", "affine_select", self.ustrict[:], self.ustrict[:],
              [[1, 128]], ALU.is_ge, 0.0, base=-1, channel_multiplier=-1,
              writes=[self.B_const])
        kb.op("pool", "iota", self.erow[:], [[1, NEXP]], base=0,
              channel_multiplier=0, allow_small_or_imprecise_dtypes=True,
              writes=[self.B_const])
        kb.op("pool", "memset", self.ones128f[:], 1.0, writes=[self.B_const])
        kb.op("pool", "memset", self.onescol[:], 1.0, writes=[self.B_const])
        kb.op("pool", "memset", self.senti[:], self.T, writes=[self.B_const])
        kb.op("pool", "memset", self.zrow[:], 0.0, writes=[self.B_const])

    def p0_tables(self):
        self.kb.tag = 'p0t'
        kb, nc, T = self.kb, self.nc, self.T
        AR = self.AR
        tabc, B_tc = AR.alloc("tabc", [128, T], F32)
        tabs, B_ts = AR.alloc("tabs", [128, T], F32)
        scr, B_sc = AR.alloc("tabscr", [128, T], F32)
        for cp in range(4):
            kb.op("pool", "iota", self.iotv[:, 0:1], [[1, 1]], base=2 * cp,
                  channel_multiplier=8, allow_small_or_imprecise_dtypes=True,
                  writes=[self.B_const])
            kb.op("pool", "memset", self.iotv[:, 1:2], 1.0, writes=[self.B_const])
            kb.op("pool", "affine_select", self.iotv[:, 1:2], self.iotv[:, 1:2],
                  [[1, 1]], ALU.is_ge, 0.0, base=-64, channel_multiplier=1,
                  writes=[self.B_const])
            kb.op("dve", "scalar_tensor_tensor", self.iotv[:, 2:3],
                  self.iotv[:, 1:2], -511.0, self.iotv[:, 0:1], ALU.mult, ALU.add,
                  reads=[self.B_const], writes=[self.B_const])
            kb.op("dve", "tensor_scalar_mul", self.iotv[:, 3:4], self.iotv[:, 2:3],
                  -math.log(ROPE_BASE) / 512.0, reads=[self.B_const],
                  writes=[self.B_const])
            kb.op("act", "activation", self.invf[:], self.iotv[:, 3:4], AF.Exp,
                  bias=0.0, scale=1.0, reads=[self.B_const], writes=[self.B_const])
            kb.op("pool", "memset", tabc[:, 0:1], 1.0, writes=[B_tc])
            kb.op("pool", "memset", tabs[:, 0:1], 0.0, writes=[B_ts])
            if T > 1:
                kb.op("act", "activation", tabs[:, 1:2], self.invf[:], AF.Sin,
                      bias=0.0, scale=1.0, reads=[self.B_const], writes=[B_ts])
                kb.op("act", "activation", tabc[:, 1:2], self.invf[:], AF.Sin,
                      bias=self.pi2[:], scale=1.0, reads=[self.B_const], writes=[B_tc])
            s = 2
            while s < T:
                ext = s - 1
                w = min(s - 1, T - s)
                ce = tabc[:, ext:ext + 1]
                se = tabs[:, ext:ext + 1]
                kb.op("dve", "tensor_scalar_mul", scr[:, 0:w], tabs[:, 1:1 + w],
                      se, reads=[B_ts], writes=[B_sc])
                kb.op("dve", "scalar_tensor_tensor", tabc[:, s:s + w],
                      tabc[:, 1:1 + w], ce, scr[:, 0:w], ALU.mult, ALU.subtract,
                      reads=[B_sc, B_tc], writes=[B_tc])
                kb.op("dve", "tensor_scalar_mul", scr[:, 0:w], tabc[:, 1:1 + w],
                      se, reads=[B_tc], writes=[B_sc])
                kb.op("dve", "scalar_tensor_tensor", tabs[:, s:s + w],
                      tabs[:, 1:1 + w], ce, scr[:, 0:w], ALU.mult, ALU.add,
                      reads=[B_sc, B_ts], writes=[B_ts])
                s = s + w
            kb.dma(self.s_tab, self.scr_tab[cp, 0], tabc[:], reads=[B_tc],
                   writes=[(self.B_scr_tab, cp)])
            kb.dma(self.s_tab, self.scr_tab[cp, 1], tabs[:], reads=[B_ts],
                   writes=[(self.B_scr_tab, cp)])
        for t in AR.freez(tabc):
            pass
        AR.freez(tabs)
        AR.freez(scr)

    def p0_weights(self):
        kb = self.kb
        semB = self.s_pc[8]

        def cast_dma(slot, src_ap, dst_ap, rows, cols):
            n = 0
            for r0 in range(0, rows, 256):
                rr = min(256, rows - r0)
                inst = kb.raw["pool"].dma_start(dst_ap[r0:r0 + rr, 0:cols],
                                                src_ap[r0:r0 + rr, 0:cols])
                inst.then_inc(slot.sem, 16)
                slot.rounds += 1
                n += 1
            return n

        for wi, n_ in enumerate(["wqb", "wkb", "wv_hm", "fwq", "fwk",
                                 "fwv", "fwo"]):
            sl_ = self.s_pc[wi]
            cast_dma(sl_, self.w_d[n_], self.scr_w[n_], D,
                     128 if n_ in ("wqb", "wkb") else D)
            tokn = (sl_.sem, 16 * sl_.rounds)
            sl_.tok = tokn
            bb = self.B_scr_w[n_]
            for u in range(8):
                bb.w[u] = tokn
        cast_dma(self.s_pc[7], self.rw_d, self.scr_rw, D, NEXP)
        tokA = (self.s_pc[7].sem, 16 * self.s_pc[7].rounds)
        self.s_pc[7].tok = tokA
        for u in range(8):
            self.B_scr_rw8.w[u] = tokA
        for e in range(NEXP):
            cast_dma(semB, self.fc1_d[e], self.scr_fc1[e], D, EDIM)
            cast_dma(semB, self.fc2_d[e], self.scr_fc2[e], EDIM, D)
        tokB = (semB.sem, 16 * semB.rounds)
        semB.tok = tokB
        for u in range(NEXP * 8):
            self.B_scr_fc1.w[u] = tokB
        for u in range(NEXP * 16):
            self.B_scr_fc2.w[u] = tokB

    def p1(self, b):
        for f in self.p1_emitters(b):
            f()

    def p1_emitters(self, b):
        loads = [self._p1_load(b, i) for i in range(self.NT)]
        comps = [self._p1_comp(b, i) for i in range(self.NT)]
        sched = [loads[0]]
        for i in range(self.NT):
            nxt = loads[i + 1] if i + 1 < self.NT else (lambda: None)
            sched.append(lambda a=nxt, c=comps[i]: (a(), c()))
        return sched

    def _p1_load(self, b, i):
        def emit():
            self.kb.tag = 'p1'
            sl = i % 2
            self.kb.dma(self.s_ldx[sl], self.xstage[:, sl, :],
                        self.x_d[b, i * 128:(i + 1) * 128, :],
                        reads=[self.B_in], writes=[(self.B_xstage, sl)])
        return emit

    def _p1_comp(self, b, i):
        def emit():
            self.kb.tag = 'p1'
            kb, T, NT = self.kb, self.T, self.NT
            sl = i % 2
            stt = self.st_t[:, sl, :]
            B_t = (self.B_stt, sl)
            xt = self.xstage[:, sl, :]
            kb.op("dve", "reduce_sum", stt[:, 0:1], xt, axis=AX,
                  reads=[(self.B_xstage, sl)], writes=[B_t])
            kb.op("act", "activation", self.sqs[:, sl, :], xt, AF.Square,
                  bias=0.0, scale=1.0, accum_out=stt[:, 1:2],
                  reads=[(self.B_xstage, sl)], writes=[(self.B_sqs, sl), B_t])
            kb.op("dve", "tensor_scalar_mul", self.st_m[b][:, i:i + 1],
                  stt[:, 0:1], 1.0 / D, reads=[B_t],
                  writes=[(self.B_stats[b], i)])
            kb.op("dve", "tensor_mul", stt[:, 2:3], self.st_m[b][:, i:i + 1],
                  self.st_m[b][:, i:i + 1], reads=[B_t, (self.B_stats[b], i)],
                  writes=[B_t])
            kb.op("dve", "scalar_tensor_tensor", stt[:, 3:4], stt[:, 1:2],
                  1.0 / D, stt[:, 2:3], ALU.mult, ALU.subtract,
                  reads=[B_t], writes=[B_t])
            kb.op("act", "activation", stt[:, 4:5], stt[:, 3:4], AF.Ln,
                  bias=self.epsv[:], scale=1.0, reads=[B_t, self.B_const],
                  writes=[B_t])
            kb.op("act", "activation", self.st_rstd[b][:, i:i + 1], stt[:, 4:5],
                  AF.Exp, bias=0.0, scale=-0.5,
                  reads=[B_t], writes=[(self.B_stats[b], i)])
            kb.op("dve", "tensor_scalar", self.st_b1[b][:, i:i + 1],
                  self.st_m[b][:, i:i + 1], self.st_rstd[b][:, i:i + 1], -1.0,
                  ALU.mult, ALU.mult, reads=[(self.B_stats[b], i)],
                  writes=[(self.B_stats[b], i)])
            kb.op("dve", "tensor_mul", stt[:, 5:6], stt[:, 3:4],
                  self.st_rstd[b][:, i:i + 1],
                  reads=[B_t, (self.B_stats[b], i)], writes=[B_t])
            kb.op("dve", "tensor_mul", stt[:, 6:7], stt[:, 5:6],
                  self.st_rstd[b][:, i:i + 1],
                  reads=[B_t, (self.B_stats[b], i)], writes=[B_t])
            kb.op("act", "activation", stt[:, 7:8], stt[:, 6:7], AF.Ln,
                  bias=self.epsv[:], scale=1.0, reads=[B_t, self.B_const],
                  writes=[B_t])
            kb.op("act", "activation", self.st_s2[b][:, i:i + 1], stt[:, 7:8],
                  AF.Exp, bias=0.0, scale=-0.5,
                  reads=[B_t], writes=[(self.B_stats[b], i)])
            hm_out = bass.AP(self.nstage, sl * D,
                             [[2 * D, 128], [1, 64], [64, 8], [512, 2]])
            kb.op("act", "activation", hm_out, xt, AF.Identity,
                  bias=self.st_b1[b][:, i:i + 1], scale=self.st_rstd[b][:, i:i + 1],
                  reads=[(self.B_xstage, sl), (self.B_stats[b], i)],
                  writes=[(self.B_nstage, sl)])
            kb.op("dve", "tensor_mul", stt[:, 2:3], self.st_rstd[b][:, i:i + 1],
                  self.st_s2[b][:, i:i + 1],
                  reads=[(self.B_stats[b], i)], writes=[B_t])
            kb.op("dve", "tensor_mul", stt[:, 3:4], self.st_b1[b][:, i:i + 1],
                  self.st_s2[b][:, i:i + 1],
                  reads=[(self.B_stats[b], i)], writes=[B_t])
            kb.op("act", "activation", self.n2stage[:, sl, :], xt, AF.Identity,
                  bias=stt[:, 3:4], scale=stt[:, 2:3],
                  reads=[(self.B_xstage, sl), B_t],
                  writes=[(self.B_n2stage, sl)])
            kb.dma(self.s_stn[sl], self.scr_xn[b][i * 128:(i + 1) * 128, :],
                   self.nstage[:, sl, :], reads=[(self.B_nstage, sl)],
                   writes=[(self.B_scr_xn[b], i)])
            kb.dma(self.s_stn2[sl], self.scr_hn[b][i * 128:(i + 1) * 128, :],
                   self.n2stage[:, sl, :], reads=[(self.B_n2stage, sl)],
                   writes=[(self.B_scr_hn[b], i)])

        return emit

    def load_T_strips(self, scr_ap, B_src, name, carryfrom=None):
        T = self.T
        chunks = []
        for c in range(8):
            h, bf = self.AR.alloc(f"{name}{c}", [128, T], BF16, carry=carryfrom)
            self.kb.dma(self.s_ldt[c % 4], h[:],
                        scr_ap[0:T, c * 128:(c + 1) * 128],
                        reads=[(B_src, list(range(self.NT)))], writes=[bf],
                        transpose=True)
            chunks.append((h, bf))
        return chunks

    def p3(self, b):
        self.kb.tag = 'p3'
        kb, T, NT, AR = self.kb, self.T, self.NT, self.AR
        TH = min(1024, T)
        NTH = T // TH
        if self._pre3 is not None:
            xnT, _ = self._pre3
            self._pre3 = None
        else:
            xnT = self.load_T_strips(self.scr_xn[b], self.B_scr_xn[b],
                                     f"xnT{b}")
        qkwh, B_qkw = AR.alloc(f"qkw{b}", [128, 16, 128], BF16, units=16)
        qkw = (qkwh, B_qkw)
        for c in range(8):
            kb.dma(self.s_ldw[c % 2], qkw[0][:, c, :],
                   self.scr_w["wqb"][c * 128:(c + 1) * 128, :],
                   reads=[(self.B_scr_w["wqb"], c)], writes=[(qkw[1], c)])
            kb.dma(self.s_ldw[c % 2], qkw[0][:, 8 + c, :],
                   self.scr_w["wkb"][c * 128:(c + 1) * 128, :],
                   reads=[(self.B_scr_w["wkb"], c)],
                   writes=[(qkw[1], 8 + c)])
        self.xnT = xnT
        tabh = [AR.alloc(f"tab{b}_{i}", [128, TH], F32) for i in range(2)]
        QK = []
        for wi, out_name in enumerate((f"Qh{b}", f"Kh{b}")):
            dst = [AR.alloc(f"{out_name}_{c}", [128, T], BF16) for c in range(8)]
            for cp in range(4):
                for th in range(NTH):
                    pp = (cp * NTH + th) % 2
                    pA, bA = self.PB(pp, 0, TH)
                    pB, bB = self.PB(2 + pp, 0, TH)
                    for gi, (ps_, bs_) in enumerate(((pA, bA), (pB, bB))):
                        c = cp + 4 * gi
                        items = []
                        for q in range(max(1, TH // 512)):
                            w = min(512, TH)
                            items.append(dict(
                                out=ps_[:, q * 512:q * 512 + w],
                                lhsT=qkw[0][:, 8 * wi + c, :],
                                rhs=xnT[c][0][:, th * TH + q * 512:
                                              th * TH + q * 512 + w],
                                reads=[(qkw[1], 8 * wi + c), xnT[c][1]],
                                start=True, stop=True))
                        kb.mmg(items, writes=[bs_])
                    kb.dma(self.s_tab, tabh[0][0][:],
                           self.scr_tab[cp, 0, :, th * TH:(th + 1) * TH],
                           reads=[(self.B_scr_tab, cp)], writes=[tabh[0][1]])
                    kb.dma(self.s_tab, tabh[1][0][:],
                           self.scr_tab[cp, 1, :, th * TH:(th + 1) * TH],
                           reads=[(self.B_scr_tab, cp)], writes=[tabh[1][1]])
                    s1 = self.sqs[:, 0, 0:TH]
                    s2 = self.sqs[:, 1, 0:TH]
                    kb.op("dve", "tensor_mul", s1, pA, tabh[0][0][:],
                          reads=[bA, tabh[0][1]], writes=[(self.B_sqs, 0)])
                    kb.op("dve", "tensor_mul", s2, pB, tabh[1][0][:],
                          reads=[bB, tabh[1][1]], writes=[(self.B_sqs, 1)])
                    kb.op("pool", "tensor_sub",
                          dst[cp][0][:, th * TH:(th + 1) * TH], s1, s2,
                          reads=[(self.B_sqs, 0), (self.B_sqs, 1)],
                          writes=[dst[cp][1]])
                    kb.op("dve", "tensor_mul", s1, pA, tabh[1][0][:],
                          reads=[bA, tabh[1][1]], writes=[(self.B_sqs, 0)])
                    kb.op("dve", "tensor_mul", s2, pB, tabh[0][0][:],
                          reads=[bB, tabh[0][1]], writes=[(self.B_sqs, 1)])
                    kb.op("pool", "tensor_add",
                          dst[cp + 4][0][:, th * TH:(th + 1) * TH], s1, s2,
                          reads=[(self.B_sqs, 0), (self.B_sqs, 1)],
                          writes=[dst[cp + 4][1]])
            QK.append(dst)
        self.Qh, self.Kh = QK
        for h, bf in tabh:
            AR.freez(h)
        AR.freez(qkw[0])
        self.kb.tag = 'p3v'
        wbuf = [AR.alloc(f"wb{b}_{c}", [128, D], BF16) for c in range(8)]
        for c in range(8):
            kb.dma(self.s_ldw[c % 2], wbuf[c][0][:],
                   self.scr_w["wv_hm"][c * 128:(c + 1) * 128, :],
                   reads=[(self.B_scr_w["wv_hm"], c)], writes=[wbuf[c][1]])
        Ve = [AR.alloc(f"Ve{b}_{i}", [128, 1040], BF16) for i in range(NT)]
        self.Ve = Ve
        for i in range(NT):
            pp = i % 2
            pV, bV = self.PB(2 + pp, 0, 1024)
            items = []
            for gh in range(2):
                for fc in range(8):
                    items.append(dict(
                        out=pV[:, gh * 512:(gh + 1) * 512],
                        lhsT=xnT[fc][0][:, i * 128:(i + 1) * 128],
                        rhs=wbuf[fc][0][:, gh * 512:(gh + 1) * 512],
                        reads=[xnT[fc][1], wbuf[fc][1]],
                        start=(fc == 0), stop=(fc == 7)))
            kb.mmg(items, writes=[bV])
            kb.op("act", "activation", Ve[i][0][:, 0:1024], pV,
                  AF.Copy, bias=0.0, scale=1.0,
                  reads=[bV], writes=[Ve[i][1]])
            kb.op("pool", "memset", Ve[i][0][:, 1024:1025], 1.0,
                  writes=[Ve[i][1]])
        for h, bf in wbuf:
            AR.freez(h)
        for h, bf in xnT:
            AR.freez(h)
        self.xnT = None

    def p4(self, b):
        self.kb.tag = 'p4'
        kb, T, NT, AR = self.kb, self.T, self.NT, self.AR
        TQ, NTQ = self.TQ, self.NTQ
        Qh, Kh, Ve = self.Qh, self.Kh, self.Ve
        P = [AR.alloc(f"P{b}_{k}", [128, TQ], BF16) for k in range(NT)]
        if True:
            KTQ = TQ // 128
            for qb in range(NTQ):
                nkt = KTQ * (qb + 1)
                self.kb.tag = 'p4s'
                for kt in range(nkt):
                    off = max(0, kt * 128 - qb * TQ)
                    w = TQ - off
                    pp = kt % 2
                    pS, bS = self.PB(pp, 512, 1024)
                    items = []
                    for gc in range(8):
                        items.append(dict(
                            out=pS[:, off:off + w],
                            lhsT=Kh[gc][0][:, kt * 128:(kt + 1) * 128],
                            rhs=Qh[gc][0][:, qb * TQ + off:qb * TQ + off + w],
                            reads=[Kh[gc][1], Qh[gc][1]],
                            start=(gc == 0), stop=(gc == 7)))
                    kb.mmg(items, writes=[bS])
                    kb.op("act", "activation", P[kt][0][:, off:off + w],
                          pS[:, off:off + w], AF.Exp, bias=0.0, scale=GAMMA1,
                          reads=[bS], writes=[P[kt][1]])
                    if kt * 128 >= qb * TQ:
                        kb.op("dve", "tensor_mul", P[kt][0][:, off:off + 128],
                              P[kt][0][:, off:off + 128], self.trimask[:],
                              reads=[P[kt][1], self.B_const], writes=[P[kt][1]])
                self.kb.tag = 'p4av'
                for qi in range(qb * TQ // 128, (qb + 1) * TQ // 128):
                    pp = qi % 2
                    pY, bY = self.PB(2 + pp, 0, 1024)
                    pR, bR = self.PB(pp, 0, 16)
                    items = []
                    for kt in range(qi + 1):
                        off = qi * 128 - qb * TQ
                        stat = P[kt][0][:, off:off + 128]
                        for gh in range(2):
                            items.append(dict(
                                out=pY[:, gh * 512:(gh + 1) * 512],
                                lhsT=stat,
                                rhs=Ve[kt][0][:, gh * 512:(gh + 1) * 512],
                                reads=[P[kt][1], Ve[kt][1]],
                                start=(kt == 0), stop=(kt == qi)))
                        items.append(dict(
                            out=pR[:, 0:1], lhsT=stat,
                            rhs=Ve[kt][0][:, 1024:1025],
                            reads=[P[kt][1], Ve[kt][1]],
                            start=(kt == 0), stop=(kt == qi)))
                    kb.mmg(items, writes=[bY, bR])
                    kb.op("dve", "reciprocal", self.st_t[:, pp, 0:1], pR[:, 0:1],
                          reads=[bR], writes=[(self.B_stt, pp)])
                    ot = self.sqs[:, pp, :]
                    kb.op("act", "activation", ot, pY, AF.Identity,
                          bias=0.0, scale=self.st_t[:, pp, 0:1],
                          reads=[bY, (self.B_stt, pp)],
                          writes=[(self.B_sqs, pp)])
                    kb.dma(self.s_acc[pp], self.scr_acc[b][qi * 128:(qi + 1) * 128, :],
                           ot, reads=[(self.B_sqs, pp)],
                           writes=[(self.B_scr_acc[b], qi)])
                    if self.debug:
                        kb.dma(self.s_dbg, self.dbg_y[b, qi * 128:(qi + 1) * 128, :],
                               ot, reads=[(self.B_sqs, pp)], writes=[])
        for h, bf in P:
            AR.freez(h)
        for lst in (self.Qh, self.Kh, self.Ve):
            for h, bf in lst:
                AR.freez(h)
        self.Qh = self.Kh = self.Ve = None

    def p5(self, b):
        self.kb.tag = 'p5'
        kb, T, NT, AR = self.kb, self.T, self.NT, self.AR
        hnT = self.load_T_strips(self.scr_hn[b], self.B_scr_hn[b], f"hnT{b}")
        wsets = [[AR.alloc(f"wf{b}_{s}_{c}", [128, D], BF16) for c in range(8)]
                 for s in range(2)]
        TH = min(1024, T)
        NTH = T // TH
        QK = []
        for wi, (wname, oname) in enumerate(
                (("fwq", f"Q2T{b}"), ("fwk", f"K2T{b}"))):
            wbuf = wsets[wi % 2]
            dst = [AR.alloc(f"{oname}_{c}", [128, T], BF16) for c in range(8)]
            for c in range(8):
                kb.dma(self.s_ldw[c % 2], wbuf[c][0][:],
                       self.scr_w[wname][c * 128:(c + 1) * 128, :],
                       reads=[(self.B_scr_w[wname], c)], writes=[wbuf[c][1]])
            if True:
                for gc in range(8):
                    for th in range(NTH):
                        pp = (gc * NTH + th) % 2
                        pQ, bQ = self.PB(pp, 0, TH)
                        items = []
                        for q in range(max(1, TH // 512)):
                            w = min(512, TH)
                            for fc in range(8):
                                items.append(dict(
                                    out=pQ[:, q * 512:q * 512 + w],
                                    lhsT=wbuf[fc][0][:, gc * 128:(gc + 1) * 128],
                                    rhs=hnT[fc][0][:, th * TH + q * 512:
                                                    th * TH + q * 512 + w],
                                    reads=[wbuf[fc][1], hnT[fc][1]],
                                    start=(fc == 0), stop=(fc == 7)))
                        kb.mmg(items, writes=[bQ])
                        kb.op("act", "activation",
                              dst[gc][0][:, th * TH:(th + 1) * TH], pQ,
                              AF.Copy, bias=0.0, scale=1.0,
                              reads=[bQ], writes=[dst[gc][1]])
            QK.append(dst)
        self.Q2T, self.K2T = QK
        wbuf = wsets[0]
        for c in range(8):
            kb.dma(self.s_ldw[c % 2], wbuf[c][0][:],
                   self.scr_w["fwv"][c * 128:(c + 1) * 128, :],
                   reads=[(self.B_scr_w["fwv"], c)], writes=[wbuf[c][1]])
        V2e = [AR.alloc(f"V2e{b}_{i}", [128, 1040], BF16) for i in range(NT)]
        self.V2e = V2e
        if True:
            for i in range(NT):
                for gh in range(2):
                    pp = (2 * i + gh) % 2
                    pV, bV = self.PB(2, pp * 512, pp * 512 + 512)
                    items = []
                    for fc in range(8):
                        items.append(dict(
                            out=pV,
                            lhsT=hnT[fc][0][:, i * 128:(i + 1) * 128],
                            rhs=wbuf[fc][0][:, gh * 512:(gh + 1) * 512],
                            reads=[hnT[fc][1], wbuf[fc][1]],
                            start=(fc == 0), stop=(fc == 7)))
                    kb.mmg(items, writes=[bV])
                    dst_ap = bass.AP(V2e[i][0], gh * 8 * 65,
                                     [[1040, 128], [65, 8], [1, 64]])
                    kb.op("act", "activation", dst_ap, pV, AF.Copy,
                          bias=0.0, scale=1.0, reads=[bV],
                          writes=[V2e[i][1]])
                ones_ap = bass.AP(V2e[i][0], 64, [[1040, 128], [65, 16]])
                kb.op("pool", "memset", ones_ap, 1.0, writes=[V2e[i][1]])
        for ws in wsets:
            for h, bf in ws:
                AR.freez(h)
        for h, bf in hnT:
            AR.freez(h)

    def p6(self, b, inject=()):
        self.kb.tag = 'p6'
        inject = list(inject)
        kb, T, NT, AR = self.kb, self.T, self.NT, self.AR
        TQ, NTQ = self.TQ, self.NTQ
        Q2T, K2T, V2e = self.Q2T, self.K2T, self.V2e
        KTQ = TQ // 128
        Pf = [AR.alloc(f"Pf{b}_{i}", [128, TQ], BF16) for i in range(6)]
        OT = [None] * 8
        wbuf = [AR.alloc(f"wo{b}_{c}", [128, D], BF16) for c in range(8)]
        for c in range(8):
            kb.dma(self.s_ldw[c % 2], wbuf[c][0][:],
                   self.scr_w["fwo"][c * 128:(c + 1) * 128, :],
                   reads=[(self.B_scr_w["fwo"], c)], writes=[wbuf[c][1]])
        self._p6_pending = None
        if True:
            for hp in range(8):
                if OT[hp] is None:
                    OT[hp] = AR.alloc(f"OT{b}_{hp}", [128, T], BF16)
                for qb in range(NTQ):
                    if inject:
                        inject.pop(0)()
                    self.kb.tag = 'p6'

                    nkt = KTQ * (qb + 1)
                    pq6 = (hp * NTQ + qb) % 2
                    pyt = []
                    for hi in range(2):
                        pY, bY = self.PB(2 + pq6, hi * 512, hi * 512 + TQ)
                        pyt.append((pY, bY))
                    def emit_scores(kt):
                        self.kb.tag = 'p6s'
                        off = max(0, kt * 128 - qb * TQ)
                        w = TQ - off
                        pp = kt % 2
                        pf = kt % 3
                        for hi in range(2):
                            base = hi * 64
                            pS, bS = self.PB(pp, hi * 512, hi * 512 + TQ)
                            kb.mmg([dict(
                                out=pS[:, off:off + w],
                                lhsT=K2T[hp][0][base:base + 64,
                                               kt * 128:(kt + 1) * 128],
                                rhs=Q2T[hp][0][base:base + 64,
                                              qb * TQ + off:qb * TQ + off + w],
                                reads=[K2T[hp][1], Q2T[hp][1]],
                                start=True, stop=True)],
                                writes=[bS])
                            kb.op("act", "activation",
                                  Pf[2 * pf + hi][0][:, off:off + w],
                                  pS[:, off:off + w], AF.Exp,
                                  bias=0.0, scale=GAMMA2,
                                  reads=[bS], writes=[Pf[2 * pf + hi][1]])
                            if kt * 128 >= qb * TQ:
                                kb.op("dve", "tensor_mul",
                                      Pf[2 * pf + hi][0][:, off:off + 128],
                                      Pf[2 * pf + hi][0][:, off:off + 128],
                                      self.trimask[:],
                                      reads=[Pf[2 * pf + hi][1], self.B_const],
                                      writes=[Pf[2 * pf + hi][1]])

                    def emit_av(kt):
                        self.kb.tag = 'p6a'
                        off = max(0, kt * 128 - qb * TQ)
                        w = TQ - off
                        pf = kt % 3
                        for hi in range(2):
                            h = 2 * hp + hi
                            kb.mmg([dict(
                                out=pyt[hi][0][0:65, off:off + w],
                                lhsT=V2e[kt][0][:, h * 65:h * 65 + 65],
                                rhs=Pf[2 * pf + hi][0][:, off:off + w],
                                reads=[V2e[kt][1], Pf[2 * pf + hi][1]],
                                start=(kt == 0), stop=(kt == nkt - 1))],
                                writes=[pyt[hi][1]])

                    emit_scores(0)
                    if nkt > 1:
                        emit_scores(1)
                    for kt in range(2, nkt):
                        emit_scores(kt)
                        emit_av(kt - 2)
                    if nkt > 1:
                        emit_av(nkt - 2)
                    emit_av(nkt - 1)
                    if self._p6_pending is not None:
                        self._p6_pending()
                        self._p6_pending = None
                    self.kb.tag = 'p6n'
                    for hi in range(2):
                        pY, bY = pyt[hi]
                        kb.op("dve", "tensor_copy", self.ysc[:, hi, 0:TQ],
                              pY[0:64, :], reads=[bY],
                              writes=[(self.B_rope, hi)])
                        kb.op("act", "activation", self.rrow[:, hi, 0:TQ],
                              pY[64:65, :], AF.Ln, bias=0.0, scale=1.0,
                              reads=[bY], writes=[(self.B_rope, hi)])
                        kb.op("act", "activation", self.rinv[:, hi, 0:TQ],
                              self.rrow[:, hi, 0:TQ], AF.Exp, bias=0.0,
                              scale=-1.0, reads=[(self.B_rope, hi)],
                              writes=[(self.B_rope, hi)])
                    def part_b(hp=hp, qb=qb):
                        self.kb.tag = 'p6n'
                        for hi in range(2):
                            pRB, bRB = self.PB(1, hi * 512, hi * 512 + TQ,
                                               p0=0, p1=64)
                            kb.mmg([dict(out=pRB, lhsT=self.ones64[:],
                                         rhs=self.rinv[:, hi, 0:TQ],
                                         reads=[(self.B_rope, hi),
                                                self.B_const],
                                         start=True, stop=True)],
                                   writes=[bRB])
                            kb.op("dve", "tensor_tensor",
                                  OT[hp][0][hi * 64:hi * 64 + 64,
                                            qb * TQ:(qb + 1) * TQ],
                                  self.ysc[:, hi, 0:TQ], pRB, ALU.mult,
                                  reads=[(self.B_rope, hi), bRB],
                                  writes=[OT[hp][1]])
                    self._p6_pending = part_b
            if self._p6_pending is not None:
                self._p6_pending()
                self._p6_pending = None
            for hp in range(8):
                AR.freez(Q2T[hp][0])
                AR.freez(K2T[hp][0])
        for h_, bf in V2e:
            AR.freez(h_)
        for h_, bf in Pf:
            AR.freez(h_)
        self.Q2T = self.K2T = self.V2e = None
        self.kb.tag = 'p6wo'
        acct, B_acct = AR.alloc(f"acct{b}", [128, 2, D], F32, units=2)
        htile, B_ht = AR.alloc(f"ht{b}", [128, 2, D], F32, units=2)
        if True:
            for i in range(NT):
                sl = i % 2
                psw_, bW = self.PB(2 * sl, 0, 1024)
                kb.dma(self.s_ldx[sl], self.xstage[:, sl, :],
                       self.x_d[b, i * 128:(i + 1) * 128, :],
                       reads=[self.B_in], writes=[(self.B_xstage, sl)])
                kb.dma(self.s_acc[sl], acct[:, sl, :],
                       self.scr_acc[b][i * 128:(i + 1) * 128, :],
                       reads=[(self.B_scr_acc[b], i)], writes=[(B_acct, sl)])
                items = []
                for gh in range(2):
                    for gc in range(8):
                        items.append(dict(
                            out=psw_[:, gh * 512:(gh + 1) * 512],
                            lhsT=OT[gc][0][:, i * 128:(i + 1) * 128],
                            rhs=wbuf[gc][0][:, gh * 512:(gh + 1) * 512],
                            reads=[OT[gc][1], wbuf[gc][1]],
                            start=(gc == 0), stop=(gc == 7)))
                kb.mmg(items, writes=[bW])
                kb.op("act", "activation", self.sqs[:, sl, :],
                      self.xstage[:, sl, :], AF.Identity,
                      bias=self.st_b1[b][:, i:i + 1],
                      scale=self.st_rstd[b][:, i:i + 1],
                      reads=[(self.B_xstage, sl), (self.B_stats[b], i)],
                      writes=[(self.B_sqs, sl)])
                kb.op("dve", "tensor_add", htile[:, sl, :], self.sqs[:, sl, :],
                      psw_, reads=[(self.B_sqs, sl), bW],
                      writes=[(B_ht, sl)])
                kb.op("act", "activation", self.sqs[:, sl, :], htile[:, sl, :],
                      AF.Square, bias=0.0, scale=1.0,
                      accum_out=self.st_t[:, sl, 0:1],
                      reads=[(B_ht, sl)],
                      writes=[(self.B_sqs, sl), (self.B_stt, sl)])
                kb.op("act", "activation", self.st_t[:, sl, 1:2],
                      self.st_t[:, sl, 0:1],
                      AF.Ln, bias=self.epsv[:], scale=1.0 / D,
                      reads=[(self.B_stt, sl), self.B_const],
                      writes=[(self.B_stt, sl)])
                kb.op("act", "activation", self.st_s3[b][:, i:i + 1],
                      self.st_t[:, sl, 1:2], AF.Exp, bias=0.0, scale=-0.5,
                      reads=[(self.B_stt, sl)],
                      writes=[(self.B_s3[b], i)])
                kb.op("act", "activation", self.n2stage[:, sl, :],
                      htile[:, sl, :], AF.Identity, bias=0.0,
                      scale=self.st_s3[b][:, i:i + 1],
                      reads=[(B_ht, sl), (self.B_s3[b], i)],
                      writes=[(self.B_n2stage, sl)])
                kb.dma(self.s_stn3[sl], self.scr_Hn[b][i * 128:(i + 1) * 128, :],
                       self.n2stage[:, sl, :], reads=[(self.B_n2stage, sl)],
                       writes=[(self.B_scr_Hn[b], i)])
                kb.op("pool", "tensor_add", acct[:, sl, :], acct[:, sl, :],
                      htile[:, sl, :], reads=[(B_acct, sl), (B_ht, sl)],
                      writes=[(B_acct, sl)])
                kb.dma(self.s_acc2[sl], self.scr_acc[b][i * 128:(i + 1) * 128, :],
                       acct[:, sl, :], reads=[(B_acct, sl)],
                       writes=[(self.B_scr_acc[b], i)])
                if self.debug:
                    kb.dma(self.s_dbg, self.dbg_h[b, i * 128:(i + 1) * 128, :],
                           htile[:, sl, :], reads=[(B_ht, sl)], writes=[])
        for c in range(8):
            AR.freez(OT[c][0])
        for h_, bf in wbuf:
            AR.freez(h_)
        AR.freez(acct)
        AR.freez(htile)

    def p7(self, b):
        self.kb.tag = 'p7r'
        kb, T, NT, AR = self.kb, self.T, self.NT, self.AR
        C2 = 192
        CE = 2 * C2
        TH7 = min(1024, T)
        NTH7 = T // TH7
        NTPT = TH7 // 128
        IOA = bass.IndirectOffsetOnAxis
        fc1b = [AR.alloc(f"fc1_{b}_{c}", [128, EDIM], BF16) for c in range(8)]
        fc2b = [AR.alloc(f"fc2_{b}_{c}", [128, D], BF16) for c in range(16)]

        def load_fc(e):
            for c in range(8):
                kb.dma(self.s_fc1[c % 4], fc1b[c][0][:],
                       self.scr_fc1[e, c * 128:(c + 1) * 128, :],
                       reads=[(self.B_scr_fc1, e * 8 + c)],
                       writes=[fc1b[c][1]])
            for c in range(16):
                kb.dma(self.s_fc2[c % 4], fc2b[c][0][:],
                       self.scr_fc2[e, c * 128:(c + 1) * 128, :],
                       reads=[(self.B_scr_fc2, e * 16 + c)],
                       writes=[fc2b[c][1]], eng="pool")

        load_fc(0)
        self.kb.tag = 'p7r'
        HnT = self.load_T_strips(self.scr_Hn[b], self.B_scr_Hn[b], f"HnT{b}")
        for fc in range(8):
            kb.dma(self.s_ldw[fc % 2], self.rwb[:, fc, :],
                   self.scr_rw[fc * 128:(fc + 1) * 128, :],
                   reads=[(self.B_scr_rw8, fc)], writes=[self.B_rwb])
        kb.dma(self.s_zz, self.scr_Hn[b][T:T + 1, :], self.zrow[0:1, :],
               reads=[self.B_const],
               writes=[(self.B_scr_Hn[b], NT)], eng="pool")
        kb.dma(self.s_stn[1], self.scr_cw[b][T:T + 1, :],
               self.zrow[0:1, 0:NEXP],
               reads=[self.B_const], writes=[(self.B_scr_cw[b], NT)])
        for th in range(NTH7):
            seg = self.scr_idx[b][th * 3072:(th + 1) * 3072]
            kb.dma(self.s_stn[0], seg.rearrange("(p a) -> p a", p=128),
                   self.senti[:, 0:24], reads=[self.B_const],
                   writes=[(self.B_scr_idx[b], th)])

        gtile, B_gt = AR.alloc(f"gt{b}", [128, 6, D], BF16, units=6)
        idxt, B_ix = AR.alloc(f"ix{b}", [128, 2, 4], mybir.dt.int32, units=2)
        cgt, B_cg = AR.alloc(f"cg{b}", [128, 2, 3 * NEXP], F32, units=2)
        base_r, B_base = AR.alloc(f"base{b}", [1, 16], F32)
        posf, B_posf = AR.alloc(f"posf{b}", [128, 2, 40], F32, units=2)
        tgti, B_tg = AR.alloc(f"tg{b}", [128, 2, 2], mybir.dt.int32, units=2)
        tki, B_tk = AR.alloc(f"tk{b}", [128, 2, 1], mybir.dt.int32, units=2)
        eqf, B_eqf = AR.alloc(f"eqf{b}", [128, 2, 16], F32, units=2)
        eqb, B_eqb = AR.alloc(f"eqb{b}", [128, 2, 16], BF16, units=2)
        Gs = [AR.alloc(f"G{b}_{c}", [128, CE], BF16) for c in range(16)]
        mst = [AR.alloc(f"mst{b}_{B}", [128, D], F32) for B in range(3)]
        msw, B_msw = AR.alloc(f"msw{b}", [64, 2, D], F32, units=2)
        ixw, B_ixw = AR.alloc(f"ixw{b}", [64, 2, 1], mybir.dt.int32, units=2)
        relu_s, B_rs = AR.alloc(f"relu{b}", [128, 2, CE], BF16, units=2)
        gthT = [[AR.alloc(f"gthT{b}_{s}_{c}", [128, CE], BF16) for c in range(8)]
                for s in range(2)]
        def compact_tile(th, tt):
            self.kb.tag = 'p7r'
            if tt == 0:
                kb.op("dve", "memset", base_r[:], 0.0, writes=[B_base])
            if True:
                i = th * NTPT + tt
                sl7 = i % 2
                stt = self.st_t[:, sl7, :]
                B_stt7 = (self.B_stt, sl7)
                psl, bL = self.PB(3, 512 + sl7 * 16, 512 + sl7 * 16 + NEXP)
                items = []
                for fc in range(8):
                    items.append(dict(
                        out=psl,
                        lhsT=HnT[fc][0][:, i * 128:(i + 1) * 128],
                        rhs=self.rwb[:, fc, :],
                        reads=[HnT[fc][1], self.B_rwb],
                        start=(fc == 0), stop=(fc == 7)))
                kb.mmg(items, writes=[bL])
                lg = self.lgt[:, sl7, :]
                B_lg7 = (self.B_lgt, sl7)
                kb.op("dve", "tensor_copy", lg, psl, reads=[bL], writes=[B_lg7])
                kb.op("dve", "reduce_max", stt[:, 0:1], lg, axis=AX,
                      reads=[B_lg7], writes=[B_stt7])
                kb.op("dve", "tensor_scalar", stt[:, 2:3], stt[:, 0:1], -1.0,
                      None, ALU.mult, reads=[B_stt7], writes=[B_stt7])
                eq1 = eqf[:, sl7, 0:NEXP]
                eq2 = eqf[:, sl7, NEXP:2 * NEXP]
                kb.op("dve", "tensor_scalar", eq1, lg, stt[:, 0:1], None,
                      ALU.is_equal, reads=[B_lg7, B_stt7],
                      writes=[(B_eqf, sl7)])
                kb.op("dve", "scalar_tensor_tensor", self.lgt2[:, sl7, :],
                      eq1, -1e30, lg, ALU.mult, ALU.add,
                      reads=[(B_eqf, sl7), B_lg7],
                      writes=[(self.B_lgt2, sl7)])
                kb.op("dve", "reduce_max", stt[:, 1:2], self.lgt2[:, sl7, :],
                      axis=AX, reads=[(self.B_lgt2, sl7)], writes=[B_stt7])
                kb.op("dve", "tensor_scalar", eq2, self.lgt2[:, sl7, :],
                      stt[:, 1:2], None, ALU.is_equal,
                      reads=[(self.B_lgt2, sl7), B_stt7],
                      writes=[(B_eqf, sl7)])
                kb.op("dve", "tensor_add", stt[:, 3:4], stt[:, 1:2],
                      stt[:, 2:3], reads=[B_stt7], writes=[B_stt7])
                kb.op("act", "activation", stt[:, 4:5], stt[:, 3:4], AF.Exp,
                      bias=0.0, scale=1.0, reads=[B_stt7], writes=[B_stt7])
                kb.op("dve", "tensor_scalar_add", stt[:, 5:6], stt[:, 4:5],
                      1.0, reads=[B_stt7], writes=[B_stt7])
                kb.op("dve", "reciprocal", stt[:, 6:7], stt[:, 5:6],
                      reads=[B_stt7], writes=[B_stt7])
                kb.op("dve", "tensor_mul", stt[:, 7:8], stt[:, 4:5],
                      stt[:, 6:7], reads=[B_stt7], writes=[B_stt7])
                kb.op("dve", "tensor_scalar_mul", self.cw[b][:, i, :], eq1,
                      stt[:, 6:7], reads=[(B_eqf, sl7), B_stt7],
                      writes=[(self.B_cw[b], i)])
                kb.op("dve", "scalar_tensor_tensor", self.cw[b][:, i, :],
                      eq2, stt[:, 7:8], self.cw[b][:, i, :],
                      ALU.mult, ALU.add,
                      reads=[(B_eqf, sl7), B_stt7, (self.B_cw[b], i)],
                      writes=[(self.B_cw[b], i)])
                kb.dma(self.s_stn2[i % 2], self.scr_cw[b][i * 128:(i + 1) * 128, :],
                       self.cw[b][:, i, :], reads=[(self.B_cw[b], i)],
                       writes=[(self.B_scr_cw[b], i)])
                if self.debug:
                    kb.dma(self.s_dbg, self.dbg_c[b, i * 128:(i + 1) * 128, :],
                           self.cw[b][:, i, :], reads=[(self.B_cw[b], i)],
                           writes=[])
                kb.op("dve", "tensor_copy", eqb[:, sl7, 0:16],
                      eqf[:, sl7, 0:16], reads=[(B_eqf, sl7)],
                      writes=[(B_eqb, sl7)])
                pP1, bP1 = self.PB(3, sl7 * 64, sl7 * 64 + NEXP)
                pP2, bP2 = self.PB(3, sl7 * 64 + 16, sl7 * 64 + 16 + NEXP)
                pB1, bB1 = self.PB(3, sl7 * 64 + 32, sl7 * 64 + 32 + NEXP)
                pB2, bB2 = self.PB(3, sl7 * 64 + 48, sl7 * 64 + 48 + NEXP)
                kb.mmg([dict(out=pP1, lhsT=self.ustrict[:],
                             rhs=eqb[:, sl7, 0:NEXP],
                             reads=[self.B_const, (B_eqb, sl7)],
                             start=True, stop=True)], writes=[bP1])
                kb.mmg([dict(out=pP2, lhsT=self.ustrict[:],
                             rhs=eqb[:, sl7, NEXP:2 * NEXP],
                             reads=[self.B_const, (B_eqb, sl7)],
                             start=True, stop=True)], writes=[bP2])
                kb.mmg([dict(out=pB1, lhsT=self.ones128f[:],
                             rhs=base_r[0:1, 0:NEXP],
                             reads=[self.B_const, B_base],
                             start=True, stop=True)], writes=[bB1])
                kb.mmg([dict(out=pB2, lhsT=self.ones128f[:],
                             rhs=base_r[0:1, NEXP:2 * NEXP],
                             reads=[self.B_const, B_base],
                             start=True, stop=True)], writes=[bB2])
                p1f = posf[:, sl7, 0:8]
                p2f = posf[:, sl7, 8:16]
                kb.op("act", "activation", p1f, pP1, AF.Copy, bias=0.0,
                      scale=1.0, reads=[bP1], writes=[(B_posf, sl7)])
                kb.op("dve", "tensor_add", p1f, p1f, pB1,
                      reads=[(B_posf, sl7), bB1], writes=[(B_posf, sl7)])
                kb.op("act", "activation", p2f, pP2, AF.Copy, bias=0.0,
                      scale=1.0, reads=[bP2], writes=[(B_posf, sl7)])
                kb.op("dve", "tensor_add", p2f, p2f, pB2,
                      reads=[(B_posf, sl7), bB2], writes=[(B_posf, sl7)])
                pC, bC = self.PB(2, sl7 * 32, sl7 * 32 + 16, p0=0, p1=1)
                kb.mmg([dict(out=pC, lhsT=self.onescol[:],
                             rhs=eqb[:, sl7, 0:16],
                             reads=[self.B_const, (B_eqb, sl7)],
                             start=True, stop=True)], writes=[bC])
                kb.op("dve", "tensor_add", base_r[0:1, 0:16],
                      base_r[0:1, 0:16], pC,
                      reads=[B_base, bC], writes=[B_base])
                t8 = posf[:, sl7, 16:24]
                e1c = posf[:, sl7, 24:25]
                e2c = posf[:, sl7, 25:26]
                q1 = posf[:, sl7, 26:27]
                q2 = posf[:, sl7, 27:28]
                ov = posf[:, sl7, 28:29]
                tg1 = posf[:, sl7, 29:30]
                tg2 = posf[:, sl7, 30:31]
                kb.op("dve", "tensor_mul", t8, eq1, self.erow[:],
                      reads=[(B_eqf, sl7), self.B_const],
                      writes=[(B_posf, sl7)])
                kb.op("dve", "reduce_sum", e1c, t8, axis=AX,
                      reads=[(B_posf, sl7)], writes=[(B_posf, sl7)])
                kb.op("dve", "tensor_mul", t8, eq2, self.erow[:],
                      reads=[(B_eqf, sl7), self.B_const],
                      writes=[(B_posf, sl7)])
                kb.op("dve", "reduce_sum", e2c, t8, axis=AX,
                      reads=[(B_posf, sl7)], writes=[(B_posf, sl7)])
                kb.op("dve", "tensor_mul", t8, eq1, p1f,
                      reads=[(B_eqf, sl7), (B_posf, sl7)],
                      writes=[(B_posf, sl7)])
                kb.op("dve", "reduce_sum", q1, t8, axis=AX,
                      reads=[(B_posf, sl7)], writes=[(B_posf, sl7)])
                kb.op("dve", "tensor_mul", t8, eq2, p2f,
                      reads=[(B_eqf, sl7), (B_posf, sl7)],
                      writes=[(B_posf, sl7)])
                kb.op("dve", "reduce_sum", q2, t8, axis=AX,
                      reads=[(B_posf, sl7)], writes=[(B_posf, sl7)])
                kb.op("dve", "scalar_tensor_tensor", tg1, e1c, float(CE), q1,
                      ALU.mult, ALU.add, reads=[(B_posf, sl7)],
                      writes=[(B_posf, sl7)])
                kb.op("dve", "tensor_scalar_add", tg1, tg1, float(th * 3072),
                      reads=[(B_posf, sl7)], writes=[(B_posf, sl7)])
                kb.op("dve", "tensor_scalar", ov, q1, float(C2), None,
                      ALU.is_ge, reads=[(B_posf, sl7)], writes=[(B_posf, sl7)])
                kb.op("dve", "scalar_tensor_tensor", tg1, ov, 1e9, tg1,
                      ALU.mult, ALU.add, reads=[(B_posf, sl7)],
                      writes=[(B_posf, sl7)])
                kb.op("dve", "scalar_tensor_tensor", tg2, e2c, float(CE), q2,
                      ALU.mult, ALU.add, reads=[(B_posf, sl7)],
                      writes=[(B_posf, sl7)])
                kb.op("dve", "tensor_scalar_add", tg2, tg2,
                      float(C2 + th * 3072),
                      reads=[(B_posf, sl7)], writes=[(B_posf, sl7)])
                kb.op("dve", "tensor_scalar", ov, q2, float(C2), None,
                      ALU.is_ge, reads=[(B_posf, sl7)], writes=[(B_posf, sl7)])
                kb.op("dve", "scalar_tensor_tensor", tg2, ov, 1e9, tg2,
                      ALU.mult, ALU.add, reads=[(B_posf, sl7)],
                      writes=[(B_posf, sl7)])
                kb.op("dve", "tensor_copy", tgti[:, sl7, 0:1], tg1,
                      reads=[(B_posf, sl7)], writes=[(B_tg, sl7)])
                kb.op("dve", "tensor_copy", tgti[:, sl7, 1:2], tg2,
                      reads=[(B_posf, sl7)], writes=[(B_tg, sl7)])
                kb.op("pool", "iota", tki[:, sl7, :], [[1, 1]], base=i * 128,
                      channel_multiplier=1, writes=[(B_tk, sl7)])
                seg = self.scr_idx[b][:].rearrange("(a b) -> a b", b=1)
                kb.idma(self.s_ix[0], seg,
                        IOA(tgti[:, sl7, 0:1], 0), tki[:, sl7, :], None,
                        bounds_check=self.rg_idx, oob_is_err=False,
                        reads=[(B_tg, sl7), (B_tk, sl7)],
                        writes=[(self.B_scr_idx[b], th)])
                kb.idma(self.s_ix[1], seg,
                        IOA(tgti[:, sl7, 1:2], 0), tki[:, sl7, :], None,
                        bounds_check=self.rg_idx, oob_is_err=False,
                        reads=[(B_tg, sl7), (B_tk, sl7)],
                        writes=[(self.B_scr_idx[b], th)])

        for tt in range(NTPT):
            compact_tile(0, tt)
        pend = [(th, tt) for th in range(1, NTH7) for tt in range(NTPT)]

        self.kb.tag = 'p7moe'
        pairs = [(th, e) for th in range(NTH7) for e in range(NEXP)]

        def gather_block(k):
            th, e = pairs[k]
            s = k % 2
            for a in range(3):
                kb.dma(self.s_ldt[a % 2],
                       idxt[:, s, a:a + 1],
                       self.scr_idx[b][th * 3072 + e * CE + a * 128:
                                       th * 3072 + e * CE + a * 128 + 128]
                       .rearrange("(p a) -> p a", p=128),
                       reads=[(self.B_scr_idx[b], th)],
                       writes=[(B_ix, s)])
            for a in range(3):
                kb.idma(self.s_gg[a],
                        gtile[:, 3 * s + a, :], None, self.scr_Hn[b][:],
                        IOA(idxt[:, s, a:a + 1], 0),
                        reads=[(B_ix, s),
                               (self.B_scr_Hn[b], list(range(NT + 1)))],
                        writes=[(B_gt, 3 * s + a)])
                kb.dma(self.s_stn2[a % 2],
                       self.scr_gth[s][a * 128:(a + 1) * 128, :],
                       gtile[:, 3 * s + a, :], reads=[(B_gt, 3 * s + a)],
                       writes=[self.B_scrg[s]])
                kb.idma(self.s_ix[s],
                        cgt[:, s, a * NEXP:(a + 1) * NEXP], None,
                        self.scr_cw[b][:],
                        IOA(idxt[:, s, a:a + 1], 0),
                        reads=[(B_ix, s),
                               (self.B_scr_cw[b], list(range(NT + 1)))],
                        writes=[(B_cg, s)])
            for fc in range(8):
                kb.dma(self.s_ldt[fc % 4], gthT[s][fc][0][:],
                       self.scr_gth[s][0:CE, fc * 128:(fc + 1) * 128],
                       reads=[self.B_scrg[s]], writes=[gthT[s][fc][1]],
                       transpose=True)

        gather_block(0)
        for k, (th, e) in enumerate(pairs):
            if k > 0:
                load_fc(e)
            s = k % 2
            if pend:
                compact_tile(*pend.pop(0))
                self.kb.tag = 'p7moe'
            if k + 1 < len(pairs):
                gather_block(k + 1)
            if True:
                for hc in range(16):
                    pp = hc % 2
                    psg, bG = self.PB(pp, 0, CE)
                    items = []
                    for fc in range(8):
                        items.append(dict(
                            out=psg,
                            lhsT=fc1b[fc][0][:, hc * 128:(hc + 1) * 128],
                            rhs=gthT[s][fc][0][:],
                            reads=[fc1b[fc][1], gthT[s][fc][1]],
                            start=(fc == 0), stop=(fc == 7)))
                    kb.mmg(items, writes=[bG])
                    kb.op("act", "activation", relu_s[:, pp, 0:CE], psg,
                          AF.Relu, bias=0.0, scale=1.0,
                          reads=[bG], writes=[(B_rs, pp)])
                    kb.op("dve", "tensor_mul", Gs[hc][0][:],
                          relu_s[:, pp, 0:CE], relu_s[:, pp, 0:CE],
                          reads=[(B_rs, pp)], writes=[Gs[hc][1]])
                for tt3 in range(3):
                    pp = tt3 % 2
                    psm, bM = self.PB(2 + pp, 0, 1024)
                    items = []
                    for gh in range(2):
                        for hc in range(16):
                            items.append(dict(
                                out=psm[:, gh * 512:(gh + 1) * 512],
                                lhsT=Gs[hc][0][:, tt3 * 128:(tt3 + 1) * 128],
                                rhs=fc2b[hc][0][:, gh * 512:(gh + 1) * 512],
                                reads=[Gs[hc][1], fc2b[hc][1]],
                                start=(hc == 0), stop=(hc == 15)))
                    kb.mmg(items, writes=[bM])
                    kb.op("dve", "tensor_scalar_mul", mst[tt3][0][:], psm,
                          cgt[:, s, tt3 * NEXP + e:tt3 * NEXP + e + 1],
                          reads=[bM, (B_cg, s)], writes=[mst[tt3][1]])
                    if tt3 == 0:
                        kb.idma(self.s_sca[0], self.scr_moe[b][:],
                                IOA(idxt[:, s, 0:1], 0),
                                mst[0][0][:], None,
                                bounds_check=self.rg_tok, oob_is_err=False,
                                reads=[mst[0][1], (B_ix, s)],
                                writes=[(self.B_scr_moe[b], e)])
                    elif tt3 == 2:
                        kb.idma(self.s_sca[1], self.scr_moe2[b][:],
                                IOA(idxt[:, s, 2:3], 0),
                                mst[2][0][:], None,
                                bounds_check=self.rg_tok, oob_is_err=False,
                                reads=[mst[2][1], (B_ix, s)],
                                writes=[(self.B_scr_moe2[b], e)])
                    else:
                        kb.op("dve", "tensor_copy", msw[:, 0, :],
                              mst[1][0][0:64, :], reads=[mst[1][1]],
                              writes=[(B_msw, 0)])
                        kb.op("dve", "tensor_copy", msw[:, 1, :],
                              mst[1][0][64:128, :], reads=[mst[1][1]],
                              writes=[(B_msw, 1)])
                        kb.op("dve", "tensor_copy", ixw[:, 0, :],
                              idxt[0:64, s, 1:2], reads=[(B_ix, s)],
                              writes=[(B_ixw, 0)])
                        kb.op("dve", "tensor_copy", ixw[:, 1, :],
                              idxt[64:128, s, 1:2], reads=[(B_ix, s)],
                              writes=[(B_ixw, 1)])
                        kb.idma(self.s_sca[0], self.scr_moe[b][:],
                                IOA(ixw[:, 0, :], 0),
                                msw[:, 0, :], None,
                                bounds_check=self.rg_tok, oob_is_err=False,
                                reads=[(B_msw, 0), (B_ixw, 0)],
                                writes=[(self.B_scr_moe[b], e)])
                        kb.idma(self.s_sca[1], self.scr_moe2[b][:],
                                IOA(ixw[:, 1, :], 0),
                                msw[:, 1, :], None,
                                bounds_check=self.rg_tok, oob_is_err=False,
                                reads=[(B_msw, 1), (B_ixw, 1)],
                                writes=[(self.B_scr_moe2[b], e)])
        for lst in (fc1b, fc2b, Gs, mst, gthT[0], gthT[1]):
            for h_, bf in lst:
                AR.freez(h_)
        for h_ in (gtile, idxt, cgt, base_r, posf, tgti, tki, eqf, eqb,
                   relu_s, msw, ixw):
            AR.freez(h_)
        for h_, bf in HnT:
            AR.freez(h_)

    def p8(self, b):
        self.kb.tag = 'p8'
        kb, T, NT, AR = self.kb, self.T, self.NT, self.AR
        acct, B_acct = AR.alloc(f"acc8{b}", [128, 2, D], F32, units=2)
        mt, B_mt = AR.alloc(f"moe8{b}", [128, 2, D], F32, units=2)
        m2t, B_m2t = AR.alloc(f"moe8b{b}", [128, 2, D], F32, units=2)
        for i in range(NT):
            sl = i % 2
            kb.dma(self.s_ldx[sl], self.xstage[:, sl, :],
                   self.x_d[b, i * 128:(i + 1) * 128, :],
                   reads=[self.B_in], writes=[(self.B_xstage, sl)])
            kb.dma(self.s_acc[sl], acct[:, sl, :],
                   self.scr_acc[b][i * 128:(i + 1) * 128, :],
                   reads=[(self.B_scr_acc[b], i)], writes=[(B_acct, sl)])
            kb.dma(self.s_ldt[sl], mt[:, sl, :],
                   self.scr_moe[b][i * 128:(i + 1) * 128, :],
                   reads=[(self.B_scr_moe[b], list(range(NEXP)))],
                   writes=[(B_mt, sl)])
            kb.dma(self.s_stn2[sl], m2t[:, sl, :],
                   self.scr_moe2[b][i * 128:(i + 1) * 128, :],
                   reads=[(self.B_scr_moe2[b], list(range(NEXP)))],
                   writes=[(B_m2t, sl)])
            kb.op("dve", "tensor_add", acct[:, sl, :], acct[:, sl, :],
                  mt[:, sl, :], reads=[(B_acct, sl), (B_mt, sl)],
                  writes=[(B_acct, sl)])
            kb.op("dve", "tensor_add", acct[:, sl, :], acct[:, sl, :],
                  m2t[:, sl, :], reads=[(B_acct, sl), (B_m2t, sl)],
                  writes=[(B_acct, sl)])
            kb.op("dve", "tensor_add", self.sqs[:, sl, :], acct[:, sl, :],
                  self.xstage[:, sl, :],
                  reads=[(B_acct, sl), (self.B_xstage, sl)],
                  writes=[(self.B_sqs, sl)])
            kb.dma(self.s_out[sl], self.y_d[b, i * 128:(i + 1) * 128, :],
                   self.sqs[:, sl, :], reads=[(self.B_sqs, sl)], writes=[])
        AR.freez(acct)
        AR.freez(mt)
        AR.freez(m2t)

    def batch_front(self, b):
        if b == 0:
            self.p3(b)
        self.p4(b)
        self.p5(b)
        inj = self.p1_emitters(b + 1) if b + 1 < self.BPC else ()
        self.p6(b, inject=inj)


def _host_weights(inputs):
    wq = np.ascontiguousarray(inputs["L0_attn_w_q"], dtype=np.float32)
    wk = np.ascontiguousarray(inputs["L0_attn_w_k"], dtype=np.float32)
    wv = np.ascontiguousarray(inputs["L0_attn_w_v"], dtype=np.float32)
    wqb = np.zeros((8, 128, 128), np.float32)
    wkb = np.zeros((8, 128, 128), np.float32)
    for ic in range(8):
        par, m = ic // 4, ic % 4
        for jj in range(2):
            h = 4 * m + 2 * jj + par
            wqb[ic, jj * 64:(jj + 1) * 64, jj * 64:(jj + 1) * 64] = wq[:, :, h]
            wkb[ic, jj * 64:(jj + 1) * 64, jj * 64:(jj + 1) * 64] = wk[:, :, h]
    kk, dd, hh = np.indices((HD, HD, NH))
    rows_pm = ((hh % 2) * 512 + (hh // 2) * 64 + kk).ravel()
    cols_o = (dd * NH + hh).ravel()
    wv_hm = np.zeros((D, D), np.float32)
    wv_hm[rows_pm, cols_o] = wv.ravel()
    return {
        "wqb": wqb.reshape(1024, 128), "wkb": wkb.reshape(1024, 128),
        "wv_hm": wv_hm,
        "fwq": np.ascontiguousarray(inputs["L0_ffn_wq"], dtype=np.float32),
        "fwk": np.ascontiguousarray(inputs["L0_ffn_wk"], dtype=np.float32),
        "fwv": np.ascontiguousarray(inputs["L0_ffn_wv"], dtype=np.float32),
        "fwo": np.ascontiguousarray(inputs["L0_ffn_wo"], dtype=np.float32),
        "rw": np.ascontiguousarray(inputs["L0_ffn_router_w"], dtype=np.float32),
        "fc1": np.ascontiguousarray(inputs["L0_ffn_exp_fc1"], dtype=np.float32),
        "fc2": np.ascontiguousarray(inputs["L0_ffn_exp_fc2"], dtype=np.float32),
    }


def kernel(**inputs):
    x = np.ascontiguousarray(inputs["L0_attn_x"], dtype=np.float32)
    B, T, d = x.shape
    assert d == D and B % N_CORES == 0
    bpc = B // N_CORES
    wmap = _host_weights(inputs)
    prog = Prog(T=T, BPC=bpc)
    nc = prog.build()
    in_maps = []
    for c in range(N_CORES):
        m = {"x": x[c * bpc:(c + 1) * bpc]}
        m.update(wmap)
        in_maps.append(m)
    res = run_bass_kernel_spmd(nc, in_maps, core_ids=list(range(N_CORES)))
    return np.concatenate([r["y"] for r in res.results], axis=0)


# version 27
# speedup vs baseline: 1.0080x; 1.0080x over previous
import math
from contextlib import ExitStack

import numpy as np

import concourse.bass as bass
import concourse.mybir as mybir
from concourse.bass_utils import run_bass_kernel_spmd

F32 = mybir.dt.float32
BF16 = mybir.dt.bfloat16
AF = mybir.ActivationFunctionType
ALU = mybir.AluOpType
AX = mybir.AxisListType.X

D = 1024
NH = 16
HD = 64
NEXP = 8
EDIM = 2048
EPS = 1e-5
ROPE_BASE = 10000.0
N_CORES = 8
GAMMA1 = 1.0 / 32.0
GAMMA2 = 1.0 / 8.0


def _tok_add(d, tok):
    if tok is None:
        return
    k = id(tok[0])
    old = d.get(k)
    if old is None or old[1] < tok[1]:
        d[k] = tok


def _tok_merge(d, other):
    for t in other.values():
        _tok_add(d, t)


class Buf:

    def __init__(self, n=1, carry=None):
        self.w = [None] * n
        self.r = [dict() for _ in range(n)]
        self.carry = dict(carry) if carry else {}


def _units(spec):
    if isinstance(spec, Buf):
        return spec, [0]
    b, u = spec
    if isinstance(u, (list, tuple, range)):
        return b, list(u)
    return b, [u]


class Slot:

    def __init__(self, kb, name):
        self.sem = kb.es.enter_context(kb.nc.semaphore(name))
        self.rounds = 0
        self.tok = None


class KB:
    def __init__(self, nc, es):
        self.nc = nc
        self.es = es
        self.raw = {"pe": nc.tensor, "dve": nc.vector, "act": nc.scalar,
                    "pool": nc.gpsimd, "sp": nc.sync}
        self.sem = {}
        self.cnt = {}
        self.marks = {}
        for e in self.raw:
            self.sem[e] = es.enter_context(nc.semaphore("sem_" + e))
            self.cnt[e] = 0
            self.marks[e] = {}
        self.tag = "init"
        self.tagmap = {}

    def slot(self, name):
        return Slot(self, name)

    def _wait(self, eng, tok):
        if tok is None:
            return
        sem, val = tok
        wm = self.marks[eng]
        key = id(sem)
        if wm.get(key, -1) >= val:
            return
        self.raw[eng].wait_ge(sem, val)
        wm[key] = val

    def _deps(self, eng, reads, writes):
        for spec in reads:
            b, us = _units(spec)
            for u in us:
                if b.w[u] is None:
                    for t in b.carry.values():
                        self._wait(eng, t)
                self._wait(eng, b.w[u])
        for spec in writes:
            b, us = _units(spec)
            for u in us:
                if b.w[u] is None:
                    for t in b.carry.values():
                        self._wait(eng, t)
                self._wait(eng, b.w[u])
                for t in b.r[u].values():
                    self._wait(eng, t)

    def _record(self, tok, reads, writes):
        for spec in reads:
            b, us = _units(spec)
            for u in us:
                _tok_add(b.r[u], tok)
        for spec in writes:
            b, us = _units(spec)
            for u in us:
                b.w[u] = tok
                b.r[u] = {}

    def op(self, eng, method, *args, reads=(), writes=(), **kw):
        self._deps(eng, reads, writes)
        inst = getattr(self.raw[eng], method)(*args, **kw)
        self.cnt[eng] += 1
        tok = (self.sem[eng], self.cnt[eng])
        inst.then_inc(self.sem[eng], 1)
        self._record(tok, reads, writes)
        return tok

    def mmg(self, items, writes):
        self._deps("pe", [], writes)
        all_reads = []
        last = None
        for it in items:
            self._deps("pe", it.get("reads", ()), [])
            all_reads.extend(it.get("reads", ()))
            kw = {}
            if it.get("tile_position") is not None:
                kw["tile_position"] = it["tile_position"]
            last = self.raw["pe"].matmul(it["out"], it["lhsT"], it["rhs"],
                                         start=it["start"], stop=it["stop"], **kw)
            self.tagmap[last.ins.name] = self.tag
        self.cnt["pe"] += 1
        tok = (self.sem["pe"], self.cnt["pe"])
        last.then_inc(self.sem["pe"], 1)
        self._record(tok, all_reads, writes)
        return tok

    def idma(self, slot, out_ap, out_off, in_ap, in_off, reads=(), writes=(),
             **kw):
        self._deps("pool", reads, writes)
        self._wait("pool", slot.tok)
        inst = self.raw["pool"].indirect_dma_start(out_ap, out_off, in_ap,
                                                   in_off, **kw)
        inst.then_inc(slot.sem, 16)
        slot.rounds += 1
        tok = (slot.sem, 16 * slot.rounds)
        slot.tok = tok
        self._record(tok, reads, writes)
        return tok

    def dma(self, slot, out_ap, in_ap, reads=(), writes=(), eng="sp",
            transpose=False):
        self._deps(eng, reads, writes)
        self._wait(eng, slot.tok)
        if transpose:
            inst = self.raw[eng].dma_start_transpose(out_ap, in_ap)
        else:
            inst = self.raw[eng].dma_start(out_ap, in_ap)
        inst.then_inc(slot.sem, 16)
        slot.rounds += 1
        tok = (slot.sem, 16 * slot.rounds)
        slot.tok = tok
        self._record(tok, reads, writes)
        return tok


class Arena:

    def __init__(self, nc, base, cap):
        self.nc = nc
        self.base = base
        self.free_list = [(base, cap, {})]
        self.live = {}
        self.uid = 0

    def alloc(self, name, shape, dtype, units=1, carry=None):
        per_part = int(np.prod(shape[1:])) * mybir.dt.size(dtype)
        per_part = (per_part + 31) & ~31
        self.uid += 1
        nm = f"{name}_u{self.uid}"
        for i, (off, size, toks) in enumerate(self.free_list):
            if size >= per_part:
                h = self.nc.alloc_sbuf_tensor_at(nm, list(shape), dtype, offset=off)
                if size == per_part:
                    self.free_list.pop(i)
                else:
                    self.free_list[i] = (off + per_part, size - per_part,
                                         dict(toks))
                cd = dict(toks)
                if carry:
                    _tok_merge(cd, carry)
                buf = Buf(units, carry=cd)
                self.live[id(h)] = (off, per_part, buf)
                return h, buf
        raise MemoryError(f"arena OOM for {name}: need {per_part}B/p free="
                          f"{[(o, s) for o, s, _ in self.free_list]}")

    def freez(self, h):
        off, size, buf = self.live.pop(id(h))
        toks = dict(buf.carry)
        for t in buf.w:
            _tok_add(toks, t)
        for rl in buf.r:
            _tok_merge(toks, rl)
        self.free_list.append((off, size, toks))
        self.free_list.sort(key=lambda s: s[0])
        out = []
        for seg in self.free_list:
            if out and out[-1][0] + out[-1][1] == seg[0]:
                merged = dict(out[-1][2])
                _tok_merge(merged, seg[2])
                out[-1] = (out[-1][0], out[-1][1] + seg[1], merged)
            else:
                out.append((seg[0], seg[1], dict(seg[2])))
        self.free_list = out
        return toks


class Prog:
    def __init__(self, T=2048, BPC=2, debug=False):
        self.T = T
        self.BPC = BPC
        self.debug = debug
        self.NT = T // 128
        self.nc = bass.Bass("TRN2", target_bir_lowering=False)

    def sb(self, name, shape, dtype):
        return self.es.enter_context(self.nc.sbuf_tensor(name, list(shape), dtype))

    def PB(self, i, lo, hi, p0=0, p1=128):
        h, bf = self.pb[i]
        return h[p0:p1, lo:hi], (bf, list(range(lo // 512, (hi + 511) // 512)))

    def build(self):
        nc, T, NT, BPC = self.nc, self.T, self.NT, self.BPC
        self.TQ = min(512, T)
        self.NTQ = T // self.TQ

        self.x_d = nc.dram_tensor("x", [BPC, T, D], F32, kind="ExternalInput").ap()
        wnames = ["wv_hm", "fwq", "fwk", "fwv", "fwo"]
        self.w_d = {n: nc.dram_tensor(n, [D, D], F32, kind="ExternalInput").ap()
                    for n in wnames}
        for n in ["wqb", "wkb"]:
            self.w_d[n] = nc.dram_tensor(n, [D, 128], F32,
                                         kind="ExternalInput").ap()
        self.rw_d = nc.dram_tensor("rw", [D, NEXP], F32, kind="ExternalInput").ap()
        self.fc1_d = nc.dram_tensor("fc1", [NEXP, D, EDIM], F32, kind="ExternalInput").ap()
        self.fc2_d = nc.dram_tensor("fc2", [NEXP, EDIM, D], F32, kind="ExternalInput").ap()
        self.y_d = nc.dram_tensor("y", [BPC, T, D], F32, kind="ExternalOutput").ap()

        self.scr_xn = [nc.dram_tensor(f"scr_xn{b}", [T, D], BF16).ap() for b in range(BPC)]
        self.scr_hn = [nc.dram_tensor(f"scr_hn{b}", [T, D], BF16).ap() for b in range(BPC)]
        self.scr_Hn = [nc.dram_tensor(f"scr_Hn{b}", [T + 1, D], BF16).ap() for b in range(BPC)]
        self.scr_cw = [nc.dram_tensor(f"scr_cw{b}", [T + 1, NEXP], F32).ap() for b in range(BPC)]
        self.scr_idx = [nc.dram_tensor(f"scr_idx{b}", [max(1, T // 1024) * 8 * 384], mybir.dt.int32).ap() for b in range(BPC)]
        self.scr_gth = [nc.dram_tensor(f"scr_gth{s}", [384, D], BF16).ap() for s in range(2)]
        self.scr_acc = [nc.dram_tensor(f"scr_acc{b}", [T, D], F32).ap() for b in range(BPC)]
        self.scr_moe = [nc.dram_tensor(f"scr_moe{b}", [T + 1, D], F32).ap() for b in range(BPC)]
        self.scr_moe2 = [nc.dram_tensor(f"scr_moe2_{b}", [T + 1, D], F32).ap() for b in range(BPC)]
        self.scr_tab = nc.dram_tensor("scr_tab", [4, 2, 128, T], F32).ap()
        self.scr_w = {n: nc.dram_tensor("scrb_" + n, [D, D], BF16).ap() for n in wnames}
        for n in ["wqb", "wkb"]:
            self.scr_w[n] = nc.dram_tensor("scrb_" + n, [D, 128], BF16).ap()
        self.scr_rw = nc.dram_tensor("scrb_rw", [D, NEXP], BF16).ap()
        self.scr_fc1 = nc.dram_tensor("scrb_fc1", [NEXP, D, EDIM], BF16).ap()
        self.scr_fc2 = nc.dram_tensor("scrb_fc2", [NEXP, EDIM, D], BF16).ap()
        if self.debug:
            self.dbg_y = nc.dram_tensor("dbg_y", [BPC, T, D], F32, kind="ExternalOutput").ap()
            self.dbg_h = nc.dram_tensor("dbg_h", [BPC, T, D], F32, kind="ExternalOutput").ap()
            self.dbg_c = nc.dram_tensor("dbg_c", [BPC, T, NEXP], F32, kind="ExternalOutput").ap()
            self.dbg_moe = nc.dram_tensor("dbg_moe", [BPC, T, D], F32, kind="ExternalOutput").ap()

        self.es = ExitStack()
        es = self.es
        kb = KB(nc, es)
        self.kb = kb
        self._pre3 = None

        self.xstage = self.sb("xstage", [128, 2, D], F32)
        self.nstage = self.sb("nstage", [128, 2, D], BF16)
        self.n2stage = self.sb("n2stage", [128, 2, D], BF16)
        self.sqs = self.sb("sqs", [128, 2, D], F32)
        self.st_m = [self.sb(f"st_m{b}", [128, NT], F32) for b in range(BPC)]
        self.st_rstd = [self.sb(f"st_rstd{b}", [128, NT], F32) for b in range(BPC)]
        self.st_b1 = [self.sb(f"st_b1{b}", [128, NT], F32) for b in range(BPC)]
        self.st_s2 = [self.sb(f"st_s2{b}", [128, NT], F32) for b in range(BPC)]
        self.st_s3 = [self.sb(f"st_s3{b}", [128, NT], F32) for b in range(BPC)]
        self.st_t = self.sb("st_t", [128, 2, 8], F32)
        self.cw = [self.sb(f"cw{b}", [128, NT, NEXP], F32) for b in range(BPC)]
        self.lgt = self.sb("lgt", [128, 2, NEXP], F32)
        self.lgt2 = self.sb("lgt2", [128, 2, NEXP], F32)
        self.lgt3 = self.sb("lgt3", [128, 2, NEXP], F32)
        self.epsv = self.sb("epsv", [128, 1], F32)
        self.pi2 = self.sb("pi2", [128, 1], F32)
        self.ones64 = self.sb("ones64", [1, 64], F32)
        self.iotv = self.sb("iotv", [128, 4], F32)
        self.invf = self.sb("invf", [128, 1], F32)
        self.rrow = self.sb("rrow", [1, 2, self.TQ], F32)
        self.rinv = self.sb("rinv", [1, 2, self.TQ], F32)
        self.ysc = self.sb("ysc", [64, 2, self.TQ], F32)
        self.rwb = self.sb("rwb", [128, 8, NEXP], BF16)
        self.trimask = self.sb("trimask", [128, 128], BF16)
        self.ustrict = self.sb("ustrict", [128, 128], BF16)
        self.erow = self.sb("erow", [128, NEXP], F32)
        self.ones128f = self.sb("ones128f", [1, 128], F32)
        self.onescol = self.sb("onescol", [128, 1], BF16)
        self.senti = self.sb("senti", [128, 24], mybir.dt.int32)
        self.zrow = self.sb("zrow", [128, D], F32)

        self.B_xstage = Buf(2)
        self.B_nstage = Buf(2)
        self.B_n2stage = Buf(2)
        self.B_sqs = Buf(2)
        self.B_stats = [Buf(NT) for _ in range(BPC)]
        self.B_s3 = [Buf(NT) for _ in range(BPC)]
        self.B_stt = Buf(2)
        self.B_cw = [Buf(NT) for _ in range(BPC)]
        self.B_lgt = Buf(2)
        self.B_lgt2 = Buf(2)
        self.B_lgt3 = Buf(2)
        self.B_const = Buf(1)
        self.B_rope = Buf(2)
        self.B_rwb = Buf(1)
        self.B_in = Buf(1)
        self.B_scr_xn = [Buf(NT) for _ in range(BPC)]
        self.B_scr_hn = [Buf(NT) for _ in range(BPC)]
        self.B_scr_Hn = [Buf(NT + 1) for _ in range(BPC)]
        self.B_scr_acc = [Buf(NT) for _ in range(BPC)]
        self.B_scr_moe = [Buf(NEXP) for _ in range(BPC)]
        self.B_scr_moe2 = [Buf(NEXP) for _ in range(BPC)]
        self.B_scrg = [Buf(1), Buf(1)]
        self.B_scr_cw = [Buf(NT + 1) for _ in range(BPC)]
        self.B_scr_idx = [Buf(max(1, T // 1024)) for _ in range(BPC)]
        self.B_scr_tab = Buf(4)
        self.B_scr_w = {n: Buf(8) for n in wnames}
        self.B_scr_w["wqb"] = Buf(8)
        self.B_scr_w["wkb"] = Buf(8)
        self.B_scr_rw8 = Buf(8)
        self.B_scr_fc1 = Buf(NEXP * 8)
        self.B_scr_fc2 = Buf(NEXP * 16)

        self.s_ldx = [kb.slot("s_ldx0"), kb.slot("s_ldx1")]
        self.s_stn = [kb.slot("s_stn0"), kb.slot("s_stn1")]
        self.s_stn2 = [kb.slot("s_stn2a"), kb.slot("s_stn2b")]
        self.s_stn3 = [kb.slot("s_stn3a"), kb.slot("s_stn3b")]
        self.s_ldt = [kb.slot(f"s_ldt{i}") for i in range(4)]
        self.s_ldw = [kb.slot("s_ldw0"), kb.slot("s_ldw1")]
        self.s_stw = [kb.slot("s_stw0"), kb.slot("s_stw1")]
        self.s_acc = [kb.slot("s_acc0"), kb.slot("s_acc1")]
        self.s_acc2 = [kb.slot("s_acc2a"), kb.slot("s_acc2b")]
        self.s_out = [kb.slot("s_out0"), kb.slot("s_out1")]
        self.s_tab = kb.slot("s_tab")
        self.s_fc1 = [kb.slot(f"s_fc1_{i}") for i in range(4)]
        self.s_fc2 = [kb.slot(f"s_fc2_{i}") for i in range(4)]
        self.s_pc = [kb.slot(f"s_pc_{i}") for i in range(9)]
        self.s_ix = [kb.slot(f"s_ix_{i}") for i in range(2)]
        self.s_gg = [kb.slot(f"s_gg_{i}") for i in range(3)]
        self.s_sca = [kb.slot(f"s_sca_{i}") for i in range(2)]
        self.s_zz = kb.slot("s_zz")
        self.s_dbg = kb.slot("s_dbg")

        self.rg_idx = nc.gpsimd.alloc_register("bc_idx")
        nc.gpsimd.reg_mov(self.rg_idx, max(1, self.T // 1024) * 3072 - 1)
        self.rg_tok = nc.gpsimd.alloc_register("bc_tok")
        nc.gpsimd.reg_mov(self.rg_tok, self.T)

        self.pb = []
        for i in range(4):
            h = es.enter_context(nc.psum_tensor(f"pb{i}", [128, 1024], F32))
            self.pb.append((h, Buf(2)))

        static_end = nc.SBUF_PARTITION_SIZE_BYTES - nc.sbuf_bytes_remaining
        self.AR = Arena(nc, static_end + 64, nc.sbuf_top - static_end - 128)

        with es:
            self.p0_consts()
            self.p0_weights()
            self.p1(0)
            self.p0_tables()
            for b in range(self.BPC):
                self.batch_front(b)
                self.p7(b)
                if b + 1 < self.BPC:
                    self.p3(b + 1)
                self.p8(b)
            for sl in self.s_out:
                if sl.tok is not None:
                    kb.raw["sp"].wait_ge(sl.sem, 16 * sl.rounds)
        return nc

    def p0_consts(self):
        self.kb.tag = 'p0t'
        kb = self.kb
        kb.op("pool", "memset", self.epsv[:], EPS, writes=[self.B_const])
        kb.op("pool", "memset", self.pi2[:], float(np.pi / 2), writes=[self.B_const])
        kb.op("pool", "memset", self.ones64[:], 1.0, writes=[self.B_const])
        kb.op("pool", "memset", self.trimask[:], 1.0, writes=[self.B_const])
        kb.op("pool", "affine_select", self.trimask[:], self.trimask[:],
              [[1, 128]], ALU.is_ge, 0.0, base=0, channel_multiplier=-1,
              writes=[self.B_const])
        kb.op("pool", "memset", self.ustrict[:], 1.0, writes=[self.B_const])
        kb.op("pool", "affine_select", self.ustrict[:], self.ustrict[:],
              [[1, 128]], ALU.is_ge, 0.0, base=-1, channel_multiplier=-1,
              writes=[self.B_const])
        kb.op("pool", "iota", self.erow[:], [[1, NEXP]], base=0,
              channel_multiplier=0, allow_small_or_imprecise_dtypes=True,
              writes=[self.B_const])
        kb.op("pool", "memset", self.ones128f[:], 1.0, writes=[self.B_const])
        kb.op("pool", "memset", self.onescol[:], 1.0, writes=[self.B_const])
        kb.op("pool", "memset", self.senti[:], self.T, writes=[self.B_const])
        kb.op("pool", "memset", self.zrow[:], 0.0, writes=[self.B_const])

    def p0_tables(self):
        self.kb.tag = 'p0t'
        kb, nc, T = self.kb, self.nc, self.T
        AR = self.AR
        tabc, B_tc = AR.alloc("tabc", [128, T], F32)
        tabs, B_ts = AR.alloc("tabs", [128, T], F32)
        scr, B_sc = AR.alloc("tabscr", [128, T], F32)
        for cp in range(4):
            kb.op("pool", "iota", self.iotv[:, 0:1], [[1, 1]], base=2 * cp,
                  channel_multiplier=8, allow_small_or_imprecise_dtypes=True,
                  writes=[self.B_const])
            kb.op("pool", "memset", self.iotv[:, 1:2], 1.0, writes=[self.B_const])
            kb.op("pool", "affine_select", self.iotv[:, 1:2], self.iotv[:, 1:2],
                  [[1, 1]], ALU.is_ge, 0.0, base=-64, channel_multiplier=1,
                  writes=[self.B_const])
            kb.op("dve", "scalar_tensor_tensor", self.iotv[:, 2:3],
                  self.iotv[:, 1:2], -511.0, self.iotv[:, 0:1], ALU.mult, ALU.add,
                  reads=[self.B_const], writes=[self.B_const])
            kb.op("dve", "tensor_scalar_mul", self.iotv[:, 3:4], self.iotv[:, 2:3],
                  -math.log(ROPE_BASE) / 512.0, reads=[self.B_const],
                  writes=[self.B_const])
            kb.op("act", "activation", self.invf[:], self.iotv[:, 3:4], AF.Exp,
                  bias=0.0, scale=1.0, reads=[self.B_const], writes=[self.B_const])
            kb.op("pool", "memset", tabc[:, 0:1], 1.0, writes=[B_tc])
            kb.op("pool", "memset", tabs[:, 0:1], 0.0, writes=[B_ts])
            if T > 1:
                kb.op("act", "activation", tabs[:, 1:2], self.invf[:], AF.Sin,
                      bias=0.0, scale=1.0, reads=[self.B_const], writes=[B_ts])
                kb.op("act", "activation", tabc[:, 1:2], self.invf[:], AF.Sin,
                      bias=self.pi2[:], scale=1.0, reads=[self.B_const], writes=[B_tc])
            s = 2
            while s < T:
                ext = s - 1
                w = min(s - 1, T - s)
                ce = tabc[:, ext:ext + 1]
                se = tabs[:, ext:ext + 1]
                kb.op("dve", "tensor_scalar_mul", scr[:, 0:w], tabs[:, 1:1 + w],
                      se, reads=[B_ts], writes=[B_sc])
                kb.op("dve", "scalar_tensor_tensor", tabc[:, s:s + w],
                      tabc[:, 1:1 + w], ce, scr[:, 0:w], ALU.mult, ALU.subtract,
                      reads=[B_sc, B_tc], writes=[B_tc])
                kb.op("dve", "tensor_scalar_mul", scr[:, 0:w], tabc[:, 1:1 + w],
                      se, reads=[B_tc], writes=[B_sc])
                kb.op("dve", "scalar_tensor_tensor", tabs[:, s:s + w],
                      tabs[:, 1:1 + w], ce, scr[:, 0:w], ALU.mult, ALU.add,
                      reads=[B_sc, B_ts], writes=[B_ts])
                s = s + w
            kb.dma(self.s_tab, self.scr_tab[cp, 0], tabc[:], reads=[B_tc],
                   writes=[(self.B_scr_tab, cp)])
            kb.dma(self.s_tab, self.scr_tab[cp, 1], tabs[:], reads=[B_ts],
                   writes=[(self.B_scr_tab, cp)])
        for t in AR.freez(tabc):
            pass
        AR.freez(tabs)
        AR.freez(scr)

    def p0_weights(self):
        kb = self.kb
        semB = self.s_pc[8]

        def cast_dma(slot, src_ap, dst_ap, rows, cols):
            n = 0
            for r0 in range(0, rows, 256):
                rr = min(256, rows - r0)
                inst = kb.raw["pool"].dma_start(dst_ap[r0:r0 + rr, 0:cols],
                                                src_ap[r0:r0 + rr, 0:cols])
                inst.then_inc(slot.sem, 16)
                slot.rounds += 1
                n += 1
            return n

        for wi, n_ in enumerate(["wqb", "wkb", "wv_hm", "fwq", "fwk",
                                 "fwv", "fwo"]):
            sl_ = self.s_pc[wi]
            cast_dma(sl_, self.w_d[n_], self.scr_w[n_], D,
                     128 if n_ in ("wqb", "wkb") else D)
            tokn = (sl_.sem, 16 * sl_.rounds)
            sl_.tok = tokn
            bb = self.B_scr_w[n_]
            for u in range(8):
                bb.w[u] = tokn
        cast_dma(self.s_pc[7], self.rw_d, self.scr_rw, D, NEXP)
        tokA = (self.s_pc[7].sem, 16 * self.s_pc[7].rounds)
        self.s_pc[7].tok = tokA
        for u in range(8):
            self.B_scr_rw8.w[u] = tokA
        for e in range(NEXP):
            cast_dma(semB, self.fc1_d[e], self.scr_fc1[e], D, EDIM)
            cast_dma(semB, self.fc2_d[e], self.scr_fc2[e], EDIM, D)
        tokB = (semB.sem, 16 * semB.rounds)
        semB.tok = tokB
        for u in range(NEXP * 8):
            self.B_scr_fc1.w[u] = tokB
        for u in range(NEXP * 16):
            self.B_scr_fc2.w[u] = tokB

    def p1(self, b):
        for f in self.p1_emitters(b):
            f()

    def p1_emitters(self, b):
        loads = [self._p1_load(b, i) for i in range(self.NT)]
        comps = [self._p1_comp(b, i) for i in range(self.NT)]
        sched = [loads[0]]
        for i in range(self.NT):
            nxt = loads[i + 1] if i + 1 < self.NT else (lambda: None)
            sched.append(lambda a=nxt, c=comps[i]: (a(), c()))
        return sched

    def _p1_load(self, b, i):
        def emit():
            self.kb.tag = 'p1'
            sl = i % 2
            self.kb.dma(self.s_ldx[sl], self.xstage[:, sl, :],
                        self.x_d[b, i * 128:(i + 1) * 128, :],
                        reads=[self.B_in], writes=[(self.B_xstage, sl)])
        return emit

    def _p1_comp(self, b, i):
        def emit():
            self.kb.tag = 'p1'
            kb, T, NT = self.kb, self.T, self.NT
            sl = i % 2
            stt = self.st_t[:, sl, :]
            B_t = (self.B_stt, sl)
            xt = self.xstage[:, sl, :]
            kb.op("dve", "reduce_sum", stt[:, 0:1], xt, axis=AX,
                  reads=[(self.B_xstage, sl)], writes=[B_t])
            kb.op("act", "activation", self.sqs[:, sl, :], xt, AF.Square,
                  bias=0.0, scale=1.0, accum_out=stt[:, 1:2],
                  reads=[(self.B_xstage, sl)], writes=[(self.B_sqs, sl), B_t])
            kb.op("dve", "tensor_scalar_mul", self.st_m[b][:, i:i + 1],
                  stt[:, 0:1], 1.0 / D, reads=[B_t],
                  writes=[(self.B_stats[b], i)])
            kb.op("dve", "tensor_mul", stt[:, 2:3], self.st_m[b][:, i:i + 1],
                  self.st_m[b][:, i:i + 1], reads=[B_t, (self.B_stats[b], i)],
                  writes=[B_t])
            kb.op("dve", "scalar_tensor_tensor", stt[:, 3:4], stt[:, 1:2],
                  1.0 / D, stt[:, 2:3], ALU.mult, ALU.subtract,
                  reads=[B_t], writes=[B_t])
            kb.op("act", "activation", stt[:, 4:5], stt[:, 3:4], AF.Ln,
                  bias=self.epsv[:], scale=1.0, reads=[B_t, self.B_const],
                  writes=[B_t])
            kb.op("act", "activation", self.st_rstd[b][:, i:i + 1], stt[:, 4:5],
                  AF.Exp, bias=0.0, scale=-0.5,
                  reads=[B_t], writes=[(self.B_stats[b], i)])
            kb.op("dve", "tensor_scalar", self.st_b1[b][:, i:i + 1],
                  self.st_m[b][:, i:i + 1], self.st_rstd[b][:, i:i + 1], -1.0,
                  ALU.mult, ALU.mult, reads=[(self.B_stats[b], i)],
                  writes=[(self.B_stats[b], i)])
            kb.op("dve", "tensor_mul", stt[:, 5:6], stt[:, 3:4],
                  self.st_rstd[b][:, i:i + 1],
                  reads=[B_t, (self.B_stats[b], i)], writes=[B_t])
            kb.op("dve", "tensor_mul", stt[:, 6:7], stt[:, 5:6],
                  self.st_rstd[b][:, i:i + 1],
                  reads=[B_t, (self.B_stats[b], i)], writes=[B_t])
            kb.op("act", "activation", stt[:, 7:8], stt[:, 6:7], AF.Ln,
                  bias=self.epsv[:], scale=1.0, reads=[B_t, self.B_const],
                  writes=[B_t])
            kb.op("act", "activation", self.st_s2[b][:, i:i + 1], stt[:, 7:8],
                  AF.Exp, bias=0.0, scale=-0.5,
                  reads=[B_t], writes=[(self.B_stats[b], i)])
            hm_out = bass.AP(self.nstage, sl * D,
                             [[2 * D, 128], [1, 64], [64, 8], [512, 2]])
            kb.op("act", "activation", hm_out, xt, AF.Identity,
                  bias=self.st_b1[b][:, i:i + 1], scale=self.st_rstd[b][:, i:i + 1],
                  reads=[(self.B_xstage, sl), (self.B_stats[b], i)],
                  writes=[(self.B_nstage, sl)])
            kb.op("dve", "tensor_mul", stt[:, 2:3], self.st_rstd[b][:, i:i + 1],
                  self.st_s2[b][:, i:i + 1],
                  reads=[(self.B_stats[b], i)], writes=[B_t])
            kb.op("dve", "tensor_mul", stt[:, 3:4], self.st_b1[b][:, i:i + 1],
                  self.st_s2[b][:, i:i + 1],
                  reads=[(self.B_stats[b], i)], writes=[B_t])
            kb.op("act", "activation", self.n2stage[:, sl, :], xt, AF.Identity,
                  bias=stt[:, 3:4], scale=stt[:, 2:3],
                  reads=[(self.B_xstage, sl), B_t],
                  writes=[(self.B_n2stage, sl)])
            kb.dma(self.s_stn[sl], self.scr_xn[b][i * 128:(i + 1) * 128, :],
                   self.nstage[:, sl, :], reads=[(self.B_nstage, sl)],
                   writes=[(self.B_scr_xn[b], i)])
            kb.dma(self.s_stn2[sl], self.scr_hn[b][i * 128:(i + 1) * 128, :],
                   self.n2stage[:, sl, :], reads=[(self.B_n2stage, sl)],
                   writes=[(self.B_scr_hn[b], i)])

        return emit

    def load_T_strips(self, scr_ap, B_src, name, carryfrom=None):
        T = self.T
        chunks = []
        for c in range(8):
            h, bf = self.AR.alloc(f"{name}{c}", [128, T], BF16, carry=carryfrom)
            self.kb.dma(self.s_ldt[c % 4], h[:],
                        scr_ap[0:T, c * 128:(c + 1) * 128],
                        reads=[(B_src, list(range(self.NT)))], writes=[bf],
                        transpose=True)
            chunks.append((h, bf))
        return chunks

    def p3(self, b):
        self.kb.tag = 'p3'
        kb, T, NT, AR = self.kb, self.T, self.NT, self.AR
        TH = min(1024, T)
        NTH = T // TH
        if self._pre3 is not None:
            xnT, _ = self._pre3
            self._pre3 = None
        else:
            xnT = self.load_T_strips(self.scr_xn[b], self.B_scr_xn[b],
                                     f"xnT{b}")
        qkwh, B_qkw = AR.alloc(f"qkw{b}", [128, 16, 128], BF16, units=16)
        qkw = (qkwh, B_qkw)
        for c in range(8):
            kb.dma(self.s_ldw[c % 2], qkw[0][:, c, :],
                   self.scr_w["wqb"][c * 128:(c + 1) * 128, :],
                   reads=[(self.B_scr_w["wqb"], c)], writes=[(qkw[1], c)])
            kb.dma(self.s_ldw[c % 2], qkw[0][:, 8 + c, :],
                   self.scr_w["wkb"][c * 128:(c + 1) * 128, :],
                   reads=[(self.B_scr_w["wkb"], c)],
                   writes=[(qkw[1], 8 + c)])
        self.xnT = xnT
        tabh = [AR.alloc(f"tab{b}_{i}", [128, TH], F32) for i in range(2)]
        QK = []
        for wi, out_name in enumerate((f"Qh{b}", f"Kh{b}")):
            dst = [AR.alloc(f"{out_name}_{c}", [128, T], BF16) for c in range(8)]
            for cp in range(4):
                for th in range(NTH):
                    pp = (cp * NTH + th) % 2
                    pA, bA = self.PB(pp, 0, TH)
                    pB, bB = self.PB(2 + pp, 0, TH)
                    for gi, (ps_, bs_) in enumerate(((pA, bA), (pB, bB))):
                        c = cp + 4 * gi
                        items = []
                        for q in range(max(1, TH // 512)):
                            w = min(512, TH)
                            items.append(dict(
                                out=ps_[:, q * 512:q * 512 + w],
                                lhsT=qkw[0][:, 8 * wi + c, :],
                                rhs=xnT[c][0][:, th * TH + q * 512:
                                              th * TH + q * 512 + w],
                                reads=[(qkw[1], 8 * wi + c), xnT[c][1]],
                                start=True, stop=True))
                        kb.mmg(items, writes=[bs_])
                    kb.dma(self.s_tab, tabh[0][0][:],
                           self.scr_tab[cp, 0, :, th * TH:(th + 1) * TH],
                           reads=[(self.B_scr_tab, cp)], writes=[tabh[0][1]])
                    kb.dma(self.s_tab, tabh[1][0][:],
                           self.scr_tab[cp, 1, :, th * TH:(th + 1) * TH],
                           reads=[(self.B_scr_tab, cp)], writes=[tabh[1][1]])
                    s1 = self.sqs[:, 0, 0:TH]
                    s2 = self.sqs[:, 1, 0:TH]
                    kb.op("dve", "tensor_mul", s1, pA, tabh[0][0][:],
                          reads=[bA, tabh[0][1]], writes=[(self.B_sqs, 0)])
                    kb.op("dve", "tensor_mul", s2, pB, tabh[1][0][:],
                          reads=[bB, tabh[1][1]], writes=[(self.B_sqs, 1)])
                    kb.op("pool", "tensor_sub",
                          dst[cp][0][:, th * TH:(th + 1) * TH], s1, s2,
                          reads=[(self.B_sqs, 0), (self.B_sqs, 1)],
                          writes=[dst[cp][1]])
                    kb.op("dve", "tensor_mul", s1, pA, tabh[1][0][:],
                          reads=[bA, tabh[1][1]], writes=[(self.B_sqs, 0)])
                    kb.op("dve", "tensor_mul", s2, pB, tabh[0][0][:],
                          reads=[bB, tabh[0][1]], writes=[(self.B_sqs, 1)])
                    kb.op("pool", "tensor_add",
                          dst[cp + 4][0][:, th * TH:(th + 1) * TH], s1, s2,
                          reads=[(self.B_sqs, 0), (self.B_sqs, 1)],
                          writes=[dst[cp + 4][1]])
            QK.append(dst)
        self.Qh, self.Kh = QK
        for h, bf in tabh:
            AR.freez(h)
        AR.freez(qkw[0])
        self.kb.tag = 'p3v'
        wbuf = [AR.alloc(f"wb{b}_{c}", [128, D], BF16) for c in range(8)]
        for c in range(8):
            kb.dma(self.s_ldw[c % 2], wbuf[c][0][:],
                   self.scr_w["wv_hm"][c * 128:(c + 1) * 128, :],
                   reads=[(self.B_scr_w["wv_hm"], c)], writes=[wbuf[c][1]])
        Ve = [AR.alloc(f"Ve{b}_{i}", [128, 1040], BF16) for i in range(NT)]
        self.Ve = Ve
        for i in range(NT):
            pp = i % 2
            pV, bV = self.PB(2 + pp, 0, 1024)
            items = []
            for gh in range(2):
                for fc in range(8):
                    items.append(dict(
                        out=pV[:, gh * 512:(gh + 1) * 512],
                        lhsT=xnT[fc][0][:, i * 128:(i + 1) * 128],
                        rhs=wbuf[fc][0][:, gh * 512:(gh + 1) * 512],
                        reads=[xnT[fc][1], wbuf[fc][1]],
                        start=(fc == 0), stop=(fc == 7)))
            kb.mmg(items, writes=[bV])
            kb.op("act", "activation", Ve[i][0][:, 0:1024], pV,
                  AF.Copy, bias=0.0, scale=1.0,
                  reads=[bV], writes=[Ve[i][1]])
            kb.op("pool", "memset", Ve[i][0][:, 1024:1025], 1.0,
                  writes=[Ve[i][1]])
        for h, bf in wbuf:
            AR.freez(h)
        for h, bf in xnT:
            AR.freez(h)
        self.xnT = None

    def p4(self, b):
        self.kb.tag = 'p4'
        kb, T, NT, AR = self.kb, self.T, self.NT, self.AR
        TQ, NTQ = self.TQ, self.NTQ
        Qh, Kh, Ve = self.Qh, self.Kh, self.Ve
        P = [AR.alloc(f"P{b}_{k}", [128, TQ], BF16) for k in range(NT)]
        if True:
            KTQ = TQ // 128
            for qb in range(NTQ):
                nkt = KTQ * (qb + 1)
                self.kb.tag = 'p4s'
                for kt in range(nkt):
                    off = max(0, kt * 128 - qb * TQ)
                    w = TQ - off
                    pp = kt % 2
                    pS, bS = self.PB(pp, 512, 1024)
                    items = []
                    for gc in range(8):
                        items.append(dict(
                            out=pS[:, off:off + w],
                            lhsT=Kh[gc][0][:, kt * 128:(kt + 1) * 128],
                            rhs=Qh[gc][0][:, qb * TQ + off:qb * TQ + off + w],
                            reads=[Kh[gc][1], Qh[gc][1]],
                            start=(gc == 0), stop=(gc == 7)))
                    kb.mmg(items, writes=[bS])
                    kb.op("act", "activation", P[kt][0][:, off:off + w],
                          pS[:, off:off + w], AF.Exp, bias=0.0, scale=GAMMA1,
                          reads=[bS], writes=[P[kt][1]])
                    if kt * 128 >= qb * TQ:
                        kb.op("dve", "tensor_mul", P[kt][0][:, off:off + 128],
                              P[kt][0][:, off:off + 128], self.trimask[:],
                              reads=[P[kt][1], self.B_const], writes=[P[kt][1]])
                self.kb.tag = 'p4av'
                for qi in range(qb * TQ // 128, (qb + 1) * TQ // 128):
                    pp = qi % 2
                    pY, bY = self.PB(2 + pp, 0, 1024)
                    pR, bR = self.PB(pp, 0, 16)
                    items = []
                    for kt in range(qi + 1):
                        off = qi * 128 - qb * TQ
                        stat = P[kt][0][:, off:off + 128]
                        for gh in range(2):
                            items.append(dict(
                                out=pY[:, gh * 512:(gh + 1) * 512],
                                lhsT=stat,
                                rhs=Ve[kt][0][:, gh * 512:(gh + 1) * 512],
                                reads=[P[kt][1], Ve[kt][1]],
                                start=(kt == 0), stop=(kt == qi)))
                        items.append(dict(
                            out=pR[:, 0:1], lhsT=stat,
                            rhs=Ve[kt][0][:, 1024:1025],
                            reads=[P[kt][1], Ve[kt][1]],
                            start=(kt == 0), stop=(kt == qi)))
                    kb.mmg(items, writes=[bY, bR])
                    kb.op("dve", "reciprocal", self.st_t[:, pp, 0:1], pR[:, 0:1],
                          reads=[bR], writes=[(self.B_stt, pp)])
                    ot = self.sqs[:, pp, :]
                    kb.op("act", "activation", ot, pY, AF.Identity,
                          bias=0.0, scale=self.st_t[:, pp, 0:1],
                          reads=[bY, (self.B_stt, pp)],
                          writes=[(self.B_sqs, pp)])
                    kb.dma(self.s_acc[pp], self.scr_acc[b][qi * 128:(qi + 1) * 128, :],
                           ot, reads=[(self.B_sqs, pp)],
                           writes=[(self.B_scr_acc[b], qi)])
                    if self.debug:
                        kb.dma(self.s_dbg, self.dbg_y[b, qi * 128:(qi + 1) * 128, :],
                               ot, reads=[(self.B_sqs, pp)], writes=[])
        for h, bf in P:
            AR.freez(h)
        for lst in (self.Qh, self.Kh, self.Ve):
            for h, bf in lst:
                AR.freez(h)
        self.Qh = self.Kh = self.Ve = None

    def p5(self, b):
        self.kb.tag = 'p5'
        kb, T, NT, AR = self.kb, self.T, self.NT, self.AR
        hnT = self.load_T_strips(self.scr_hn[b], self.B_scr_hn[b], f"hnT{b}")
        wsets = [[AR.alloc(f"wf{b}_{s}_{c}", [128, D], BF16) for c in range(8)]
                 for s in range(2)]
        TH = min(1024, T)
        NTH = T // TH
        QK = []
        for wi, (wname, oname) in enumerate(
                (("fwq", f"Q2T{b}"), ("fwk", f"K2T{b}"))):
            wbuf = wsets[wi % 2]
            dst = [AR.alloc(f"{oname}_{c}", [128, T], BF16) for c in range(8)]
            for c in range(8):
                kb.dma(self.s_ldw[c % 2], wbuf[c][0][:],
                       self.scr_w[wname][c * 128:(c + 1) * 128, :],
                       reads=[(self.B_scr_w[wname], c)], writes=[wbuf[c][1]])
            if True:
                for gc in range(8):
                    for th in range(NTH):
                        pp = (gc * NTH + th) % 2
                        pQ, bQ = self.PB(pp, 0, TH)
                        items = []
                        for q in range(max(1, TH // 512)):
                            w = min(512, TH)
                            for fc in range(8):
                                items.append(dict(
                                    out=pQ[:, q * 512:q * 512 + w],
                                    lhsT=wbuf[fc][0][:, gc * 128:(gc + 1) * 128],
                                    rhs=hnT[fc][0][:, th * TH + q * 512:
                                                    th * TH + q * 512 + w],
                                    reads=[wbuf[fc][1], hnT[fc][1]],
                                    start=(fc == 0), stop=(fc == 7)))
                        kb.mmg(items, writes=[bQ])
                        kb.op("act", "activation",
                              dst[gc][0][:, th * TH:(th + 1) * TH], pQ,
                              AF.Copy, bias=0.0, scale=1.0,
                              reads=[bQ], writes=[dst[gc][1]])
            QK.append(dst)
        self.Q2T, self.K2T = QK
        wbuf = wsets[0]
        for c in range(8):
            kb.dma(self.s_ldw[c % 2], wbuf[c][0][:],
                   self.scr_w["fwv"][c * 128:(c + 1) * 128, :],
                   reads=[(self.B_scr_w["fwv"], c)], writes=[wbuf[c][1]])
        V2e = [AR.alloc(f"V2e{b}_{i}", [128, 1040], BF16) for i in range(NT)]
        self.V2e = V2e
        if True:
            for i in range(NT):
                for gh in range(2):
                    pp = (2 * i + gh) % 2
                    pV, bV = self.PB(2, pp * 512, pp * 512 + 512)
                    items = []
                    for fc in range(8):
                        items.append(dict(
                            out=pV,
                            lhsT=hnT[fc][0][:, i * 128:(i + 1) * 128],
                            rhs=wbuf[fc][0][:, gh * 512:(gh + 1) * 512],
                            reads=[hnT[fc][1], wbuf[fc][1]],
                            start=(fc == 0), stop=(fc == 7)))
                    kb.mmg(items, writes=[bV])
                    dst_ap = bass.AP(V2e[i][0], gh * 8 * 65,
                                     [[1040, 128], [65, 8], [1, 64]])
                    kb.op("act", "activation", dst_ap, pV, AF.Copy,
                          bias=0.0, scale=1.0, reads=[bV],
                          writes=[V2e[i][1]])
                ones_ap = bass.AP(V2e[i][0], 64, [[1040, 128], [65, 16]])
                kb.op("pool", "memset", ones_ap, 1.0, writes=[V2e[i][1]])
        for ws in wsets:
            for h, bf in ws:
                AR.freez(h)
        for h, bf in hnT:
            AR.freez(h)

    def p6(self, b, inject=()):
        self.kb.tag = 'p6'
        inject = list(inject)
        kb, T, NT, AR = self.kb, self.T, self.NT, self.AR
        TQ, NTQ = self.TQ, self.NTQ
        Q2T, K2T, V2e = self.Q2T, self.K2T, self.V2e
        KTQ = TQ // 128
        Pf = [AR.alloc(f"Pf{b}_{i}", [128, TQ], BF16) for i in range(6)]
        OT = [None] * 8
        wbuf = [AR.alloc(f"wo{b}_{c}", [128, D], BF16) for c in range(8)]
        for c in range(8):
            kb.dma(self.s_ldw[c % 2], wbuf[c][0][:],
                   self.scr_w["fwo"][c * 128:(c + 1) * 128, :],
                   reads=[(self.B_scr_w["fwo"], c)], writes=[wbuf[c][1]])
        self._p6_pending = None
        if True:
            for hp in range(8):
                if OT[hp] is None:
                    OT[hp] = AR.alloc(f"OT{b}_{hp}", [128, T], BF16)
                for qb in range(NTQ):
                    if inject:
                        inject.pop(0)()
                    self.kb.tag = 'p6'

                    nkt = KTQ * (qb + 1)
                    pq6 = (hp * NTQ + qb) % 2
                    pyt = []
                    for hi in range(2):
                        pY, bY = self.PB(2 + pq6, hi * 512, hi * 512 + TQ)
                        pyt.append((pY, bY))
                    def emit_scores(kt):
                        self.kb.tag = 'p6s'
                        off = max(0, kt * 128 - qb * TQ)
                        w = TQ - off
                        pp = kt % 2
                        pf = kt % 3
                        for hi in range(2):
                            base = hi * 64
                            pS, bS = self.PB(pp, hi * 512, hi * 512 + TQ)
                            kb.mmg([dict(
                                out=pS[:, off:off + w],
                                lhsT=K2T[hp][0][base:base + 64,
                                               kt * 128:(kt + 1) * 128],
                                rhs=Q2T[hp][0][base:base + 64,
                                              qb * TQ + off:qb * TQ + off + w],
                                reads=[K2T[hp][1], Q2T[hp][1]],
                                start=True, stop=True)],
                                writes=[bS])
                            kb.op("act", "activation",
                                  Pf[2 * pf + hi][0][:, off:off + w],
                                  pS[:, off:off + w], AF.Exp,
                                  bias=0.0, scale=GAMMA2,
                                  reads=[bS], writes=[Pf[2 * pf + hi][1]])
                            if kt * 128 >= qb * TQ:
                                kb.op("dve", "tensor_mul",
                                      Pf[2 * pf + hi][0][:, off:off + 128],
                                      Pf[2 * pf + hi][0][:, off:off + 128],
                                      self.trimask[:],
                                      reads=[Pf[2 * pf + hi][1], self.B_const],
                                      writes=[Pf[2 * pf + hi][1]])

                    def emit_av(kt):
                        self.kb.tag = 'p6a'
                        off = max(0, kt * 128 - qb * TQ)
                        w = TQ - off
                        pf = kt % 3
                        for hi in range(2):
                            h = 2 * hp + hi
                            kb.mmg([dict(
                                out=pyt[hi][0][0:65, off:off + w],
                                lhsT=V2e[kt][0][:, h * 65:h * 65 + 65],
                                rhs=Pf[2 * pf + hi][0][:, off:off + w],
                                reads=[V2e[kt][1], Pf[2 * pf + hi][1]],
                                start=(kt == 0), stop=(kt == nkt - 1))],
                                writes=[pyt[hi][1]])

                    emit_scores(0)
                    if nkt > 1:
                        emit_scores(1)
                    for kt in range(2, nkt):
                        emit_scores(kt)
                        emit_av(kt - 2)
                    if nkt > 1:
                        emit_av(nkt - 2)
                    emit_av(nkt - 1)
                    if self._p6_pending is not None:
                        self._p6_pending()
                        self._p6_pending = None
                    self.kb.tag = 'p6n'
                    for hi in range(2):
                        pY, bY = pyt[hi]
                        kb.op("dve", "tensor_copy", self.ysc[:, hi, 0:TQ],
                              pY[0:64, :], reads=[bY],
                              writes=[(self.B_rope, hi)])
                        kb.op("act", "activation", self.rrow[:, hi, 0:TQ],
                              pY[64:65, :], AF.Ln, bias=0.0, scale=1.0,
                              reads=[bY], writes=[(self.B_rope, hi)])
                        kb.op("act", "activation", self.rinv[:, hi, 0:TQ],
                              self.rrow[:, hi, 0:TQ], AF.Exp, bias=0.0,
                              scale=-1.0, reads=[(self.B_rope, hi)],
                              writes=[(self.B_rope, hi)])
                    def part_b(hp=hp, qb=qb):
                        self.kb.tag = 'p6n'
                        for hi in range(2):
                            pRB, bRB = self.PB(1, hi * 512, hi * 512 + TQ,
                                               p0=0, p1=64)
                            kb.mmg([dict(out=pRB, lhsT=self.ones64[:],
                                         rhs=self.rinv[:, hi, 0:TQ],
                                         reads=[(self.B_rope, hi),
                                                self.B_const],
                                         start=True, stop=True)],
                                   writes=[bRB])
                            kb.op("dve", "tensor_tensor",
                                  OT[hp][0][hi * 64:hi * 64 + 64,
                                            qb * TQ:(qb + 1) * TQ],
                                  self.ysc[:, hi, 0:TQ], pRB, ALU.mult,
                                  reads=[(self.B_rope, hi), bRB],
                                  writes=[OT[hp][1]])
                    self._p6_pending = part_b
            if self._p6_pending is not None:
                self._p6_pending()
                self._p6_pending = None
            for hp in range(8):
                AR.freez(Q2T[hp][0])
                AR.freez(K2T[hp][0])
        for h_, bf in V2e:
            AR.freez(h_)
        for h_, bf in Pf:
            AR.freez(h_)
        self.Q2T = self.K2T = self.V2e = None
        self.kb.tag = 'p6wo'
        acct, B_acct = AR.alloc(f"acct{b}", [128, 2, D], F32, units=2)
        htile, B_ht = AR.alloc(f"ht{b}", [128, 2, D], F32, units=2)
        if True:
            for i in range(NT):
                sl = i % 2
                psw_, bW = self.PB(2 * sl, 0, 1024)
                kb.dma(self.s_ldx[sl], self.xstage[:, sl, :],
                       self.x_d[b, i * 128:(i + 1) * 128, :],
                       reads=[self.B_in], writes=[(self.B_xstage, sl)])
                kb.dma(self.s_acc[sl], acct[:, sl, :],
                       self.scr_acc[b][i * 128:(i + 1) * 128, :],
                       reads=[(self.B_scr_acc[b], i)], writes=[(B_acct, sl)])
                items = []
                for gh in range(2):
                    for gc in range(8):
                        items.append(dict(
                            out=psw_[:, gh * 512:(gh + 1) * 512],
                            lhsT=OT[gc][0][:, i * 128:(i + 1) * 128],
                            rhs=wbuf[gc][0][:, gh * 512:(gh + 1) * 512],
                            reads=[OT[gc][1], wbuf[gc][1]],
                            start=(gc == 0), stop=(gc == 7)))
                kb.mmg(items, writes=[bW])
                kb.op("act", "activation", self.sqs[:, sl, :],
                      self.xstage[:, sl, :], AF.Identity,
                      bias=self.st_b1[b][:, i:i + 1],
                      scale=self.st_rstd[b][:, i:i + 1],
                      reads=[(self.B_xstage, sl), (self.B_stats[b], i)],
                      writes=[(self.B_sqs, sl)])
                kb.op("dve", "tensor_add", htile[:, sl, :], self.sqs[:, sl, :],
                      psw_, reads=[(self.B_sqs, sl), bW],
                      writes=[(B_ht, sl)])
                kb.op("act", "activation", self.sqs[:, sl, :], htile[:, sl, :],
                      AF.Square, bias=0.0, scale=1.0,
                      accum_out=self.st_t[:, sl, 0:1],
                      reads=[(B_ht, sl)],
                      writes=[(self.B_sqs, sl), (self.B_stt, sl)])
                kb.op("act", "activation", self.st_t[:, sl, 1:2],
                      self.st_t[:, sl, 0:1],
                      AF.Ln, bias=self.epsv[:], scale=1.0 / D,
                      reads=[(self.B_stt, sl), self.B_const],
                      writes=[(self.B_stt, sl)])
                kb.op("act", "activation", self.st_s3[b][:, i:i + 1],
                      self.st_t[:, sl, 1:2], AF.Exp, bias=0.0, scale=-0.5,
                      reads=[(self.B_stt, sl)],
                      writes=[(self.B_s3[b], i)])
                kb.op("act", "activation", self.n2stage[:, sl, :],
                      htile[:, sl, :], AF.Identity, bias=0.0,
                      scale=self.st_s3[b][:, i:i + 1],
                      reads=[(B_ht, sl), (self.B_s3[b], i)],
                      writes=[(self.B_n2stage, sl)])
                kb.dma(self.s_stn3[sl], self.scr_Hn[b][i * 128:(i + 1) * 128, :],
                       self.n2stage[:, sl, :], reads=[(self.B_n2stage, sl)],
                       writes=[(self.B_scr_Hn[b], i)])
                kb.op("pool", "tensor_add", acct[:, sl, :], acct[:, sl, :],
                      htile[:, sl, :], reads=[(B_acct, sl), (B_ht, sl)],
                      writes=[(B_acct, sl)])
                kb.dma(self.s_acc2[sl], self.scr_acc[b][i * 128:(i + 1) * 128, :],
                       acct[:, sl, :], reads=[(B_acct, sl)],
                       writes=[(self.B_scr_acc[b], i)])
                if self.debug:
                    kb.dma(self.s_dbg, self.dbg_h[b, i * 128:(i + 1) * 128, :],
                           htile[:, sl, :], reads=[(B_ht, sl)], writes=[])
        for c in range(8):
            AR.freez(OT[c][0])
        for h_, bf in wbuf:
            AR.freez(h_)
        AR.freez(acct)
        AR.freez(htile)

    def p7(self, b):
        self.kb.tag = 'p7r'
        kb, T, NT, AR = self.kb, self.T, self.NT, self.AR
        C2 = 192
        CE = 2 * C2
        TH7 = min(1024, T)
        NTH7 = T // TH7
        NTPT = TH7 // 128
        IOA = bass.IndirectOffsetOnAxis
        fc1b = [AR.alloc(f"fc1_{b}_{c}", [128, EDIM], BF16) for c in range(8)]
        fc2b = [AR.alloc(f"fc2_{b}_{c}", [128, D], BF16) for c in range(16)]

        def load_fc(e):
            for c in range(8):
                kb.dma(self.s_fc1[c % 4], fc1b[c][0][:],
                       self.scr_fc1[e, c * 128:(c + 1) * 128, :],
                       reads=[(self.B_scr_fc1, e * 8 + c)],
                       writes=[fc1b[c][1]])
            for c in range(16):
                kb.dma(self.s_fc2[c % 4], fc2b[c][0][:],
                       self.scr_fc2[e, c * 128:(c + 1) * 128, :],
                       reads=[(self.B_scr_fc2, e * 16 + c)],
                       writes=[fc2b[c][1]], eng="pool")

        load_fc(0)
        self.kb.tag = 'p7r'
        HnT = self.load_T_strips(self.scr_Hn[b], self.B_scr_Hn[b], f"HnT{b}")
        for fc in range(8):
            kb.dma(self.s_ldw[fc % 2], self.rwb[:, fc, :],
                   self.scr_rw[fc * 128:(fc + 1) * 128, :],
                   reads=[(self.B_scr_rw8, fc)], writes=[self.B_rwb])
        kb.dma(self.s_zz, self.scr_Hn[b][T:T + 1, :], self.zrow[0:1, :],
               reads=[self.B_const],
               writes=[(self.B_scr_Hn[b], NT)], eng="pool")
        for z0 in range(0, T, 2048):
            zr = min(2048, T - z0)
            kb.dma(self.s_stn[1],
                   self.scr_cw[b][z0:z0 + zr, :]
                   .rearrange("(p a) b -> p (a b)", p=128),
                   self.zrow[:, 0:zr // 128 * NEXP],
                   reads=[self.B_const],
                   writes=[(self.B_scr_cw[b], list(range(NT + 1)))])
        kb.dma(self.s_stn[1], self.scr_cw[b][T:T + 1, :],
               self.zrow[0:1, 0:NEXP],
               reads=[self.B_const], writes=[(self.B_scr_cw[b], NT)])
        for th in range(NTH7):
            seg = self.scr_idx[b][th * 3072:(th + 1) * 3072]
            kb.dma(self.s_stn[0], seg.rearrange("(p a) -> p a", p=128),
                   self.senti[:, 0:24], reads=[self.B_const],
                   writes=[(self.B_scr_idx[b], th)])

        gtile, B_gt = AR.alloc(f"gt{b}", [128, 6, D], BF16, units=6)
        idxt, B_ix = AR.alloc(f"ix{b}", [128, 2, 4], mybir.dt.int32, units=2)
        cgt, B_cg = AR.alloc(f"cg{b}", [128, 2, 3 * NEXP], F32, units=2)
        base_r, B_base = AR.alloc(f"base{b}", [1, 16], F32)
        posf, B_posf = AR.alloc(f"posf{b}", [128, 2, 40], F32, units=2)
        tgti, B_tg = AR.alloc(f"tg{b}", [128, 2, 2], mybir.dt.int32, units=2)
        tki, B_tk = AR.alloc(f"tk{b}", [128, 2, 1], mybir.dt.int32, units=2)
        eqf, B_eqf = AR.alloc(f"eqf{b}", [128, 2, 16], F32, units=2)
        eqb, B_eqb = AR.alloc(f"eqb{b}", [128, 2, 16], BF16, units=2)
        Gs = [AR.alloc(f"G{b}_{c}", [128, CE], BF16) for c in range(16)]
        mst = [AR.alloc(f"mst{b}_{B}", [128, D], F32) for B in range(3)]
        msw, B_msw = AR.alloc(f"msw{b}", [64, 2, D], F32, units=2)
        ixw, B_ixw = AR.alloc(f"ixw{b}", [64, 2, 1], mybir.dt.int32, units=2)
        relu_s, B_rs = AR.alloc(f"relu{b}", [128, 2, CE], BF16, units=2)
        gthT = [[AR.alloc(f"gthT{b}_{s}_{c}", [128, CE], BF16) for c in range(8)]
                for s in range(2)]
        def compact_tile(th, tt):
            self.kb.tag = 'p7r'
            if tt == 0:
                kb.op("dve", "memset", base_r[:], 0.0, writes=[B_base])
            if True:
                i = th * NTPT + tt
                sl7 = i % 2
                stt = self.st_t[:, sl7, :]
                B_stt7 = (self.B_stt, sl7)
                psl, bL = self.PB(3, 512 + sl7 * 16, 512 + sl7 * 16 + NEXP)
                items = []
                for fc in range(8):
                    items.append(dict(
                        out=psl,
                        lhsT=HnT[fc][0][:, i * 128:(i + 1) * 128],
                        rhs=self.rwb[:, fc, :],
                        reads=[HnT[fc][1], self.B_rwb],
                        start=(fc == 0), stop=(fc == 7)))
                kb.mmg(items, writes=[bL])
                lg = self.lgt[:, sl7, :]
                B_lg7 = (self.B_lgt, sl7)
                kb.op("dve", "tensor_copy", lg, psl, reads=[bL], writes=[B_lg7])
                kb.op("dve", "reduce_max", stt[:, 0:1], lg, axis=AX,
                      reads=[B_lg7], writes=[B_stt7])
                kb.op("dve", "tensor_scalar", stt[:, 2:3], stt[:, 0:1], -1.0,
                      None, ALU.mult, reads=[B_stt7], writes=[B_stt7])
                eq1 = eqf[:, sl7, 0:NEXP]
                eq2 = eqf[:, sl7, NEXP:2 * NEXP]
                kb.op("dve", "tensor_scalar", eq1, lg, stt[:, 0:1], None,
                      ALU.is_equal, reads=[B_lg7, B_stt7],
                      writes=[(B_eqf, sl7)])
                kb.op("dve", "scalar_tensor_tensor", self.lgt2[:, sl7, :],
                      eq1, -1e30, lg, ALU.mult, ALU.add,
                      reads=[(B_eqf, sl7), B_lg7],
                      writes=[(self.B_lgt2, sl7)])
                kb.op("dve", "reduce_max", stt[:, 1:2], self.lgt2[:, sl7, :],
                      axis=AX, reads=[(self.B_lgt2, sl7)], writes=[B_stt7])
                kb.op("dve", "tensor_scalar", eq2, self.lgt2[:, sl7, :],
                      stt[:, 1:2], None, ALU.is_equal,
                      reads=[(self.B_lgt2, sl7), B_stt7],
                      writes=[(B_eqf, sl7)])
                kb.op("dve", "tensor_add", stt[:, 3:4], stt[:, 1:2],
                      stt[:, 2:3], reads=[B_stt7], writes=[B_stt7])
                kb.op("act", "activation", stt[:, 4:5], stt[:, 3:4], AF.Exp,
                      bias=0.0, scale=1.0, reads=[B_stt7], writes=[B_stt7])
                kb.op("dve", "tensor_scalar_add", stt[:, 5:6], stt[:, 4:5],
                      1.0, reads=[B_stt7], writes=[B_stt7])
                kb.op("dve", "reciprocal", stt[:, 6:7], stt[:, 5:6],
                      reads=[B_stt7], writes=[B_stt7])
                kb.op("dve", "tensor_mul", stt[:, 7:8], stt[:, 4:5],
                      stt[:, 6:7], reads=[B_stt7], writes=[B_stt7])
                kb.op("dve", "tensor_scalar_mul", self.cw[b][:, i, :], eq1,
                      stt[:, 6:7], reads=[(B_eqf, sl7), B_stt7],
                      writes=[(self.B_cw[b], i)])
                kb.op("dve", "scalar_tensor_tensor", self.cw[b][:, i, :],
                      eq2, stt[:, 7:8], self.cw[b][:, i, :],
                      ALU.mult, ALU.add,
                      reads=[(B_eqf, sl7), B_stt7, (self.B_cw[b], i)],
                      writes=[(self.B_cw[b], i)])
                kb.dma(self.s_stn2[i % 2], self.scr_cw[b][i * 128:(i + 1) * 128, :],
                       self.cw[b][:, i, :], reads=[(self.B_cw[b], i)],
                       writes=[(self.B_scr_cw[b], i)])
                if self.debug:
                    kb.dma(self.s_dbg, self.dbg_c[b, i * 128:(i + 1) * 128, :],
                           self.cw[b][:, i, :], reads=[(self.B_cw[b], i)],
                           writes=[])
                kb.op("dve", "tensor_copy", eqb[:, sl7, 0:16],
                      eqf[:, sl7, 0:16], reads=[(B_eqf, sl7)],
                      writes=[(B_eqb, sl7)])
                pP1, bP1 = self.PB(3, sl7 * 64, sl7 * 64 + NEXP)
                pP2, bP2 = self.PB(3, sl7 * 64 + 16, sl7 * 64 + 16 + NEXP)
                pB1, bB1 = self.PB(3, sl7 * 64 + 32, sl7 * 64 + 32 + NEXP)
                pB2, bB2 = self.PB(3, sl7 * 64 + 48, sl7 * 64 + 48 + NEXP)
                kb.mmg([dict(out=pP1, lhsT=self.ustrict[:],
                             rhs=eqb[:, sl7, 0:NEXP],
                             reads=[self.B_const, (B_eqb, sl7)],
                             start=True, stop=True)], writes=[bP1])
                kb.mmg([dict(out=pP2, lhsT=self.ustrict[:],
                             rhs=eqb[:, sl7, NEXP:2 * NEXP],
                             reads=[self.B_const, (B_eqb, sl7)],
                             start=True, stop=True)], writes=[bP2])
                kb.mmg([dict(out=pB1, lhsT=self.ones128f[:],
                             rhs=base_r[0:1, 0:NEXP],
                             reads=[self.B_const, B_base],
                             start=True, stop=True)], writes=[bB1])
                kb.mmg([dict(out=pB2, lhsT=self.ones128f[:],
                             rhs=base_r[0:1, NEXP:2 * NEXP],
                             reads=[self.B_const, B_base],
                             start=True, stop=True)], writes=[bB2])
                p1f = posf[:, sl7, 0:8]
                p2f = posf[:, sl7, 8:16]
                kb.op("act", "activation", p1f, pP1, AF.Copy, bias=0.0,
                      scale=1.0, reads=[bP1], writes=[(B_posf, sl7)])
                kb.op("dve", "tensor_add", p1f, p1f, pB1,
                      reads=[(B_posf, sl7), bB1], writes=[(B_posf, sl7)])
                kb.op("act", "activation", p2f, pP2, AF.Copy, bias=0.0,
                      scale=1.0, reads=[bP2], writes=[(B_posf, sl7)])
                kb.op("dve", "tensor_add", p2f, p2f, pB2,
                      reads=[(B_posf, sl7), bB2], writes=[(B_posf, sl7)])
                pC, bC = self.PB(2, sl7 * 32, sl7 * 32 + 16, p0=0, p1=1)
                kb.mmg([dict(out=pC, lhsT=self.onescol[:],
                             rhs=eqb[:, sl7, 0:16],
                             reads=[self.B_const, (B_eqb, sl7)],
                             start=True, stop=True)], writes=[bC])
                kb.op("dve", "tensor_add", base_r[0:1, 0:16],
                      base_r[0:1, 0:16], pC,
                      reads=[B_base, bC], writes=[B_base])
                t8 = posf[:, sl7, 16:24]
                e1c = posf[:, sl7, 24:25]
                e2c = posf[:, sl7, 25:26]
                q1 = posf[:, sl7, 26:27]
                q2 = posf[:, sl7, 27:28]
                ov = posf[:, sl7, 28:29]
                tg1 = posf[:, sl7, 29:30]
                tg2 = posf[:, sl7, 30:31]
                kb.op("dve", "tensor_mul", t8, eq1, self.erow[:],
                      reads=[(B_eqf, sl7), self.B_const],
                      writes=[(B_posf, sl7)])
                kb.op("dve", "reduce_sum", e1c, t8, axis=AX,
                      reads=[(B_posf, sl7)], writes=[(B_posf, sl7)])
                kb.op("dve", "tensor_mul", t8, eq2, self.erow[:],
                      reads=[(B_eqf, sl7), self.B_const],
                      writes=[(B_posf, sl7)])
                kb.op("dve", "reduce_sum", e2c, t8, axis=AX,
                      reads=[(B_posf, sl7)], writes=[(B_posf, sl7)])
                kb.op("dve", "tensor_mul", t8, eq1, p1f,
                      reads=[(B_eqf, sl7), (B_posf, sl7)],
                      writes=[(B_posf, sl7)])
                kb.op("dve", "reduce_sum", q1, t8, axis=AX,
                      reads=[(B_posf, sl7)], writes=[(B_posf, sl7)])
                kb.op("dve", "tensor_mul", t8, eq2, p2f,
                      reads=[(B_eqf, sl7), (B_posf, sl7)],
                      writes=[(B_posf, sl7)])
                kb.op("dve", "reduce_sum", q2, t8, axis=AX,
                      reads=[(B_posf, sl7)], writes=[(B_posf, sl7)])
                kb.op("dve", "scalar_tensor_tensor", tg1, e1c, float(CE), q1,
                      ALU.mult, ALU.add, reads=[(B_posf, sl7)],
                      writes=[(B_posf, sl7)])
                kb.op("dve", "tensor_scalar_add", tg1, tg1, float(th * 3072),
                      reads=[(B_posf, sl7)], writes=[(B_posf, sl7)])
                kb.op("dve", "tensor_scalar", ov, q1, float(C2), None,
                      ALU.is_ge, reads=[(B_posf, sl7)], writes=[(B_posf, sl7)])
                kb.op("dve", "scalar_tensor_tensor", tg1, ov, 1e9, tg1,
                      ALU.mult, ALU.add, reads=[(B_posf, sl7)],
                      writes=[(B_posf, sl7)])
                kb.op("dve", "scalar_tensor_tensor", tg2, e2c, float(CE), q2,
                      ALU.mult, ALU.add, reads=[(B_posf, sl7)],
                      writes=[(B_posf, sl7)])
                kb.op("dve", "tensor_scalar_add", tg2, tg2,
                      float(C2 + th * 3072),
                      reads=[(B_posf, sl7)], writes=[(B_posf, sl7)])
                kb.op("dve", "tensor_scalar", ov, q2, float(C2), None,
                      ALU.is_ge, reads=[(B_posf, sl7)], writes=[(B_posf, sl7)])
                kb.op("dve", "scalar_tensor_tensor", tg2, ov, 1e9, tg2,
                      ALU.mult, ALU.add, reads=[(B_posf, sl7)],
                      writes=[(B_posf, sl7)])
                kb.op("dve", "tensor_copy", tgti[:, sl7, 0:1], tg1,
                      reads=[(B_posf, sl7)], writes=[(B_tg, sl7)])
                kb.op("dve", "tensor_copy", tgti[:, sl7, 1:2], tg2,
                      reads=[(B_posf, sl7)], writes=[(B_tg, sl7)])
                kb.op("pool", "iota", tki[:, sl7, :], [[1, 1]], base=i * 128,
                      channel_multiplier=1, writes=[(B_tk, sl7)])
                seg = self.scr_idx[b][:].rearrange("(a b) -> a b", b=1)
                kb.idma(self.s_ix[0], seg,
                        IOA(tgti[:, sl7, 0:1], 0), tki[:, sl7, :], None,
                        bounds_check=self.rg_idx, oob_is_err=False,
                        reads=[(B_tg, sl7), (B_tk, sl7)],
                        writes=[(self.B_scr_idx[b], th)])
                kb.idma(self.s_ix[1], seg,
                        IOA(tgti[:, sl7, 1:2], 0), tki[:, sl7, :], None,
                        bounds_check=self.rg_idx, oob_is_err=False,
                        reads=[(B_tg, sl7), (B_tk, sl7)],
                        writes=[(self.B_scr_idx[b], th)])

        for tt in range(NTPT):
            compact_tile(0, tt)
        pend = [(th, tt) for th in range(1, NTH7) for tt in range(NTPT)]

        self.kb.tag = 'p7moe'
        pairs = [(th, e) for th in range(NTH7) for e in range(NEXP)]

        def gather_block(k):
            th, e = pairs[k]
            s = k % 2
            for a in range(3):
                kb.dma(self.s_ldt[a % 2],
                       idxt[:, s, a:a + 1],
                       self.scr_idx[b][th * 3072 + e * CE + a * 128:
                                       th * 3072 + e * CE + a * 128 + 128]
                       .rearrange("(p a) -> p a", p=128),
                       reads=[(self.B_scr_idx[b], th)],
                       writes=[(B_ix, s)])
            for a in range(3):
                kb.idma(self.s_gg[a],
                        gtile[:, 3 * s + a, :], None, self.scr_Hn[b][:],
                        IOA(idxt[:, s, a:a + 1], 0),
                        reads=[(B_ix, s),
                               (self.B_scr_Hn[b], list(range(NT + 1)))],
                        writes=[(B_gt, 3 * s + a)])
                kb.dma(self.s_stn2[a % 2],
                       self.scr_gth[s][a * 128:(a + 1) * 128, :],
                       gtile[:, 3 * s + a, :], reads=[(B_gt, 3 * s + a)],
                       writes=[self.B_scrg[s]])
                kb.idma(self.s_ix[s],
                        cgt[:, s, a * NEXP:(a + 1) * NEXP], None,
                        self.scr_cw[b][:],
                        IOA(idxt[:, s, a:a + 1], 0),
                        reads=[(B_ix, s),
                               (self.B_scr_cw[b], list(range(NT + 1)))],
                        writes=[(B_cg, s)])
            for fc in range(8):
                kb.dma(self.s_ldt[fc % 4], gthT[s][fc][0][:],
                       self.scr_gth[s][0:CE, fc * 128:(fc + 1) * 128],
                       reads=[self.B_scrg[s]], writes=[gthT[s][fc][1]],
                       transpose=True)

        gather_block(0)
        for k, (th, e) in enumerate(pairs):
            if k > 0:
                load_fc(e)
            s = k % 2
            if pend:
                compact_tile(*pend.pop(0))
                self.kb.tag = 'p7moe'
            if k + 1 < len(pairs):
                gather_block(k + 1)
            if True:
                for hc in range(16):
                    pp = hc % 2
                    psg, bG = self.PB(pp, 0, CE)
                    items = []
                    for fc in range(8):
                        items.append(dict(
                            out=psg,
                            lhsT=fc1b[fc][0][:, hc * 128:(hc + 1) * 128],
                            rhs=gthT[s][fc][0][:],
                            reads=[fc1b[fc][1], gthT[s][fc][1]],
                            start=(fc == 0), stop=(fc == 7)))
                    kb.mmg(items, writes=[bG])
                    kb.op("act", "activation", relu_s[:, pp, 0:CE], psg,
                          AF.Relu, bias=0.0, scale=1.0,
                          reads=[bG], writes=[(B_rs, pp)])
                    kb.op("dve", "tensor_mul", Gs[hc][0][:],
                          relu_s[:, pp, 0:CE], relu_s[:, pp, 0:CE],
                          reads=[(B_rs, pp)], writes=[Gs[hc][1]])
                for tt3 in range(3):
                    pp = tt3 % 2
                    psm, bM = self.PB(2 + pp, 0, 1024)
                    items = []
                    for gh in range(2):
                        for hc in range(16):
                            items.append(dict(
                                out=psm[:, gh * 512:(gh + 1) * 512],
                                lhsT=Gs[hc][0][:, tt3 * 128:(tt3 + 1) * 128],
                                rhs=fc2b[hc][0][:, gh * 512:(gh + 1) * 512],
                                reads=[Gs[hc][1], fc2b[hc][1]],
                                start=(hc == 0), stop=(hc == 15)))
                    kb.mmg(items, writes=[bM])
                    kb.op("dve", "tensor_scalar_mul", mst[tt3][0][:], psm,
                          cgt[:, s, tt3 * NEXP + e:tt3 * NEXP + e + 1],
                          reads=[bM, (B_cg, s)], writes=[mst[tt3][1]])
                    if tt3 == 0:
                        kb.idma(self.s_sca[0], self.scr_moe[b][:],
                                IOA(idxt[:, s, 0:1], 0),
                                mst[0][0][:], None,
                                bounds_check=self.rg_tok, oob_is_err=False,
                                reads=[mst[0][1], (B_ix, s)],
                                writes=[(self.B_scr_moe[b], e)])
                    elif tt3 == 2:
                        kb.idma(self.s_sca[1], self.scr_moe2[b][:],
                                IOA(idxt[:, s, 2:3], 0),
                                mst[2][0][:], None,
                                bounds_check=self.rg_tok, oob_is_err=False,
                                reads=[mst[2][1], (B_ix, s)],
                                writes=[(self.B_scr_moe2[b], e)])
                    else:
                        kb.op("dve", "tensor_copy", msw[:, 0, :],
                              mst[1][0][0:64, :], reads=[mst[1][1]],
                              writes=[(B_msw, 0)])
                        kb.op("dve", "tensor_copy", msw[:, 1, :],
                              mst[1][0][64:128, :], reads=[mst[1][1]],
                              writes=[(B_msw, 1)])
                        kb.op("dve", "tensor_copy", ixw[:, 0, :],
                              idxt[0:64, s, 1:2], reads=[(B_ix, s)],
                              writes=[(B_ixw, 0)])
                        kb.op("dve", "tensor_copy", ixw[:, 1, :],
                              idxt[64:128, s, 1:2], reads=[(B_ix, s)],
                              writes=[(B_ixw, 1)])
                        kb.idma(self.s_sca[0], self.scr_moe[b][:],
                                IOA(ixw[:, 0, :], 0),
                                msw[:, 0, :], None,
                                bounds_check=self.rg_tok, oob_is_err=False,
                                reads=[(B_msw, 0), (B_ixw, 0)],
                                writes=[(self.B_scr_moe[b], e)])
                        kb.idma(self.s_sca[1], self.scr_moe2[b][:],
                                IOA(ixw[:, 1, :], 0),
                                msw[:, 1, :], None,
                                bounds_check=self.rg_tok, oob_is_err=False,
                                reads=[(B_msw, 1), (B_ixw, 1)],
                                writes=[(self.B_scr_moe2[b], e)])
        for lst in (fc1b, fc2b, Gs, mst, gthT[0], gthT[1]):
            for h_, bf in lst:
                AR.freez(h_)
        for h_ in (gtile, idxt, cgt, base_r, posf, tgti, tki, eqf, eqb,
                   relu_s, msw, ixw):
            AR.freez(h_)
        for h_, bf in HnT:
            AR.freez(h_)

    def p8(self, b):
        self.kb.tag = 'p8'
        kb, T, NT, AR = self.kb, self.T, self.NT, self.AR
        acct, B_acct = AR.alloc(f"acc8{b}", [128, 2, D], F32, units=2)
        mt, B_mt = AR.alloc(f"moe8{b}", [128, 2, D], F32, units=2)
        m2t, B_m2t = AR.alloc(f"moe8b{b}", [128, 2, D], F32, units=2)
        for i in range(NT):
            sl = i % 2
            kb.dma(self.s_ldx[sl], self.xstage[:, sl, :],
                   self.x_d[b, i * 128:(i + 1) * 128, :],
                   reads=[self.B_in], writes=[(self.B_xstage, sl)])
            kb.dma(self.s_acc[sl], acct[:, sl, :],
                   self.scr_acc[b][i * 128:(i + 1) * 128, :],
                   reads=[(self.B_scr_acc[b], i)], writes=[(B_acct, sl)])
            kb.dma(self.s_ldt[sl], mt[:, sl, :],
                   self.scr_moe[b][i * 128:(i + 1) * 128, :],
                   reads=[(self.B_scr_moe[b], list(range(NEXP)))],
                   writes=[(B_mt, sl)])
            kb.dma(self.s_stn2[sl], m2t[:, sl, :],
                   self.scr_moe2[b][i * 128:(i + 1) * 128, :],
                   reads=[(self.B_scr_moe2[b], list(range(NEXP)))],
                   writes=[(B_m2t, sl)])
            kb.op("dve", "tensor_add", acct[:, sl, :], acct[:, sl, :],
                  mt[:, sl, :], reads=[(B_acct, sl), (B_mt, sl)],
                  writes=[(B_acct, sl)])
            kb.op("dve", "tensor_add", acct[:, sl, :], acct[:, sl, :],
                  m2t[:, sl, :], reads=[(B_acct, sl), (B_m2t, sl)],
                  writes=[(B_acct, sl)])
            kb.op("dve", "tensor_add", self.sqs[:, sl, :], acct[:, sl, :],
                  self.xstage[:, sl, :],
                  reads=[(B_acct, sl), (self.B_xstage, sl)],
                  writes=[(self.B_sqs, sl)])
            kb.dma(self.s_out[sl], self.y_d[b, i * 128:(i + 1) * 128, :],
                   self.sqs[:, sl, :], reads=[(self.B_sqs, sl)], writes=[])
        AR.freez(acct)
        AR.freez(mt)
        AR.freez(m2t)

    def batch_front(self, b):
        if b == 0:
            self.p3(b)
        self.p4(b)
        self.p5(b)
        inj = self.p1_emitters(b + 1) if b + 1 < self.BPC else ()
        self.p6(b, inject=inj)


def _host_weights(inputs):
    wq = np.ascontiguousarray(inputs["L0_attn_w_q"], dtype=np.float32)
    wk = np.ascontiguousarray(inputs["L0_attn_w_k"], dtype=np.float32)
    wv = np.ascontiguousarray(inputs["L0_attn_w_v"], dtype=np.float32)
    wqb = np.zeros((8, 128, 128), np.float32)
    wkb = np.zeros((8, 128, 128), np.float32)
    for ic in range(8):
        par, m = ic // 4, ic % 4
        for jj in range(2):
            h = 4 * m + 2 * jj + par
            wqb[ic, jj * 64:(jj + 1) * 64, jj * 64:(jj + 1) * 64] = wq[:, :, h]
            wkb[ic, jj * 64:(jj + 1) * 64, jj * 64:(jj + 1) * 64] = wk[:, :, h]
    kk, dd, hh = np.indices((HD, HD, NH))
    rows_pm = ((hh % 2) * 512 + (hh // 2) * 64 + kk).ravel()
    cols_o = (dd * NH + hh).ravel()
    wv_hm = np.zeros((D, D), np.float32)
    wv_hm[rows_pm, cols_o] = wv.ravel()
    return {
        "wqb": wqb.reshape(1024, 128), "wkb": wkb.reshape(1024, 128),
        "wv_hm": wv_hm,
        "fwq": np.ascontiguousarray(inputs["L0_ffn_wq"], dtype=np.float32),
        "fwk": np.ascontiguousarray(inputs["L0_ffn_wk"], dtype=np.float32),
        "fwv": np.ascontiguousarray(inputs["L0_ffn_wv"], dtype=np.float32),
        "fwo": np.ascontiguousarray(inputs["L0_ffn_wo"], dtype=np.float32),
        "rw": np.ascontiguousarray(inputs["L0_ffn_router_w"], dtype=np.float32),
        "fc1": np.ascontiguousarray(inputs["L0_ffn_exp_fc1"], dtype=np.float32),
        "fc2": np.ascontiguousarray(inputs["L0_ffn_exp_fc2"], dtype=np.float32),
    }


def kernel(**inputs):
    x = np.ascontiguousarray(inputs["L0_attn_x"], dtype=np.float32)
    B, T, d = x.shape
    assert d == D and B % N_CORES == 0
    bpc = B // N_CORES
    wmap = _host_weights(inputs)
    prog = Prog(T=T, BPC=bpc)
    nc = prog.build()
    in_maps = []
    for c in range(N_CORES):
        m = {"x": x[c * bpc:(c + 1) * bpc]}
        m.update(wmap)
        in_maps.append(m)
    res = run_bass_kernel_spmd(nc, in_maps, core_ids=list(range(N_CORES)))
    return np.concatenate([r["y"] for r in res.results], axis=0)


# version 28
# speedup vs baseline: 1.0659x; 1.0575x over previous
import math
from contextlib import ExitStack

import numpy as np

import concourse.bass as bass
import concourse.mybir as mybir
from concourse.bass_utils import run_bass_kernel_spmd

F32 = mybir.dt.float32
BF16 = mybir.dt.bfloat16
AF = mybir.ActivationFunctionType
ALU = mybir.AluOpType
AX = mybir.AxisListType.X

D = 1024
NH = 16
HD = 64
NEXP = 8
EDIM = 2048
EPS = 1e-5
ROPE_BASE = 10000.0
N_CORES = 8
GAMMA1 = 1.0 / 32.0
GAMMA2 = 1.0 / 8.0


def _tok_add(d, tok):
    if tok is None:
        return
    k = id(tok[0])
    old = d.get(k)
    if old is None or old[1] < tok[1]:
        d[k] = tok


def _tok_merge(d, other):
    for t in other.values():
        _tok_add(d, t)


class Buf:

    def __init__(self, n=1, carry=None):
        self.w = [None] * n
        self.r = [dict() for _ in range(n)]
        self.carry = dict(carry) if carry else {}


def _units(spec):
    if isinstance(spec, Buf):
        return spec, [0]
    b, u = spec
    if isinstance(u, (list, tuple, range)):
        return b, list(u)
    return b, [u]


class Slot:

    def __init__(self, kb, name):
        self.sem = kb.es.enter_context(kb.nc.semaphore(name))
        self.rounds = 0
        self.tok = None


class KB:
    def __init__(self, nc, es):
        self.nc = nc
        self.es = es
        self.raw = {"pe": nc.tensor, "dve": nc.vector, "act": nc.scalar,
                    "pool": nc.gpsimd, "sp": nc.sync}
        self.sem = {}
        self.cnt = {}
        self.marks = {}
        for e in self.raw:
            self.sem[e] = es.enter_context(nc.semaphore("sem_" + e))
            self.cnt[e] = 0
            self.marks[e] = {}
        self.tag = "init"
        self.tagmap = {}

    def slot(self, name):
        return Slot(self, name)

    def _wait(self, eng, tok):
        if tok is None:
            return
        sem, val = tok
        wm = self.marks[eng]
        key = id(sem)
        if wm.get(key, -1) >= val:
            return
        self.raw[eng].wait_ge(sem, val)
        wm[key] = val

    def _deps(self, eng, reads, writes):
        for spec in reads:
            b, us = _units(spec)
            for u in us:
                if b.w[u] is None:
                    for t in b.carry.values():
                        self._wait(eng, t)
                self._wait(eng, b.w[u])
        for spec in writes:
            b, us = _units(spec)
            for u in us:
                if b.w[u] is None:
                    for t in b.carry.values():
                        self._wait(eng, t)
                self._wait(eng, b.w[u])
                for t in b.r[u].values():
                    self._wait(eng, t)

    def _record(self, tok, reads, writes):
        for spec in reads:
            b, us = _units(spec)
            for u in us:
                _tok_add(b.r[u], tok)
        for spec in writes:
            b, us = _units(spec)
            for u in us:
                b.w[u] = tok
                b.r[u] = {}

    def op(self, eng, method, *args, reads=(), writes=(), **kw):
        self._deps(eng, reads, writes)
        inst = getattr(self.raw[eng], method)(*args, **kw)
        self.cnt[eng] += 1
        tok = (self.sem[eng], self.cnt[eng])
        inst.then_inc(self.sem[eng], 1)
        self._record(tok, reads, writes)
        return tok

    def mmg(self, items, writes):
        self._deps("pe", [], writes)
        all_reads = []
        last = None
        for it in items:
            self._deps("pe", it.get("reads", ()), [])
            all_reads.extend(it.get("reads", ()))
            kw = {}
            if it.get("tile_position") is not None:
                kw["tile_position"] = it["tile_position"]
            last = self.raw["pe"].matmul(it["out"], it["lhsT"], it["rhs"],
                                         start=it["start"], stop=it["stop"], **kw)
            self.tagmap[last.ins.name] = self.tag
        self.cnt["pe"] += 1
        tok = (self.sem["pe"], self.cnt["pe"])
        last.then_inc(self.sem["pe"], 1)
        self._record(tok, all_reads, writes)
        return tok

    def idma(self, slot, out_ap, out_off, in_ap, in_off, reads=(), writes=(),
             **kw):
        self._deps("pool", reads, writes)
        self._wait("pool", slot.tok)
        inst = self.raw["pool"].indirect_dma_start(out_ap, out_off, in_ap,
                                                   in_off, **kw)
        inst.then_inc(slot.sem, 16)
        slot.rounds += 1
        tok = (slot.sem, 16 * slot.rounds)
        slot.tok = tok
        self._record(tok, reads, writes)
        return tok

    def dma(self, slot, out_ap, in_ap, reads=(), writes=(), eng="sp",
            transpose=False):
        self._deps(eng, reads, writes)
        self._wait(eng, slot.tok)
        if transpose:
            inst = self.raw[eng].dma_start_transpose(out_ap, in_ap)
        else:
            inst = self.raw[eng].dma_start(out_ap, in_ap)
        inst.then_inc(slot.sem, 16)
        slot.rounds += 1
        tok = (slot.sem, 16 * slot.rounds)
        slot.tok = tok
        self._record(tok, reads, writes)
        return tok


class Arena:

    def __init__(self, nc, base, cap):
        self.nc = nc
        self.base = base
        self.free_list = [(base, cap, {})]
        self.live = {}
        self.uid = 0

    def alloc(self, name, shape, dtype, units=1, carry=None):
        per_part = int(np.prod(shape[1:])) * mybir.dt.size(dtype)
        per_part = (per_part + 31) & ~31
        self.uid += 1
        nm = f"{name}_u{self.uid}"
        for i, (off, size, toks) in enumerate(self.free_list):
            if size >= per_part:
                h = self.nc.alloc_sbuf_tensor_at(nm, list(shape), dtype, offset=off)
                if size == per_part:
                    self.free_list.pop(i)
                else:
                    self.free_list[i] = (off + per_part, size - per_part,
                                         dict(toks))
                cd = dict(toks)
                if carry:
                    _tok_merge(cd, carry)
                buf = Buf(units, carry=cd)
                self.live[id(h)] = (off, per_part, buf)
                return h, buf
        raise MemoryError(f"arena OOM for {name}: need {per_part}B/p free="
                          f"{[(o, s) for o, s, _ in self.free_list]}")

    def freez(self, h):
        off, size, buf = self.live.pop(id(h))
        toks = dict(buf.carry)
        for t in buf.w:
            _tok_add(toks, t)
        for rl in buf.r:
            _tok_merge(toks, rl)
        self.free_list.append((off, size, toks))
        self.free_list.sort(key=lambda s: s[0])
        out = []
        for seg in self.free_list:
            if out and out[-1][0] + out[-1][1] == seg[0]:
                merged = dict(out[-1][2])
                _tok_merge(merged, seg[2])
                out[-1] = (out[-1][0], out[-1][1] + seg[1], merged)
            else:
                out.append((seg[0], seg[1], dict(seg[2])))
        self.free_list = out
        return toks


class Prog:
    def __init__(self, T=2048, BPC=2, debug=False):
        self.T = T
        self.BPC = BPC
        self.debug = debug
        self.NT = T // 128
        self.nc = bass.Bass("TRN2", target_bir_lowering=False)

    def sb(self, name, shape, dtype):
        return self.es.enter_context(self.nc.sbuf_tensor(name, list(shape), dtype))

    def PB(self, i, lo, hi, p0=0, p1=128):
        h, bf = self.pb[i]
        return h[p0:p1, lo:hi], (bf, list(range(lo // 512, (hi + 511) // 512)))

    def build(self):
        nc, T, NT, BPC = self.nc, self.T, self.NT, self.BPC
        self.TQ = min(512, T)
        self.NTQ = T // self.TQ

        self.x_d = nc.dram_tensor("x", [BPC, T, D], F32, kind="ExternalInput").ap()
        wnames = ["wv_hm", "fwq", "fwk", "fwv", "fwo"]
        self.w_d = {n: nc.dram_tensor(n, [D, D], F32, kind="ExternalInput").ap()
                    for n in wnames}
        for n in ["wqb", "wkb"]:
            self.w_d[n] = nc.dram_tensor(n, [D, 128], F32,
                                         kind="ExternalInput").ap()
        self.rw_d = nc.dram_tensor("rw", [D, NEXP], F32, kind="ExternalInput").ap()
        self.fc1_d = nc.dram_tensor("fc1", [NEXP, D, EDIM], F32, kind="ExternalInput").ap()
        self.fc2_d = nc.dram_tensor("fc2", [NEXP, EDIM, D], F32, kind="ExternalInput").ap()
        self.y_d = nc.dram_tensor("y", [BPC, T, D], F32, kind="ExternalOutput").ap()

        self.scr_xn = [nc.dram_tensor(f"scr_xn{b}", [T, D], BF16).ap() for b in range(BPC)]
        self.scr_hn = [nc.dram_tensor(f"scr_hn{b}", [T, D], BF16).ap() for b in range(BPC)]
        self.scr_Hn = [nc.dram_tensor(f"scr_Hn{b}", [T + 1, D], BF16).ap() for b in range(BPC)]
        self.scr_cw = [nc.dram_tensor(f"scr_cw{b}", [T + 1, NEXP], F32).ap() for b in range(BPC)]
        self.scr_idx = [nc.dram_tensor(f"scr_idx{b}", [max(1, T // 1024) * 8 * 384], mybir.dt.int32).ap() for b in range(BPC)]
        self.scr_gth = [nc.dram_tensor(f"scr_gth{s}", [384, D], BF16).ap() for s in range(2)]
        self.scr_acc = [nc.dram_tensor(f"scr_acc{b}", [T, D], F32).ap() for b in range(BPC)]
        self.scr_moe = [nc.dram_tensor(f"scr_moe{b}", [T + 1, D], F32).ap() for b in range(BPC)]
        self.scr_moe2 = [nc.dram_tensor(f"scr_moe2_{b}", [T + 1, D], F32).ap() for b in range(BPC)]
        self.scr_w = {n: nc.dram_tensor("scrb_" + n, [D, D], BF16).ap() for n in wnames}
        for n in ["wqb", "wkb"]:
            self.scr_w[n] = nc.dram_tensor("scrb_" + n, [D, 128], BF16).ap()
        self.scr_rw = nc.dram_tensor("scrb_rw", [D, NEXP], BF16).ap()
        self.scr_fc1 = nc.dram_tensor("scrb_fc1", [NEXP, D, EDIM], BF16).ap()
        self.scr_fc2 = nc.dram_tensor("scrb_fc2", [NEXP, EDIM, D], BF16).ap()
        if self.debug:
            self.dbg_y = nc.dram_tensor("dbg_y", [BPC, T, D], F32, kind="ExternalOutput").ap()
            self.dbg_h = nc.dram_tensor("dbg_h", [BPC, T, D], F32, kind="ExternalOutput").ap()
            self.dbg_c = nc.dram_tensor("dbg_c", [BPC, T, NEXP], F32, kind="ExternalOutput").ap()
            self.dbg_moe = nc.dram_tensor("dbg_moe", [BPC, T, D], F32, kind="ExternalOutput").ap()

        self.es = ExitStack()
        es = self.es
        kb = KB(nc, es)
        self.kb = kb
        self._pre3 = None

        self.xstage = self.sb("xstage", [128, 2, D], F32)
        self.nstage = self.sb("nstage", [128, 2, D], BF16)
        self.n2stage = self.sb("n2stage", [128, 2, D], BF16)
        self.sqs = self.sb("sqs", [128, 2, D], F32)
        self.st_m = [self.sb(f"st_m{b}", [128, NT], F32) for b in range(BPC)]
        self.st_rstd = [self.sb(f"st_rstd{b}", [128, NT], F32) for b in range(BPC)]
        self.st_b1 = [self.sb(f"st_b1{b}", [128, NT], F32) for b in range(BPC)]
        self.st_s2 = [self.sb(f"st_s2{b}", [128, NT], F32) for b in range(BPC)]
        self.st_s3 = [self.sb(f"st_s3{b}", [128, NT], F32) for b in range(BPC)]
        self.st_sq = [self.sb(f"st_sq{b}", [128, NT], F32) for b in range(BPC)]
        self.st_rs2 = [self.sb(f"st_rs2{b}", [128, NT], F32) for b in range(BPC)]
        self.st_bs2 = [self.sb(f"st_bs2{b}", [128, NT], F32) for b in range(BPC)]
        self.st_tmp = self.sb("st_tmp", [128, 2 * NT], F32)
        self.st_t = self.sb("st_t", [128, 2, 8], F32)
        self.cw = [self.sb(f"cw{b}", [128, NT, NEXP], F32) for b in range(BPC)]
        self.lgt = self.sb("lgt", [128, 2, NEXP], F32)
        self.lgt2 = self.sb("lgt2", [128, 2, NEXP], F32)
        self.lgt3 = self.sb("lgt3", [128, 2, NEXP], F32)
        self.epsv = self.sb("epsv", [128, 1], F32)
        self.pi2 = self.sb("pi2", [128, 1], F32)
        self.ones64 = self.sb("ones64", [1, 64], F32)
        self.iotv = self.sb("iotv", [128, 4], F32)
        self.invf = self.sb("invf", [128, 1], F32)
        self.rrow = self.sb("rrow", [1, 2, self.TQ], F32)
        self.rinv = self.sb("rinv", [1, 2, self.TQ], F32)
        self.ysc = self.sb("ysc", [64, 2, self.TQ], F32)
        self.rwb = self.sb("rwb", [128, 8, NEXP], BF16)
        self.trimask = self.sb("trimask", [128, 128], BF16)
        self.ustrict = self.sb("ustrict", [128, 128], BF16)
        self.erow = self.sb("erow", [128, NEXP], F32)
        self.ones128f = self.sb("ones128f", [1, 128], F32)
        self.onescol = self.sb("onescol", [128, 1], BF16)
        self.senti = self.sb("senti", [128, 24], mybir.dt.int32)
        self.zrow = self.sb("zrow", [128, D], F32)

        self.B_xstage = Buf(2)
        self.B_nstage = Buf(2)
        self.B_n2stage = Buf(2)
        self.B_sqs = Buf(2)
        self.B_stats = [Buf(NT) for _ in range(BPC)]
        self.B_stsq = [Buf(NT) for _ in range(BPC)]
        self.B_tmp = Buf(1)
        self.B_tbl = Buf(1)
        self.B_s3 = [Buf(NT) for _ in range(BPC)]
        self.B_stt = Buf(2)
        self.B_cw = [Buf(NT) for _ in range(BPC)]
        self.B_lgt = Buf(2)
        self.B_lgt2 = Buf(2)
        self.B_lgt3 = Buf(2)
        self.B_const = Buf(1)
        self.B_rope = Buf(2)
        self.B_rwb = Buf(1)
        self.B_in = Buf(1)
        self.B_scr_xn = [Buf(NT) for _ in range(BPC)]
        self.B_scr_hn = [Buf(NT) for _ in range(BPC)]
        self.B_scr_Hn = [Buf(NT + 1) for _ in range(BPC)]
        self.B_scr_acc = [Buf(NT) for _ in range(BPC)]
        self.B_scr_moe = [Buf(NEXP) for _ in range(BPC)]
        self.B_scr_moe2 = [Buf(NEXP) for _ in range(BPC)]
        self.B_scrg = [Buf(1), Buf(1)]
        self.B_scr_cw = [Buf(NT + 1) for _ in range(BPC)]
        self.B_scr_idx = [Buf(max(1, T // 1024)) for _ in range(BPC)]
        self.B_scr_w = {n: Buf(8) for n in wnames}
        self.B_scr_w["wqb"] = Buf(8)
        self.B_scr_w["wkb"] = Buf(8)
        self.B_scr_rw8 = Buf(8)
        self.B_scr_fc1 = Buf(NEXP * 8)
        self.B_scr_fc2 = Buf(NEXP * 16)

        self.s_ldx = [kb.slot("s_ldx0"), kb.slot("s_ldx1")]
        self.s_stn = [kb.slot("s_stn0"), kb.slot("s_stn1")]
        self.s_stn2 = [kb.slot("s_stn2a"), kb.slot("s_stn2b")]
        self.s_stn3 = [kb.slot("s_stn3a"), kb.slot("s_stn3b")]
        self.s_ldt = [kb.slot(f"s_ldt{i}") for i in range(4)]
        self.s_ldw = [kb.slot("s_ldw0"), kb.slot("s_ldw1")]
        self.s_stw = [kb.slot("s_stw0"), kb.slot("s_stw1")]
        self.s_acc = [kb.slot("s_acc0"), kb.slot("s_acc1")]
        self.s_acc2 = [kb.slot("s_acc2a"), kb.slot("s_acc2b")]
        self.s_out = [kb.slot("s_out0"), kb.slot("s_out1")]
        self.s_tab = kb.slot("s_tab")
        self.s_fc1 = [kb.slot(f"s_fc1_{i}") for i in range(4)]
        self.s_fc2 = [kb.slot(f"s_fc2_{i}") for i in range(4)]
        self.s_pc = [kb.slot(f"s_pc_{i}") for i in range(9)]
        self.s_ix = [kb.slot(f"s_ix_{i}") for i in range(2)]
        self.s_gg = [kb.slot(f"s_gg_{i}") for i in range(3)]
        self.s_sca = [kb.slot(f"s_sca_{i}") for i in range(2)]
        self.s_zz = kb.slot("s_zz")
        self.s_dbg = kb.slot("s_dbg")

        self.rg_idx = nc.gpsimd.alloc_register("bc_idx")
        nc.gpsimd.reg_mov(self.rg_idx, max(1, self.T // 1024) * 3072 - 1)
        self.rg_tok = nc.gpsimd.alloc_register("bc_tok")
        nc.gpsimd.reg_mov(self.rg_tok, self.T)

        self.pb = []
        for i in range(4):
            h = es.enter_context(nc.psum_tensor(f"pb{i}", [128, 1024], F32))
            self.pb.append((h, Buf(2)))

        static_end = nc.SBUF_PARTITION_SIZE_BYTES - nc.sbuf_bytes_remaining
        self.AR = Arena(nc, static_end + 64, nc.sbuf_top - static_end - 128)

        with es:
            self.p0_consts()
            self.p0_weights()
            self.p1(0)
            for b in range(self.BPC):
                self.batch_front(b)
                self.p7(b)
                if b + 1 < self.BPC:
                    self.p3(b + 1)
                self.p8(b)
            for sl in self.s_out:
                if sl.tok is not None:
                    kb.raw["sp"].wait_ge(sl.sem, 16 * sl.rounds)
        return nc

    def p0_consts(self):
        self.kb.tag = 'p0t'
        kb = self.kb
        kb.op("pool", "memset", self.epsv[:], EPS, writes=[self.B_const])
        kb.op("pool", "memset", self.pi2[:], float(np.pi / 2), writes=[self.B_const])
        kb.op("pool", "memset", self.ones64[:], 1.0, writes=[self.B_const])
        kb.op("pool", "memset", self.trimask[:], 1.0, writes=[self.B_const])
        kb.op("pool", "affine_select", self.trimask[:], self.trimask[:],
              [[1, 128]], ALU.is_ge, 0.0, base=0, channel_multiplier=-1,
              writes=[self.B_const])
        kb.op("pool", "memset", self.ustrict[:], 1.0, writes=[self.B_const])
        kb.op("pool", "affine_select", self.ustrict[:], self.ustrict[:],
              [[1, 128]], ALU.is_ge, 0.0, base=-1, channel_multiplier=-1,
              writes=[self.B_const])
        kb.op("pool", "iota", self.erow[:], [[1, NEXP]], base=0,
              channel_multiplier=0, allow_small_or_imprecise_dtypes=True,
              writes=[self.B_const])
        kb.op("pool", "memset", self.ones128f[:], 1.0, writes=[self.B_const])
        kb.op("pool", "memset", self.onescol[:], 1.0, writes=[self.B_const])
        kb.op("pool", "memset", self.senti[:], self.T, writes=[self.B_const])
        kb.op("pool", "memset", self.zrow[:], 0.0, writes=[self.B_const])

    def build_tables_sbuf(self, cp, tag):
        self.kb.tag = 'p0t'
        kb, T, AR = self.kb, self.T, self.AR
        BT = self.B_tbl
        tabc, B_tc = AR.alloc(f"tabc{tag}", [128, T], F32)
        tabs, B_ts = AR.alloc(f"tabs{tag}", [128, T], F32)
        scr, B_sc = AR.alloc(f"tabscr{tag}", [128, T], F32)
        kb.op("pool", "iota", self.iotv[:, 0:1], [[1, 1]], base=2 * cp,
              channel_multiplier=8, allow_small_or_imprecise_dtypes=True,
              writes=[BT])
        kb.op("pool", "memset", self.iotv[:, 1:2], 1.0, writes=[BT])
        kb.op("pool", "affine_select", self.iotv[:, 1:2], self.iotv[:, 1:2],
              [[1, 1]], ALU.is_ge, 0.0, base=-64, channel_multiplier=1,
              writes=[BT])
        kb.op("dve", "scalar_tensor_tensor", self.iotv[:, 2:3],
              self.iotv[:, 1:2], -511.0, self.iotv[:, 0:1], ALU.mult, ALU.add,
              reads=[BT], writes=[BT])
        kb.op("dve", "tensor_scalar_mul", self.iotv[:, 3:4], self.iotv[:, 2:3],
              -math.log(ROPE_BASE) / 512.0, reads=[BT], writes=[BT])
        kb.op("act", "activation", self.invf[:], self.iotv[:, 3:4], AF.Exp,
              bias=0.0, scale=1.0, reads=[BT], writes=[BT])
        kb.op("pool", "memset", tabc[:, 0:1], 1.0, writes=[B_tc])
        kb.op("pool", "memset", tabs[:, 0:1], 0.0, writes=[B_ts])
        if T > 1:
            kb.op("act", "activation", tabs[:, 1:2], self.invf[:], AF.Sin,
                  bias=0.0, scale=1.0, reads=[BT], writes=[B_ts])
            kb.op("act", "activation", tabc[:, 1:2], self.invf[:], AF.Sin,
                  bias=self.pi2[:], scale=1.0, reads=[BT, self.B_const],
                  writes=[B_tc])
        s = 2
        while s < T:
            ext = s - 1
            w = min(s - 1, T - s)
            ce = tabc[:, ext:ext + 1]
            se = tabs[:, ext:ext + 1]
            kb.op("dve", "tensor_scalar_mul", scr[:, 0:w], tabs[:, 1:1 + w],
                  se, reads=[B_ts], writes=[B_sc])
            kb.op("dve", "scalar_tensor_tensor", tabc[:, s:s + w],
                  tabc[:, 1:1 + w], ce, scr[:, 0:w], ALU.mult, ALU.subtract,
                  reads=[B_sc, B_tc], writes=[B_tc])
            kb.op("dve", "tensor_scalar_mul", scr[:, 0:w], tabc[:, 1:1 + w],
                  se, reads=[B_tc], writes=[B_sc])
            kb.op("dve", "scalar_tensor_tensor", tabs[:, s:s + w],
                  tabs[:, 1:1 + w], ce, scr[:, 0:w], ALU.mult, ALU.add,
                  reads=[B_sc, B_ts], writes=[B_ts])
            s = s + w
        AR.freez(scr)
        return tabc, B_tc, tabs, B_ts

    def p0_weights(self):
        kb = self.kb
        semB = self.s_pc[8]

        def cast_dma(slot, src_ap, dst_ap, rows, cols):
            n = 0
            for r0 in range(0, rows, 256):
                rr = min(256, rows - r0)
                inst = kb.raw["pool"].dma_start(dst_ap[r0:r0 + rr, 0:cols],
                                                src_ap[r0:r0 + rr, 0:cols])
                inst.then_inc(slot.sem, 16)
                slot.rounds += 1
                n += 1
            return n

        for wi, n_ in enumerate(["wqb", "wkb", "wv_hm", "fwq", "fwk",
                                 "fwv", "fwo"]):
            sl_ = self.s_pc[wi]
            cast_dma(sl_, self.w_d[n_], self.scr_w[n_], D,
                     128 if n_ in ("wqb", "wkb") else D)
            tokn = (sl_.sem, 16 * sl_.rounds)
            sl_.tok = tokn
            bb = self.B_scr_w[n_]
            for u in range(8):
                bb.w[u] = tokn
        cast_dma(self.s_pc[7], self.rw_d, self.scr_rw, D, NEXP)
        tokA = (self.s_pc[7].sem, 16 * self.s_pc[7].rounds)
        self.s_pc[7].tok = tokA
        for u in range(8):
            self.B_scr_rw8.w[u] = tokA
        for e in range(NEXP):
            cast_dma(semB, self.fc1_d[e], self.scr_fc1[e], D, EDIM)
            cast_dma(semB, self.fc2_d[e], self.scr_fc2[e], EDIM, D)
        tokB = (semB.sem, 16 * semB.rounds)
        semB.tok = tokB
        for u in range(NEXP * 8):
            self.B_scr_fc1.w[u] = tokB
        for u in range(NEXP * 16):
            self.B_scr_fc2.w[u] = tokB

    def p1(self, b):
        for f in self.p1_emitters(b):
            f()

    def p1_emitters(self, b):
        NT = self.NT

        def loadx(i):
            self.kb.tag = 'p1'
            sl = i % 2
            self.kb.dma(self.s_ldx[sl], self.xstage[:, sl, :],
                        self.x_d[b, i * 128:(i + 1) * 128, :],
                        reads=[self.B_in], writes=[(self.B_xstage, sl)])

        def statsA(i):
            self.kb.tag = 'p1'
            kb = self.kb
            sl = i % 2
            xt = self.xstage[:, sl, :]
            kb.op("dve", "reduce_sum", self.st_m[b][:, i:i + 1], xt, axis=AX,
                  reads=[(self.B_xstage, sl)], writes=[(self.B_stats[b], i)])
            kb.op("act", "activation", self.sqs[:, sl, :], xt, AF.Square,
                  bias=0.0, scale=1.0, accum_out=self.st_sq[b][:, i:i + 1],
                  reads=[(self.B_xstage, sl)],
                  writes=[(self.B_sqs, sl), (self.B_stsq[b], i)])

        def batch_stats():
            self.kb.tag = 'p1'
            kb = self.kb
            allu = list(range(NT))
            R = [(self.B_stats[b], allu)]
            W = [(self.B_stats[b], allu)]
            RS = [(self.B_stsq[b], allu)]
            BT = self.B_tmp
            tmp0 = self.st_tmp[:, 0:NT]
            tmp1 = self.st_tmp[:, NT:2 * NT]
            m = self.st_m[b][:, 0:NT]
            rstd = self.st_rstd[b][:, 0:NT]
            b1 = self.st_b1[b][:, 0:NT]
            s2 = self.st_s2[b][:, 0:NT]
            kb.op("dve", "tensor_scalar_mul", m, m, 1.0 / D, reads=R, writes=W)
            kb.op("dve", "tensor_mul", tmp0, m, m, reads=R, writes=[BT])
            kb.op("dve", "scalar_tensor_tensor", tmp1,
                  self.st_sq[b][:, 0:NT], 1.0 / D, tmp0, ALU.mult,
                  ALU.subtract, reads=RS + [BT], writes=[BT])
            kb.op("act", "activation", tmp0, tmp1, AF.Ln, bias=self.epsv[:],
                  scale=1.0, reads=[BT, self.B_const], writes=[BT])
            kb.op("act", "activation", rstd, tmp0, AF.Exp, bias=0.0,
                  scale=-0.5, reads=[BT], writes=W)
            kb.op("dve", "scalar_tensor_tensor", b1, m, -1.0, rstd,
                  ALU.mult, ALU.mult, reads=R, writes=W)
            kb.op("dve", "tensor_mul", tmp1, tmp1, rstd, reads=[BT] + R,
                  writes=[BT])
            kb.op("dve", "tensor_mul", tmp1, tmp1, rstd, reads=[BT] + R,
                  writes=[BT])
            kb.op("act", "activation", tmp0, tmp1, AF.Ln, bias=self.epsv[:],
                  scale=1.0, reads=[BT, self.B_const], writes=[BT])
            kb.op("act", "activation", s2, tmp0, AF.Exp, bias=0.0, scale=-0.5,
                  reads=[BT], writes=W)
            kb.op("dve", "tensor_mul", self.st_rs2[b][:, 0:NT], rstd, s2,
                  reads=R, writes=W)
            kb.op("dve", "tensor_mul", self.st_bs2[b][:, 0:NT], b1, s2,
                  reads=R, writes=W)

        def compB(i):
            self.kb.tag = 'p1'
            kb = self.kb
            sl = i % 2
            xt = self.xstage[:, sl, :]
            hm_out = bass.AP(self.nstage, sl * D,
                             [[2 * D, 128], [1, 64], [64, 8], [512, 2]])
            kb.op("act", "activation", hm_out, xt, AF.Identity,
                  bias=self.st_b1[b][:, i:i + 1],
                  scale=self.st_rstd[b][:, i:i + 1],
                  reads=[(self.B_xstage, sl), (self.B_stats[b], i)],
                  writes=[(self.B_nstage, sl)])
            kb.op("act", "activation", self.n2stage[:, sl, :], xt, AF.Identity,
                  bias=self.st_bs2[b][:, i:i + 1],
                  scale=self.st_rs2[b][:, i:i + 1],
                  reads=[(self.B_xstage, sl), (self.B_stats[b], i)],
                  writes=[(self.B_n2stage, sl)])
            kb.dma(self.s_stn[sl], self.scr_xn[b][i * 128:(i + 1) * 128, :],
                   self.nstage[:, sl, :], reads=[(self.B_nstage, sl)],
                   writes=[(self.B_scr_xn[b], i)])
            kb.dma(self.s_stn2[sl], self.scr_hn[b][i * 128:(i + 1) * 128, :],
                   self.n2stage[:, sl, :], reads=[(self.B_n2stage, sl)],
                   writes=[(self.B_scr_hn[b], i)])

        sched = [lambda: (loadx(0), loadx(1))]
        for p in range(NT // 2):
            def emA(p=p):
                statsA(2 * p)
                if 2 * p + 2 < NT:
                    loadx(2 * p + 2)
                statsA(2 * p + 1)
                if 2 * p + 3 < NT:
                    loadx(2 * p + 3)
            sched.append(emA)
        sched.append(lambda: (batch_stats(), loadx(0), loadx(1)))
        for p in range(NT // 2):
            def emB(p=p):
                compB(2 * p)
                if 2 * p + 2 < NT:
                    loadx(2 * p + 2)
                compB(2 * p + 1)
                if 2 * p + 3 < NT:
                    loadx(2 * p + 3)
            sched.append(emB)
        return sched

    def load_T_strips(self, scr_ap, B_src, name, carryfrom=None):
        T = self.T
        chunks = []
        for c in range(8):
            h, bf = self.AR.alloc(f"{name}{c}", [128, T], BF16, carry=carryfrom)
            self.kb.dma(self.s_ldt[c % 4], h[:],
                        scr_ap[0:T, c * 128:(c + 1) * 128],
                        reads=[(B_src, list(range(self.NT)))], writes=[bf],
                        transpose=True)
            chunks.append((h, bf))
        return chunks

    def p3(self, b):
        self.kb.tag = 'p3'
        kb, T, NT, AR = self.kb, self.T, self.NT, self.AR
        TH = min(1024, T)
        NTH = T // TH
        if self._pre3 is not None:
            xnT, _ = self._pre3
            self._pre3 = None
        else:
            xnT = self.load_T_strips(self.scr_xn[b], self.B_scr_xn[b],
                                     f"xnT{b}")
        qkwh, B_qkw = AR.alloc(f"qkw{b}", [128, 16, 128], BF16, units=16)
        qkw = (qkwh, B_qkw)
        for c in range(8):
            kb.dma(self.s_ldw[c % 2], qkw[0][:, c, :],
                   self.scr_w["wqb"][c * 128:(c + 1) * 128, :],
                   reads=[(self.B_scr_w["wqb"], c)], writes=[(qkw[1], c)])
            kb.dma(self.s_ldw[c % 2], qkw[0][:, 8 + c, :],
                   self.scr_w["wkb"][c * 128:(c + 1) * 128, :],
                   reads=[(self.B_scr_w["wkb"], c)],
                   writes=[(qkw[1], 8 + c)])
        self.xnT = xnT
        dstQ = [AR.alloc(f"Qh{b}_{c}", [128, T], BF16) for c in range(8)]
        dstK = [AR.alloc(f"Kh{b}_{c}", [128, T], BF16) for c in range(8)]
        QK = (dstQ, dstK)
        ppc = 0
        for cp in range(4):
            tabc, B_tc, tabs, B_ts = self.build_tables_sbuf(cp, f"{b}_{cp}")
            self.kb.tag = 'p3'
            for wi in range(2):
                dst = QK[wi]
                for th in range(NTH):
                    pp = ppc % 2
                    ppc += 1
                    tc = tabc[:, th * TH:(th + 1) * TH]
                    ts = tabs[:, th * TH:(th + 1) * TH]
                    pA, bA = self.PB(pp, 0, TH)
                    pB, bB = self.PB(2 + pp, 0, TH)
                    for gi, (ps_, bs_) in enumerate(((pA, bA), (pB, bB))):
                        c = cp + 4 * gi
                        items = []
                        for q in range(max(1, TH // 512)):
                            w = min(512, TH)
                            items.append(dict(
                                out=ps_[:, q * 512:q * 512 + w],
                                lhsT=qkw[0][:, 8 * wi + c, :],
                                rhs=xnT[c][0][:, th * TH + q * 512:
                                              th * TH + q * 512 + w],
                                reads=[(qkw[1], 8 * wi + c), xnT[c][1]],
                                start=True, stop=True))
                        kb.mmg(items, writes=[bs_])
                    s1 = self.sqs[:, 0, 0:TH]
                    s2 = self.sqs[:, 1, 0:TH]
                    kb.op("dve", "tensor_mul", s1, pA, tc,
                          reads=[bA, B_tc], writes=[(self.B_sqs, 0)])
                    kb.op("dve", "tensor_mul", s2, pB, ts,
                          reads=[bB, B_ts], writes=[(self.B_sqs, 1)])
                    kb.op("pool", "tensor_sub",
                          dst[cp][0][:, th * TH:(th + 1) * TH], s1, s2,
                          reads=[(self.B_sqs, 0), (self.B_sqs, 1)],
                          writes=[dst[cp][1]])
                    kb.op("dve", "tensor_mul", s1, pA, ts,
                          reads=[bA, B_ts], writes=[(self.B_sqs, 0)])
                    kb.op("dve", "tensor_mul", s2, pB, tc,
                          reads=[bB, B_tc], writes=[(self.B_sqs, 1)])
                    kb.op("pool", "tensor_add",
                          dst[cp + 4][0][:, th * TH:(th + 1) * TH], s1, s2,
                          reads=[(self.B_sqs, 0), (self.B_sqs, 1)],
                          writes=[dst[cp + 4][1]])
            AR.freez(tabc)
            AR.freez(tabs)
        self.Qh, self.Kh = QK
        AR.freez(qkw[0])
        self.kb.tag = 'p3v'
        wbuf = [AR.alloc(f"wb{b}_{c}", [128, D], BF16) for c in range(8)]
        for c in range(8):
            kb.dma(self.s_ldw[c % 2], wbuf[c][0][:],
                   self.scr_w["wv_hm"][c * 128:(c + 1) * 128, :],
                   reads=[(self.B_scr_w["wv_hm"], c)], writes=[wbuf[c][1]])
        Ve = [AR.alloc(f"Ve{b}_{i}", [128, 1040], BF16) for i in range(NT)]
        self.Ve = Ve
        for i in range(NT):
            pp = i % 2
            pV, bV = self.PB(2 + pp, 0, 1024)
            items = []
            for gh in range(2):
                for fc in range(8):
                    items.append(dict(
                        out=pV[:, gh * 512:(gh + 1) * 512],
                        lhsT=xnT[fc][0][:, i * 128:(i + 1) * 128],
                        rhs=wbuf[fc][0][:, gh * 512:(gh + 1) * 512],
                        reads=[xnT[fc][1], wbuf[fc][1]],
                        start=(fc == 0), stop=(fc == 7)))
            kb.mmg(items, writes=[bV])
            kb.op("act", "activation", Ve[i][0][:, 0:1024], pV,
                  AF.Copy, bias=0.0, scale=1.0,
                  reads=[bV], writes=[Ve[i][1]])
            kb.op("pool", "memset", Ve[i][0][:, 1024:1025], 1.0,
                  writes=[Ve[i][1]])
        for h, bf in wbuf:
            AR.freez(h)
        for h, bf in xnT:
            AR.freez(h)
        self.xnT = None

    def p4(self, b):
        self.kb.tag = 'p4'
        kb, T, NT, AR = self.kb, self.T, self.NT, self.AR
        TQ, NTQ = self.TQ, self.NTQ
        Qh, Kh, Ve = self.Qh, self.Kh, self.Ve
        P = [AR.alloc(f"P{b}_{k}", [128, TQ], BF16) for k in range(NT)]
        self._pre5 = self.load_T_strips(self.scr_hn[b], self.B_scr_hn[b],
                                        f"hnT{b}")
        if True:
            KTQ = TQ // 128
            for qb in range(NTQ):
                nkt = KTQ * (qb + 1)
                self.kb.tag = 'p4s'
                for kt in range(nkt):
                    off = max(0, kt * 128 - qb * TQ)
                    w = TQ - off
                    pp = kt % 2
                    pS, bS = self.PB(pp, 512, 1024)
                    items = []
                    for gc in range(8):
                        items.append(dict(
                            out=pS[:, off:off + w],
                            lhsT=Kh[gc][0][:, kt * 128:(kt + 1) * 128],
                            rhs=Qh[gc][0][:, qb * TQ + off:qb * TQ + off + w],
                            reads=[Kh[gc][1], Qh[gc][1]],
                            start=(gc == 0), stop=(gc == 7)))
                    kb.mmg(items, writes=[bS])
                    kb.op("act", "activation", P[kt][0][:, off:off + w],
                          pS[:, off:off + w], AF.Exp, bias=0.0, scale=GAMMA1,
                          reads=[bS], writes=[P[kt][1]])
                    if kt * 128 >= qb * TQ:
                        kb.op("dve", "tensor_mul", P[kt][0][:, off:off + 128],
                              P[kt][0][:, off:off + 128], self.trimask[:],
                              reads=[P[kt][1], self.B_const], writes=[P[kt][1]])
                self.kb.tag = 'p4av'
                for qi in range(qb * TQ // 128, (qb + 1) * TQ // 128):
                    pp = qi % 2
                    pY, bY = self.PB(2 + pp, 0, 1024)
                    pR, bR = self.PB(pp, 0, 16)
                    items = []
                    for kt in range(qi + 1):
                        off = qi * 128 - qb * TQ
                        stat = P[kt][0][:, off:off + 128]
                        for gh in range(2):
                            items.append(dict(
                                out=pY[:, gh * 512:(gh + 1) * 512],
                                lhsT=stat,
                                rhs=Ve[kt][0][:, gh * 512:(gh + 1) * 512],
                                reads=[P[kt][1], Ve[kt][1]],
                                start=(kt == 0), stop=(kt == qi)))
                        items.append(dict(
                            out=pR[:, 0:1], lhsT=stat,
                            rhs=Ve[kt][0][:, 1024:1025],
                            reads=[P[kt][1], Ve[kt][1]],
                            start=(kt == 0), stop=(kt == qi)))
                    kb.mmg(items, writes=[bY, bR])
                    kb.op("dve", "reciprocal", self.st_t[:, pp, 0:1], pR[:, 0:1],
                          reads=[bR], writes=[(self.B_stt, pp)])
                    ot = self.sqs[:, pp, :]
                    kb.op("act", "activation", ot, pY, AF.Identity,
                          bias=0.0, scale=self.st_t[:, pp, 0:1],
                          reads=[bY, (self.B_stt, pp)],
                          writes=[(self.B_sqs, pp)])
                    kb.dma(self.s_acc[pp], self.scr_acc[b][qi * 128:(qi + 1) * 128, :],
                           ot, reads=[(self.B_sqs, pp)],
                           writes=[(self.B_scr_acc[b], qi)])
                    if self.debug:
                        kb.dma(self.s_dbg, self.dbg_y[b, qi * 128:(qi + 1) * 128, :],
                               ot, reads=[(self.B_sqs, pp)], writes=[])
        for h, bf in P:
            AR.freez(h)
        for lst in (self.Qh, self.Kh, self.Ve):
            for h, bf in lst:
                AR.freez(h)
        self.Qh = self.Kh = self.Ve = None

    def p5(self, b):
        self.kb.tag = 'p5'
        kb, T, NT, AR = self.kb, self.T, self.NT, self.AR
        if getattr(self, "_pre5", None) is not None:
            hnT = self._pre5
            self._pre5 = None
        else:
            hnT = self.load_T_strips(self.scr_hn[b], self.B_scr_hn[b],
                                     f"hnT{b}")
        wsets = [[AR.alloc(f"wf{b}_{s}_{c}", [128, D], BF16) for c in range(8)]
                 for s in range(2)]
        TH = min(1024, T)
        NTH = T // TH
        QK = []
        for wi, (wname, oname) in enumerate(
                (("fwq", f"Q2T{b}"), ("fwk", f"K2T{b}"))):
            wbuf = wsets[wi % 2]
            dst = [AR.alloc(f"{oname}_{c}", [128, T], BF16) for c in range(8)]
            for c in range(8):
                kb.dma(self.s_ldw[c % 2], wbuf[c][0][:],
                       self.scr_w[wname][c * 128:(c + 1) * 128, :],
                       reads=[(self.B_scr_w[wname], c)], writes=[wbuf[c][1]])
            if True:
                for gc in range(8):
                    for th in range(NTH):
                        pp = (gc * NTH + th) % 2
                        pQ, bQ = self.PB(pp, 0, TH)
                        items = []
                        for q in range(max(1, TH // 512)):
                            w = min(512, TH)
                            for fc in range(8):
                                items.append(dict(
                                    out=pQ[:, q * 512:q * 512 + w],
                                    lhsT=wbuf[fc][0][:, gc * 128:(gc + 1) * 128],
                                    rhs=hnT[fc][0][:, th * TH + q * 512:
                                                    th * TH + q * 512 + w],
                                    reads=[wbuf[fc][1], hnT[fc][1]],
                                    start=(fc == 0), stop=(fc == 7)))
                        kb.mmg(items, writes=[bQ])
                        kb.op("act", "activation",
                              dst[gc][0][:, th * TH:(th + 1) * TH], pQ,
                              AF.Copy, bias=0.0, scale=1.0,
                              reads=[bQ], writes=[dst[gc][1]])
            QK.append(dst)
        self.Q2T, self.K2T = QK
        wbuf = wsets[0]
        for c in range(8):
            kb.dma(self.s_ldw[c % 2], wbuf[c][0][:],
                   self.scr_w["fwv"][c * 128:(c + 1) * 128, :],
                   reads=[(self.B_scr_w["fwv"], c)], writes=[wbuf[c][1]])
        V2e = [AR.alloc(f"V2e{b}_{i}", [128, 1040], BF16) for i in range(NT)]
        self.V2e = V2e
        if True:
            for i in range(NT):
                for gh in range(2):
                    pp = (2 * i + gh) % 2
                    pV, bV = self.PB(2, pp * 512, pp * 512 + 512)
                    items = []
                    for fc in range(8):
                        items.append(dict(
                            out=pV,
                            lhsT=hnT[fc][0][:, i * 128:(i + 1) * 128],
                            rhs=wbuf[fc][0][:, gh * 512:(gh + 1) * 512],
                            reads=[hnT[fc][1], wbuf[fc][1]],
                            start=(fc == 0), stop=(fc == 7)))
                    kb.mmg(items, writes=[bV])
                    dst_ap = bass.AP(V2e[i][0], gh * 8 * 65,
                                     [[1040, 128], [65, 8], [1, 64]])
                    kb.op("act", "activation", dst_ap, pV, AF.Copy,
                          bias=0.0, scale=1.0, reads=[bV],
                          writes=[V2e[i][1]])
                ones_ap = bass.AP(V2e[i][0], 64, [[1040, 128], [65, 16]])
                kb.op("pool", "memset", ones_ap, 1.0, writes=[V2e[i][1]])
        for ws in wsets:
            for h, bf in ws:
                AR.freez(h)
        for h, bf in hnT:
            AR.freez(h)

    def p6(self, b, inject=()):
        self.kb.tag = 'p6'
        inject = list(inject)
        kb, T, NT, AR = self.kb, self.T, self.NT, self.AR
        TQ, NTQ = self.TQ, self.NTQ
        Q2T, K2T, V2e = self.Q2T, self.K2T, self.V2e
        KTQ = TQ // 128
        Pf = [AR.alloc(f"Pf{b}_{i}", [128, TQ], BF16) for i in range(6)]
        OT = [None] * 8
        wbuf = [AR.alloc(f"wo{b}_{c}", [128, D], BF16) for c in range(8)]
        for c in range(8):
            kb.dma(self.s_ldw[c % 2], wbuf[c][0][:],
                   self.scr_w["fwo"][c * 128:(c + 1) * 128, :],
                   reads=[(self.B_scr_w["fwo"], c)], writes=[wbuf[c][1]])
        self._p6_pending = None
        if True:
            for hp in range(8):
                if OT[hp] is None:
                    OT[hp] = AR.alloc(f"OT{b}_{hp}", [128, T], BF16)
                for qb in range(NTQ):
                    if inject:
                        inject.pop(0)()
                    self.kb.tag = 'p6'

                    nkt = KTQ * (qb + 1)
                    pq6 = (hp * NTQ + qb) % 2
                    pyt = []
                    for hi in range(2):
                        pY, bY = self.PB(2 + pq6, hi * 512, hi * 512 + TQ)
                        pyt.append((pY, bY))
                    def emit_scores(kt):
                        self.kb.tag = 'p6s'
                        off = max(0, kt * 128 - qb * TQ)
                        w = TQ - off
                        pp = kt % 2
                        pf = kt % 3
                        for hi in range(2):
                            base = hi * 64
                            pS, bS = self.PB(pp, hi * 512, hi * 512 + TQ)
                            kb.mmg([dict(
                                out=pS[:, off:off + w],
                                lhsT=K2T[hp][0][base:base + 64,
                                               kt * 128:(kt + 1) * 128],
                                rhs=Q2T[hp][0][base:base + 64,
                                              qb * TQ + off:qb * TQ + off + w],
                                reads=[K2T[hp][1], Q2T[hp][1]],
                                start=True, stop=True)],
                                writes=[bS])
                            kb.op("act", "activation",
                                  Pf[2 * pf + hi][0][:, off:off + w],
                                  pS[:, off:off + w], AF.Exp,
                                  bias=0.0, scale=GAMMA2,
                                  reads=[bS], writes=[Pf[2 * pf + hi][1]])
                            if kt * 128 >= qb * TQ:
                                kb.op("dve", "tensor_mul",
                                      Pf[2 * pf + hi][0][:, off:off + 128],
                                      Pf[2 * pf + hi][0][:, off:off + 128],
                                      self.trimask[:],
                                      reads=[Pf[2 * pf + hi][1], self.B_const],
                                      writes=[Pf[2 * pf + hi][1]])

                    def emit_av(kt):
                        self.kb.tag = 'p6a'
                        off = max(0, kt * 128 - qb * TQ)
                        w = TQ - off
                        pf = kt % 3
                        for hi in range(2):
                            h = 2 * hp + hi
                            kb.mmg([dict(
                                out=pyt[hi][0][0:65, off:off + w],
                                lhsT=V2e[kt][0][:, h * 65:h * 65 + 65],
                                rhs=Pf[2 * pf + hi][0][:, off:off + w],
                                reads=[V2e[kt][1], Pf[2 * pf + hi][1]],
                                start=(kt == 0), stop=(kt == nkt - 1))],
                                writes=[pyt[hi][1]])

                    emit_scores(0)
                    if nkt > 1:
                        emit_scores(1)
                    for kt in range(2, nkt):
                        emit_scores(kt)
                        emit_av(kt - 2)
                    if nkt > 1:
                        emit_av(nkt - 2)
                    emit_av(nkt - 1)
                    if self._p6_pending is not None:
                        self._p6_pending()
                        self._p6_pending = None
                    self.kb.tag = 'p6n'
                    for hi in range(2):
                        pY, bY = pyt[hi]
                        kb.op("dve", "tensor_copy", self.ysc[:, hi, 0:TQ],
                              pY[0:64, :], reads=[bY],
                              writes=[(self.B_rope, hi)])
                        kb.op("act", "activation", self.rrow[:, hi, 0:TQ],
                              pY[64:65, :], AF.Ln, bias=0.0, scale=1.0,
                              reads=[bY], writes=[(self.B_rope, hi)])
                        kb.op("act", "activation", self.rinv[:, hi, 0:TQ],
                              self.rrow[:, hi, 0:TQ], AF.Exp, bias=0.0,
                              scale=-1.0, reads=[(self.B_rope, hi)],
                              writes=[(self.B_rope, hi)])
                    def part_b(hp=hp, qb=qb):
                        self.kb.tag = 'p6n'
                        for hi in range(2):
                            pRB, bRB = self.PB(1, hi * 512, hi * 512 + TQ,
                                               p0=0, p1=64)
                            kb.mmg([dict(out=pRB, lhsT=self.ones64[:],
                                         rhs=self.rinv[:, hi, 0:TQ],
                                         reads=[(self.B_rope, hi),
                                                self.B_const],
                                         start=True, stop=True)],
                                   writes=[bRB])
                            kb.op("dve", "tensor_tensor",
                                  OT[hp][0][hi * 64:hi * 64 + 64,
                                            qb * TQ:(qb + 1) * TQ],
                                  self.ysc[:, hi, 0:TQ], pRB, ALU.mult,
                                  reads=[(self.B_rope, hi), bRB],
                                  writes=[OT[hp][1]])
                    self._p6_pending = part_b
            if self._p6_pending is not None:
                self._p6_pending()
                self._p6_pending = None
            for hp in range(8):
                AR.freez(Q2T[hp][0])
                AR.freez(K2T[hp][0])
        for h_, bf in V2e:
            AR.freez(h_)
        for h_, bf in Pf:
            AR.freez(h_)
        self.Q2T = self.K2T = self.V2e = None
        self.kb.tag = 'p6wo'
        acct, B_acct = AR.alloc(f"acct{b}", [128, 2, D], F32, units=2)
        htile, B_ht = AR.alloc(f"ht{b}", [128, 2, D], F32, units=2)
        if True:
            for i in range(NT):
                sl = i % 2
                psw_, bW = self.PB(2 * sl, 0, 1024)
                kb.dma(self.s_ldx[sl], self.xstage[:, sl, :],
                       self.x_d[b, i * 128:(i + 1) * 128, :],
                       reads=[self.B_in], writes=[(self.B_xstage, sl)])
                kb.dma(self.s_acc[sl], acct[:, sl, :],
                       self.scr_acc[b][i * 128:(i + 1) * 128, :],
                       reads=[(self.B_scr_acc[b], i)], writes=[(B_acct, sl)])
                items = []
                for gh in range(2):
                    for gc in range(8):
                        items.append(dict(
                            out=psw_[:, gh * 512:(gh + 1) * 512],
                            lhsT=OT[gc][0][:, i * 128:(i + 1) * 128],
                            rhs=wbuf[gc][0][:, gh * 512:(gh + 1) * 512],
                            reads=[OT[gc][1], wbuf[gc][1]],
                            start=(gc == 0), stop=(gc == 7)))
                kb.mmg(items, writes=[bW])
                kb.op("act", "activation", self.sqs[:, sl, :],
                      self.xstage[:, sl, :], AF.Identity,
                      bias=self.st_b1[b][:, i:i + 1],
                      scale=self.st_rstd[b][:, i:i + 1],
                      reads=[(self.B_xstage, sl), (self.B_stats[b], i)],
                      writes=[(self.B_sqs, sl)])
                kb.op("dve", "tensor_add", htile[:, sl, :], self.sqs[:, sl, :],
                      psw_, reads=[(self.B_sqs, sl), bW],
                      writes=[(B_ht, sl)])
                kb.op("act", "activation", self.sqs[:, sl, :], htile[:, sl, :],
                      AF.Square, bias=0.0, scale=1.0,
                      accum_out=self.st_t[:, sl, 0:1],
                      reads=[(B_ht, sl)],
                      writes=[(self.B_sqs, sl), (self.B_stt, sl)])
                kb.op("act", "activation", self.st_t[:, sl, 1:2],
                      self.st_t[:, sl, 0:1],
                      AF.Ln, bias=self.epsv[:], scale=1.0 / D,
                      reads=[(self.B_stt, sl), self.B_const],
                      writes=[(self.B_stt, sl)])
                kb.op("act", "activation", self.st_s3[b][:, i:i + 1],
                      self.st_t[:, sl, 1:2], AF.Exp, bias=0.0, scale=-0.5,
                      reads=[(self.B_stt, sl)],
                      writes=[(self.B_s3[b], i)])
                kb.op("act", "activation", self.n2stage[:, sl, :],
                      htile[:, sl, :], AF.Identity, bias=0.0,
                      scale=self.st_s3[b][:, i:i + 1],
                      reads=[(B_ht, sl), (self.B_s3[b], i)],
                      writes=[(self.B_n2stage, sl)])
                kb.dma(self.s_stn3[sl], self.scr_Hn[b][i * 128:(i + 1) * 128, :],
                       self.n2stage[:, sl, :], reads=[(self.B_n2stage, sl)],
                       writes=[(self.B_scr_Hn[b], i)])
                kb.op("pool", "tensor_add", acct[:, sl, :], acct[:, sl, :],
                      htile[:, sl, :], reads=[(B_acct, sl), (B_ht, sl)],
                      writes=[(B_acct, sl)])
                kb.dma(self.s_acc2[sl], self.scr_acc[b][i * 128:(i + 1) * 128, :],
                       acct[:, sl, :], reads=[(B_acct, sl)],
                       writes=[(self.B_scr_acc[b], i)])
                if self.debug:
                    kb.dma(self.s_dbg, self.dbg_h[b, i * 128:(i + 1) * 128, :],
                           htile[:, sl, :], reads=[(B_ht, sl)], writes=[])
        for c in range(8):
            AR.freez(OT[c][0])
        for h_, bf in wbuf:
            AR.freez(h_)
        AR.freez(acct)
        AR.freez(htile)

    def p7(self, b):
        self.kb.tag = 'p7r'
        kb, T, NT, AR = self.kb, self.T, self.NT, self.AR
        C2 = 192
        CE = 2 * C2
        TH7 = min(1024, T)
        NTH7 = T // TH7
        NTPT = TH7 // 128
        IOA = bass.IndirectOffsetOnAxis
        fc1b = [AR.alloc(f"fc1_{b}_{c}", [128, EDIM], BF16) for c in range(8)]
        fc2b = [AR.alloc(f"fc2_{b}_{c}", [128, D], BF16) for c in range(16)]

        def load_fc(e):
            for c in range(8):
                kb.dma(self.s_fc1[c % 4], fc1b[c][0][:],
                       self.scr_fc1[e, c * 128:(c + 1) * 128, :],
                       reads=[(self.B_scr_fc1, e * 8 + c)],
                       writes=[fc1b[c][1]])
            for c in range(16):
                kb.dma(self.s_fc2[c % 4], fc2b[c][0][:],
                       self.scr_fc2[e, c * 128:(c + 1) * 128, :],
                       reads=[(self.B_scr_fc2, e * 16 + c)],
                       writes=[fc2b[c][1]], eng="pool")

        load_fc(0)
        self.kb.tag = 'p7r'
        HnT = self.load_T_strips(self.scr_Hn[b], self.B_scr_Hn[b], f"HnT{b}")
        for fc in range(8):
            kb.dma(self.s_ldw[fc % 2], self.rwb[:, fc, :],
                   self.scr_rw[fc * 128:(fc + 1) * 128, :],
                   reads=[(self.B_scr_rw8, fc)], writes=[self.B_rwb])
        kb.dma(self.s_zz, self.scr_Hn[b][T:T + 1, :], self.zrow[0:1, :],
               reads=[self.B_const],
               writes=[(self.B_scr_Hn[b], NT)], eng="pool")
        for z0 in range(0, T, 2048):
            zr = min(2048, T - z0)
            kb.dma(self.s_stn[1],
                   self.scr_cw[b][z0:z0 + zr, :]
                   .rearrange("(p a) b -> p (a b)", p=128),
                   self.zrow[:, 0:zr // 128 * NEXP],
                   reads=[self.B_const],
                   writes=[(self.B_scr_cw[b], list(range(NT + 1)))])
        kb.dma(self.s_stn[1], self.scr_cw[b][T:T + 1, :],
               self.zrow[0:1, 0:NEXP],
               reads=[self.B_const], writes=[(self.B_scr_cw[b], NT)])
        for th in range(NTH7):
            seg = self.scr_idx[b][th * 3072:(th + 1) * 3072]
            kb.dma(self.s_stn[0], seg.rearrange("(p a) -> p a", p=128),
                   self.senti[:, 0:24], reads=[self.B_const],
                   writes=[(self.B_scr_idx[b], th)])

        gtile, B_gt = AR.alloc(f"gt{b}", [128, 6, D], BF16, units=6)
        idxt, B_ix = AR.alloc(f"ix{b}", [128, 2, 4], mybir.dt.int32, units=2)
        cgt, B_cg = AR.alloc(f"cg{b}", [128, 2, 3 * NEXP], F32, units=2)
        base_r, B_base = AR.alloc(f"base{b}", [1, 16], F32)
        posf, B_posf = AR.alloc(f"posf{b}", [128, 2, 40], F32, units=2)
        tgti, B_tg = AR.alloc(f"tg{b}", [128, 2, 2], mybir.dt.int32, units=2)
        tki, B_tk = AR.alloc(f"tk{b}", [128, 2, 1], mybir.dt.int32, units=2)
        eqf, B_eqf = AR.alloc(f"eqf{b}", [128, 2, 16], F32, units=2)
        eqb, B_eqb = AR.alloc(f"eqb{b}", [128, 2, 16], BF16, units=2)
        Gs = [AR.alloc(f"G{b}_{c}", [128, CE], BF16) for c in range(16)]
        mst = [AR.alloc(f"mst{b}_{B}", [128, D], F32) for B in range(3)]
        msw, B_msw = AR.alloc(f"msw{b}", [64, 2, D], F32, units=2)
        ixw, B_ixw = AR.alloc(f"ixw{b}", [64, 2, 1], mybir.dt.int32, units=2)
        relu_s, B_rs = AR.alloc(f"relu{b}", [128, 2, CE], BF16, units=2)
        gthT = [[AR.alloc(f"gthT{b}_{s}_{c}", [128, CE], BF16) for c in range(8)]
                for s in range(2)]
        def compact_tile(th, tt):
            self.kb.tag = 'p7r'
            if tt == 0:
                kb.op("dve", "memset", base_r[:], 0.0, writes=[B_base])
            if True:
                i = th * NTPT + tt
                sl7 = i % 2
                stt = self.st_t[:, sl7, :]
                B_stt7 = (self.B_stt, sl7)
                psl, bL = self.PB(3, 512 + sl7 * 16, 512 + sl7 * 16 + NEXP)
                items = []
                for fc in range(8):
                    items.append(dict(
                        out=psl,
                        lhsT=HnT[fc][0][:, i * 128:(i + 1) * 128],
                        rhs=self.rwb[:, fc, :],
                        reads=[HnT[fc][1], self.B_rwb],
                        start=(fc == 0), stop=(fc == 7)))
                kb.mmg(items, writes=[bL])
                lg = self.lgt[:, sl7, :]
                B_lg7 = (self.B_lgt, sl7)
                kb.op("dve", "tensor_copy", lg, psl, reads=[bL], writes=[B_lg7])
                kb.op("dve", "reduce_max", stt[:, 0:1], lg, axis=AX,
                      reads=[B_lg7], writes=[B_stt7])
                kb.op("dve", "tensor_scalar", stt[:, 2:3], stt[:, 0:1], -1.0,
                      None, ALU.mult, reads=[B_stt7], writes=[B_stt7])
                eq1 = eqf[:, sl7, 0:NEXP]
                eq2 = eqf[:, sl7, NEXP:2 * NEXP]
                kb.op("dve", "tensor_scalar", eq1, lg, stt[:, 0:1], None,
                      ALU.is_equal, reads=[B_lg7, B_stt7],
                      writes=[(B_eqf, sl7)])
                kb.op("dve", "scalar_tensor_tensor", self.lgt2[:, sl7, :],
                      eq1, -1e30, lg, ALU.mult, ALU.add,
                      reads=[(B_eqf, sl7), B_lg7],
                      writes=[(self.B_lgt2, sl7)])
                kb.op("dve", "reduce_max", stt[:, 1:2], self.lgt2[:, sl7, :],
                      axis=AX, reads=[(self.B_lgt2, sl7)], writes=[B_stt7])
                kb.op("dve", "tensor_scalar", eq2, self.lgt2[:, sl7, :],
                      stt[:, 1:2], None, ALU.is_equal,
                      reads=[(self.B_lgt2, sl7), B_stt7],
                      writes=[(B_eqf, sl7)])
                kb.op("dve", "tensor_add", stt[:, 3:4], stt[:, 1:2],
                      stt[:, 2:3], reads=[B_stt7], writes=[B_stt7])
                kb.op("act", "activation", stt[:, 4:5], stt[:, 3:4], AF.Exp,
                      bias=0.0, scale=1.0, reads=[B_stt7], writes=[B_stt7])
                kb.op("dve", "tensor_scalar_add", stt[:, 5:6], stt[:, 4:5],
                      1.0, reads=[B_stt7], writes=[B_stt7])
                kb.op("dve", "reciprocal", stt[:, 6:7], stt[:, 5:6],
                      reads=[B_stt7], writes=[B_stt7])
                kb.op("dve", "tensor_mul", stt[:, 7:8], stt[:, 4:5],
                      stt[:, 6:7], reads=[B_stt7], writes=[B_stt7])
                kb.op("dve", "tensor_scalar_mul", self.cw[b][:, i, :], eq1,
                      stt[:, 6:7], reads=[(B_eqf, sl7), B_stt7],
                      writes=[(self.B_cw[b], i)])
                kb.op("dve", "scalar_tensor_tensor", self.cw[b][:, i, :],
                      eq2, stt[:, 7:8], self.cw[b][:, i, :],
                      ALU.mult, ALU.add,
                      reads=[(B_eqf, sl7), B_stt7, (self.B_cw[b], i)],
                      writes=[(self.B_cw[b], i)])
                kb.dma(self.s_stn2[i % 2], self.scr_cw[b][i * 128:(i + 1) * 128, :],
                       self.cw[b][:, i, :], reads=[(self.B_cw[b], i)],
                       writes=[(self.B_scr_cw[b], i)])
                if self.debug:
                    kb.dma(self.s_dbg, self.dbg_c[b, i * 128:(i + 1) * 128, :],
                           self.cw[b][:, i, :], reads=[(self.B_cw[b], i)],
                           writes=[])
                kb.op("dve", "tensor_copy", eqb[:, sl7, 0:16],
                      eqf[:, sl7, 0:16], reads=[(B_eqf, sl7)],
                      writes=[(B_eqb, sl7)])
                pP1, bP1 = self.PB(3, sl7 * 64, sl7 * 64 + NEXP)
                pP2, bP2 = self.PB(3, sl7 * 64 + 16, sl7 * 64 + 16 + NEXP)
                pB1, bB1 = self.PB(3, sl7 * 64 + 32, sl7 * 64 + 32 + NEXP)
                pB2, bB2 = self.PB(3, sl7 * 64 + 48, sl7 * 64 + 48 + NEXP)
                kb.mmg([dict(out=pP1, lhsT=self.ustrict[:],
                             rhs=eqb[:, sl7, 0:NEXP],
                             reads=[self.B_const, (B_eqb, sl7)],
                             start=True, stop=True)], writes=[bP1])
                kb.mmg([dict(out=pP2, lhsT=self.ustrict[:],
                             rhs=eqb[:, sl7, NEXP:2 * NEXP],
                             reads=[self.B_const, (B_eqb, sl7)],
                             start=True, stop=True)], writes=[bP2])
                kb.mmg([dict(out=pB1, lhsT=self.ones128f[:],
                             rhs=base_r[0:1, 0:NEXP],
                             reads=[self.B_const, B_base],
                             start=True, stop=True)], writes=[bB1])
                kb.mmg([dict(out=pB2, lhsT=self.ones128f[:],
                             rhs=base_r[0:1, NEXP:2 * NEXP],
                             reads=[self.B_const, B_base],
                             start=True, stop=True)], writes=[bB2])
                p1f = posf[:, sl7, 0:8]
                p2f = posf[:, sl7, 8:16]
                kb.op("act", "activation", p1f, pP1, AF.Copy, bias=0.0,
                      scale=1.0, reads=[bP1], writes=[(B_posf, sl7)])
                kb.op("dve", "tensor_add", p1f, p1f, pB1,
                      reads=[(B_posf, sl7), bB1], writes=[(B_posf, sl7)])
                kb.op("act", "activation", p2f, pP2, AF.Copy, bias=0.0,
                      scale=1.0, reads=[bP2], writes=[(B_posf, sl7)])
                kb.op("dve", "tensor_add", p2f, p2f, pB2,
                      reads=[(B_posf, sl7), bB2], writes=[(B_posf, sl7)])
                pC, bC = self.PB(2, sl7 * 32, sl7 * 32 + 16, p0=0, p1=1)
                kb.mmg([dict(out=pC, lhsT=self.onescol[:],
                             rhs=eqb[:, sl7, 0:16],
                             reads=[self.B_const, (B_eqb, sl7)],
                             start=True, stop=True)], writes=[bC])
                kb.op("dve", "tensor_add", base_r[0:1, 0:16],
                      base_r[0:1, 0:16], pC,
                      reads=[B_base, bC], writes=[B_base])
                t8 = posf[:, sl7, 16:24]
                e1c = posf[:, sl7, 24:25]
                e2c = posf[:, sl7, 25:26]
                q1 = posf[:, sl7, 26:27]
                q2 = posf[:, sl7, 27:28]
                ov = posf[:, sl7, 28:29]
                tg1 = posf[:, sl7, 29:30]
                tg2 = posf[:, sl7, 30:31]
                kb.op("dve", "tensor_mul", t8, eq1, self.erow[:],
                      reads=[(B_eqf, sl7), self.B_const],
                      writes=[(B_posf, sl7)])
                kb.op("dve", "reduce_sum", e1c, t8, axis=AX,
                      reads=[(B_posf, sl7)], writes=[(B_posf, sl7)])
                kb.op("dve", "tensor_mul", t8, eq2, self.erow[:],
                      reads=[(B_eqf, sl7), self.B_const],
                      writes=[(B_posf, sl7)])
                kb.op("dve", "reduce_sum", e2c, t8, axis=AX,
                      reads=[(B_posf, sl7)], writes=[(B_posf, sl7)])
                kb.op("dve", "tensor_mul", t8, eq1, p1f,
                      reads=[(B_eqf, sl7), (B_posf, sl7)],
                      writes=[(B_posf, sl7)])
                kb.op("dve", "reduce_sum", q1, t8, axis=AX,
                      reads=[(B_posf, sl7)], writes=[(B_posf, sl7)])
                kb.op("dve", "tensor_mul", t8, eq2, p2f,
                      reads=[(B_eqf, sl7), (B_posf, sl7)],
                      writes=[(B_posf, sl7)])
                kb.op("dve", "reduce_sum", q2, t8, axis=AX,
                      reads=[(B_posf, sl7)], writes=[(B_posf, sl7)])
                kb.op("dve", "scalar_tensor_tensor", tg1, e1c, float(CE), q1,
                      ALU.mult, ALU.add, reads=[(B_posf, sl7)],
                      writes=[(B_posf, sl7)])
                kb.op("dve", "tensor_scalar_add", tg1, tg1, float(th * 3072),
                      reads=[(B_posf, sl7)], writes=[(B_posf, sl7)])
                kb.op("dve", "tensor_scalar", ov, q1, float(C2), None,
                      ALU.is_ge, reads=[(B_posf, sl7)], writes=[(B_posf, sl7)])
                kb.op("dve", "scalar_tensor_tensor", tg1, ov, 1e9, tg1,
                      ALU.mult, ALU.add, reads=[(B_posf, sl7)],
                      writes=[(B_posf, sl7)])
                kb.op("dve", "scalar_tensor_tensor", tg2, e2c, float(CE), q2,
                      ALU.mult, ALU.add, reads=[(B_posf, sl7)],
                      writes=[(B_posf, sl7)])
                kb.op("dve", "tensor_scalar_add", tg2, tg2,
                      float(C2 + th * 3072),
                      reads=[(B_posf, sl7)], writes=[(B_posf, sl7)])
                kb.op("dve", "tensor_scalar", ov, q2, float(C2), None,
                      ALU.is_ge, reads=[(B_posf, sl7)], writes=[(B_posf, sl7)])
                kb.op("dve", "scalar_tensor_tensor", tg2, ov, 1e9, tg2,
                      ALU.mult, ALU.add, reads=[(B_posf, sl7)],
                      writes=[(B_posf, sl7)])
                kb.op("dve", "tensor_copy", tgti[:, sl7, 0:1], tg1,
                      reads=[(B_posf, sl7)], writes=[(B_tg, sl7)])
                kb.op("dve", "tensor_copy", tgti[:, sl7, 1:2], tg2,
                      reads=[(B_posf, sl7)], writes=[(B_tg, sl7)])
                kb.op("pool", "iota", tki[:, sl7, :], [[1, 1]], base=i * 128,
                      channel_multiplier=1, writes=[(B_tk, sl7)])
                seg = self.scr_idx[b][:].rearrange("(a b) -> a b", b=1)
                kb.idma(self.s_ix[0], seg,
                        IOA(tgti[:, sl7, 0:1], 0), tki[:, sl7, :], None,
                        bounds_check=self.rg_idx, oob_is_err=False,
                        reads=[(B_tg, sl7), (B_tk, sl7)],
                        writes=[(self.B_scr_idx[b], th)])
                kb.idma(self.s_ix[1], seg,
                        IOA(tgti[:, sl7, 1:2], 0), tki[:, sl7, :], None,
                        bounds_check=self.rg_idx, oob_is_err=False,
                        reads=[(B_tg, sl7), (B_tk, sl7)],
                        writes=[(self.B_scr_idx[b], th)])

        for tt in range(NTPT):
            compact_tile(0, tt)
        pend = [(th, tt) for th in range(1, NTH7) for tt in range(NTPT)]

        self.kb.tag = 'p7moe'
        pairs = [(th, e) for th in range(NTH7) for e in range(NEXP)]

        def gather_block(k):
            th, e = pairs[k]
            s = k % 2
            for a in range(3):
                kb.dma(self.s_ldt[a % 2],
                       idxt[:, s, a:a + 1],
                       self.scr_idx[b][th * 3072 + e * CE + a * 128:
                                       th * 3072 + e * CE + a * 128 + 128]
                       .rearrange("(p a) -> p a", p=128),
                       reads=[(self.B_scr_idx[b], th)],
                       writes=[(B_ix, s)])
            for a in range(3):
                kb.idma(self.s_gg[a],
                        gtile[:, 3 * s + a, :], None, self.scr_Hn[b][:],
                        IOA(idxt[:, s, a:a + 1], 0),
                        reads=[(B_ix, s),
                               (self.B_scr_Hn[b], list(range(NT + 1)))],
                        writes=[(B_gt, 3 * s + a)])
                kb.dma(self.s_stn2[a % 2],
                       self.scr_gth[s][a * 128:(a + 1) * 128, :],
                       gtile[:, 3 * s + a, :], reads=[(B_gt, 3 * s + a)],
                       writes=[self.B_scrg[s]])
                kb.idma(self.s_ix[s],
                        cgt[:, s, a * NEXP:(a + 1) * NEXP], None,
                        self.scr_cw[b][:],
                        IOA(idxt[:, s, a:a + 1], 0),
                        reads=[(B_ix, s),
                               (self.B_scr_cw[b], list(range(NT + 1)))],
                        writes=[(B_cg, s)])
            for fc in range(8):
                kb.dma(self.s_ldt[fc % 4], gthT[s][fc][0][:],
                       self.scr_gth[s][0:CE, fc * 128:(fc + 1) * 128],
                       reads=[self.B_scrg[s]], writes=[gthT[s][fc][1]],
                       transpose=True)

        gather_block(0)
        for k, (th, e) in enumerate(pairs):
            if k > 0:
                load_fc(e)
            s = k % 2
            if pend:
                compact_tile(*pend.pop(0))
                self.kb.tag = 'p7moe'
            if k + 1 < len(pairs):
                gather_block(k + 1)
            if True:
                for hc in range(16):
                    pp = hc % 2
                    psg, bG = self.PB(pp, 0, CE)
                    items = []
                    for fc in range(8):
                        items.append(dict(
                            out=psg,
                            lhsT=fc1b[fc][0][:, hc * 128:(hc + 1) * 128],
                            rhs=gthT[s][fc][0][:],
                            reads=[fc1b[fc][1], gthT[s][fc][1]],
                            start=(fc == 0), stop=(fc == 7)))
                    kb.mmg(items, writes=[bG])
                    kb.op("act", "activation", relu_s[:, pp, 0:CE], psg,
                          AF.Relu, bias=0.0, scale=1.0,
                          reads=[bG], writes=[(B_rs, pp)])
                    kb.op("dve", "tensor_mul", Gs[hc][0][:],
                          relu_s[:, pp, 0:CE], relu_s[:, pp, 0:CE],
                          reads=[(B_rs, pp)], writes=[Gs[hc][1]])
                for tt3 in range(3):
                    pp = tt3 % 2
                    psm, bM = self.PB(2 + pp, 0, 1024)
                    items = []
                    for gh in range(2):
                        for hc in range(16):
                            items.append(dict(
                                out=psm[:, gh * 512:(gh + 1) * 512],
                                lhsT=Gs[hc][0][:, tt3 * 128:(tt3 + 1) * 128],
                                rhs=fc2b[hc][0][:, gh * 512:(gh + 1) * 512],
                                reads=[Gs[hc][1], fc2b[hc][1]],
                                start=(hc == 0), stop=(hc == 15)))
                    kb.mmg(items, writes=[bM])
                    kb.op("dve", "tensor_scalar_mul", mst[tt3][0][:], psm,
                          cgt[:, s, tt3 * NEXP + e:tt3 * NEXP + e + 1],
                          reads=[bM, (B_cg, s)], writes=[mst[tt3][1]])
                    if tt3 == 0:
                        kb.idma(self.s_sca[0], self.scr_moe[b][:],
                                IOA(idxt[:, s, 0:1], 0),
                                mst[0][0][:], None,
                                bounds_check=self.rg_tok, oob_is_err=False,
                                reads=[mst[0][1], (B_ix, s)],
                                writes=[(self.B_scr_moe[b], e)])
                    elif tt3 == 2:
                        kb.idma(self.s_sca[1], self.scr_moe2[b][:],
                                IOA(idxt[:, s, 2:3], 0),
                                mst[2][0][:], None,
                                bounds_check=self.rg_tok, oob_is_err=False,
                                reads=[mst[2][1], (B_ix, s)],
                                writes=[(self.B_scr_moe2[b], e)])
                    else:
                        kb.op("dve", "tensor_copy", msw[:, 0, :],
                              mst[1][0][0:64, :], reads=[mst[1][1]],
                              writes=[(B_msw, 0)])
                        kb.op("dve", "tensor_copy", msw[:, 1, :],
                              mst[1][0][64:128, :], reads=[mst[1][1]],
                              writes=[(B_msw, 1)])
                        kb.op("dve", "tensor_copy", ixw[:, 0, :],
                              idxt[0:64, s, 1:2], reads=[(B_ix, s)],
                              writes=[(B_ixw, 0)])
                        kb.op("dve", "tensor_copy", ixw[:, 1, :],
                              idxt[64:128, s, 1:2], reads=[(B_ix, s)],
                              writes=[(B_ixw, 1)])
                        kb.idma(self.s_sca[0], self.scr_moe[b][:],
                                IOA(ixw[:, 0, :], 0),
                                msw[:, 0, :], None,
                                bounds_check=self.rg_tok, oob_is_err=False,
                                reads=[(B_msw, 0), (B_ixw, 0)],
                                writes=[(self.B_scr_moe[b], e)])
                        kb.idma(self.s_sca[1], self.scr_moe2[b][:],
                                IOA(ixw[:, 1, :], 0),
                                msw[:, 1, :], None,
                                bounds_check=self.rg_tok, oob_is_err=False,
                                reads=[(B_msw, 1), (B_ixw, 1)],
                                writes=[(self.B_scr_moe2[b], e)])
        for lst in (fc1b, fc2b, Gs, mst, gthT[0], gthT[1]):
            for h_, bf in lst:
                AR.freez(h_)
        for h_ in (gtile, idxt, cgt, base_r, posf, tgti, tki, eqf, eqb,
                   relu_s, msw, ixw):
            AR.freez(h_)
        for h_, bf in HnT:
            AR.freez(h_)

    def p8(self, b):
        self.kb.tag = 'p8'
        kb, T, NT, AR = self.kb, self.T, self.NT, self.AR
        acct, B_acct = AR.alloc(f"acc8{b}", [128, 2, D], F32, units=2)
        mt, B_mt = AR.alloc(f"moe8{b}", [128, 2, D], F32, units=2)
        m2t, B_m2t = AR.alloc(f"moe8b{b}", [128, 2, D], F32, units=2)
        for i in range(NT):
            sl = i % 2
            kb.dma(self.s_ldx[sl], self.xstage[:, sl, :],
                   self.x_d[b, i * 128:(i + 1) * 128, :],
                   reads=[self.B_in], writes=[(self.B_xstage, sl)])
            kb.dma(self.s_acc[sl], acct[:, sl, :],
                   self.scr_acc[b][i * 128:(i + 1) * 128, :],
                   reads=[(self.B_scr_acc[b], i)], writes=[(B_acct, sl)])
            kb.dma(self.s_ldt[sl], mt[:, sl, :],
                   self.scr_moe[b][i * 128:(i + 1) * 128, :],
                   reads=[(self.B_scr_moe[b], list(range(NEXP)))],
                   writes=[(B_mt, sl)])
            kb.dma(self.s_stn2[sl], m2t[:, sl, :],
                   self.scr_moe2[b][i * 128:(i + 1) * 128, :],
                   reads=[(self.B_scr_moe2[b], list(range(NEXP)))],
                   writes=[(B_m2t, sl)])
            kb.op("dve", "tensor_add", acct[:, sl, :], acct[:, sl, :],
                  mt[:, sl, :], reads=[(B_acct, sl), (B_mt, sl)],
                  writes=[(B_acct, sl)])
            kb.op("dve", "tensor_add", acct[:, sl, :], acct[:, sl, :],
                  m2t[:, sl, :], reads=[(B_acct, sl), (B_m2t, sl)],
                  writes=[(B_acct, sl)])
            kb.op("dve", "tensor_add", self.sqs[:, sl, :], acct[:, sl, :],
                  self.xstage[:, sl, :],
                  reads=[(B_acct, sl), (self.B_xstage, sl)],
                  writes=[(self.B_sqs, sl)])
            kb.dma(self.s_out[sl], self.y_d[b, i * 128:(i + 1) * 128, :],
                   self.sqs[:, sl, :], reads=[(self.B_sqs, sl)], writes=[])
        AR.freez(acct)
        AR.freez(mt)
        AR.freez(m2t)

    def batch_front(self, b):
        if b == 0:
            self.p3(b)
        self.p4(b)
        self.p5(b)
        inj = self.p1_emitters(b + 1) if b + 1 < self.BPC else ()
        self.p6(b, inject=inj)


def _host_weights(inputs):
    wq = np.ascontiguousarray(inputs["L0_attn_w_q"], dtype=np.float32)
    wk = np.ascontiguousarray(inputs["L0_attn_w_k"], dtype=np.float32)
    wv = np.ascontiguousarray(inputs["L0_attn_w_v"], dtype=np.float32)
    wqb = np.zeros((8, 128, 128), np.float32)
    wkb = np.zeros((8, 128, 128), np.float32)
    for ic in range(8):
        par, m = ic // 4, ic % 4
        for jj in range(2):
            h = 4 * m + 2 * jj + par
            wqb[ic, jj * 64:(jj + 1) * 64, jj * 64:(jj + 1) * 64] = wq[:, :, h]
            wkb[ic, jj * 64:(jj + 1) * 64, jj * 64:(jj + 1) * 64] = wk[:, :, h]
    kk, dd, hh = np.indices((HD, HD, NH))
    rows_pm = ((hh % 2) * 512 + (hh // 2) * 64 + kk).ravel()
    cols_o = (dd * NH + hh).ravel()
    wv_hm = np.zeros((D, D), np.float32)
    wv_hm[rows_pm, cols_o] = wv.ravel()
    return {
        "wqb": wqb.reshape(1024, 128), "wkb": wkb.reshape(1024, 128),
        "wv_hm": wv_hm,
        "fwq": np.ascontiguousarray(inputs["L0_ffn_wq"], dtype=np.float32),
        "fwk": np.ascontiguousarray(inputs["L0_ffn_wk"], dtype=np.float32),
        "fwv": np.ascontiguousarray(inputs["L0_ffn_wv"], dtype=np.float32),
        "fwo": np.ascontiguousarray(inputs["L0_ffn_wo"], dtype=np.float32),
        "rw": np.ascontiguousarray(inputs["L0_ffn_router_w"], dtype=np.float32),
        "fc1": np.ascontiguousarray(inputs["L0_ffn_exp_fc1"], dtype=np.float32),
        "fc2": np.ascontiguousarray(inputs["L0_ffn_exp_fc2"], dtype=np.float32),
    }


def kernel(**inputs):
    x = np.ascontiguousarray(inputs["L0_attn_x"], dtype=np.float32)
    B, T, d = x.shape
    assert d == D and B % N_CORES == 0
    bpc = B // N_CORES
    wmap = _host_weights(inputs)
    prog = Prog(T=T, BPC=bpc)
    nc = prog.build()
    in_maps = []
    for c in range(N_CORES):
        m = {"x": x[c * bpc:(c + 1) * bpc]}
        m.update(wmap)
        in_maps.append(m)
    res = run_bass_kernel_spmd(nc, in_maps, core_ids=list(range(N_CORES)))
    return np.concatenate([r["y"] for r in res.results], axis=0)


# version 30 (final)
# speedup vs baseline: 1.0773x; 1.0106x over previous
import math
from contextlib import ExitStack

import numpy as np

import concourse.bass as bass
import concourse.mybir as mybir
from concourse.bass_utils import run_bass_kernel_spmd

F32 = mybir.dt.float32
BF16 = mybir.dt.bfloat16
AF = mybir.ActivationFunctionType
ALU = mybir.AluOpType
AX = mybir.AxisListType.X

D = 1024
NH = 16
HD = 64
NEXP = 8
EDIM = 2048
EPS = 1e-5
ROPE_BASE = 10000.0
N_CORES = 8
GAMMA1 = 1.0 / 32.0
GAMMA2 = 1.0 / 8.0


def _tok_add(d, tok):
    if tok is None:
        return
    k = id(tok[0])
    old = d.get(k)
    if old is None or old[1] < tok[1]:
        d[k] = tok


def _tok_merge(d, other):
    for t in other.values():
        _tok_add(d, t)


class Buf:

    def __init__(self, n=1, carry=None):
        self.w = [None] * n
        self.r = [dict() for _ in range(n)]
        self.carry = dict(carry) if carry else {}


def _units(spec):
    if isinstance(spec, Buf):
        return spec, [0]
    b, u = spec
    if isinstance(u, (list, tuple, range)):
        return b, list(u)
    return b, [u]


class Slot:

    def __init__(self, kb, name):
        self.sem = kb.es.enter_context(kb.nc.semaphore(name))
        self.rounds = 0
        self.tok = None


class KB:
    def __init__(self, nc, es):
        self.nc = nc
        self.es = es
        self.raw = {"pe": nc.tensor, "dve": nc.vector, "act": nc.scalar,
                    "pool": nc.gpsimd, "sp": nc.sync}
        self.sem = {}
        self.cnt = {}
        self.marks = {}
        for e in self.raw:
            self.sem[e] = es.enter_context(nc.semaphore("sem_" + e))
            self.cnt[e] = 0
            self.marks[e] = {}
        self.tag = "init"
        self.tagmap = {}

    def slot(self, name):
        return Slot(self, name)

    def _wait(self, eng, tok):
        if tok is None:
            return
        sem, val = tok
        wm = self.marks[eng]
        key = id(sem)
        if wm.get(key, -1) >= val:
            return
        self.raw[eng].wait_ge(sem, val)
        wm[key] = val

    def _deps(self, eng, reads, writes):
        for spec in reads:
            b, us = _units(spec)
            for u in us:
                if b.w[u] is None:
                    for t in b.carry.values():
                        self._wait(eng, t)
                self._wait(eng, b.w[u])
        for spec in writes:
            b, us = _units(spec)
            for u in us:
                if b.w[u] is None:
                    for t in b.carry.values():
                        self._wait(eng, t)
                self._wait(eng, b.w[u])
                for t in b.r[u].values():
                    self._wait(eng, t)

    def _record(self, tok, reads, writes):
        for spec in reads:
            b, us = _units(spec)
            for u in us:
                _tok_add(b.r[u], tok)
        for spec in writes:
            b, us = _units(spec)
            for u in us:
                b.w[u] = tok
                b.r[u] = {}

    def op(self, eng, method, *args, reads=(), writes=(), **kw):
        self._deps(eng, reads, writes)
        inst = getattr(self.raw[eng], method)(*args, **kw)
        self.cnt[eng] += 1
        tok = (self.sem[eng], self.cnt[eng])
        inst.then_inc(self.sem[eng], 1)
        self._record(tok, reads, writes)
        return tok

    def mmg(self, items, writes):
        self._deps("pe", [], writes)
        all_reads = []
        last = None
        for it in items:
            self._deps("pe", it.get("reads", ()), [])
            all_reads.extend(it.get("reads", ()))
            kw = {}
            if it.get("tile_position") is not None:
                kw["tile_position"] = it["tile_position"]
            last = self.raw["pe"].matmul(it["out"], it["lhsT"], it["rhs"],
                                         start=it["start"], stop=it["stop"], **kw)
            self.tagmap[last.ins.name] = self.tag
        self.cnt["pe"] += 1
        tok = (self.sem["pe"], self.cnt["pe"])
        last.then_inc(self.sem["pe"], 1)
        self._record(tok, all_reads, writes)
        return tok

    def idma(self, slot, out_ap, out_off, in_ap, in_off, reads=(), writes=(),
             **kw):
        self._deps("pool", reads, writes)
        self._wait("pool", slot.tok)
        inst = self.raw["pool"].indirect_dma_start(out_ap, out_off, in_ap,
                                                   in_off, **kw)
        inst.then_inc(slot.sem, 16)
        slot.rounds += 1
        tok = (slot.sem, 16 * slot.rounds)
        slot.tok = tok
        self._record(tok, reads, writes)
        return tok

    def dma(self, slot, out_ap, in_ap, reads=(), writes=(), eng="sp",
            transpose=False):
        self._deps(eng, reads, writes)
        self._wait(eng, slot.tok)
        if transpose:
            inst = self.raw[eng].dma_start_transpose(out_ap, in_ap)
        else:
            inst = self.raw[eng].dma_start(out_ap, in_ap)
        inst.then_inc(slot.sem, 16)
        slot.rounds += 1
        tok = (slot.sem, 16 * slot.rounds)
        slot.tok = tok
        self._record(tok, reads, writes)
        return tok


class Arena:

    def __init__(self, nc, base, cap):
        self.nc = nc
        self.base = base
        self.free_list = [(base, cap, {})]
        self.live = {}
        self.uid = 0

    def alloc(self, name, shape, dtype, units=1, carry=None):
        per_part = int(np.prod(shape[1:])) * mybir.dt.size(dtype)
        per_part = (per_part + 31) & ~31
        self.uid += 1
        nm = f"{name}_u{self.uid}"
        for i, (off, size, toks) in enumerate(self.free_list):
            if size >= per_part:
                h = self.nc.alloc_sbuf_tensor_at(nm, list(shape), dtype, offset=off)
                if size == per_part:
                    self.free_list.pop(i)
                else:
                    self.free_list[i] = (off + per_part, size - per_part,
                                         dict(toks))
                cd = dict(toks)
                if carry:
                    _tok_merge(cd, carry)
                buf = Buf(units, carry=cd)
                self.live[id(h)] = (off, per_part, buf)
                return h, buf
        raise MemoryError(f"arena OOM for {name}: need {per_part}B/p free="
                          f"{[(o, s) for o, s, _ in self.free_list]}")

    def freez(self, h):
        off, size, buf = self.live.pop(id(h))
        toks = dict(buf.carry)
        for t in buf.w:
            _tok_add(toks, t)
        for rl in buf.r:
            _tok_merge(toks, rl)
        self.free_list.append((off, size, toks))
        self.free_list.sort(key=lambda s: s[0])
        out = []
        for seg in self.free_list:
            if out and out[-1][0] + out[-1][1] == seg[0]:
                merged = dict(out[-1][2])
                _tok_merge(merged, seg[2])
                out[-1] = (out[-1][0], out[-1][1] + seg[1], merged)
            else:
                out.append((seg[0], seg[1], dict(seg[2])))
        self.free_list = out
        return toks


class Prog:
    def __init__(self, T=2048, BPC=2, debug=False):
        self.T = T
        self.BPC = BPC
        self.debug = debug
        self.NT = T // 128
        self.nc = bass.Bass("TRN2", target_bir_lowering=False)

    def sb(self, name, shape, dtype):
        return self.es.enter_context(self.nc.sbuf_tensor(name, list(shape), dtype))

    def PB(self, i, lo, hi, p0=0, p1=128):
        h, bf = self.pb[i]
        return h[p0:p1, lo:hi], (bf, list(range(lo // 512, (hi + 511) // 512)))

    def build(self):
        nc, T, NT, BPC = self.nc, self.T, self.NT, self.BPC
        self.TQ = min(512, T)
        self.NTQ = T // self.TQ

        self.x_d = nc.dram_tensor("x", [BPC, T, D], F32, kind="ExternalInput").ap()
        wnames = ["wv_hm", "fwq", "fwk", "fwv", "fwo"]
        self.w_d = {n: nc.dram_tensor(n, [D, D], F32, kind="ExternalInput").ap()
                    for n in wnames}
        for n in ["wqb", "wkb"]:
            self.w_d[n] = nc.dram_tensor(n, [D, 128], F32,
                                         kind="ExternalInput").ap()
        self.rw_d = nc.dram_tensor("rw", [D, NEXP], F32, kind="ExternalInput").ap()
        self.fc1_d = nc.dram_tensor("fc1", [NEXP, D, EDIM], F32, kind="ExternalInput").ap()
        self.fc2_d = nc.dram_tensor("fc2", [NEXP, EDIM, D], F32, kind="ExternalInput").ap()
        self.y_d = nc.dram_tensor("y", [BPC, T, D], F32, kind="ExternalOutput").ap()

        self.scr_xn = [nc.dram_tensor(f"scr_xn{b}", [T, D], BF16).ap() for b in range(BPC)]
        self.scr_hn = [nc.dram_tensor(f"scr_hn{b}", [T, D], BF16).ap() for b in range(BPC)]
        self.scr_Hn = [nc.dram_tensor(f"scr_Hn{b}", [T + 1, D], BF16).ap() for b in range(BPC)]
        self.scr_cw = [nc.dram_tensor(f"scr_cw{b}", [T + 1, NEXP], F32).ap() for b in range(BPC)]
        self.scr_idx = [nc.dram_tensor(f"scr_idx{b}", [max(1, T // 1024) * 8 * 384], mybir.dt.int32).ap() for b in range(BPC)]
        self.scr_gth = [nc.dram_tensor(f"scr_gth{s}", [384, D], BF16).ap() for s in range(2)]
        self.scr_acc = [nc.dram_tensor(f"scr_acc{b}", [T, D], F32).ap() for b in range(BPC)]
        self.scr_moe = [nc.dram_tensor(f"scr_moe{b}", [T + 1, D], F32).ap() for b in range(BPC)]
        self.scr_moe2 = [nc.dram_tensor(f"scr_moe2_{b}", [T + 1, D], F32).ap() for b in range(BPC)]
        self.scr_w = {n: nc.dram_tensor("scrb_" + n, [D, D], BF16).ap() for n in wnames}
        for n in ["wqb", "wkb"]:
            self.scr_w[n] = nc.dram_tensor("scrb_" + n, [D, 128], BF16).ap()
        self.scr_rw = nc.dram_tensor("scrb_rw", [D, NEXP], BF16).ap()
        self.scr_fc1 = nc.dram_tensor("scrb_fc1", [NEXP, D, EDIM], BF16).ap()
        self.scr_fc2 = nc.dram_tensor("scrb_fc2", [NEXP, EDIM, D], BF16).ap()
        if self.debug:
            self.dbg_y = nc.dram_tensor("dbg_y", [BPC, T, D], F32, kind="ExternalOutput").ap()
            self.dbg_h = nc.dram_tensor("dbg_h", [BPC, T, D], F32, kind="ExternalOutput").ap()
            self.dbg_c = nc.dram_tensor("dbg_c", [BPC, T, NEXP], F32, kind="ExternalOutput").ap()
            self.dbg_moe = nc.dram_tensor("dbg_moe", [BPC, T, D], F32, kind="ExternalOutput").ap()

        self.es = ExitStack()
        es = self.es
        kb = KB(nc, es)
        self.kb = kb
        self._pre3 = None

        self.xstage = self.sb("xstage", [128, 2, D], F32)
        self.nstage = self.sb("nstage", [128, 2, D], BF16)
        self.n2stage = self.sb("n2stage", [128, 2, D], BF16)
        self.sqs = self.sb("sqs", [128, 2, D], F32)
        self.st_m = [self.sb(f"st_m{b}", [128, NT], F32) for b in range(BPC)]
        self.st_rstd = [self.sb(f"st_rstd{b}", [128, NT], F32) for b in range(BPC)]
        self.st_b1 = [self.sb(f"st_b1{b}", [128, NT], F32) for b in range(BPC)]
        self.st_s2 = [self.sb(f"st_s2{b}", [128, NT], F32) for b in range(BPC)]
        self.st_s3 = [self.sb(f"st_s3{b}", [128, NT], F32) for b in range(BPC)]
        self.st_sq = [self.sb(f"st_sq{b}", [128, NT], F32) for b in range(BPC)]
        self.st_rs2 = [self.sb(f"st_rs2{b}", [128, NT], F32) for b in range(BPC)]
        self.st_bs2 = [self.sb(f"st_bs2{b}", [128, NT], F32) for b in range(BPC)]
        self.st_tmp = self.sb("st_tmp", [128, 2 * NT], F32)
        self.st_t = self.sb("st_t", [128, 2, 8], F32)
        self.cw = [self.sb(f"cw{b}", [128, NT, NEXP], F32) for b in range(BPC)]
        self.lgt = self.sb("lgt", [128, 2, NEXP], F32)
        self.lgt2 = self.sb("lgt2", [128, 2, NEXP], F32)
        self.lgt3 = self.sb("lgt3", [128, 2, NEXP], F32)
        self.epsv = self.sb("epsv", [128, 1], F32)
        self.pi2 = self.sb("pi2", [128, 1], F32)
        self.ones64 = self.sb("ones64", [1, 64], F32)
        self.iotv = self.sb("iotv", [128, 4], F32)
        self.invf = self.sb("invf", [128, 1], F32)
        self.rrow = self.sb("rrow", [1, 2, self.TQ], F32)
        self.rinv = self.sb("rinv", [1, 2, self.TQ], F32)
        self.ysc = self.sb("ysc", [64, 2, self.TQ], F32)
        self.rwb = self.sb("rwb", [128, 8, NEXP], BF16)
        self.trimask = self.sb("trimask", [128, 128], BF16)
        self.ustrict = self.sb("ustrict", [128, 128], BF16)
        self.erow = self.sb("erow", [128, NEXP], F32)
        self.ones128f = self.sb("ones128f", [1, 128], F32)
        self.onescol = self.sb("onescol", [128, 1], BF16)
        self.senti = self.sb("senti", [128, 24], mybir.dt.int32)
        self.zrow = self.sb("zrow", [128, D], F32)

        self.B_xstage = Buf(2)
        self.B_nstage = Buf(2)
        self.B_n2stage = Buf(2)
        self.B_sqs = Buf(2)
        self.B_stats = [Buf(NT) for _ in range(BPC)]
        self.B_stsq = [Buf(NT) for _ in range(BPC)]
        self.B_tmp = Buf(1)
        self.B_tbl = Buf(1)
        self.B_s3 = [Buf(NT) for _ in range(BPC)]
        self.B_stt = Buf(2)
        self.B_cw = [Buf(NT) for _ in range(BPC)]
        self.B_lgt = Buf(2)
        self.B_lgt2 = Buf(2)
        self.B_lgt3 = Buf(2)
        self.B_const = Buf(1)
        self.B_rope = Buf(2)
        self.B_rwb = Buf(1)
        self.B_in = Buf(1)
        self.B_scr_xn = [Buf(NT) for _ in range(BPC)]
        self.B_scr_hn = [Buf(NT) for _ in range(BPC)]
        self.B_scr_Hn = [Buf(NT + 1) for _ in range(BPC)]
        self.B_scr_acc = [Buf(NT) for _ in range(BPC)]
        self.B_scr_moe = [Buf(NEXP) for _ in range(BPC)]
        self.B_scr_moe2 = [Buf(NEXP) for _ in range(BPC)]
        self.B_scrg = [Buf(1), Buf(1)]
        self.B_scr_cw = [Buf(NT + 1) for _ in range(BPC)]
        self.B_scr_idx = [Buf(max(1, T // 1024)) for _ in range(BPC)]
        self.B_scr_w = {n: Buf(8) for n in wnames}
        self.B_scr_w["wqb"] = Buf(8)
        self.B_scr_w["wkb"] = Buf(8)
        self.B_scr_rw8 = Buf(8)
        self.B_scr_fc1 = Buf(NEXP * 8)
        self.B_scr_fc2 = Buf(NEXP * 16)

        self.s_ldx = [kb.slot("s_ldx0"), kb.slot("s_ldx1")]
        self.s_stn = [kb.slot("s_stn0"), kb.slot("s_stn1")]
        self.s_stn2 = [kb.slot("s_stn2a"), kb.slot("s_stn2b")]
        self.s_stn3 = [kb.slot("s_stn3a"), kb.slot("s_stn3b")]
        self.s_ldt = [kb.slot(f"s_ldt{i}") for i in range(4)]
        self.s_ldw = [kb.slot("s_ldw0"), kb.slot("s_ldw1")]
        self.s_stw = [kb.slot("s_stw0"), kb.slot("s_stw1")]
        self.s_acc = [kb.slot("s_acc0"), kb.slot("s_acc1")]
        self.s_acc2 = [kb.slot("s_acc2a"), kb.slot("s_acc2b")]
        self.s_out = [kb.slot("s_out0"), kb.slot("s_out1")]
        self.s_tab = kb.slot("s_tab")
        self.s_fc1 = [kb.slot(f"s_fc1_{i}") for i in range(4)]
        self.s_fc2 = [kb.slot(f"s_fc2_{i}") for i in range(4)]
        self.s_pc = [kb.slot(f"s_pc_{i}") for i in range(9)]
        self.s_ix = [kb.slot(f"s_ix_{i}") for i in range(2)]
        self.s_gg = [kb.slot(f"s_gg_{i}") for i in range(3)]
        self.s_sca = [kb.slot(f"s_sca_{i}") for i in range(2)]
        self.s_zz = kb.slot("s_zz")
        self.s_dbg = kb.slot("s_dbg")

        self.rg_idx = nc.gpsimd.alloc_register("bc_idx")
        nc.gpsimd.reg_mov(self.rg_idx, max(1, self.T // 1024) * 3072 - 1)
        self.rg_tok = nc.gpsimd.alloc_register("bc_tok")
        nc.gpsimd.reg_mov(self.rg_tok, self.T)

        self.pb = []
        for i in range(4):
            h = es.enter_context(nc.psum_tensor(f"pb{i}", [128, 1024], F32))
            self.pb.append((h, Buf(2)))

        static_end = nc.SBUF_PARTITION_SIZE_BYTES - nc.sbuf_bytes_remaining
        self.AR = Arena(nc, static_end + 64, nc.sbuf_top - static_end - 128)

        with es:
            self.p0_consts()
            self.p0_weights()
            self.p0_weights_fc()
            self.p1(0)
            for b in range(self.BPC):
                self.batch_front(b)
                self.p7(b)
                if b + 1 < self.BPC:
                    self.p3(b + 1)
                self.p8(b)
            for sl in self.s_out:
                if sl.tok is not None:
                    kb.raw["sp"].wait_ge(sl.sem, 16 * sl.rounds)
        return nc

    def p0_consts(self):
        self.kb.tag = 'p0t'
        kb = self.kb
        kb.op("pool", "memset", self.epsv[:], EPS, writes=[self.B_const])
        kb.op("pool", "memset", self.pi2[:], float(np.pi / 2), writes=[self.B_const])
        kb.op("pool", "memset", self.ones64[:], 1.0, writes=[self.B_const])
        kb.op("pool", "memset", self.trimask[:], 1.0, writes=[self.B_const])
        kb.op("pool", "affine_select", self.trimask[:], self.trimask[:],
              [[1, 128]], ALU.is_ge, 0.0, base=0, channel_multiplier=-1,
              writes=[self.B_const])
        kb.op("pool", "memset", self.ustrict[:], 1.0, writes=[self.B_const])
        kb.op("pool", "affine_select", self.ustrict[:], self.ustrict[:],
              [[1, 128]], ALU.is_ge, 0.0, base=-1, channel_multiplier=-1,
              writes=[self.B_const])
        kb.op("pool", "iota", self.erow[:], [[1, NEXP]], base=0,
              channel_multiplier=0, allow_small_or_imprecise_dtypes=True,
              writes=[self.B_const])
        kb.op("pool", "memset", self.ones128f[:], 1.0, writes=[self.B_const])
        kb.op("pool", "memset", self.onescol[:], 1.0, writes=[self.B_const])
        kb.op("pool", "memset", self.senti[:], self.T, writes=[self.B_const])
        kb.op("pool", "memset", self.zrow[:], 0.0, writes=[self.B_const])

    def build_tables_sbuf(self, cp, tag):
        self.kb.tag = 'p0t'
        kb, T, AR = self.kb, self.T, self.AR
        BT = self.B_tbl
        tabc, B_tc = AR.alloc(f"tabc{tag}", [128, T], F32)
        tabs, B_ts = AR.alloc(f"tabs{tag}", [128, T], F32)
        if T <= 2 * D:
            scr = bass.AP(self.xstage, 0, [[2 * D, 128], [1, D]])
            B_sc = (self.B_xstage, 0)
        else:
            scr, B_sc = AR.alloc(f"tabscr{tag}", [128, T], F32)
        kb.op("pool", "iota", self.iotv[:, 0:1], [[1, 1]], base=2 * cp,
              channel_multiplier=8, allow_small_or_imprecise_dtypes=True,
              writes=[BT])
        kb.op("pool", "memset", self.iotv[:, 1:2], 1.0, writes=[BT])
        kb.op("pool", "affine_select", self.iotv[:, 1:2], self.iotv[:, 1:2],
              [[1, 1]], ALU.is_ge, 0.0, base=-64, channel_multiplier=1,
              writes=[BT])
        kb.op("dve", "scalar_tensor_tensor", self.iotv[:, 2:3],
              self.iotv[:, 1:2], -511.0, self.iotv[:, 0:1], ALU.mult, ALU.add,
              reads=[BT], writes=[BT])
        kb.op("dve", "tensor_scalar_mul", self.iotv[:, 3:4], self.iotv[:, 2:3],
              -math.log(ROPE_BASE) / 512.0, reads=[BT], writes=[BT])
        kb.op("act", "activation", self.invf[:], self.iotv[:, 3:4], AF.Exp,
              bias=0.0, scale=1.0, reads=[BT], writes=[BT])
        kb.op("pool", "memset", tabc[:, 0:1], 1.0, writes=[B_tc])
        kb.op("pool", "memset", tabs[:, 0:1], 0.0, writes=[B_ts])
        if T > 1:
            kb.op("act", "activation", tabs[:, 1:2], self.invf[:], AF.Sin,
                  bias=0.0, scale=1.0, reads=[BT], writes=[B_ts])
            kb.op("act", "activation", tabc[:, 1:2], self.invf[:], AF.Sin,
                  bias=self.pi2[:], scale=1.0, reads=[BT, self.B_const],
                  writes=[B_tc])
        s = 2
        while s < T:
            ext = s - 1
            w = min(s - 1, T - s)
            ce = tabc[:, ext:ext + 1]
            se = tabs[:, ext:ext + 1]
            kb.op("dve", "tensor_scalar_mul", scr[:, 0:w], tabs[:, 1:1 + w],
                  se, reads=[B_ts], writes=[B_sc])
            kb.op("dve", "scalar_tensor_tensor", tabc[:, s:s + w],
                  tabc[:, 1:1 + w], ce, scr[:, 0:w], ALU.mult, ALU.subtract,
                  reads=[B_sc, B_tc], writes=[B_tc])
            kb.op("dve", "tensor_scalar_mul", scr[:, 0:w], tabc[:, 1:1 + w],
                  se, reads=[B_tc], writes=[B_sc])
            kb.op("dve", "scalar_tensor_tensor", tabs[:, s:s + w],
                  tabs[:, 1:1 + w], ce, scr[:, 0:w], ALU.mult, ALU.add,
                  reads=[B_sc, B_ts], writes=[B_ts])
            s = s + w
        if T > 2 * D:
            AR.freez(scr)
        return tabc, B_tc, tabs, B_ts

    def p0_weights(self):
        kb = self.kb
        semB = self.s_pc[8]

        def cast_dma(slot, src_ap, dst_ap, rows, cols):
            n = 0
            for r0 in range(0, rows, 256):
                rr = min(256, rows - r0)
                inst = kb.raw["pool"].dma_start(dst_ap[r0:r0 + rr, 0:cols],
                                                src_ap[r0:r0 + rr, 0:cols])
                inst.then_inc(slot.sem, 16)
                slot.rounds += 1
                n += 1
            return n

        for wi, n_ in enumerate(["wqb", "wkb", "wv_hm", "fwq", "fwk",
                                 "fwv", "fwo"]):
            sl_ = self.s_pc[wi]
            cast_dma(sl_, self.w_d[n_], self.scr_w[n_], D,
                     128 if n_ in ("wqb", "wkb") else D)
            tokn = (sl_.sem, 16 * sl_.rounds)
            sl_.tok = tokn
            bb = self.B_scr_w[n_]
            for u in range(8):
                bb.w[u] = tokn
        cast_dma(self.s_pc[7], self.rw_d, self.scr_rw, D, NEXP)
        tokA = (self.s_pc[7].sem, 16 * self.s_pc[7].rounds)
        self.s_pc[7].tok = tokA
        for u in range(8):
            self.B_scr_rw8.w[u] = tokA

    def p0_weights_fc(self):
        kb = self.kb
        semB = self.s_pc[8]
        self.kb._deps("pool", [(self.B_scr_hn[0], self.NT - 1)], [])

        def cast_dma(slot, src_ap, dst_ap, rows, cols):
            for r0 in range(0, rows, 256):
                rr = min(256, rows - r0)
                inst = kb.raw["pool"].dma_start(dst_ap[r0:r0 + rr, 0:cols],
                                                src_ap[r0:r0 + rr, 0:cols])
                inst.then_inc(slot.sem, 16)
                slot.rounds += 1

        for e in range(NEXP):
            cast_dma(semB, self.fc1_d[e], self.scr_fc1[e], D, EDIM)
            cast_dma(semB, self.fc2_d[e], self.scr_fc2[e], EDIM, D)
        tokB = (semB.sem, 16 * semB.rounds)
        semB.tok = tokB
        for u in range(NEXP * 8):
            self.B_scr_fc1.w[u] = tokB
        for u in range(NEXP * 16):
            self.B_scr_fc2.w[u] = tokB

    def p1(self, b):
        for f in self.p1_emitters(b):
            f()

    def p1_emitters(self, b):
        NT = self.NT

        def loadx(i):
            self.kb.tag = 'p1'
            sl = i % 2
            self.kb.dma(self.s_ldx[sl], self.xstage[:, sl, :],
                        self.x_d[b, i * 128:(i + 1) * 128, :],
                        reads=[self.B_in], writes=[(self.B_xstage, sl)])

        def statsA(i):
            self.kb.tag = 'p1'
            kb = self.kb
            sl = i % 2
            xt = self.xstage[:, sl, :]
            kb.op("dve", "reduce_sum", self.st_m[b][:, i:i + 1], xt, axis=AX,
                  reads=[(self.B_xstage, sl)], writes=[(self.B_stats[b], i)])
            kb.op("act", "activation", self.sqs[:, sl, :], xt, AF.Square,
                  bias=0.0, scale=1.0, accum_out=self.st_sq[b][:, i:i + 1],
                  reads=[(self.B_xstage, sl)],
                  writes=[(self.B_sqs, sl), (self.B_stsq[b], i)])

        def batch_stats():
            self.kb.tag = 'p1'
            kb = self.kb
            allu = list(range(NT))
            R = [(self.B_stats[b], allu)]
            W = [(self.B_stats[b], allu)]
            RS = [(self.B_stsq[b], allu)]
            BT = self.B_tmp
            tmp0 = self.st_tmp[:, 0:NT]
            tmp1 = self.st_tmp[:, NT:2 * NT]
            m = self.st_m[b][:, 0:NT]
            rstd = self.st_rstd[b][:, 0:NT]
            b1 = self.st_b1[b][:, 0:NT]
            s2 = self.st_s2[b][:, 0:NT]
            kb.op("dve", "tensor_scalar_mul", m, m, 1.0 / D, reads=R, writes=W)
            kb.op("dve", "tensor_mul", tmp0, m, m, reads=R, writes=[BT])
            kb.op("dve", "scalar_tensor_tensor", tmp1,
                  self.st_sq[b][:, 0:NT], 1.0 / D, tmp0, ALU.mult,
                  ALU.subtract, reads=RS + [BT], writes=[BT])
            kb.op("act", "activation", tmp0, tmp1, AF.Ln, bias=self.epsv[:],
                  scale=1.0, reads=[BT, self.B_const], writes=[BT])
            kb.op("act", "activation", rstd, tmp0, AF.Exp, bias=0.0,
                  scale=-0.5, reads=[BT], writes=W)
            kb.op("dve", "scalar_tensor_tensor", b1, m, -1.0, rstd,
                  ALU.mult, ALU.mult, reads=R, writes=W)
            kb.op("dve", "tensor_mul", tmp1, tmp1, rstd, reads=[BT] + R,
                  writes=[BT])
            kb.op("dve", "tensor_mul", tmp1, tmp1, rstd, reads=[BT] + R,
                  writes=[BT])
            kb.op("act", "activation", tmp0, tmp1, AF.Ln, bias=self.epsv[:],
                  scale=1.0, reads=[BT, self.B_const], writes=[BT])
            kb.op("act", "activation", s2, tmp0, AF.Exp, bias=0.0, scale=-0.5,
                  reads=[BT], writes=W)
            kb.op("dve", "tensor_mul", self.st_rs2[b][:, 0:NT], rstd, s2,
                  reads=R, writes=W)
            kb.op("dve", "tensor_mul", self.st_bs2[b][:, 0:NT], b1, s2,
                  reads=R, writes=W)

        def compB(i):
            self.kb.tag = 'p1'
            kb = self.kb
            sl = i % 2
            xt = self.xstage[:, sl, :]
            hm_out = bass.AP(self.nstage, sl * D,
                             [[2 * D, 128], [1, 64], [64, 8], [512, 2]])
            kb.op("act", "activation", hm_out, xt, AF.Identity,
                  bias=self.st_b1[b][:, i:i + 1],
                  scale=self.st_rstd[b][:, i:i + 1],
                  reads=[(self.B_xstage, sl), (self.B_stats[b], i)],
                  writes=[(self.B_nstage, sl)])
            kb.op("act", "activation", self.n2stage[:, sl, :], xt, AF.Identity,
                  bias=self.st_bs2[b][:, i:i + 1],
                  scale=self.st_rs2[b][:, i:i + 1],
                  reads=[(self.B_xstage, sl), (self.B_stats[b], i)],
                  writes=[(self.B_n2stage, sl)])
            kb.dma(self.s_stn[sl], self.scr_xn[b][i * 128:(i + 1) * 128, :],
                   self.nstage[:, sl, :], reads=[(self.B_nstage, sl)],
                   writes=[(self.B_scr_xn[b], i)])
            kb.dma(self.s_stn2[sl], self.scr_hn[b][i * 128:(i + 1) * 128, :],
                   self.n2stage[:, sl, :], reads=[(self.B_n2stage, sl)],
                   writes=[(self.B_scr_hn[b], i)])

        sched = [lambda: (loadx(0), loadx(1))]
        for p in range(NT // 2):
            def emA(p=p):
                statsA(2 * p)
                if 2 * p + 2 < NT:
                    loadx(2 * p + 2)
                statsA(2 * p + 1)
                if 2 * p + 3 < NT:
                    loadx(2 * p + 3)
            sched.append(emA)
        sched.append(lambda: (batch_stats(), loadx(0), loadx(1)))
        for p in range(NT // 2):
            def emB(p=p):
                compB(2 * p)
                if 2 * p + 2 < NT:
                    loadx(2 * p + 2)
                compB(2 * p + 1)
                if 2 * p + 3 < NT:
                    loadx(2 * p + 3)
            sched.append(emB)
        return sched

    def load_T_strips(self, scr_ap, B_src, name, carryfrom=None):
        T = self.T
        chunks = []
        for c in range(8):
            h, bf = self.AR.alloc(f"{name}{c}", [128, T], BF16, carry=carryfrom)
            self.kb.dma(self.s_ldt[c % 4], h[:],
                        scr_ap[0:T, c * 128:(c + 1) * 128],
                        reads=[(B_src, list(range(self.NT)))], writes=[bf],
                        transpose=True)
            chunks.append((h, bf))
        return chunks

    def p3(self, b):
        self.kb.tag = 'p3'
        kb, T, NT, AR = self.kb, self.T, self.NT, self.AR
        TH = min(1024, T)
        NTH = T // TH
        if self._pre3 is not None:
            xnT, _ = self._pre3
            self._pre3 = None
        else:
            xnT = self.load_T_strips(self.scr_xn[b], self.B_scr_xn[b],
                                     f"xnT{b}")
        qkwh, B_qkw = AR.alloc(f"qkw{b}", [128, 16, 128], BF16, units=16)
        qkw = (qkwh, B_qkw)
        for c in range(8):
            kb.dma(self.s_ldw[c % 2], qkw[0][:, c, :],
                   self.scr_w["wqb"][c * 128:(c + 1) * 128, :],
                   reads=[(self.B_scr_w["wqb"], c)], writes=[(qkw[1], c)])
            kb.dma(self.s_ldw[c % 2], qkw[0][:, 8 + c, :],
                   self.scr_w["wkb"][c * 128:(c + 1) * 128, :],
                   reads=[(self.B_scr_w["wkb"], c)],
                   writes=[(qkw[1], 8 + c)])
        self.xnT = xnT
        dstQ = [AR.alloc(f"Qh{b}_{c}", [128, T], BF16) for c in range(8)]
        dstK = [AR.alloc(f"Kh{b}_{c}", [128, T], BF16) for c in range(8)]
        QK = (dstQ, dstK)
        wbuf = [AR.alloc(f"wb{b}_{c}", [128, D], BF16) for c in range(8)]
        for c in range(8):
            kb.dma(self.s_ldw[c % 2], wbuf[c][0][:],
                   self.scr_w["wv_hm"][c * 128:(c + 1) * 128, :],
                   reads=[(self.B_scr_w["wv_hm"], c)], writes=[wbuf[c][1]])
        Ve = [None] * NT
        self.Ve = Ve

        def emit_v(i):
            self.kb.tag = 'p3v'
            Ve[i] = AR.alloc(f"Ve{b}_{i}", [128, 1040], BF16)
            pV, bV = self.PB(2 + i % 2, 0, 1024)
            items = []
            for gh in range(2):
                for fc in range(8):
                    items.append(dict(
                        out=pV[:, gh * 512:(gh + 1) * 512],
                        lhsT=xnT[fc][0][:, i * 128:(i + 1) * 128],
                        rhs=wbuf[fc][0][:, gh * 512:(gh + 1) * 512],
                        reads=[xnT[fc][1], wbuf[fc][1]],
                        start=(fc == 0), stop=(fc == 7)))
            kb.mmg(items, writes=[bV])
            kb.op("act", "activation", Ve[i][0][:, 0:1024], pV,
                  AF.Copy, bias=0.0, scale=1.0,
                  reads=[bV], writes=[Ve[i][1]])
            kb.op("pool", "memset", Ve[i][0][:, 1024:1025], 1.0,
                  writes=[Ve[i][1]])

        for cp in range(4):
            tabc, B_tc, tabs, B_ts = self.build_tables_sbuf(cp, f"{b}_{cp}")
            self.kb.tag = 'p3'
            for wi in range(2):
                dst = QK[wi]
                for th in range(NTH):
                    tc = tabc[:, th * TH:(th + 1) * TH]
                    ts = tabs[:, th * TH:(th + 1) * TH]
                    pA, bA = self.PB(0, 0, TH)
                    pB, bB = self.PB(1, 0, TH)
                    for gi, (ps_, bs_) in enumerate(((pA, bA), (pB, bB))):
                        c = cp + 4 * gi
                        items = []
                        for q in range(max(1, TH // 512)):
                            w = min(512, TH)
                            items.append(dict(
                                out=ps_[:, q * 512:q * 512 + w],
                                lhsT=qkw[0][:, 8 * wi + c, :],
                                rhs=xnT[c][0][:, th * TH + q * 512:
                                              th * TH + q * 512 + w],
                                reads=[(qkw[1], 8 * wi + c), xnT[c][1]],
                                start=True, stop=True))
                        kb.mmg(items, writes=[bs_])
                    s1 = self.sqs[:, 0, 0:TH]
                    s2 = self.sqs[:, 1, 0:TH]
                    kb.op("dve", "tensor_mul", s1, pA, tc,
                          reads=[bA, B_tc], writes=[(self.B_sqs, 0)])
                    kb.op("dve", "tensor_mul", s2, pB, ts,
                          reads=[bB, B_ts], writes=[(self.B_sqs, 1)])
                    kb.op("pool", "tensor_sub",
                          dst[cp][0][:, th * TH:(th + 1) * TH], s1, s2,
                          reads=[(self.B_sqs, 0), (self.B_sqs, 1)],
                          writes=[dst[cp][1]])
                    kb.op("dve", "tensor_mul", s1, pA, ts,
                          reads=[bA, B_ts], writes=[(self.B_sqs, 0)])
                    kb.op("dve", "tensor_mul", s2, pB, tc,
                          reads=[bB, B_tc], writes=[(self.B_sqs, 1)])
                    kb.op("pool", "tensor_add",
                          dst[cp + 4][0][:, th * TH:(th + 1) * TH], s1, s2,
                          reads=[(self.B_sqs, 0), (self.B_sqs, 1)],
                          writes=[dst[cp + 4][1]])
            AR.freez(tabc)
            AR.freez(tabs)
            for i in range(cp * NT // 4, (cp + 1) * NT // 4):
                emit_v(i)
        self.Qh, self.Kh = QK
        AR.freez(qkw[0])
        for h, bf in wbuf:
            AR.freez(h)
        for h, bf in xnT:
            AR.freez(h)
        self.xnT = None

    def p4(self, b):
        self.kb.tag = 'p4'
        kb, T, NT, AR = self.kb, self.T, self.NT, self.AR
        TQ, NTQ = self.TQ, self.NTQ
        Qh, Kh, Ve = self.Qh, self.Kh, self.Ve
        P = [AR.alloc(f"P{b}_{k}", [128, TQ], BF16) for k in range(NT)]
        self._pre5 = self.load_T_strips(self.scr_hn[b], self.B_scr_hn[b],
                                        f"hnT{b}")
        if True:
            KTQ = TQ // 128
            for qb in range(NTQ):
                nkt = KTQ * (qb + 1)
                self.kb.tag = 'p4s'
                for kt in range(nkt):
                    off = max(0, kt * 128 - qb * TQ)
                    w = TQ - off
                    pp = kt % 2
                    pS, bS = self.PB(pp, 512, 1024)
                    items = []
                    for gc in range(8):
                        items.append(dict(
                            out=pS[:, off:off + w],
                            lhsT=Kh[gc][0][:, kt * 128:(kt + 1) * 128],
                            rhs=Qh[gc][0][:, qb * TQ + off:qb * TQ + off + w],
                            reads=[Kh[gc][1], Qh[gc][1]],
                            start=(gc == 0), stop=(gc == 7)))
                    kb.mmg(items, writes=[bS])
                    kb.op("act", "activation", P[kt][0][:, off:off + w],
                          pS[:, off:off + w], AF.Exp, bias=0.0, scale=GAMMA1,
                          reads=[bS], writes=[P[kt][1]])
                    if kt * 128 >= qb * TQ:
                        kb.op("dve", "tensor_mul", P[kt][0][:, off:off + 128],
                              P[kt][0][:, off:off + 128], self.trimask[:],
                              reads=[P[kt][1], self.B_const], writes=[P[kt][1]])
                self.kb.tag = 'p4av'
                for qi in range(qb * TQ // 128, (qb + 1) * TQ // 128):
                    pp = qi % 2
                    pY, bY = self.PB(2 + pp, 0, 1024)
                    pR, bR = self.PB(pp, 0, 16)
                    items = []
                    for kt in range(qi + 1):
                        off = qi * 128 - qb * TQ
                        stat = P[kt][0][:, off:off + 128]
                        for gh in range(2):
                            items.append(dict(
                                out=pY[:, gh * 512:(gh + 1) * 512],
                                lhsT=stat,
                                rhs=Ve[kt][0][:, gh * 512:(gh + 1) * 512],
                                reads=[P[kt][1], Ve[kt][1]],
                                start=(kt == 0), stop=(kt == qi)))
                        items.append(dict(
                            out=pR[:, 0:1], lhsT=stat,
                            rhs=Ve[kt][0][:, 1024:1025],
                            reads=[P[kt][1], Ve[kt][1]],
                            start=(kt == 0), stop=(kt == qi)))
                    kb.mmg(items, writes=[bY, bR])
                    kb.op("dve", "reciprocal", self.st_t[:, pp, 0:1], pR[:, 0:1],
                          reads=[bR], writes=[(self.B_stt, pp)])
                    ot = self.sqs[:, pp, :]
                    kb.op("act", "activation", ot, pY, AF.Identity,
                          bias=0.0, scale=self.st_t[:, pp, 0:1],
                          reads=[bY, (self.B_stt, pp)],
                          writes=[(self.B_sqs, pp)])
                    kb.dma(self.s_acc[pp], self.scr_acc[b][qi * 128:(qi + 1) * 128, :],
                           ot, reads=[(self.B_sqs, pp)],
                           writes=[(self.B_scr_acc[b], qi)])
                    if self.debug:
                        kb.dma(self.s_dbg, self.dbg_y[b, qi * 128:(qi + 1) * 128, :],
                               ot, reads=[(self.B_sqs, pp)], writes=[])
        for h, bf in P:
            AR.freez(h)
        for lst in (self.Qh, self.Kh, self.Ve):
            for h, bf in lst:
                AR.freez(h)
        self.Qh = self.Kh = self.Ve = None

    def p5(self, b):
        self.kb.tag = 'p5'
        kb, T, NT, AR = self.kb, self.T, self.NT, self.AR
        if getattr(self, "_pre5", None) is not None:
            hnT = self._pre5
            self._pre5 = None
        else:
            hnT = self.load_T_strips(self.scr_hn[b], self.B_scr_hn[b],
                                     f"hnT{b}")
        wsets = [[AR.alloc(f"wf{b}_{s}_{c}", [128, D], BF16) for c in range(8)]
                 for s in range(2)]
        TH = min(1024, T)
        NTH = T // TH
        QK = []
        for wi, (wname, oname) in enumerate(
                (("fwq", f"Q2T{b}"), ("fwk", f"K2T{b}"))):
            wbuf = wsets[wi % 2]
            dst = [AR.alloc(f"{oname}_{c}", [128, T], BF16) for c in range(8)]
            for c in range(8):
                kb.dma(self.s_ldw[c % 2], wbuf[c][0][:],
                       self.scr_w[wname][c * 128:(c + 1) * 128, :],
                       reads=[(self.B_scr_w[wname], c)], writes=[wbuf[c][1]])
            if True:
                for gc in range(8):
                    for th in range(NTH):
                        pp = (gc * NTH + th) % 2
                        pQ, bQ = self.PB(pp, 0, TH)
                        items = []
                        for q in range(max(1, TH // 512)):
                            w = min(512, TH)
                            for fc in range(8):
                                items.append(dict(
                                    out=pQ[:, q * 512:q * 512 + w],
                                    lhsT=wbuf[fc][0][:, gc * 128:(gc + 1) * 128],
                                    rhs=hnT[fc][0][:, th * TH + q * 512:
                                                    th * TH + q * 512 + w],
                                    reads=[wbuf[fc][1], hnT[fc][1]],
                                    start=(fc == 0), stop=(fc == 7)))
                        kb.mmg(items, writes=[bQ])
                        kb.op("act", "activation",
                              dst[gc][0][:, th * TH:(th + 1) * TH], pQ,
                              AF.Copy, bias=0.0, scale=1.0,
                              reads=[bQ], writes=[dst[gc][1]])
            QK.append(dst)
        self.Q2T, self.K2T = QK
        wbuf = wsets[0]
        for c in range(8):
            kb.dma(self.s_ldw[c % 2], wbuf[c][0][:],
                   self.scr_w["fwv"][c * 128:(c + 1) * 128, :],
                   reads=[(self.B_scr_w["fwv"], c)], writes=[wbuf[c][1]])
        V2e = [AR.alloc(f"V2e{b}_{i}", [128, 1040], BF16) for i in range(NT)]
        self.V2e = V2e
        if True:
            for i in range(NT):
                for gh in range(2):
                    pp = (2 * i + gh) % 2
                    pV, bV = self.PB(2, pp * 512, pp * 512 + 512)
                    items = []
                    for fc in range(8):
                        items.append(dict(
                            out=pV,
                            lhsT=hnT[fc][0][:, i * 128:(i + 1) * 128],
                            rhs=wbuf[fc][0][:, gh * 512:(gh + 1) * 512],
                            reads=[hnT[fc][1], wbuf[fc][1]],
                            start=(fc == 0), stop=(fc == 7)))
                    kb.mmg(items, writes=[bV])
                    dst_ap = bass.AP(V2e[i][0], gh * 8 * 65,
                                     [[1040, 128], [65, 8], [1, 64]])
                    kb.op("act", "activation", dst_ap, pV, AF.Copy,
                          bias=0.0, scale=1.0, reads=[bV],
                          writes=[V2e[i][1]])
                ones_ap = bass.AP(V2e[i][0], 64, [[1040, 128], [65, 16]])
                kb.op("pool", "memset", ones_ap, 1.0, writes=[V2e[i][1]])
        for ws in wsets:
            for h, bf in ws:
                AR.freez(h)
        for h, bf in hnT:
            AR.freez(h)

    def p6(self, b, inject=()):
        self.kb.tag = 'p6'
        inject = list(inject)
        kb, T, NT, AR = self.kb, self.T, self.NT, self.AR
        TQ, NTQ = self.TQ, self.NTQ
        Q2T, K2T, V2e = self.Q2T, self.K2T, self.V2e
        KTQ = TQ // 128
        Pf = [AR.alloc(f"Pf{b}_{i}", [128, TQ], BF16) for i in range(6)]
        OT = [None] * 8
        wbuf = [AR.alloc(f"wo{b}_{c}", [128, D], BF16) for c in range(8)]
        for c in range(8):
            kb.dma(self.s_ldw[c % 2], wbuf[c][0][:],
                   self.scr_w["fwo"][c * 128:(c + 1) * 128, :],
                   reads=[(self.B_scr_w["fwo"], c)], writes=[wbuf[c][1]])
        self._p6_pending = None
        if True:
            for hp in range(8):
                if OT[hp] is None:
                    OT[hp] = AR.alloc(f"OT{b}_{hp}", [128, T], BF16)
                for qb in range(NTQ):
                    if inject:
                        inject.pop(0)()
                    self.kb.tag = 'p6'

                    nkt = KTQ * (qb + 1)
                    pq6 = (hp * NTQ + qb) % 2
                    pyt = []
                    for hi in range(2):
                        pY, bY = self.PB(2 + pq6, hi * 512, hi * 512 + TQ)
                        pyt.append((pY, bY))
                    def emit_scores(kt):
                        self.kb.tag = 'p6s'
                        off = max(0, kt * 128 - qb * TQ)
                        w = TQ - off
                        pp = kt % 2
                        pf = kt % 3
                        for hi in range(2):
                            base = hi * 64
                            pS, bS = self.PB(pp, hi * 512, hi * 512 + TQ)
                            kb.mmg([dict(
                                out=pS[:, off:off + w],
                                lhsT=K2T[hp][0][base:base + 64,
                                               kt * 128:(kt + 1) * 128],
                                rhs=Q2T[hp][0][base:base + 64,
                                              qb * TQ + off:qb * TQ + off + w],
                                reads=[K2T[hp][1], Q2T[hp][1]],
                                start=True, stop=True)],
                                writes=[bS])
                            kb.op("act", "activation",
                                  Pf[2 * pf + hi][0][:, off:off + w],
                                  pS[:, off:off + w], AF.Exp,
                                  bias=0.0, scale=GAMMA2,
                                  reads=[bS], writes=[Pf[2 * pf + hi][1]])
                            if kt * 128 >= qb * TQ:
                                kb.op("dve", "tensor_mul",
                                      Pf[2 * pf + hi][0][:, off:off + 128],
                                      Pf[2 * pf + hi][0][:, off:off + 128],
                                      self.trimask[:],
                                      reads=[Pf[2 * pf + hi][1], self.B_const],
                                      writes=[Pf[2 * pf + hi][1]])

                    def emit_av(kt):
                        self.kb.tag = 'p6a'
                        off = max(0, kt * 128 - qb * TQ)
                        w = TQ - off
                        pf = kt % 3
                        for hi in range(2):
                            h = 2 * hp + hi
                            kb.mmg([dict(
                                out=pyt[hi][0][0:65, off:off + w],
                                lhsT=V2e[kt][0][:, h * 65:h * 65 + 65],
                                rhs=Pf[2 * pf + hi][0][:, off:off + w],
                                reads=[V2e[kt][1], Pf[2 * pf + hi][1]],
                                start=(kt == 0), stop=(kt == nkt - 1))],
                                writes=[pyt[hi][1]])

                    emit_scores(0)
                    if nkt > 1:
                        emit_scores(1)
                    for kt in range(2, nkt):
                        emit_scores(kt)
                        emit_av(kt - 2)
                    if nkt > 1:
                        emit_av(nkt - 2)
                    emit_av(nkt - 1)
                    if self._p6_pending is not None:
                        self._p6_pending()
                        self._p6_pending = None
                    self.kb.tag = 'p6n'
                    for hi in range(2):
                        pY, bY = pyt[hi]
                        kb.op("dve", "tensor_copy", self.ysc[:, hi, 0:TQ],
                              pY[0:64, :], reads=[bY],
                              writes=[(self.B_rope, hi)])
                        kb.op("act", "activation", self.rrow[:, hi, 0:TQ],
                              pY[64:65, :], AF.Ln, bias=0.0, scale=1.0,
                              reads=[bY], writes=[(self.B_rope, hi)])
                        kb.op("act", "activation", self.rinv[:, hi, 0:TQ],
                              self.rrow[:, hi, 0:TQ], AF.Exp, bias=0.0,
                              scale=-1.0, reads=[(self.B_rope, hi)],
                              writes=[(self.B_rope, hi)])
                    def part_b(hp=hp, qb=qb):
                        self.kb.tag = 'p6n'
                        for hi in range(2):
                            pRB, bRB = self.PB(1, hi * 512, hi * 512 + TQ,
                                               p0=0, p1=64)
                            kb.mmg([dict(out=pRB, lhsT=self.ones64[:],
                                         rhs=self.rinv[:, hi, 0:TQ],
                                         reads=[(self.B_rope, hi),
                                                self.B_const],
                                         start=True, stop=True)],
                                   writes=[bRB])
                            kb.op("dve", "tensor_tensor",
                                  OT[hp][0][hi * 64:hi * 64 + 64,
                                            qb * TQ:(qb + 1) * TQ],
                                  self.ysc[:, hi, 0:TQ], pRB, ALU.mult,
                                  reads=[(self.B_rope, hi), bRB],
                                  writes=[OT[hp][1]])
                    self._p6_pending = part_b
            if self._p6_pending is not None:
                self._p6_pending()
                self._p6_pending = None
            for hp in range(8):
                AR.freez(Q2T[hp][0])
                AR.freez(K2T[hp][0])
        for h_, bf in V2e:
            AR.freez(h_)
        for h_, bf in Pf:
            AR.freez(h_)
        self.Q2T = self.K2T = self.V2e = None
        self.kb.tag = 'p6wo'
        acct, B_acct = AR.alloc(f"acct{b}", [128, 2, D], F32, units=2)
        htile, B_ht = AR.alloc(f"ht{b}", [128, 2, D], F32, units=2)
        if True:
            for i in range(NT):
                sl = i % 2
                psw_, bW = self.PB(2 * sl, 0, 1024)
                kb.dma(self.s_ldx[sl], self.xstage[:, sl, :],
                       self.x_d[b, i * 128:(i + 1) * 128, :],
                       reads=[self.B_in], writes=[(self.B_xstage, sl)])
                kb.dma(self.s_acc[sl], acct[:, sl, :],
                       self.scr_acc[b][i * 128:(i + 1) * 128, :],
                       reads=[(self.B_scr_acc[b], i)], writes=[(B_acct, sl)])
                items = []
                for gh in range(2):
                    for gc in range(8):
                        items.append(dict(
                            out=psw_[:, gh * 512:(gh + 1) * 512],
                            lhsT=OT[gc][0][:, i * 128:(i + 1) * 128],
                            rhs=wbuf[gc][0][:, gh * 512:(gh + 1) * 512],
                            reads=[OT[gc][1], wbuf[gc][1]],
                            start=(gc == 0), stop=(gc == 7)))
                kb.mmg(items, writes=[bW])
                kb.op("act", "activation", self.sqs[:, sl, :],
                      self.xstage[:, sl, :], AF.Identity,
                      bias=self.st_b1[b][:, i:i + 1],
                      scale=self.st_rstd[b][:, i:i + 1],
                      reads=[(self.B_xstage, sl), (self.B_stats[b], i)],
                      writes=[(self.B_sqs, sl)])
                kb.op("dve", "tensor_add", htile[:, sl, :], self.sqs[:, sl, :],
                      psw_, reads=[(self.B_sqs, sl), bW],
                      writes=[(B_ht, sl)])
                kb.op("act", "activation", self.sqs[:, sl, :], htile[:, sl, :],
                      AF.Square, bias=0.0, scale=1.0,
                      accum_out=self.st_t[:, sl, 0:1],
                      reads=[(B_ht, sl)],
                      writes=[(self.B_sqs, sl), (self.B_stt, sl)])
                kb.op("act", "activation", self.st_t[:, sl, 1:2],
                      self.st_t[:, sl, 0:1],
                      AF.Ln, bias=self.epsv[:], scale=1.0 / D,
                      reads=[(self.B_stt, sl), self.B_const],
                      writes=[(self.B_stt, sl)])
                kb.op("act", "activation", self.st_s3[b][:, i:i + 1],
                      self.st_t[:, sl, 1:2], AF.Exp, bias=0.0, scale=-0.5,
                      reads=[(self.B_stt, sl)],
                      writes=[(self.B_s3[b], i)])
                kb.op("act", "activation", self.n2stage[:, sl, :],
                      htile[:, sl, :], AF.Identity, bias=0.0,
                      scale=self.st_s3[b][:, i:i + 1],
                      reads=[(B_ht, sl), (self.B_s3[b], i)],
                      writes=[(self.B_n2stage, sl)])
                kb.dma(self.s_stn3[sl], self.scr_Hn[b][i * 128:(i + 1) * 128, :],
                       self.n2stage[:, sl, :], reads=[(self.B_n2stage, sl)],
                       writes=[(self.B_scr_Hn[b], i)])
                kb.op("pool", "tensor_add", acct[:, sl, :], acct[:, sl, :],
                      htile[:, sl, :], reads=[(B_acct, sl), (B_ht, sl)],
                      writes=[(B_acct, sl)])
                kb.dma(self.s_acc2[sl], self.scr_acc[b][i * 128:(i + 1) * 128, :],
                       acct[:, sl, :], reads=[(B_acct, sl)],
                       writes=[(self.B_scr_acc[b], i)])
                if self.debug:
                    kb.dma(self.s_dbg, self.dbg_h[b, i * 128:(i + 1) * 128, :],
                           htile[:, sl, :], reads=[(B_ht, sl)], writes=[])
        for c in range(8):
            AR.freez(OT[c][0])
        for h_, bf in wbuf:
            AR.freez(h_)
        AR.freez(acct)
        AR.freez(htile)

    def p7(self, b):
        self.kb.tag = 'p7r'
        kb, T, NT, AR = self.kb, self.T, self.NT, self.AR
        C2 = 192
        CE = 2 * C2
        TH7 = min(1024, T)
        NTH7 = T // TH7
        NTPT = TH7 // 128
        IOA = bass.IndirectOffsetOnAxis
        fc1b = [AR.alloc(f"fc1_{b}_{c}", [128, EDIM], BF16) for c in range(8)]
        fc2b = [AR.alloc(f"fc2_{b}_{c}", [128, D], BF16) for c in range(16)]

        def load_fc(e):
            for c in range(8):
                kb.dma(self.s_fc1[c % 4], fc1b[c][0][:],
                       self.scr_fc1[e, c * 128:(c + 1) * 128, :],
                       reads=[(self.B_scr_fc1, e * 8 + c)],
                       writes=[fc1b[c][1]])
            for c in range(16):
                kb.dma(self.s_fc2[c % 4], fc2b[c][0][:],
                       self.scr_fc2[e, c * 128:(c + 1) * 128, :],
                       reads=[(self.B_scr_fc2, e * 16 + c)],
                       writes=[fc2b[c][1]], eng="pool")

        load_fc(0)
        self.kb.tag = 'p7r'
        HnT = self.load_T_strips(self.scr_Hn[b], self.B_scr_Hn[b], f"HnT{b}")
        for fc in range(8):
            kb.dma(self.s_ldw[fc % 2], self.rwb[:, fc, :],
                   self.scr_rw[fc * 128:(fc + 1) * 128, :],
                   reads=[(self.B_scr_rw8, fc)], writes=[self.B_rwb])
        kb.dma(self.s_zz, self.scr_Hn[b][T:T + 1, :], self.zrow[0:1, :],
               reads=[self.B_const],
               writes=[(self.B_scr_Hn[b], NT)], eng="pool")
        for z0 in range(0, T, 2048):
            zr = min(2048, T - z0)
            kb.dma(self.s_stn[1],
                   self.scr_cw[b][z0:z0 + zr, :]
                   .rearrange("(p a) b -> p (a b)", p=128),
                   self.zrow[:, 0:zr // 128 * NEXP],
                   reads=[self.B_const],
                   writes=[(self.B_scr_cw[b], list(range(NT + 1)))])
        kb.dma(self.s_stn[1], self.scr_cw[b][T:T + 1, :],
               self.zrow[0:1, 0:NEXP],
               reads=[self.B_const], writes=[(self.B_scr_cw[b], NT)])
        for th in range(NTH7):
            seg = self.scr_idx[b][th * 3072:(th + 1) * 3072]
            kb.dma(self.s_stn[0], seg.rearrange("(p a) -> p a", p=128),
                   self.senti[:, 0:24], reads=[self.B_const],
                   writes=[(self.B_scr_idx[b], th)])

        gtile, B_gt = AR.alloc(f"gt{b}", [128, 6, D], BF16, units=6)
        idxt, B_ix = AR.alloc(f"ix{b}", [128, 2, 4], mybir.dt.int32, units=2)
        cgt, B_cg = AR.alloc(f"cg{b}", [128, 2, 3 * NEXP], F32, units=2)
        base_r, B_base = AR.alloc(f"base{b}", [1, 16], F32)
        posf, B_posf = AR.alloc(f"posf{b}", [128, 2, 40], F32, units=2)
        tgti, B_tg = AR.alloc(f"tg{b}", [128, 2, 2], mybir.dt.int32, units=2)
        tki, B_tk = AR.alloc(f"tk{b}", [128, 2, 1], mybir.dt.int32, units=2)
        eqf, B_eqf = AR.alloc(f"eqf{b}", [128, 2, 16], F32, units=2)
        eqb, B_eqb = AR.alloc(f"eqb{b}", [128, 2, 16], BF16, units=2)
        Gs = [AR.alloc(f"G{b}_{c}", [128, CE], BF16) for c in range(16)]
        mst = [AR.alloc(f"mst{b}_{B}", [128, D], F32) for B in range(3)]
        msw, B_msw = AR.alloc(f"msw{b}", [64, 2, D], F32, units=2)
        ixw, B_ixw = AR.alloc(f"ixw{b}", [64, 2, 1], mybir.dt.int32, units=2)
        relu_s, B_rs = AR.alloc(f"relu{b}", [128, 2, CE], BF16, units=2)
        gthT = [[AR.alloc(f"gthT{b}_{s}_{c}", [128, CE], BF16) for c in range(8)]
                for s in range(2)]
        def compact_tile(th, tt):
            self.kb.tag = 'p7r'
            if tt == 0:
                kb.op("dve", "memset", base_r[:], 0.0, writes=[B_base])
            if True:
                i = th * NTPT + tt
                sl7 = i % 2
                stt = self.st_t[:, sl7, :]
                B_stt7 = (self.B_stt, sl7)
                psl, bL = self.PB(3, 512 + sl7 * 16, 512 + sl7 * 16 + NEXP)
                items = []
                for fc in range(8):
                    items.append(dict(
                        out=psl,
                        lhsT=HnT[fc][0][:, i * 128:(i + 1) * 128],
                        rhs=self.rwb[:, fc, :],
                        reads=[HnT[fc][1], self.B_rwb],
                        start=(fc == 0), stop=(fc == 7)))
                kb.mmg(items, writes=[bL])
                lg = self.lgt[:, sl7, :]
                B_lg7 = (self.B_lgt, sl7)
                kb.op("dve", "tensor_copy", lg, psl, reads=[bL], writes=[B_lg7])
                kb.op("dve", "reduce_max", stt[:, 0:1], lg, axis=AX,
                      reads=[B_lg7], writes=[B_stt7])
                kb.op("dve", "tensor_scalar", stt[:, 2:3], stt[:, 0:1], -1.0,
                      None, ALU.mult, reads=[B_stt7], writes=[B_stt7])
                eq1 = eqf[:, sl7, 0:NEXP]
                eq2 = eqf[:, sl7, NEXP:2 * NEXP]
                kb.op("dve", "tensor_scalar", eq1, lg, stt[:, 0:1], None,
                      ALU.is_equal, reads=[B_lg7, B_stt7],
                      writes=[(B_eqf, sl7)])
                kb.op("dve", "scalar_tensor_tensor", self.lgt2[:, sl7, :],
                      eq1, -1e30, lg, ALU.mult, ALU.add,
                      reads=[(B_eqf, sl7), B_lg7],
                      writes=[(self.B_lgt2, sl7)])
                kb.op("dve", "reduce_max", stt[:, 1:2], self.lgt2[:, sl7, :],
                      axis=AX, reads=[(self.B_lgt2, sl7)], writes=[B_stt7])
                kb.op("dve", "tensor_scalar", eq2, self.lgt2[:, sl7, :],
                      stt[:, 1:2], None, ALU.is_equal,
                      reads=[(self.B_lgt2, sl7), B_stt7],
                      writes=[(B_eqf, sl7)])
                kb.op("dve", "tensor_add", stt[:, 3:4], stt[:, 1:2],
                      stt[:, 2:3], reads=[B_stt7], writes=[B_stt7])
                kb.op("act", "activation", stt[:, 4:5], stt[:, 3:4], AF.Exp,
                      bias=0.0, scale=1.0, reads=[B_stt7], writes=[B_stt7])
                kb.op("dve", "tensor_scalar_add", stt[:, 5:6], stt[:, 4:5],
                      1.0, reads=[B_stt7], writes=[B_stt7])
                kb.op("dve", "reciprocal", stt[:, 6:7], stt[:, 5:6],
                      reads=[B_stt7], writes=[B_stt7])
                kb.op("dve", "tensor_mul", stt[:, 7:8], stt[:, 4:5],
                      stt[:, 6:7], reads=[B_stt7], writes=[B_stt7])
                kb.op("dve", "tensor_scalar_mul", self.cw[b][:, i, :], eq1,
                      stt[:, 6:7], reads=[(B_eqf, sl7), B_stt7],
                      writes=[(self.B_cw[b], i)])
                kb.op("dve", "scalar_tensor_tensor", self.cw[b][:, i, :],
                      eq2, stt[:, 7:8], self.cw[b][:, i, :],
                      ALU.mult, ALU.add,
                      reads=[(B_eqf, sl7), B_stt7, (self.B_cw[b], i)],
                      writes=[(self.B_cw[b], i)])
                kb.dma(self.s_stn2[i % 2], self.scr_cw[b][i * 128:(i + 1) * 128, :],
                       self.cw[b][:, i, :], reads=[(self.B_cw[b], i)],
                       writes=[(self.B_scr_cw[b], i)])
                if self.debug:
                    kb.dma(self.s_dbg, self.dbg_c[b, i * 128:(i + 1) * 128, :],
                           self.cw[b][:, i, :], reads=[(self.B_cw[b], i)],
                           writes=[])
                kb.op("dve", "tensor_copy", eqb[:, sl7, 0:16],
                      eqf[:, sl7, 0:16], reads=[(B_eqf, sl7)],
                      writes=[(B_eqb, sl7)])
                pP1, bP1 = self.PB(3, sl7 * 64, sl7 * 64 + NEXP)
                pP2, bP2 = self.PB(3, sl7 * 64 + 16, sl7 * 64 + 16 + NEXP)
                pB1, bB1 = self.PB(3, sl7 * 64 + 32, sl7 * 64 + 32 + NEXP)
                pB2, bB2 = self.PB(3, sl7 * 64 + 48, sl7 * 64 + 48 + NEXP)
                kb.mmg([dict(out=pP1, lhsT=self.ustrict[:],
                             rhs=eqb[:, sl7, 0:NEXP],
                             reads=[self.B_const, (B_eqb, sl7)],
                             start=True, stop=True)], writes=[bP1])
                kb.mmg([dict(out=pP2, lhsT=self.ustrict[:],
                             rhs=eqb[:, sl7, NEXP:2 * NEXP],
                             reads=[self.B_const, (B_eqb, sl7)],
                             start=True, stop=True)], writes=[bP2])
                kb.mmg([dict(out=pB1, lhsT=self.ones128f[:],
                             rhs=base_r[0:1, 0:NEXP],
                             reads=[self.B_const, B_base],
                             start=True, stop=True)], writes=[bB1])
                kb.mmg([dict(out=pB2, lhsT=self.ones128f[:],
                             rhs=base_r[0:1, NEXP:2 * NEXP],
                             reads=[self.B_const, B_base],
                             start=True, stop=True)], writes=[bB2])
                p1f = posf[:, sl7, 0:8]
                p2f = posf[:, sl7, 8:16]
                kb.op("act", "activation", p1f, pP1, AF.Copy, bias=0.0,
                      scale=1.0, reads=[bP1], writes=[(B_posf, sl7)])
                kb.op("dve", "tensor_add", p1f, p1f, pB1,
                      reads=[(B_posf, sl7), bB1], writes=[(B_posf, sl7)])
                kb.op("act", "activation", p2f, pP2, AF.Copy, bias=0.0,
                      scale=1.0, reads=[bP2], writes=[(B_posf, sl7)])
                kb.op("dve", "tensor_add", p2f, p2f, pB2,
                      reads=[(B_posf, sl7), bB2], writes=[(B_posf, sl7)])
                pC, bC = self.PB(2, sl7 * 32, sl7 * 32 + 16, p0=0, p1=1)
                kb.mmg([dict(out=pC, lhsT=self.onescol[:],
                             rhs=eqb[:, sl7, 0:16],
                             reads=[self.B_const, (B_eqb, sl7)],
                             start=True, stop=True)], writes=[bC])
                kb.op("dve", "tensor_add", base_r[0:1, 0:16],
                      base_r[0:1, 0:16], pC,
                      reads=[B_base, bC], writes=[B_base])
                t8 = posf[:, sl7, 16:24]
                e1c = posf[:, sl7, 24:25]
                e2c = posf[:, sl7, 25:26]
                q1 = posf[:, sl7, 26:27]
                q2 = posf[:, sl7, 27:28]
                ov = posf[:, sl7, 28:29]
                tg1 = posf[:, sl7, 29:30]
                tg2 = posf[:, sl7, 30:31]
                kb.op("dve", "tensor_mul", t8, eq1, self.erow[:],
                      reads=[(B_eqf, sl7), self.B_const],
                      writes=[(B_posf, sl7)])
                kb.op("dve", "reduce_sum", e1c, t8, axis=AX,
                      reads=[(B_posf, sl7)], writes=[(B_posf, sl7)])
                kb.op("dve", "tensor_mul", t8, eq2, self.erow[:],
                      reads=[(B_eqf, sl7), self.B_const],
                      writes=[(B_posf, sl7)])
                kb.op("dve", "reduce_sum", e2c, t8, axis=AX,
                      reads=[(B_posf, sl7)], writes=[(B_posf, sl7)])
                kb.op("dve", "tensor_mul", t8, eq1, p1f,
                      reads=[(B_eqf, sl7), (B_posf, sl7)],
                      writes=[(B_posf, sl7)])
                kb.op("dve", "reduce_sum", q1, t8, axis=AX,
                      reads=[(B_posf, sl7)], writes=[(B_posf, sl7)])
                kb.op("dve", "tensor_mul", t8, eq2, p2f,
                      reads=[(B_eqf, sl7), (B_posf, sl7)],
                      writes=[(B_posf, sl7)])
                kb.op("dve", "reduce_sum", q2, t8, axis=AX,
                      reads=[(B_posf, sl7)], writes=[(B_posf, sl7)])
                kb.op("dve", "scalar_tensor_tensor", tg1, e1c, float(CE), q1,
                      ALU.mult, ALU.add, reads=[(B_posf, sl7)],
                      writes=[(B_posf, sl7)])
                kb.op("dve", "tensor_scalar_add", tg1, tg1, float(th * 3072),
                      reads=[(B_posf, sl7)], writes=[(B_posf, sl7)])
                kb.op("dve", "tensor_scalar", ov, q1, float(C2), None,
                      ALU.is_ge, reads=[(B_posf, sl7)], writes=[(B_posf, sl7)])
                kb.op("dve", "scalar_tensor_tensor", tg1, ov, 1e9, tg1,
                      ALU.mult, ALU.add, reads=[(B_posf, sl7)],
                      writes=[(B_posf, sl7)])
                kb.op("dve", "scalar_tensor_tensor", tg2, e2c, float(CE), q2,
                      ALU.mult, ALU.add, reads=[(B_posf, sl7)],
                      writes=[(B_posf, sl7)])
                kb.op("dve", "tensor_scalar_add", tg2, tg2,
                      float(C2 + th * 3072),
                      reads=[(B_posf, sl7)], writes=[(B_posf, sl7)])
                kb.op("dve", "tensor_scalar", ov, q2, float(C2), None,
                      ALU.is_ge, reads=[(B_posf, sl7)], writes=[(B_posf, sl7)])
                kb.op("dve", "scalar_tensor_tensor", tg2, ov, 1e9, tg2,
                      ALU.mult, ALU.add, reads=[(B_posf, sl7)],
                      writes=[(B_posf, sl7)])
                kb.op("dve", "tensor_copy", tgti[:, sl7, 0:1], tg1,
                      reads=[(B_posf, sl7)], writes=[(B_tg, sl7)])
                kb.op("dve", "tensor_copy", tgti[:, sl7, 1:2], tg2,
                      reads=[(B_posf, sl7)], writes=[(B_tg, sl7)])
                kb.op("pool", "iota", tki[:, sl7, :], [[1, 1]], base=i * 128,
                      channel_multiplier=1, writes=[(B_tk, sl7)])
                seg = self.scr_idx[b][:].rearrange("(a b) -> a b", b=1)
                kb.idma(self.s_ix[0], seg,
                        IOA(tgti[:, sl7, 0:1], 0), tki[:, sl7, :], None,
                        bounds_check=self.rg_idx, oob_is_err=False,
                        reads=[(B_tg, sl7), (B_tk, sl7)],
                        writes=[(self.B_scr_idx[b], th)])
                kb.idma(self.s_ix[1], seg,
                        IOA(tgti[:, sl7, 1:2], 0), tki[:, sl7, :], None,
                        bounds_check=self.rg_idx, oob_is_err=False,
                        reads=[(B_tg, sl7), (B_tk, sl7)],
                        writes=[(self.B_scr_idx[b], th)])

        for tt in range(NTPT):
            compact_tile(0, tt)
        pend = [(th, tt) for th in range(1, NTH7) for tt in range(NTPT)]

        self.kb.tag = 'p7moe'
        pairs = [(th, e) for th in range(NTH7) for e in range(NEXP)]

        def gather_block(k):
            th, e = pairs[k]
            s = k % 2
            for a in range(3):
                kb.dma(self.s_ldt[a % 2],
                       idxt[:, s, a:a + 1],
                       self.scr_idx[b][th * 3072 + e * CE + a * 128:
                                       th * 3072 + e * CE + a * 128 + 128]
                       .rearrange("(p a) -> p a", p=128),
                       reads=[(self.B_scr_idx[b], th)],
                       writes=[(B_ix, s)])
            for a in range(3):
                kb.idma(self.s_gg[a],
                        gtile[:, 3 * s + a, :], None, self.scr_Hn[b][:],
                        IOA(idxt[:, s, a:a + 1], 0),
                        reads=[(B_ix, s),
                               (self.B_scr_Hn[b], list(range(NT + 1)))],
                        writes=[(B_gt, 3 * s + a)])
                kb.dma(self.s_stn2[a % 2],
                       self.scr_gth[s][a * 128:(a + 1) * 128, :],
                       gtile[:, 3 * s + a, :], reads=[(B_gt, 3 * s + a)],
                       writes=[self.B_scrg[s]])
                kb.idma(self.s_ix[s],
                        cgt[:, s, a * NEXP:(a + 1) * NEXP], None,
                        self.scr_cw[b][:],
                        IOA(idxt[:, s, a:a + 1], 0),
                        reads=[(B_ix, s),
                               (self.B_scr_cw[b], list(range(NT + 1)))],
                        writes=[(B_cg, s)])
            for fc in range(8):
                kb.dma(self.s_ldt[fc % 4], gthT[s][fc][0][:],
                       self.scr_gth[s][0:CE, fc * 128:(fc + 1) * 128],
                       reads=[self.B_scrg[s]], writes=[gthT[s][fc][1]],
                       transpose=True)

        gather_block(0)
        for k, (th, e) in enumerate(pairs):
            if k > 0:
                load_fc(e)
            s = k % 2
            if pend:
                compact_tile(*pend.pop(0))
                self.kb.tag = 'p7moe'
            if k + 1 < len(pairs):
                gather_block(k + 1)
            if True:
                for hc in range(16):
                    pp = hc % 2
                    psg, bG = self.PB(pp, 0, CE)
                    items = []
                    for fc in range(8):
                        items.append(dict(
                            out=psg,
                            lhsT=fc1b[fc][0][:, hc * 128:(hc + 1) * 128],
                            rhs=gthT[s][fc][0][:],
                            reads=[fc1b[fc][1], gthT[s][fc][1]],
                            start=(fc == 0), stop=(fc == 7)))
                    kb.mmg(items, writes=[bG])
                    kb.op("act", "activation", relu_s[:, pp, 0:CE], psg,
                          AF.Relu, bias=0.0, scale=1.0,
                          reads=[bG], writes=[(B_rs, pp)])
                    kb.op("dve", "tensor_mul", Gs[hc][0][:],
                          relu_s[:, pp, 0:CE], relu_s[:, pp, 0:CE],
                          reads=[(B_rs, pp)], writes=[Gs[hc][1]])
                for tt3 in range(3):
                    pp = tt3 % 2
                    psm, bM = self.PB(2 + pp, 0, 1024)
                    items = []
                    for gh in range(2):
                        for hc in range(16):
                            items.append(dict(
                                out=psm[:, gh * 512:(gh + 1) * 512],
                                lhsT=Gs[hc][0][:, tt3 * 128:(tt3 + 1) * 128],
                                rhs=fc2b[hc][0][:, gh * 512:(gh + 1) * 512],
                                reads=[Gs[hc][1], fc2b[hc][1]],
                                start=(hc == 0), stop=(hc == 15)))
                    kb.mmg(items, writes=[bM])
                    kb.op("dve", "tensor_scalar_mul", mst[tt3][0][:], psm,
                          cgt[:, s, tt3 * NEXP + e:tt3 * NEXP + e + 1],
                          reads=[bM, (B_cg, s)], writes=[mst[tt3][1]])
                    if tt3 == 0:
                        kb.idma(self.s_sca[0], self.scr_moe[b][:],
                                IOA(idxt[:, s, 0:1], 0),
                                mst[0][0][:], None,
                                bounds_check=self.rg_tok, oob_is_err=False,
                                reads=[mst[0][1], (B_ix, s)],
                                writes=[(self.B_scr_moe[b], e)])
                    elif tt3 == 2:
                        kb.idma(self.s_sca[1], self.scr_moe2[b][:],
                                IOA(idxt[:, s, 2:3], 0),
                                mst[2][0][:], None,
                                bounds_check=self.rg_tok, oob_is_err=False,
                                reads=[mst[2][1], (B_ix, s)],
                                writes=[(self.B_scr_moe2[b], e)])
                    else:
                        kb.op("dve", "tensor_copy", msw[:, 0, :],
                              mst[1][0][0:64, :], reads=[mst[1][1]],
                              writes=[(B_msw, 0)])
                        kb.op("dve", "tensor_copy", msw[:, 1, :],
                              mst[1][0][64:128, :], reads=[mst[1][1]],
                              writes=[(B_msw, 1)])
                        kb.op("dve", "tensor_copy", ixw[:, 0, :],
                              idxt[0:64, s, 1:2], reads=[(B_ix, s)],
                              writes=[(B_ixw, 0)])
                        kb.op("dve", "tensor_copy", ixw[:, 1, :],
                              idxt[64:128, s, 1:2], reads=[(B_ix, s)],
                              writes=[(B_ixw, 1)])
                        kb.idma(self.s_sca[0], self.scr_moe[b][:],
                                IOA(ixw[:, 0, :], 0),
                                msw[:, 0, :], None,
                                bounds_check=self.rg_tok, oob_is_err=False,
                                reads=[(B_msw, 0), (B_ixw, 0)],
                                writes=[(self.B_scr_moe[b], e)])
                        kb.idma(self.s_sca[1], self.scr_moe2[b][:],
                                IOA(ixw[:, 1, :], 0),
                                msw[:, 1, :], None,
                                bounds_check=self.rg_tok, oob_is_err=False,
                                reads=[(B_msw, 1), (B_ixw, 1)],
                                writes=[(self.B_scr_moe2[b], e)])
        for lst in (fc1b, fc2b, Gs, mst, gthT[0], gthT[1]):
            for h_, bf in lst:
                AR.freez(h_)
        for h_ in (gtile, idxt, cgt, base_r, posf, tgti, tki, eqf, eqb,
                   relu_s, msw, ixw):
            AR.freez(h_)
        for h_, bf in HnT:
            AR.freez(h_)

    def p8(self, b):
        self.kb.tag = 'p8'
        kb, T, NT, AR = self.kb, self.T, self.NT, self.AR
        acct, B_acct = AR.alloc(f"acc8{b}", [128, 2, D], F32, units=2)
        mt, B_mt = AR.alloc(f"moe8{b}", [128, 2, D], F32, units=2)
        m2t, B_m2t = AR.alloc(f"moe8b{b}", [128, 2, D], F32, units=2)
        for i in range(NT):
            sl = i % 2
            kb.dma(self.s_ldx[sl], self.xstage[:, sl, :],
                   self.x_d[b, i * 128:(i + 1) * 128, :],
                   reads=[self.B_in], writes=[(self.B_xstage, sl)])
            kb.dma(self.s_acc[sl], acct[:, sl, :],
                   self.scr_acc[b][i * 128:(i + 1) * 128, :],
                   reads=[(self.B_scr_acc[b], i)], writes=[(B_acct, sl)])
            kb.dma(self.s_ldt[sl], mt[:, sl, :],
                   self.scr_moe[b][i * 128:(i + 1) * 128, :],
                   reads=[(self.B_scr_moe[b], list(range(NEXP)))],
                   writes=[(B_mt, sl)])
            kb.dma(self.s_stn2[sl], m2t[:, sl, :],
                   self.scr_moe2[b][i * 128:(i + 1) * 128, :],
                   reads=[(self.B_scr_moe2[b], list(range(NEXP)))],
                   writes=[(B_m2t, sl)])
            kb.op("dve", "tensor_add", acct[:, sl, :], acct[:, sl, :],
                  mt[:, sl, :], reads=[(B_acct, sl), (B_mt, sl)],
                  writes=[(B_acct, sl)])
            kb.op("dve", "tensor_add", acct[:, sl, :], acct[:, sl, :],
                  m2t[:, sl, :], reads=[(B_acct, sl), (B_m2t, sl)],
                  writes=[(B_acct, sl)])
            kb.op("dve", "tensor_add", self.sqs[:, sl, :], acct[:, sl, :],
                  self.xstage[:, sl, :],
                  reads=[(B_acct, sl), (self.B_xstage, sl)],
                  writes=[(self.B_sqs, sl)])
            kb.dma(self.s_out[sl], self.y_d[b, i * 128:(i + 1) * 128, :],
                   self.sqs[:, sl, :], reads=[(self.B_sqs, sl)], writes=[])
        AR.freez(acct)
        AR.freez(mt)
        AR.freez(m2t)

    def batch_front(self, b):
        if b == 0:
            self.p3(b)
        self.p4(b)
        self.p5(b)
        inj = self.p1_emitters(b + 1) if b + 1 < self.BPC else ()
        self.p6(b, inject=inj)


def _host_weights(inputs):
    wq = np.ascontiguousarray(inputs["L0_attn_w_q"], dtype=np.float32)
    wk = np.ascontiguousarray(inputs["L0_attn_w_k"], dtype=np.float32)
    wv = np.ascontiguousarray(inputs["L0_attn_w_v"], dtype=np.float32)
    wqb = np.zeros((8, 128, 128), np.float32)
    wkb = np.zeros((8, 128, 128), np.float32)
    for ic in range(8):
        par, m = ic // 4, ic % 4
        for jj in range(2):
            h = 4 * m + 2 * jj + par
            wqb[ic, jj * 64:(jj + 1) * 64, jj * 64:(jj + 1) * 64] = wq[:, :, h]
            wkb[ic, jj * 64:(jj + 1) * 64, jj * 64:(jj + 1) * 64] = wk[:, :, h]
    kk, dd, hh = np.indices((HD, HD, NH))
    rows_pm = ((hh % 2) * 512 + (hh // 2) * 64 + kk).ravel()
    cols_o = (dd * NH + hh).ravel()
    wv_hm = np.zeros((D, D), np.float32)
    wv_hm[rows_pm, cols_o] = wv.ravel()
    return {
        "wqb": wqb.reshape(1024, 128), "wkb": wkb.reshape(1024, 128),
        "wv_hm": wv_hm,
        "fwq": np.ascontiguousarray(inputs["L0_ffn_wq"], dtype=np.float32),
        "fwk": np.ascontiguousarray(inputs["L0_ffn_wk"], dtype=np.float32),
        "fwv": np.ascontiguousarray(inputs["L0_ffn_wv"], dtype=np.float32),
        "fwo": np.ascontiguousarray(inputs["L0_ffn_wo"], dtype=np.float32),
        "rw": np.ascontiguousarray(inputs["L0_ffn_router_w"], dtype=np.float32),
        "fc1": np.ascontiguousarray(inputs["L0_ffn_exp_fc1"], dtype=np.float32),
        "fc2": np.ascontiguousarray(inputs["L0_ffn_exp_fc2"], dtype=np.float32),
    }


def kernel(**inputs):
    x = np.ascontiguousarray(inputs["L0_attn_x"], dtype=np.float32)
    B, T, d = x.shape
    assert d == D and B % N_CORES == 0
    bpc = B // N_CORES
    wmap = _host_weights(inputs)
    prog = Prog(T=T, BPC=bpc)
    nc = prog.build()
    in_maps = []
    for c in range(N_CORES):
        m = {"x": x[c * bpc:(c + 1) * bpc]}
        m.update(wmap)
        in_maps.append(m)
    res = run_bass_kernel_spmd(nc, in_maps, core_ids=list(range(N_CORES)))
    return np.concatenate([r["y"] for r in res.results], axis=0)


# version 31
# speedup vs baseline: 1.0794x; 1.0020x over previous
import math
from contextlib import ExitStack

import numpy as np

import concourse.bass as bass
import concourse.mybir as mybir
from concourse.bass_utils import run_bass_kernel_spmd

F32 = mybir.dt.float32
BF16 = mybir.dt.bfloat16
AF = mybir.ActivationFunctionType
ALU = mybir.AluOpType
AX = mybir.AxisListType.X

D = 1024
NH = 16
HD = 64
NEXP = 8
EDIM = 2048
EPS = 1e-5
ROPE_BASE = 10000.0
N_CORES = 8
GAMMA1 = 1.0 / 32.0
GAMMA2 = 1.0 / 8.0


def _tok_add(d, tok):
    if tok is None:
        return
    k = id(tok[0])
    old = d.get(k)
    if old is None or old[1] < tok[1]:
        d[k] = tok


def _tok_merge(d, other):
    for t in other.values():
        _tok_add(d, t)


class Buf:

    def __init__(self, n=1, carry=None):
        self.w = [None] * n
        self.r = [dict() for _ in range(n)]
        self.carry = dict(carry) if carry else {}


def _units(spec):
    if isinstance(spec, Buf):
        return spec, [0]
    b, u = spec
    if isinstance(u, (list, tuple, range)):
        return b, list(u)
    return b, [u]


class Slot:

    def __init__(self, kb, name):
        self.sem = kb.es.enter_context(kb.nc.semaphore(name))
        self.rounds = 0
        self.tok = None


class KB:
    def __init__(self, nc, es):
        self.nc = nc
        self.es = es
        self.raw = {"pe": nc.tensor, "dve": nc.vector, "act": nc.scalar,
                    "pool": nc.gpsimd, "sp": nc.sync}
        self.sem = {}
        self.cnt = {}
        self.marks = {}
        for e in self.raw:
            self.sem[e] = es.enter_context(nc.semaphore("sem_" + e))
            self.cnt[e] = 0
            self.marks[e] = {}
        self.tag = "init"
        self.tagmap = {}

    def slot(self, name):
        return Slot(self, name)

    def _wait(self, eng, tok):
        if tok is None:
            return
        sem, val = tok
        wm = self.marks[eng]
        key = id(sem)
        if wm.get(key, -1) >= val:
            return
        self.raw[eng].wait_ge(sem, val)
        wm[key] = val

    def _deps(self, eng, reads, writes):
        for spec in reads:
            b, us = _units(spec)
            for u in us:
                if b.w[u] is None:
                    for t in b.carry.values():
                        self._wait(eng, t)
                self._wait(eng, b.w[u])
        for spec in writes:
            b, us = _units(spec)
            for u in us:
                if b.w[u] is None:
                    for t in b.carry.values():
                        self._wait(eng, t)
                self._wait(eng, b.w[u])
                for t in b.r[u].values():
                    self._wait(eng, t)

    def _record(self, tok, reads, writes):
        for spec in reads:
            b, us = _units(spec)
            for u in us:
                _tok_add(b.r[u], tok)
        for spec in writes:
            b, us = _units(spec)
            for u in us:
                b.w[u] = tok
                b.r[u] = {}

    def op(self, eng, method, *args, reads=(), writes=(), **kw):
        self._deps(eng, reads, writes)
        inst = getattr(self.raw[eng], method)(*args, **kw)
        self.cnt[eng] += 1
        tok = (self.sem[eng], self.cnt[eng])
        inst.then_inc(self.sem[eng], 1)
        self._record(tok, reads, writes)
        return tok

    def mmg(self, items, writes):
        self._deps("pe", [], writes)
        all_reads = []
        last = None
        for it in items:
            self._deps("pe", it.get("reads", ()), [])
            all_reads.extend(it.get("reads", ()))
            kw = {}
            if it.get("tile_position") is not None:
                kw["tile_position"] = it["tile_position"]
            last = self.raw["pe"].matmul(it["out"], it["lhsT"], it["rhs"],
                                         start=it["start"], stop=it["stop"], **kw)
            self.tagmap[last.ins.name] = self.tag
        self.cnt["pe"] += 1
        tok = (self.sem["pe"], self.cnt["pe"])
        last.then_inc(self.sem["pe"], 1)
        self._record(tok, all_reads, writes)
        return tok

    def idma(self, slot, out_ap, out_off, in_ap, in_off, reads=(), writes=(),
             **kw):
        self._deps("pool", reads, writes)
        self._wait("pool", slot.tok)
        inst = self.raw["pool"].indirect_dma_start(out_ap, out_off, in_ap,
                                                   in_off, **kw)
        inst.then_inc(slot.sem, 16)
        slot.rounds += 1
        tok = (slot.sem, 16 * slot.rounds)
        slot.tok = tok
        self._record(tok, reads, writes)
        return tok

    def dma(self, slot, out_ap, in_ap, reads=(), writes=(), eng="sp",
            transpose=False):
        self._deps(eng, reads, writes)
        self._wait(eng, slot.tok)
        if transpose:
            inst = self.raw[eng].dma_start_transpose(out_ap, in_ap)
        else:
            inst = self.raw[eng].dma_start(out_ap, in_ap)
        inst.then_inc(slot.sem, 16)
        slot.rounds += 1
        tok = (slot.sem, 16 * slot.rounds)
        slot.tok = tok
        self._record(tok, reads, writes)
        return tok


class Arena:

    def __init__(self, nc, base, cap):
        self.nc = nc
        self.base = base
        self.free_list = [(base, cap, {})]
        self.live = {}
        self.uid = 0

    def alloc(self, name, shape, dtype, units=1, carry=None):
        per_part = int(np.prod(shape[1:])) * mybir.dt.size(dtype)
        per_part = (per_part + 31) & ~31
        self.uid += 1
        nm = f"{name}_u{self.uid}"
        for i, (off, size, toks) in enumerate(self.free_list):
            if size >= per_part:
                h = self.nc.alloc_sbuf_tensor_at(nm, list(shape), dtype, offset=off)
                if size == per_part:
                    self.free_list.pop(i)
                else:
                    self.free_list[i] = (off + per_part, size - per_part,
                                         dict(toks))
                cd = dict(toks)
                if carry:
                    _tok_merge(cd, carry)
                buf = Buf(units, carry=cd)
                self.live[id(h)] = (off, per_part, buf)
                return h, buf
        raise MemoryError(f"arena OOM for {name}: need {per_part}B/p free="
                          f"{[(o, s) for o, s, _ in self.free_list]}")

    def freez(self, h):
        off, size, buf = self.live.pop(id(h))
        toks = dict(buf.carry)
        for t in buf.w:
            _tok_add(toks, t)
        for rl in buf.r:
            _tok_merge(toks, rl)
        self.free_list.append((off, size, toks))
        self.free_list.sort(key=lambda s: s[0])
        out = []
        for seg in self.free_list:
            if out and out[-1][0] + out[-1][1] == seg[0]:
                merged = dict(out[-1][2])
                _tok_merge(merged, seg[2])
                out[-1] = (out[-1][0], out[-1][1] + seg[1], merged)
            else:
                out.append((seg[0], seg[1], dict(seg[2])))
        self.free_list = out
        return toks


class Prog:
    def __init__(self, T=2048, BPC=2, debug=False):
        self.T = T
        self.BPC = BPC
        self.debug = debug
        self.NT = T // 128
        self.nc = bass.Bass("TRN2", target_bir_lowering=False)

    def sb(self, name, shape, dtype):
        return self.es.enter_context(self.nc.sbuf_tensor(name, list(shape), dtype))

    def PB(self, i, lo, hi, p0=0, p1=128):
        h, bf = self.pb[i]
        return h[p0:p1, lo:hi], (bf, list(range(lo // 512, (hi + 511) // 512)))

    def build(self):
        nc, T, NT, BPC = self.nc, self.T, self.NT, self.BPC
        self.TQ = min(512, T)
        self.NTQ = T // self.TQ

        self.x_d = nc.dram_tensor("x", [BPC, T, D], F32, kind="ExternalInput").ap()
        wnames = ["wv_hm", "fwq", "fwk", "fwv", "fwo"]
        self.w_d = {n: nc.dram_tensor(n, [D, D], F32, kind="ExternalInput").ap()
                    for n in wnames}
        for n in ["wqb", "wkb"]:
            self.w_d[n] = nc.dram_tensor(n, [D, 128], F32,
                                         kind="ExternalInput").ap()
        self.rw_d = nc.dram_tensor("rw", [D, NEXP], F32, kind="ExternalInput").ap()
        self.fc1_d = nc.dram_tensor("fc1", [NEXP, D, EDIM], F32, kind="ExternalInput").ap()
        self.fc2_d = nc.dram_tensor("fc2", [NEXP, EDIM, D], F32, kind="ExternalInput").ap()
        self.y_d = nc.dram_tensor("y", [BPC, T, D], F32, kind="ExternalOutput").ap()

        self.scr_xn = [nc.dram_tensor(f"scr_xn{b}", [T, D], BF16).ap() for b in range(BPC)]
        self.scr_hn = [nc.dram_tensor(f"scr_hn{b}", [T, D], BF16).ap() for b in range(BPC)]
        self.scr_Hn = [nc.dram_tensor(f"scr_Hn{b}", [T + 1, D], BF16).ap() for b in range(BPC)]
        self.scr_cw = [nc.dram_tensor(f"scr_cw{b}", [T + 1, NEXP], F32).ap() for b in range(BPC)]
        self.scr_idx = [nc.dram_tensor(f"scr_idx{b}", [max(1, T // 1024) * 8 * 384], mybir.dt.int32).ap() for b in range(BPC)]
        self.scr_gth = [nc.dram_tensor(f"scr_gth{s}", [384, D], BF16).ap() for s in range(2)]
        self.scr_acc = [nc.dram_tensor(f"scr_acc{b}", [T, D], F32).ap() for b in range(BPC)]
        self.scr_moe = [nc.dram_tensor(f"scr_moe{b}", [T + 1, D], F32).ap() for b in range(BPC)]
        self.scr_moe2 = [nc.dram_tensor(f"scr_moe2_{b}", [T + 1, D], F32).ap() for b in range(BPC)]
        self.scr_w = {n: nc.dram_tensor("scrb_" + n, [D, D], BF16).ap() for n in wnames}
        for n in ["wqb", "wkb"]:
            self.scr_w[n] = nc.dram_tensor("scrb_" + n, [D, 128], BF16).ap()
        self.scr_rw = nc.dram_tensor("scrb_rw", [D, NEXP], BF16).ap()
        self.scr_fc1 = nc.dram_tensor("scrb_fc1", [NEXP, D, EDIM], BF16).ap()
        self.scr_fc2 = nc.dram_tensor("scrb_fc2", [NEXP, EDIM, D], BF16).ap()
        if self.debug:
            self.dbg_y = nc.dram_tensor("dbg_y", [BPC, T, D], F32, kind="ExternalOutput").ap()
            self.dbg_h = nc.dram_tensor("dbg_h", [BPC, T, D], F32, kind="ExternalOutput").ap()
            self.dbg_c = nc.dram_tensor("dbg_c", [BPC, T, NEXP], F32, kind="ExternalOutput").ap()
            self.dbg_moe = nc.dram_tensor("dbg_moe", [BPC, T, D], F32, kind="ExternalOutput").ap()

        self.es = ExitStack()
        es = self.es
        kb = KB(nc, es)
        self.kb = kb
        self._pre3 = None

        self.xstage = self.sb("xstage", [128, 2, D], F32)
        self.nstage = self.sb("nstage", [128, 2, D], BF16)
        self.n2stage = self.sb("n2stage", [128, 2, D], BF16)
        self.sqs = self.sb("sqs", [128, 2, D], F32)
        self.st_m = [self.sb(f"st_m{b}", [128, NT], F32) for b in range(BPC)]
        self.st_rstd = [self.sb(f"st_rstd{b}", [128, NT], F32) for b in range(BPC)]
        self.st_b1 = [self.sb(f"st_b1{b}", [128, NT], F32) for b in range(BPC)]
        self.st_s2 = [self.sb(f"st_s2{b}", [128, NT], F32) for b in range(BPC)]
        self.st_s3 = [self.sb(f"st_s3{b}", [128, NT], F32) for b in range(BPC)]
        self.st_sq = [self.sb(f"st_sq{b}", [128, NT], F32) for b in range(BPC)]
        self.st_rs2 = [self.sb(f"st_rs2{b}", [128, NT], F32) for b in range(BPC)]
        self.st_bs2 = [self.sb(f"st_bs2{b}", [128, NT], F32) for b in range(BPC)]
        self.st_tmp = self.sb("st_tmp", [128, 2 * NT], F32)
        self.st_t = self.sb("st_t", [128, 2, 8], F32)
        self.cw = [self.sb(f"cw{b}", [128, NT, NEXP], F32) for b in range(BPC)]
        self.lgt = self.sb("lgt", [128, 2, NEXP], F32)
        self.lgt2 = self.sb("lgt2", [128, 2, NEXP], F32)
        self.lgt3 = self.sb("lgt3", [128, 2, NEXP], F32)
        self.epsv = self.sb("epsv", [128, 1], F32)
        self.pi2 = self.sb("pi2", [128, 1], F32)
        self.ones64 = self.sb("ones64", [1, 64], F32)
        self.iotv = self.sb("iotv", [128, 4], F32)
        self.invf = self.sb("invf", [128, 1], F32)
        self.rrow = self.sb("rrow", [1, 2, self.TQ], F32)
        self.rinv = self.sb("rinv", [1, 2, self.TQ], F32)
        self.ysc = self.sb("ysc", [64, 2, self.TQ], F32)
        self.rwb = self.sb("rwb", [128, 8, NEXP], BF16)
        self.trimask = self.sb("trimask", [128, 128], BF16)
        self.ustrict = self.sb("ustrict", [128, 128], BF16)
        self.erow = self.sb("erow", [128, NEXP], F32)
        self.ones128f = self.sb("ones128f", [1, 128], F32)
        self.onescol = self.sb("onescol", [128, 1], BF16)
        self.senti = self.sb("senti", [128, 24], mybir.dt.int32)
        self.zrow = self.sb("zrow", [128, D], F32)

        self.B_xstage = Buf(2)
        self.B_nstage = Buf(2)
        self.B_n2stage = Buf(2)
        self.B_sqs = Buf(2)
        self.B_stats = [Buf(NT) for _ in range(BPC)]
        self.B_stsq = [Buf(NT) for _ in range(BPC)]
        self.B_tmp = Buf(1)
        self.B_tbl = Buf(1)
        self.B_s3 = [Buf(NT) for _ in range(BPC)]
        self.B_stt = Buf(2)
        self.B_cw = [Buf(NT) for _ in range(BPC)]
        self.B_lgt = Buf(2)
        self.B_lgt2 = Buf(2)
        self.B_lgt3 = Buf(2)
        self.B_const = Buf(1)
        self.B_rope = Buf(2)
        self.B_rwb = Buf(1)
        self.B_in = Buf(1)
        self.B_scr_xn = [Buf(NT) for _ in range(BPC)]
        self.B_scr_hn = [Buf(NT) for _ in range(BPC)]
        self.B_scr_Hn = [Buf(NT + 1) for _ in range(BPC)]
        self.B_scr_acc = [Buf(NT) for _ in range(BPC)]
        self.B_scr_moe = [Buf(NEXP) for _ in range(BPC)]
        self.B_scr_moe2 = [Buf(NEXP) for _ in range(BPC)]
        self.B_scrg = [Buf(1), Buf(1)]
        self.B_scr_cw = [Buf(NT + 1) for _ in range(BPC)]
        self.B_scr_idx = [Buf(max(1, T // 1024)) for _ in range(BPC)]
        self.B_scr_w = {n: Buf(8) for n in wnames}
        self.B_scr_w["wqb"] = Buf(8)
        self.B_scr_w["wkb"] = Buf(8)
        self.B_scr_rw8 = Buf(8)
        self.B_scr_fc1 = Buf(NEXP * 8)
        self.B_scr_fc2 = Buf(NEXP * 16)

        self.s_ldx = [kb.slot("s_ldx0"), kb.slot("s_ldx1")]
        self.s_ldxp = [kb.slot("s_ldxp0"), kb.slot("s_ldxp1")]
        self.s_stnp = [kb.slot("s_stnp0"), kb.slot("s_stnp1")]
        self.s_moep = [kb.slot("s_moep0"), kb.slot("s_moep1")]
        self.s_moev = [kb.slot("s_moev0"), kb.slot("s_moev1")]
        self.s_stn = [kb.slot("s_stn0"), kb.slot("s_stn1")]
        self.s_stn2 = [kb.slot("s_stn2a"), kb.slot("s_stn2b")]
        self.s_stn3 = [kb.slot("s_stn3a"), kb.slot("s_stn3b")]
        self.s_ldt = [kb.slot(f"s_ldt{i}") for i in range(4)]
        self.s_ldw = [kb.slot("s_ldw0"), kb.slot("s_ldw1")]
        self.s_stw = [kb.slot("s_stw0"), kb.slot("s_stw1")]
        self.s_acc = [kb.slot("s_acc0"), kb.slot("s_acc1")]
        self.s_acc2 = [kb.slot("s_acc2a"), kb.slot("s_acc2b")]
        self.s_out = [kb.slot("s_out0"), kb.slot("s_out1")]
        self.s_tab = kb.slot("s_tab")
        self.s_fc1 = [kb.slot(f"s_fc1_{i}") for i in range(4)]
        self.s_fc2 = [kb.slot(f"s_fc2_{i}") for i in range(4)]
        self.s_pc = [kb.slot(f"s_pc_{i}") for i in range(9)]
        self.s_ix = [kb.slot(f"s_ix_{i}") for i in range(2)]
        self.s_gg = [kb.slot(f"s_gg_{i}") for i in range(3)]
        self.s_sca = [kb.slot(f"s_sca_{i}") for i in range(2)]
        self.s_zz = kb.slot("s_zz")
        self.s_dbg = kb.slot("s_dbg")

        self.rg_idx = nc.gpsimd.alloc_register("bc_idx")
        nc.gpsimd.reg_mov(self.rg_idx, max(1, self.T // 1024) * 3072 - 1)
        self.rg_tok = nc.gpsimd.alloc_register("bc_tok")
        nc.gpsimd.reg_mov(self.rg_tok, self.T)

        self.pb = []
        for i in range(4):
            h = es.enter_context(nc.psum_tensor(f"pb{i}", [128, 1024], F32))
            self.pb.append((h, Buf(2)))

        static_end = nc.SBUF_PARTITION_SIZE_BYTES - nc.sbuf_bytes_remaining
        self.AR = Arena(nc, static_end + 64, nc.sbuf_top - static_end - 128)

        with es:
            self.p0_consts()
            self.p0_weights()
            self.p0_weights_fc()
            self.p1(0)
            for b in range(self.BPC):
                self.batch_front(b)
                self.p7(b)
                if b + 1 < self.BPC:
                    self.p3(b + 1)
                self.p8(b)
            for sl in self.s_out:
                if sl.tok is not None:
                    kb.raw["sp"].wait_ge(sl.sem, 16 * sl.rounds)
        return nc

    def p0_consts(self):
        self.kb.tag = 'p0t'
        kb = self.kb
        kb.op("pool", "memset", self.epsv[:], EPS, writes=[self.B_const])
        kb.op("pool", "memset", self.pi2[:], float(np.pi / 2), writes=[self.B_const])
        kb.op("pool", "memset", self.ones64[:], 1.0, writes=[self.B_const])
        kb.op("pool", "memset", self.trimask[:], 1.0, writes=[self.B_const])
        kb.op("pool", "affine_select", self.trimask[:], self.trimask[:],
              [[1, 128]], ALU.is_ge, 0.0, base=0, channel_multiplier=-1,
              writes=[self.B_const])
        kb.op("pool", "memset", self.ustrict[:], 1.0, writes=[self.B_const])
        kb.op("pool", "affine_select", self.ustrict[:], self.ustrict[:],
              [[1, 128]], ALU.is_ge, 0.0, base=-1, channel_multiplier=-1,
              writes=[self.B_const])
        kb.op("pool", "iota", self.erow[:], [[1, NEXP]], base=0,
              channel_multiplier=0, allow_small_or_imprecise_dtypes=True,
              writes=[self.B_const])
        kb.op("pool", "memset", self.ones128f[:], 1.0, writes=[self.B_const])
        kb.op("pool", "memset", self.onescol[:], 1.0, writes=[self.B_const])
        kb.op("pool", "memset", self.senti[:], self.T, writes=[self.B_const])
        kb.op("pool", "memset", self.zrow[:], 0.0, writes=[self.B_const])

    def build_tables_sbuf(self, cp, tag):
        self.kb.tag = 'p0t'
        kb, T, AR = self.kb, self.T, self.AR
        BT = self.B_tbl
        tabc, B_tc = AR.alloc(f"tabc{tag}", [128, T], F32)
        tabs, B_ts = AR.alloc(f"tabs{tag}", [128, T], F32)
        if T <= 2 * D:
            scr = bass.AP(self.xstage, 0, [[2 * D, 128], [1, D]])
            B_sc = (self.B_xstage, 0)
        else:
            scr, B_sc = AR.alloc(f"tabscr{tag}", [128, T], F32)
        kb.op("pool", "iota", self.iotv[:, 0:1], [[1, 1]], base=2 * cp,
              channel_multiplier=8, allow_small_or_imprecise_dtypes=True,
              writes=[BT])
        kb.op("pool", "memset", self.iotv[:, 1:2], 1.0, writes=[BT])
        kb.op("pool", "affine_select", self.iotv[:, 1:2], self.iotv[:, 1:2],
              [[1, 1]], ALU.is_ge, 0.0, base=-64, channel_multiplier=1,
              writes=[BT])
        kb.op("dve", "scalar_tensor_tensor", self.iotv[:, 2:3],
              self.iotv[:, 1:2], -511.0, self.iotv[:, 0:1], ALU.mult, ALU.add,
              reads=[BT], writes=[BT])
        kb.op("dve", "tensor_scalar_mul", self.iotv[:, 3:4], self.iotv[:, 2:3],
              -math.log(ROPE_BASE) / 512.0, reads=[BT], writes=[BT])
        kb.op("act", "activation", self.invf[:], self.iotv[:, 3:4], AF.Exp,
              bias=0.0, scale=1.0, reads=[BT], writes=[BT])
        kb.op("pool", "memset", tabc[:, 0:1], 1.0, writes=[B_tc])
        kb.op("pool", "memset", tabs[:, 0:1], 0.0, writes=[B_ts])
        if T > 1:
            kb.op("act", "activation", tabs[:, 1:2], self.invf[:], AF.Sin,
                  bias=0.0, scale=1.0, reads=[BT], writes=[B_ts])
            kb.op("act", "activation", tabc[:, 1:2], self.invf[:], AF.Sin,
                  bias=self.pi2[:], scale=1.0, reads=[BT, self.B_const],
                  writes=[B_tc])
        s = 2
        while s < T:
            ext = s - 1
            w = min(s - 1, T - s)
            ce = tabc[:, ext:ext + 1]
            se = tabs[:, ext:ext + 1]
            kb.op("dve", "tensor_scalar_mul", scr[:, 0:w], tabs[:, 1:1 + w],
                  se, reads=[B_ts], writes=[B_sc])
            kb.op("dve", "scalar_tensor_tensor", tabc[:, s:s + w],
                  tabc[:, 1:1 + w], ce, scr[:, 0:w], ALU.mult, ALU.subtract,
                  reads=[B_sc, B_tc], writes=[B_tc])
            kb.op("dve", "tensor_scalar_mul", scr[:, 0:w], tabc[:, 1:1 + w],
                  se, reads=[B_tc], writes=[B_sc])
            kb.op("dve", "scalar_tensor_tensor", tabs[:, s:s + w],
                  tabs[:, 1:1 + w], ce, scr[:, 0:w], ALU.mult, ALU.add,
                  reads=[B_sc, B_ts], writes=[B_ts])
            s = s + w
        if T > 2 * D:
            AR.freez(scr)
        return tabc, B_tc, tabs, B_ts

    def p0_weights(self):
        kb = self.kb
        semB = self.s_pc[8]

        def cast_dma(slot, src_ap, dst_ap, rows, cols):
            n = 0
            for r0 in range(0, rows, 256):
                rr = min(256, rows - r0)
                inst = kb.raw["pool"].dma_start(dst_ap[r0:r0 + rr, 0:cols],
                                                src_ap[r0:r0 + rr, 0:cols])
                inst.then_inc(slot.sem, 16)
                slot.rounds += 1
                n += 1
            return n

        for wi, n_ in enumerate(["wqb", "wkb", "wv_hm", "fwq", "fwk",
                                 "fwv", "fwo"]):
            sl_ = self.s_pc[wi]
            cast_dma(sl_, self.w_d[n_], self.scr_w[n_], D,
                     128 if n_ in ("wqb", "wkb") else D)
            tokn = (sl_.sem, 16 * sl_.rounds)
            sl_.tok = tokn
            bb = self.B_scr_w[n_]
            for u in range(8):
                bb.w[u] = tokn
        cast_dma(self.s_pc[7], self.rw_d, self.scr_rw, D, NEXP)
        tokA = (self.s_pc[7].sem, 16 * self.s_pc[7].rounds)
        self.s_pc[7].tok = tokA
        for u in range(8):
            self.B_scr_rw8.w[u] = tokA

    def p0_weights_fc(self):
        kb = self.kb
        semB = self.s_pc[8]
        self.kb._deps("pool", [(self.B_scr_hn[0], self.NT - 1)], [])

        def cast_dma(slot, src_ap, dst_ap, rows, cols):
            for r0 in range(0, rows, 256):
                rr = min(256, rows - r0)
                inst = kb.raw["pool"].dma_start(dst_ap[r0:r0 + rr, 0:cols],
                                                src_ap[r0:r0 + rr, 0:cols])
                inst.then_inc(slot.sem, 16)
                slot.rounds += 1

        for e in range(NEXP):
            cast_dma(semB, self.fc1_d[e], self.scr_fc1[e], D, EDIM)
            cast_dma(semB, self.fc2_d[e], self.scr_fc2[e], EDIM, D)
        tokB = (semB.sem, 16 * semB.rounds)
        semB.tok = tokB
        for u in range(NEXP * 8):
            self.B_scr_fc1.w[u] = tokB
        for u in range(NEXP * 16):
            self.B_scr_fc2.w[u] = tokB

    def p1(self, b):
        for f in self.p1_emitters(b):
            f()

    def p1_emitters(self, b):
        NT = self.NT
        alt = b == 0

        def loadx(i):
            self.kb.tag = 'p1'
            sl = i % 2
            if alt and (i // 2) % 2 == 1:
                self.kb.dma(self.s_ldxp[sl], self.xstage[:, sl, :],
                            self.x_d[b, i * 128:(i + 1) * 128, :],
                            reads=[self.B_in], writes=[(self.B_xstage, sl)],
                            eng="act")
            else:
                self.kb.dma(self.s_ldx[sl], self.xstage[:, sl, :],
                            self.x_d[b, i * 128:(i + 1) * 128, :],
                            reads=[self.B_in], writes=[(self.B_xstage, sl)])

        def statsA(i):
            self.kb.tag = 'p1'
            kb = self.kb
            sl = i % 2
            xt = self.xstage[:, sl, :]
            kb.op("dve", "reduce_sum", self.st_m[b][:, i:i + 1], xt, axis=AX,
                  reads=[(self.B_xstage, sl)], writes=[(self.B_stats[b], i)])
            kb.op("act", "activation", self.sqs[:, sl, :], xt, AF.Square,
                  bias=0.0, scale=1.0, accum_out=self.st_sq[b][:, i:i + 1],
                  reads=[(self.B_xstage, sl)],
                  writes=[(self.B_sqs, sl), (self.B_stsq[b], i)])

        def batch_stats():
            self.kb.tag = 'p1'
            kb = self.kb
            allu = list(range(NT))
            R = [(self.B_stats[b], allu)]
            W = [(self.B_stats[b], allu)]
            RS = [(self.B_stsq[b], allu)]
            BT = self.B_tmp
            tmp0 = self.st_tmp[:, 0:NT]
            tmp1 = self.st_tmp[:, NT:2 * NT]
            m = self.st_m[b][:, 0:NT]
            rstd = self.st_rstd[b][:, 0:NT]
            b1 = self.st_b1[b][:, 0:NT]
            s2 = self.st_s2[b][:, 0:NT]
            kb.op("dve", "tensor_scalar_mul", m, m, 1.0 / D, reads=R, writes=W)
            kb.op("dve", "tensor_mul", tmp0, m, m, reads=R, writes=[BT])
            kb.op("dve", "scalar_tensor_tensor", tmp1,
                  self.st_sq[b][:, 0:NT], 1.0 / D, tmp0, ALU.mult,
                  ALU.subtract, reads=RS + [BT], writes=[BT])
            kb.op("act", "activation", tmp0, tmp1, AF.Ln, bias=self.epsv[:],
                  scale=1.0, reads=[BT, self.B_const], writes=[BT])
            kb.op("act", "activation", rstd, tmp0, AF.Exp, bias=0.0,
                  scale=-0.5, reads=[BT], writes=W)
            kb.op("dve", "scalar_tensor_tensor", b1, m, -1.0, rstd,
                  ALU.mult, ALU.mult, reads=R, writes=W)
            kb.op("dve", "tensor_mul", tmp1, tmp1, rstd, reads=[BT] + R,
                  writes=[BT])
            kb.op("dve", "tensor_mul", tmp1, tmp1, rstd, reads=[BT] + R,
                  writes=[BT])
            kb.op("act", "activation", tmp0, tmp1, AF.Ln, bias=self.epsv[:],
                  scale=1.0, reads=[BT, self.B_const], writes=[BT])
            kb.op("act", "activation", s2, tmp0, AF.Exp, bias=0.0, scale=-0.5,
                  reads=[BT], writes=W)
            kb.op("dve", "tensor_mul", self.st_rs2[b][:, 0:NT], rstd, s2,
                  reads=R, writes=W)
            kb.op("dve", "tensor_mul", self.st_bs2[b][:, 0:NT], b1, s2,
                  reads=R, writes=W)

        def compB(i):
            self.kb.tag = 'p1'
            kb = self.kb
            sl = i % 2
            xt = self.xstage[:, sl, :]
            hm_out = bass.AP(self.nstage, sl * D,
                             [[2 * D, 128], [1, 64], [64, 8], [512, 2]])
            kb.op("act", "activation", hm_out, xt, AF.Identity,
                  bias=self.st_b1[b][:, i:i + 1],
                  scale=self.st_rstd[b][:, i:i + 1],
                  reads=[(self.B_xstage, sl), (self.B_stats[b], i)],
                  writes=[(self.B_nstage, sl)])
            kb.op("act", "activation", self.n2stage[:, sl, :], xt, AF.Identity,
                  bias=self.st_bs2[b][:, i:i + 1],
                  scale=self.st_rs2[b][:, i:i + 1],
                  reads=[(self.B_xstage, sl), (self.B_stats[b], i)],
                  writes=[(self.B_n2stage, sl)])
            kb.dma(self.s_stn[sl], self.scr_xn[b][i * 128:(i + 1) * 128, :],
                   self.nstage[:, sl, :], reads=[(self.B_nstage, sl)],
                   writes=[(self.B_scr_xn[b], i)])
            kb.dma(self.s_stnp[sl] if alt else self.s_stn2[sl],
                   self.scr_hn[b][i * 128:(i + 1) * 128, :],
                   self.n2stage[:, sl, :], reads=[(self.B_n2stage, sl)],
                   writes=[(self.B_scr_hn[b], i)],
                   eng="act" if alt else "sp")

        sched = [lambda: (loadx(0), loadx(1))]
        for p in range(NT // 2):
            def emA(p=p):
                statsA(2 * p)
                if 2 * p + 2 < NT:
                    loadx(2 * p + 2)
                statsA(2 * p + 1)
                if 2 * p + 3 < NT:
                    loadx(2 * p + 3)
            sched.append(emA)
        sched.append(lambda: (batch_stats(), loadx(0), loadx(1)))
        for p in range(NT // 2):
            def emB(p=p):
                compB(2 * p)
                if 2 * p + 2 < NT:
                    loadx(2 * p + 2)
                compB(2 * p + 1)
                if 2 * p + 3 < NT:
                    loadx(2 * p + 3)
            sched.append(emB)
        return sched

    def load_T_strips(self, scr_ap, B_src, name, carryfrom=None):
        T = self.T
        chunks = []
        for c in range(8):
            h, bf = self.AR.alloc(f"{name}{c}", [128, T], BF16, carry=carryfrom)
            self.kb.dma(self.s_ldt[c % 4], h[:],
                        scr_ap[0:T, c * 128:(c + 1) * 128],
                        reads=[(B_src, list(range(self.NT)))], writes=[bf],
                        transpose=True)
            chunks.append((h, bf))
        return chunks

    def p3(self, b):
        self.kb.tag = 'p3'
        kb, T, NT, AR = self.kb, self.T, self.NT, self.AR
        TH = min(1024, T)
        NTH = T // TH
        if self._pre3 is not None:
            xnT, _ = self._pre3
            self._pre3 = None
        else:
            xnT = self.load_T_strips(self.scr_xn[b], self.B_scr_xn[b],
                                     f"xnT{b}")
        qkwh, B_qkw = AR.alloc(f"qkw{b}", [128, 16, 128], BF16, units=16)
        qkw = (qkwh, B_qkw)
        for c in range(8):
            kb.dma(self.s_ldw[c % 2], qkw[0][:, c, :],
                   self.scr_w["wqb"][c * 128:(c + 1) * 128, :],
                   reads=[(self.B_scr_w["wqb"], c)], writes=[(qkw[1], c)])
            kb.dma(self.s_ldw[c % 2], qkw[0][:, 8 + c, :],
                   self.scr_w["wkb"][c * 128:(c + 1) * 128, :],
                   reads=[(self.B_scr_w["wkb"], c)],
                   writes=[(qkw[1], 8 + c)])
        self.xnT = xnT
        dstQ = [AR.alloc(f"Qh{b}_{c}", [128, T], BF16) for c in range(8)]
        dstK = [AR.alloc(f"Kh{b}_{c}", [128, T], BF16) for c in range(8)]
        QK = (dstQ, dstK)
        wbuf = [AR.alloc(f"wb{b}_{c}", [128, D], BF16) for c in range(8)]
        for c in range(8):
            kb.dma(self.s_ldw[c % 2], wbuf[c][0][:],
                   self.scr_w["wv_hm"][c * 128:(c + 1) * 128, :],
                   reads=[(self.B_scr_w["wv_hm"], c)], writes=[wbuf[c][1]])
        Ve = [None] * NT
        self.Ve = Ve

        def emit_v(i):
            self.kb.tag = 'p3v'
            Ve[i] = AR.alloc(f"Ve{b}_{i}", [128, 1040], BF16)
            pV, bV = self.PB(2 + i % 2, 0, 1024)
            items = []
            for gh in range(2):
                for fc in range(8):
                    items.append(dict(
                        out=pV[:, gh * 512:(gh + 1) * 512],
                        lhsT=xnT[fc][0][:, i * 128:(i + 1) * 128],
                        rhs=wbuf[fc][0][:, gh * 512:(gh + 1) * 512],
                        reads=[xnT[fc][1], wbuf[fc][1]],
                        start=(fc == 0), stop=(fc == 7)))
            kb.mmg(items, writes=[bV])
            kb.op("act", "activation", Ve[i][0][:, 0:1024], pV,
                  AF.Copy, bias=0.0, scale=1.0,
                  reads=[bV], writes=[Ve[i][1]])
            kb.op("pool", "memset", Ve[i][0][:, 1024:1025], 1.0,
                  writes=[Ve[i][1]])

        for cp in range(4):
            tabc, B_tc, tabs, B_ts = self.build_tables_sbuf(cp, f"{b}_{cp}")
            self.kb.tag = 'p3'
            for wi in range(2):
                dst = QK[wi]
                for th in range(NTH):
                    tc = tabc[:, th * TH:(th + 1) * TH]
                    ts = tabs[:, th * TH:(th + 1) * TH]
                    pA, bA = self.PB(0, 0, TH)
                    pB, bB = self.PB(1, 0, TH)
                    for gi, (ps_, bs_) in enumerate(((pA, bA), (pB, bB))):
                        c = cp + 4 * gi
                        items = []
                        for q in range(max(1, TH // 512)):
                            w = min(512, TH)
                            items.append(dict(
                                out=ps_[:, q * 512:q * 512 + w],
                                lhsT=qkw[0][:, 8 * wi + c, :],
                                rhs=xnT[c][0][:, th * TH + q * 512:
                                              th * TH + q * 512 + w],
                                reads=[(qkw[1], 8 * wi + c), xnT[c][1]],
                                start=True, stop=True))
                        kb.mmg(items, writes=[bs_])
                    s1 = self.sqs[:, 0, 0:TH]
                    s2 = self.sqs[:, 1, 0:TH]
                    kb.op("dve", "tensor_mul", s1, pA, tc,
                          reads=[bA, B_tc], writes=[(self.B_sqs, 0)])
                    kb.op("dve", "tensor_mul", s2, pB, ts,
                          reads=[bB, B_ts], writes=[(self.B_sqs, 1)])
                    kb.op("pool", "tensor_sub",
                          dst[cp][0][:, th * TH:(th + 1) * TH], s1, s2,
                          reads=[(self.B_sqs, 0), (self.B_sqs, 1)],
                          writes=[dst[cp][1]])
                    kb.op("dve", "tensor_mul", s1, pA, ts,
                          reads=[bA, B_ts], writes=[(self.B_sqs, 0)])
                    kb.op("dve", "tensor_mul", s2, pB, tc,
                          reads=[bB, B_tc], writes=[(self.B_sqs, 1)])
                    kb.op("pool", "tensor_add",
                          dst[cp + 4][0][:, th * TH:(th + 1) * TH], s1, s2,
                          reads=[(self.B_sqs, 0), (self.B_sqs, 1)],
                          writes=[dst[cp + 4][1]])
            AR.freez(tabc)
            AR.freez(tabs)
            for i in range(cp * NT // 4, (cp + 1) * NT // 4):
                emit_v(i)
        self.Qh, self.Kh = QK
        AR.freez(qkw[0])
        for h, bf in wbuf:
            AR.freez(h)
        for h, bf in xnT:
            AR.freez(h)
        self.xnT = None

    def p4(self, b):
        self.kb.tag = 'p4'
        kb, T, NT, AR = self.kb, self.T, self.NT, self.AR
        TQ, NTQ = self.TQ, self.NTQ
        Qh, Kh, Ve = self.Qh, self.Kh, self.Ve
        P = [AR.alloc(f"P{b}_{k}", [128, TQ], BF16) for k in range(NT)]
        self._pre5 = self.load_T_strips(self.scr_hn[b], self.B_scr_hn[b],
                                        f"hnT{b}")
        if True:
            KTQ = TQ // 128
            for qb in range(NTQ):
                nkt = KTQ * (qb + 1)
                self.kb.tag = 'p4s'
                for kt in range(nkt):
                    off = max(0, kt * 128 - qb * TQ)
                    w = TQ - off
                    pp = kt % 2
                    pS, bS = self.PB(pp, 512, 1024)
                    items = []
                    for gc in range(8):
                        items.append(dict(
                            out=pS[:, off:off + w],
                            lhsT=Kh[gc][0][:, kt * 128:(kt + 1) * 128],
                            rhs=Qh[gc][0][:, qb * TQ + off:qb * TQ + off + w],
                            reads=[Kh[gc][1], Qh[gc][1]],
                            start=(gc == 0), stop=(gc == 7)))
                    kb.mmg(items, writes=[bS])
                    kb.op("act", "activation", P[kt][0][:, off:off + w],
                          pS[:, off:off + w], AF.Exp, bias=0.0, scale=GAMMA1,
                          reads=[bS], writes=[P[kt][1]])
                    if kt * 128 >= qb * TQ:
                        kb.op("dve", "tensor_mul", P[kt][0][:, off:off + 128],
                              P[kt][0][:, off:off + 128], self.trimask[:],
                              reads=[P[kt][1], self.B_const], writes=[P[kt][1]])
                self.kb.tag = 'p4av'
                for qi in range(qb * TQ // 128, (qb + 1) * TQ // 128):
                    pp = qi % 2
                    pY, bY = self.PB(2 + pp, 0, 1024)
                    pR, bR = self.PB(pp, 0, 16)
                    items = []
                    for kt in range(qi + 1):
                        off = qi * 128 - qb * TQ
                        stat = P[kt][0][:, off:off + 128]
                        for gh in range(2):
                            items.append(dict(
                                out=pY[:, gh * 512:(gh + 1) * 512],
                                lhsT=stat,
                                rhs=Ve[kt][0][:, gh * 512:(gh + 1) * 512],
                                reads=[P[kt][1], Ve[kt][1]],
                                start=(kt == 0), stop=(kt == qi)))
                        items.append(dict(
                            out=pR[:, 0:1], lhsT=stat,
                            rhs=Ve[kt][0][:, 1024:1025],
                            reads=[P[kt][1], Ve[kt][1]],
                            start=(kt == 0), stop=(kt == qi)))
                    kb.mmg(items, writes=[bY, bR])
                    kb.op("dve", "reciprocal", self.st_t[:, pp, 0:1], pR[:, 0:1],
                          reads=[bR], writes=[(self.B_stt, pp)])
                    ot = self.sqs[:, pp, :]
                    kb.op("act", "activation", ot, pY, AF.Identity,
                          bias=0.0, scale=self.st_t[:, pp, 0:1],
                          reads=[bY, (self.B_stt, pp)],
                          writes=[(self.B_sqs, pp)])
                    kb.dma(self.s_acc[pp], self.scr_acc[b][qi * 128:(qi + 1) * 128, :],
                           ot, reads=[(self.B_sqs, pp)],
                           writes=[(self.B_scr_acc[b], qi)])
                    if self.debug:
                        kb.dma(self.s_dbg, self.dbg_y[b, qi * 128:(qi + 1) * 128, :],
                               ot, reads=[(self.B_sqs, pp)], writes=[])
        for h, bf in P:
            AR.freez(h)
        for lst in (self.Qh, self.Kh, self.Ve):
            for h, bf in lst:
                AR.freez(h)
        self.Qh = self.Kh = self.Ve = None

    def p5(self, b):
        self.kb.tag = 'p5'
        kb, T, NT, AR = self.kb, self.T, self.NT, self.AR
        if getattr(self, "_pre5", None) is not None:
            hnT = self._pre5
            self._pre5 = None
        else:
            hnT = self.load_T_strips(self.scr_hn[b], self.B_scr_hn[b],
                                     f"hnT{b}")
        wsets = [[AR.alloc(f"wf{b}_{s}_{c}", [128, D], BF16) for c in range(8)]
                 for s in range(2)]
        TH = min(1024, T)
        NTH = T // TH
        QK = []
        for wi, (wname, oname) in enumerate(
                (("fwq", f"Q2T{b}"), ("fwk", f"K2T{b}"))):
            wbuf = wsets[wi % 2]
            dst = [AR.alloc(f"{oname}_{c}", [128, T], BF16) for c in range(8)]
            for c in range(8):
                kb.dma(self.s_ldw[c % 2], wbuf[c][0][:],
                       self.scr_w[wname][c * 128:(c + 1) * 128, :],
                       reads=[(self.B_scr_w[wname], c)], writes=[wbuf[c][1]])
            if True:
                for gc in range(8):
                    for th in range(NTH):
                        pp = (gc * NTH + th) % 2
                        pQ, bQ = self.PB(pp, 0, TH)
                        items = []
                        for q in range(max(1, TH // 512)):
                            w = min(512, TH)
                            for fc in range(8):
                                items.append(dict(
                                    out=pQ[:, q * 512:q * 512 + w],
                                    lhsT=wbuf[fc][0][:, gc * 128:(gc + 1) * 128],
                                    rhs=hnT[fc][0][:, th * TH + q * 512:
                                                    th * TH + q * 512 + w],
                                    reads=[wbuf[fc][1], hnT[fc][1]],
                                    start=(fc == 0), stop=(fc == 7)))
                        kb.mmg(items, writes=[bQ])
                        kb.op("act", "activation",
                              dst[gc][0][:, th * TH:(th + 1) * TH], pQ,
                              AF.Copy, bias=0.0, scale=1.0,
                              reads=[bQ], writes=[dst[gc][1]])
            QK.append(dst)
        self.Q2T, self.K2T = QK
        wbuf = wsets[0]
        for c in range(8):
            kb.dma(self.s_ldw[c % 2], wbuf[c][0][:],
                   self.scr_w["fwv"][c * 128:(c + 1) * 128, :],
                   reads=[(self.B_scr_w["fwv"], c)], writes=[wbuf[c][1]])
        V2e = [AR.alloc(f"V2e{b}_{i}", [128, 1040], BF16) for i in range(NT)]
        self.V2e = V2e
        if True:
            for i in range(NT):
                for gh in range(2):
                    pp = (2 * i + gh) % 2
                    pV, bV = self.PB(2, pp * 512, pp * 512 + 512)
                    items = []
                    for fc in range(8):
                        items.append(dict(
                            out=pV,
                            lhsT=hnT[fc][0][:, i * 128:(i + 1) * 128],
                            rhs=wbuf[fc][0][:, gh * 512:(gh + 1) * 512],
                            reads=[hnT[fc][1], wbuf[fc][1]],
                            start=(fc == 0), stop=(fc == 7)))
                    kb.mmg(items, writes=[bV])
                    dst_ap = bass.AP(V2e[i][0], gh * 8 * 65,
                                     [[1040, 128], [65, 8], [1, 64]])
                    kb.op("act", "activation", dst_ap, pV, AF.Copy,
                          bias=0.0, scale=1.0, reads=[bV],
                          writes=[V2e[i][1]])
                ones_ap = bass.AP(V2e[i][0], 64, [[1040, 128], [65, 16]])
                kb.op("pool", "memset", ones_ap, 1.0, writes=[V2e[i][1]])
        for ws in wsets:
            for h, bf in ws:
                AR.freez(h)
        for h, bf in hnT:
            AR.freez(h)

    def p6(self, b, inject=()):
        self.kb.tag = 'p6'
        inject = list(inject)
        kb, T, NT, AR = self.kb, self.T, self.NT, self.AR
        TQ, NTQ = self.TQ, self.NTQ
        Q2T, K2T, V2e = self.Q2T, self.K2T, self.V2e
        KTQ = TQ // 128
        Pf = [AR.alloc(f"Pf{b}_{i}", [128, TQ], BF16) for i in range(6)]
        OT = [None] * 8
        wbuf = [AR.alloc(f"wo{b}_{c}", [128, D], BF16) for c in range(8)]
        for c in range(8):
            kb.dma(self.s_ldw[c % 2], wbuf[c][0][:],
                   self.scr_w["fwo"][c * 128:(c + 1) * 128, :],
                   reads=[(self.B_scr_w["fwo"], c)], writes=[wbuf[c][1]])
        self._p6_pending = None
        if True:
            for hp in range(8):
                if OT[hp] is None:
                    OT[hp] = AR.alloc(f"OT{b}_{hp}", [128, T], BF16)
                for qb in range(NTQ):
                    if inject:
                        inject.pop(0)()
                    self.kb.tag = 'p6'

                    nkt = KTQ * (qb + 1)
                    pq6 = (hp * NTQ + qb) % 2
                    pyt = []
                    for hi in range(2):
                        pY, bY = self.PB(2 + pq6, hi * 512, hi * 512 + TQ)
                        pyt.append((pY, bY))
                    def emit_scores(kt):
                        self.kb.tag = 'p6s'
                        off = max(0, kt * 128 - qb * TQ)
                        w = TQ - off
                        pp = kt % 2
                        pf = kt % 3
                        for hi in range(2):
                            base = hi * 64
                            pS, bS = self.PB(pp, hi * 512, hi * 512 + TQ)
                            kb.mmg([dict(
                                out=pS[:, off:off + w],
                                lhsT=K2T[hp][0][base:base + 64,
                                               kt * 128:(kt + 1) * 128],
                                rhs=Q2T[hp][0][base:base + 64,
                                              qb * TQ + off:qb * TQ + off + w],
                                reads=[K2T[hp][1], Q2T[hp][1]],
                                start=True, stop=True)],
                                writes=[bS])
                            kb.op("act", "activation",
                                  Pf[2 * pf + hi][0][:, off:off + w],
                                  pS[:, off:off + w], AF.Exp,
                                  bias=0.0, scale=GAMMA2,
                                  reads=[bS], writes=[Pf[2 * pf + hi][1]])
                            if kt * 128 >= qb * TQ:
                                kb.op("dve", "tensor_mul",
                                      Pf[2 * pf + hi][0][:, off:off + 128],
                                      Pf[2 * pf + hi][0][:, off:off + 128],
                                      self.trimask[:],
                                      reads=[Pf[2 * pf + hi][1], self.B_const],
                                      writes=[Pf[2 * pf + hi][1]])

                    def emit_av(kt):
                        self.kb.tag = 'p6a'
                        off = max(0, kt * 128 - qb * TQ)
                        w = TQ - off
                        pf = kt % 3
                        for hi in range(2):
                            h = 2 * hp + hi
                            kb.mmg([dict(
                                out=pyt[hi][0][0:65, off:off + w],
                                lhsT=V2e[kt][0][:, h * 65:h * 65 + 65],
                                rhs=Pf[2 * pf + hi][0][:, off:off + w],
                                reads=[V2e[kt][1], Pf[2 * pf + hi][1]],
                                start=(kt == 0), stop=(kt == nkt - 1))],
                                writes=[pyt[hi][1]])

                    emit_scores(0)
                    if nkt > 1:
                        emit_scores(1)
                    for kt in range(2, nkt):
                        emit_scores(kt)
                        emit_av(kt - 2)
                    if nkt > 1:
                        emit_av(nkt - 2)
                    emit_av(nkt - 1)
                    if self._p6_pending is not None:
                        self._p6_pending()
                        self._p6_pending = None
                    self.kb.tag = 'p6n'
                    for hi in range(2):
                        pY, bY = pyt[hi]
                        kb.op("dve", "tensor_copy", self.ysc[:, hi, 0:TQ],
                              pY[0:64, :], reads=[bY],
                              writes=[(self.B_rope, hi)])
                        kb.op("act", "activation", self.rrow[:, hi, 0:TQ],
                              pY[64:65, :], AF.Ln, bias=0.0, scale=1.0,
                              reads=[bY], writes=[(self.B_rope, hi)])
                        kb.op("act", "activation", self.rinv[:, hi, 0:TQ],
                              self.rrow[:, hi, 0:TQ], AF.Exp, bias=0.0,
                              scale=-1.0, reads=[(self.B_rope, hi)],
                              writes=[(self.B_rope, hi)])
                    def part_b(hp=hp, qb=qb):
                        self.kb.tag = 'p6n'
                        for hi in range(2):
                            pRB, bRB = self.PB(1, hi * 512, hi * 512 + TQ,
                                               p0=0, p1=64)
                            kb.mmg([dict(out=pRB, lhsT=self.ones64[:],
                                         rhs=self.rinv[:, hi, 0:TQ],
                                         reads=[(self.B_rope, hi),
                                                self.B_const],
                                         start=True, stop=True)],
                                   writes=[bRB])
                            kb.op("dve", "tensor_tensor",
                                  OT[hp][0][hi * 64:hi * 64 + 64,
                                            qb * TQ:(qb + 1) * TQ],
                                  self.ysc[:, hi, 0:TQ], pRB, ALU.mult,
                                  reads=[(self.B_rope, hi), bRB],
                                  writes=[OT[hp][1]])
                    self._p6_pending = part_b
            if self._p6_pending is not None:
                self._p6_pending()
                self._p6_pending = None
            for hp in range(8):
                AR.freez(Q2T[hp][0])
                AR.freez(K2T[hp][0])
        for h_, bf in V2e:
            AR.freez(h_)
        for h_, bf in Pf:
            AR.freez(h_)
        self.Q2T = self.K2T = self.V2e = None
        self.kb.tag = 'p6wo'
        acct, B_acct = AR.alloc(f"acct{b}", [128, 2, D], F32, units=2)
        htile, B_ht = AR.alloc(f"ht{b}", [128, 2, D], F32, units=2)
        if True:
            for i in range(NT):
                sl = i % 2
                psw_, bW = self.PB(2 * sl, 0, 1024)
                kb.dma(self.s_ldx[sl], self.xstage[:, sl, :],
                       self.x_d[b, i * 128:(i + 1) * 128, :],
                       reads=[self.B_in], writes=[(self.B_xstage, sl)])
                kb.dma(self.s_acc[sl], acct[:, sl, :],
                       self.scr_acc[b][i * 128:(i + 1) * 128, :],
                       reads=[(self.B_scr_acc[b], i)], writes=[(B_acct, sl)])
                items = []
                for gh in range(2):
                    for gc in range(8):
                        items.append(dict(
                            out=psw_[:, gh * 512:(gh + 1) * 512],
                            lhsT=OT[gc][0][:, i * 128:(i + 1) * 128],
                            rhs=wbuf[gc][0][:, gh * 512:(gh + 1) * 512],
                            reads=[OT[gc][1], wbuf[gc][1]],
                            start=(gc == 0), stop=(gc == 7)))
                kb.mmg(items, writes=[bW])
                kb.op("act", "activation", self.sqs[:, sl, :],
                      self.xstage[:, sl, :], AF.Identity,
                      bias=self.st_b1[b][:, i:i + 1],
                      scale=self.st_rstd[b][:, i:i + 1],
                      reads=[(self.B_xstage, sl), (self.B_stats[b], i)],
                      writes=[(self.B_sqs, sl)])
                kb.op("dve", "tensor_add", htile[:, sl, :], self.sqs[:, sl, :],
                      psw_, reads=[(self.B_sqs, sl), bW],
                      writes=[(B_ht, sl)])
                kb.op("act", "activation", self.sqs[:, sl, :], htile[:, sl, :],
                      AF.Square, bias=0.0, scale=1.0,
                      accum_out=self.st_t[:, sl, 0:1],
                      reads=[(B_ht, sl)],
                      writes=[(self.B_sqs, sl), (self.B_stt, sl)])
                kb.op("act", "activation", self.st_t[:, sl, 1:2],
                      self.st_t[:, sl, 0:1],
                      AF.Ln, bias=self.epsv[:], scale=1.0 / D,
                      reads=[(self.B_stt, sl), self.B_const],
                      writes=[(self.B_stt, sl)])
                kb.op("act", "activation", self.st_s3[b][:, i:i + 1],
                      self.st_t[:, sl, 1:2], AF.Exp, bias=0.0, scale=-0.5,
                      reads=[(self.B_stt, sl)],
                      writes=[(self.B_s3[b], i)])
                kb.op("act", "activation", self.n2stage[:, sl, :],
                      htile[:, sl, :], AF.Identity, bias=0.0,
                      scale=self.st_s3[b][:, i:i + 1],
                      reads=[(B_ht, sl), (self.B_s3[b], i)],
                      writes=[(self.B_n2stage, sl)])
                kb.dma(self.s_stn3[sl], self.scr_Hn[b][i * 128:(i + 1) * 128, :],
                       self.n2stage[:, sl, :], reads=[(self.B_n2stage, sl)],
                       writes=[(self.B_scr_Hn[b], i)])
                kb.op("pool", "tensor_add", acct[:, sl, :], acct[:, sl, :],
                      htile[:, sl, :], reads=[(B_acct, sl), (B_ht, sl)],
                      writes=[(B_acct, sl)])
                kb.op("pool", "tensor_add", acct[:, sl, :], acct[:, sl, :],
                      self.xstage[:, sl, :],
                      reads=[(B_acct, sl), (self.B_xstage, sl)],
                      writes=[(B_acct, sl)])
                kb.dma(self.s_acc2[sl], self.scr_acc[b][i * 128:(i + 1) * 128, :],
                       acct[:, sl, :], reads=[(B_acct, sl)],
                       writes=[(self.B_scr_acc[b], i)])
                if self.debug:
                    kb.dma(self.s_dbg, self.dbg_h[b, i * 128:(i + 1) * 128, :],
                           htile[:, sl, :], reads=[(B_ht, sl)], writes=[])
        for c in range(8):
            AR.freez(OT[c][0])
        for h_, bf in wbuf:
            AR.freez(h_)
        AR.freez(acct)
        AR.freez(htile)

    def p7(self, b):
        self.kb.tag = 'p7r'
        kb, T, NT, AR = self.kb, self.T, self.NT, self.AR
        C2 = 192
        CE = 2 * C2
        TH7 = min(1024, T)
        NTH7 = T // TH7
        NTPT = TH7 // 128
        IOA = bass.IndirectOffsetOnAxis
        fc1b = [AR.alloc(f"fc1_{b}_{c}", [128, EDIM], BF16) for c in range(8)]
        fc2b = [AR.alloc(f"fc2_{b}_{c}", [128, D], BF16) for c in range(16)]

        def load_fc(e):
            for c in range(8):
                kb.dma(self.s_fc1[c % 4], fc1b[c][0][:],
                       self.scr_fc1[e, c * 128:(c + 1) * 128, :],
                       reads=[(self.B_scr_fc1, e * 8 + c)],
                       writes=[fc1b[c][1]])
            for c in range(16):
                kb.dma(self.s_fc2[c % 4], fc2b[c][0][:],
                       self.scr_fc2[e, c * 128:(c + 1) * 128, :],
                       reads=[(self.B_scr_fc2, e * 16 + c)],
                       writes=[fc2b[c][1]], eng="pool")

        load_fc(0)
        self.kb.tag = 'p7r'
        HnT = self.load_T_strips(self.scr_Hn[b], self.B_scr_Hn[b], f"HnT{b}")
        for fc in range(8):
            kb.dma(self.s_ldw[fc % 2], self.rwb[:, fc, :],
                   self.scr_rw[fc * 128:(fc + 1) * 128, :],
                   reads=[(self.B_scr_rw8, fc)], writes=[self.B_rwb])
        kb.dma(self.s_zz, self.scr_Hn[b][T:T + 1, :], self.zrow[0:1, :],
               reads=[self.B_const],
               writes=[(self.B_scr_Hn[b], NT)], eng="pool")
        for z0 in range(0, T, 2048):
            zr = min(2048, T - z0)
            kb.dma(self.s_stn[1],
                   self.scr_cw[b][z0:z0 + zr, :]
                   .rearrange("(p a) b -> p (a b)", p=128),
                   self.zrow[:, 0:zr // 128 * NEXP],
                   reads=[self.B_const],
                   writes=[(self.B_scr_cw[b], list(range(NT + 1)))])
        kb.dma(self.s_stn[1], self.scr_cw[b][T:T + 1, :],
               self.zrow[0:1, 0:NEXP],
               reads=[self.B_const], writes=[(self.B_scr_cw[b], NT)])
        for th in range(NTH7):
            seg = self.scr_idx[b][th * 3072:(th + 1) * 3072]
            kb.dma(self.s_stn[0], seg.rearrange("(p a) -> p a", p=128),
                   self.senti[:, 0:24], reads=[self.B_const],
                   writes=[(self.B_scr_idx[b], th)])

        gtile, B_gt = AR.alloc(f"gt{b}", [128, 6, D], BF16, units=6)
        idxt, B_ix = AR.alloc(f"ix{b}", [128, 2, 4], mybir.dt.int32, units=2)
        cgt, B_cg = AR.alloc(f"cg{b}", [128, 2, 3 * NEXP], F32, units=2)
        base_r, B_base = AR.alloc(f"base{b}", [1, 16], F32)
        posf, B_posf = AR.alloc(f"posf{b}", [128, 2, 40], F32, units=2)
        tgti, B_tg = AR.alloc(f"tg{b}", [128, 2, 2], mybir.dt.int32, units=2)
        tki, B_tk = AR.alloc(f"tk{b}", [128, 2, 1], mybir.dt.int32, units=2)
        eqf, B_eqf = AR.alloc(f"eqf{b}", [128, 2, 16], F32, units=2)
        eqb, B_eqb = AR.alloc(f"eqb{b}", [128, 2, 16], BF16, units=2)
        Gs = [AR.alloc(f"G{b}_{c}", [128, CE], BF16) for c in range(16)]
        mst = [AR.alloc(f"mst{b}_{B}", [128, D], F32) for B in range(3)]
        msw, B_msw = AR.alloc(f"msw{b}", [64, 2, D], F32, units=2)
        ixw, B_ixw = AR.alloc(f"ixw{b}", [64, 2, 1], mybir.dt.int32, units=2)
        relu_s, B_rs = AR.alloc(f"relu{b}", [128, 2, CE], BF16, units=2)
        gthT = [[AR.alloc(f"gthT{b}_{s}_{c}", [128, CE], BF16) for c in range(8)]
                for s in range(2)]
        def compact_tile(th, tt):
            self.kb.tag = 'p7r'
            if tt == 0:
                kb.op("dve", "memset", base_r[:], 0.0, writes=[B_base])
            if True:
                i = th * NTPT + tt
                sl7 = i % 2
                stt = self.st_t[:, sl7, :]
                B_stt7 = (self.B_stt, sl7)
                psl, bL = self.PB(3, 512 + sl7 * 16, 512 + sl7 * 16 + NEXP)
                items = []
                for fc in range(8):
                    items.append(dict(
                        out=psl,
                        lhsT=HnT[fc][0][:, i * 128:(i + 1) * 128],
                        rhs=self.rwb[:, fc, :],
                        reads=[HnT[fc][1], self.B_rwb],
                        start=(fc == 0), stop=(fc == 7)))
                kb.mmg(items, writes=[bL])
                lg = self.lgt[:, sl7, :]
                B_lg7 = (self.B_lgt, sl7)
                kb.op("dve", "tensor_copy", lg, psl, reads=[bL], writes=[B_lg7])
                kb.op("dve", "reduce_max", stt[:, 0:1], lg, axis=AX,
                      reads=[B_lg7], writes=[B_stt7])
                kb.op("dve", "tensor_scalar", stt[:, 2:3], stt[:, 0:1], -1.0,
                      None, ALU.mult, reads=[B_stt7], writes=[B_stt7])
                eq1 = eqf[:, sl7, 0:NEXP]
                eq2 = eqf[:, sl7, NEXP:2 * NEXP]
                kb.op("dve", "tensor_scalar", eq1, lg, stt[:, 0:1], None,
                      ALU.is_equal, reads=[B_lg7, B_stt7],
                      writes=[(B_eqf, sl7)])
                kb.op("dve", "scalar_tensor_tensor", self.lgt2[:, sl7, :],
                      eq1, -1e30, lg, ALU.mult, ALU.add,
                      reads=[(B_eqf, sl7), B_lg7],
                      writes=[(self.B_lgt2, sl7)])
                kb.op("dve", "reduce_max", stt[:, 1:2], self.lgt2[:, sl7, :],
                      axis=AX, reads=[(self.B_lgt2, sl7)], writes=[B_stt7])
                kb.op("dve", "tensor_scalar", eq2, self.lgt2[:, sl7, :],
                      stt[:, 1:2], None, ALU.is_equal,
                      reads=[(self.B_lgt2, sl7), B_stt7],
                      writes=[(B_eqf, sl7)])
                kb.op("dve", "tensor_add", stt[:, 3:4], stt[:, 1:2],
                      stt[:, 2:3], reads=[B_stt7], writes=[B_stt7])
                kb.op("act", "activation", stt[:, 4:5], stt[:, 3:4], AF.Exp,
                      bias=0.0, scale=1.0, reads=[B_stt7], writes=[B_stt7])
                kb.op("dve", "tensor_scalar_add", stt[:, 5:6], stt[:, 4:5],
                      1.0, reads=[B_stt7], writes=[B_stt7])
                kb.op("dve", "reciprocal", stt[:, 6:7], stt[:, 5:6],
                      reads=[B_stt7], writes=[B_stt7])
                kb.op("dve", "tensor_mul", stt[:, 7:8], stt[:, 4:5],
                      stt[:, 6:7], reads=[B_stt7], writes=[B_stt7])
                kb.op("dve", "tensor_scalar_mul", self.cw[b][:, i, :], eq1,
                      stt[:, 6:7], reads=[(B_eqf, sl7), B_stt7],
                      writes=[(self.B_cw[b], i)])
                kb.op("dve", "scalar_tensor_tensor", self.cw[b][:, i, :],
                      eq2, stt[:, 7:8], self.cw[b][:, i, :],
                      ALU.mult, ALU.add,
                      reads=[(B_eqf, sl7), B_stt7, (self.B_cw[b], i)],
                      writes=[(self.B_cw[b], i)])
                kb.dma(self.s_stn2[i % 2], self.scr_cw[b][i * 128:(i + 1) * 128, :],
                       self.cw[b][:, i, :], reads=[(self.B_cw[b], i)],
                       writes=[(self.B_scr_cw[b], i)])
                if self.debug:
                    kb.dma(self.s_dbg, self.dbg_c[b, i * 128:(i + 1) * 128, :],
                           self.cw[b][:, i, :], reads=[(self.B_cw[b], i)],
                           writes=[])
                kb.op("dve", "tensor_copy", eqb[:, sl7, 0:16],
                      eqf[:, sl7, 0:16], reads=[(B_eqf, sl7)],
                      writes=[(B_eqb, sl7)])
                pP1, bP1 = self.PB(3, sl7 * 64, sl7 * 64 + NEXP)
                pP2, bP2 = self.PB(3, sl7 * 64 + 16, sl7 * 64 + 16 + NEXP)
                pB1, bB1 = self.PB(3, sl7 * 64 + 32, sl7 * 64 + 32 + NEXP)
                pB2, bB2 = self.PB(3, sl7 * 64 + 48, sl7 * 64 + 48 + NEXP)
                kb.mmg([dict(out=pP1, lhsT=self.ustrict[:],
                             rhs=eqb[:, sl7, 0:NEXP],
                             reads=[self.B_const, (B_eqb, sl7)],
                             start=True, stop=True)], writes=[bP1])
                kb.mmg([dict(out=pP2, lhsT=self.ustrict[:],
                             rhs=eqb[:, sl7, NEXP:2 * NEXP],
                             reads=[self.B_const, (B_eqb, sl7)],
                             start=True, stop=True)], writes=[bP2])
                kb.mmg([dict(out=pB1, lhsT=self.ones128f[:],
                             rhs=base_r[0:1, 0:NEXP],
                             reads=[self.B_const, B_base],
                             start=True, stop=True)], writes=[bB1])
                kb.mmg([dict(out=pB2, lhsT=self.ones128f[:],
                             rhs=base_r[0:1, NEXP:2 * NEXP],
                             reads=[self.B_const, B_base],
                             start=True, stop=True)], writes=[bB2])
                p1f = posf[:, sl7, 0:8]
                p2f = posf[:, sl7, 8:16]
                kb.op("act", "activation", p1f, pP1, AF.Copy, bias=0.0,
                      scale=1.0, reads=[bP1], writes=[(B_posf, sl7)])
                kb.op("dve", "tensor_add", p1f, p1f, pB1,
                      reads=[(B_posf, sl7), bB1], writes=[(B_posf, sl7)])
                kb.op("act", "activation", p2f, pP2, AF.Copy, bias=0.0,
                      scale=1.0, reads=[bP2], writes=[(B_posf, sl7)])
                kb.op("dve", "tensor_add", p2f, p2f, pB2,
                      reads=[(B_posf, sl7), bB2], writes=[(B_posf, sl7)])
                pC, bC = self.PB(2, sl7 * 32, sl7 * 32 + 16, p0=0, p1=1)
                kb.mmg([dict(out=pC, lhsT=self.onescol[:],
                             rhs=eqb[:, sl7, 0:16],
                             reads=[self.B_const, (B_eqb, sl7)],
                             start=True, stop=True)], writes=[bC])
                kb.op("dve", "tensor_add", base_r[0:1, 0:16],
                      base_r[0:1, 0:16], pC,
                      reads=[B_base, bC], writes=[B_base])
                t8 = posf[:, sl7, 16:24]
                e1c = posf[:, sl7, 24:25]
                e2c = posf[:, sl7, 25:26]
                q1 = posf[:, sl7, 26:27]
                q2 = posf[:, sl7, 27:28]
                ov = posf[:, sl7, 28:29]
                tg1 = posf[:, sl7, 29:30]
                tg2 = posf[:, sl7, 30:31]
                kb.op("dve", "tensor_mul", t8, eq1, self.erow[:],
                      reads=[(B_eqf, sl7), self.B_const],
                      writes=[(B_posf, sl7)])
                kb.op("dve", "reduce_sum", e1c, t8, axis=AX,
                      reads=[(B_posf, sl7)], writes=[(B_posf, sl7)])
                kb.op("dve", "tensor_mul", t8, eq2, self.erow[:],
                      reads=[(B_eqf, sl7), self.B_const],
                      writes=[(B_posf, sl7)])
                kb.op("dve", "reduce_sum", e2c, t8, axis=AX,
                      reads=[(B_posf, sl7)], writes=[(B_posf, sl7)])
                kb.op("dve", "tensor_mul", t8, eq1, p1f,
                      reads=[(B_eqf, sl7), (B_posf, sl7)],
                      writes=[(B_posf, sl7)])
                kb.op("dve", "reduce_sum", q1, t8, axis=AX,
                      reads=[(B_posf, sl7)], writes=[(B_posf, sl7)])
                kb.op("dve", "tensor_mul", t8, eq2, p2f,
                      reads=[(B_eqf, sl7), (B_posf, sl7)],
                      writes=[(B_posf, sl7)])
                kb.op("dve", "reduce_sum", q2, t8, axis=AX,
                      reads=[(B_posf, sl7)], writes=[(B_posf, sl7)])
                kb.op("dve", "scalar_tensor_tensor", tg1, e1c, float(CE), q1,
                      ALU.mult, ALU.add, reads=[(B_posf, sl7)],
                      writes=[(B_posf, sl7)])
                kb.op("dve", "tensor_scalar_add", tg1, tg1, float(th * 3072),
                      reads=[(B_posf, sl7)], writes=[(B_posf, sl7)])
                kb.op("dve", "tensor_scalar", ov, q1, float(C2), None,
                      ALU.is_ge, reads=[(B_posf, sl7)], writes=[(B_posf, sl7)])
                kb.op("dve", "scalar_tensor_tensor", tg1, ov, 1e9, tg1,
                      ALU.mult, ALU.add, reads=[(B_posf, sl7)],
                      writes=[(B_posf, sl7)])
                kb.op("dve", "scalar_tensor_tensor", tg2, e2c, float(CE), q2,
                      ALU.mult, ALU.add, reads=[(B_posf, sl7)],
                      writes=[(B_posf, sl7)])
                kb.op("dve", "tensor_scalar_add", tg2, tg2,
                      float(C2 + th * 3072),
                      reads=[(B_posf, sl7)], writes=[(B_posf, sl7)])
                kb.op("dve", "tensor_scalar", ov, q2, float(C2), None,
                      ALU.is_ge, reads=[(B_posf, sl7)], writes=[(B_posf, sl7)])
                kb.op("dve", "scalar_tensor_tensor", tg2, ov, 1e9, tg2,
                      ALU.mult, ALU.add, reads=[(B_posf, sl7)],
                      writes=[(B_posf, sl7)])
                kb.op("dve", "tensor_copy", tgti[:, sl7, 0:1], tg1,
                      reads=[(B_posf, sl7)], writes=[(B_tg, sl7)])
                kb.op("dve", "tensor_copy", tgti[:, sl7, 1:2], tg2,
                      reads=[(B_posf, sl7)], writes=[(B_tg, sl7)])
                kb.op("pool", "iota", tki[:, sl7, :], [[1, 1]], base=i * 128,
                      channel_multiplier=1, writes=[(B_tk, sl7)])
                seg = self.scr_idx[b][:].rearrange("(a b) -> a b", b=1)
                kb.idma(self.s_ix[0], seg,
                        IOA(tgti[:, sl7, 0:1], 0), tki[:, sl7, :], None,
                        bounds_check=self.rg_idx, oob_is_err=False,
                        reads=[(B_tg, sl7), (B_tk, sl7)],
                        writes=[(self.B_scr_idx[b], th)])
                kb.idma(self.s_ix[1], seg,
                        IOA(tgti[:, sl7, 1:2], 0), tki[:, sl7, :], None,
                        bounds_check=self.rg_idx, oob_is_err=False,
                        reads=[(B_tg, sl7), (B_tk, sl7)],
                        writes=[(self.B_scr_idx[b], th)])

        for tt in range(NTPT):
            compact_tile(0, tt)
        pend = [(th, tt) for th in range(1, NTH7) for tt in range(NTPT)]

        self.kb.tag = 'p7moe'
        pairs = [(th, e) for th in range(NTH7) for e in range(NEXP)]

        def gather_block(k):
            th, e = pairs[k]
            s = k % 2
            for a in range(3):
                kb.dma(self.s_ldt[a % 2],
                       idxt[:, s, a:a + 1],
                       self.scr_idx[b][th * 3072 + e * CE + a * 128:
                                       th * 3072 + e * CE + a * 128 + 128]
                       .rearrange("(p a) -> p a", p=128),
                       reads=[(self.B_scr_idx[b], th)],
                       writes=[(B_ix, s)])
            for a in range(3):
                kb.idma(self.s_gg[a],
                        gtile[:, 3 * s + a, :], None, self.scr_Hn[b][:],
                        IOA(idxt[:, s, a:a + 1], 0),
                        reads=[(B_ix, s),
                               (self.B_scr_Hn[b], list(range(NT + 1)))],
                        writes=[(B_gt, 3 * s + a)])
                kb.dma(self.s_stn2[a % 2],
                       self.scr_gth[s][a * 128:(a + 1) * 128, :],
                       gtile[:, 3 * s + a, :], reads=[(B_gt, 3 * s + a)],
                       writes=[self.B_scrg[s]])
                kb.idma(self.s_ix[s],
                        cgt[:, s, a * NEXP:(a + 1) * NEXP], None,
                        self.scr_cw[b][:],
                        IOA(idxt[:, s, a:a + 1], 0),
                        reads=[(B_ix, s),
                               (self.B_scr_cw[b], list(range(NT + 1)))],
                        writes=[(B_cg, s)])
            for fc in range(8):
                kb.dma(self.s_ldt[fc % 4], gthT[s][fc][0][:],
                       self.scr_gth[s][0:CE, fc * 128:(fc + 1) * 128],
                       reads=[self.B_scrg[s]], writes=[gthT[s][fc][1]],
                       transpose=True)

        gather_block(0)
        for k, (th, e) in enumerate(pairs):
            if k > 0:
                load_fc(e)
            s = k % 2
            if pend:
                compact_tile(*pend.pop(0))
                self.kb.tag = 'p7moe'
            if k + 1 < len(pairs):
                gather_block(k + 1)
            if True:
                for hc in range(16):
                    pp = hc % 2
                    psg, bG = self.PB(pp, 0, CE)
                    items = []
                    for fc in range(8):
                        items.append(dict(
                            out=psg,
                            lhsT=fc1b[fc][0][:, hc * 128:(hc + 1) * 128],
                            rhs=gthT[s][fc][0][:],
                            reads=[fc1b[fc][1], gthT[s][fc][1]],
                            start=(fc == 0), stop=(fc == 7)))
                    kb.mmg(items, writes=[bG])
                    kb.op("act", "activation", relu_s[:, pp, 0:CE], psg,
                          AF.Relu, bias=0.0, scale=1.0,
                          reads=[bG], writes=[(B_rs, pp)])
                    kb.op("dve", "tensor_mul", Gs[hc][0][:],
                          relu_s[:, pp, 0:CE], relu_s[:, pp, 0:CE],
                          reads=[(B_rs, pp)], writes=[Gs[hc][1]])
                for tt3 in range(3):
                    pp = tt3 % 2
                    psm, bM = self.PB(2 + pp, 0, 1024)
                    items = []
                    for gh in range(2):
                        for hc in range(16):
                            items.append(dict(
                                out=psm[:, gh * 512:(gh + 1) * 512],
                                lhsT=Gs[hc][0][:, tt3 * 128:(tt3 + 1) * 128],
                                rhs=fc2b[hc][0][:, gh * 512:(gh + 1) * 512],
                                reads=[Gs[hc][1], fc2b[hc][1]],
                                start=(hc == 0), stop=(hc == 15)))
                    kb.mmg(items, writes=[bM])
                    kb.op("dve", "tensor_scalar_mul", mst[tt3][0][:], psm,
                          cgt[:, s, tt3 * NEXP + e:tt3 * NEXP + e + 1],
                          reads=[bM, (B_cg, s)], writes=[mst[tt3][1]])
                    if tt3 == 0:
                        kb.idma(self.s_sca[0], self.scr_moe[b][:],
                                IOA(idxt[:, s, 0:1], 0),
                                mst[0][0][:], None,
                                bounds_check=self.rg_tok, oob_is_err=False,
                                reads=[mst[0][1], (B_ix, s)],
                                writes=[(self.B_scr_moe[b], e)])
                    elif tt3 == 2:
                        kb.idma(self.s_sca[1], self.scr_moe2[b][:],
                                IOA(idxt[:, s, 2:3], 0),
                                mst[2][0][:], None,
                                bounds_check=self.rg_tok, oob_is_err=False,
                                reads=[mst[2][1], (B_ix, s)],
                                writes=[(self.B_scr_moe2[b], e)])
                    else:
                        kb.op("dve", "tensor_copy", msw[:, 0, :],
                              mst[1][0][0:64, :], reads=[mst[1][1]],
                              writes=[(B_msw, 0)])
                        kb.op("dve", "tensor_copy", msw[:, 1, :],
                              mst[1][0][64:128, :], reads=[mst[1][1]],
                              writes=[(B_msw, 1)])
                        kb.op("dve", "tensor_copy", ixw[:, 0, :],
                              idxt[0:64, s, 1:2], reads=[(B_ix, s)],
                              writes=[(B_ixw, 0)])
                        kb.op("dve", "tensor_copy", ixw[:, 1, :],
                              idxt[64:128, s, 1:2], reads=[(B_ix, s)],
                              writes=[(B_ixw, 1)])
                        kb.idma(self.s_sca[0], self.scr_moe[b][:],
                                IOA(ixw[:, 0, :], 0),
                                msw[:, 0, :], None,
                                bounds_check=self.rg_tok, oob_is_err=False,
                                reads=[(B_msw, 0), (B_ixw, 0)],
                                writes=[(self.B_scr_moe[b], e)])
                        kb.idma(self.s_sca[1], self.scr_moe2[b][:],
                                IOA(ixw[:, 1, :], 0),
                                msw[:, 1, :], None,
                                bounds_check=self.rg_tok, oob_is_err=False,
                                reads=[(B_msw, 1), (B_ixw, 1)],
                                writes=[(self.B_scr_moe2[b], e)])
        for lst in (fc1b, fc2b, Gs, mst, gthT[0], gthT[1]):
            for h_, bf in lst:
                AR.freez(h_)
        for h_ in (gtile, idxt, cgt, base_r, posf, tgti, tki, eqf, eqb,
                   relu_s, msw, ixw):
            AR.freez(h_)
        for h_, bf in HnT:
            AR.freez(h_)

    def p8(self, b):
        self.kb.tag = 'p8'
        kb, T, NT, AR = self.kb, self.T, self.NT, self.AR
        acct, B_acct = AR.alloc(f"acc8{b}", [128, 2, D], F32, units=2)
        mt, B_mt = AR.alloc(f"moe8{b}", [128, 2, D], F32, units=2)
        m2t, B_m2t = AR.alloc(f"moe8b{b}", [128, 2, D], F32, units=2)
        for i in range(NT):
            sl = i % 2
            kb.dma(self.s_acc[sl], acct[:, sl, :],
                   self.scr_acc[b][i * 128:(i + 1) * 128, :],
                   reads=[(self.B_scr_acc[b], i)], writes=[(B_acct, sl)])
            kb.dma(self.s_moep[sl], mt[:, sl, :],
                   self.scr_moe[b][i * 128:(i + 1) * 128, :],
                   reads=[(self.B_scr_moe[b], list(range(NEXP)))],
                   writes=[(B_mt, sl)], eng="act")
            kb.dma(self.s_moev[sl], m2t[:, sl, :],
                   self.scr_moe2[b][i * 128:(i + 1) * 128, :],
                   reads=[(self.B_scr_moe2[b], list(range(NEXP)))],
                   writes=[(B_m2t, sl)], eng="pool")
            kb.op("dve", "tensor_add", acct[:, sl, :], acct[:, sl, :],
                  mt[:, sl, :], reads=[(B_acct, sl), (B_mt, sl)],
                  writes=[(B_acct, sl)])
            kb.op("dve", "tensor_add", self.sqs[:, sl, :], acct[:, sl, :],
                  m2t[:, sl, :], reads=[(B_acct, sl), (B_m2t, sl)],
                  writes=[(self.B_sqs, sl)])
            kb.dma(self.s_out[sl], self.y_d[b, i * 128:(i + 1) * 128, :],
                   self.sqs[:, sl, :], reads=[(self.B_sqs, sl)], writes=[])
        AR.freez(acct)
        AR.freez(mt)
        AR.freez(m2t)

    def batch_front(self, b):
        if b == 0:
            self.p3(b)
        self.p4(b)
        self.p5(b)
        inj = self.p1_emitters(b + 1) if b + 1 < self.BPC else ()
        self.p6(b, inject=inj)


def _host_weights(inputs):
    wq = np.ascontiguousarray(inputs["L0_attn_w_q"], dtype=np.float32)
    wk = np.ascontiguousarray(inputs["L0_attn_w_k"], dtype=np.float32)
    wv = np.ascontiguousarray(inputs["L0_attn_w_v"], dtype=np.float32)
    wqb = np.zeros((8, 128, 128), np.float32)
    wkb = np.zeros((8, 128, 128), np.float32)
    for ic in range(8):
        par, m = ic // 4, ic % 4
        for jj in range(2):
            h = 4 * m + 2 * jj + par
            wqb[ic, jj * 64:(jj + 1) * 64, jj * 64:(jj + 1) * 64] = wq[:, :, h]
            wkb[ic, jj * 64:(jj + 1) * 64, jj * 64:(jj + 1) * 64] = wk[:, :, h]
    kk, dd, hh = np.indices((HD, HD, NH))
    rows_pm = ((hh % 2) * 512 + (hh // 2) * 64 + kk).ravel()
    cols_o = (dd * NH + hh).ravel()
    wv_hm = np.zeros((D, D), np.float32)
    wv_hm[rows_pm, cols_o] = wv.ravel()
    return {
        "wqb": wqb.reshape(1024, 128), "wkb": wkb.reshape(1024, 128),
        "wv_hm": wv_hm,
        "fwq": np.ascontiguousarray(inputs["L0_ffn_wq"], dtype=np.float32),
        "fwk": np.ascontiguousarray(inputs["L0_ffn_wk"], dtype=np.float32),
        "fwv": np.ascontiguousarray(inputs["L0_ffn_wv"], dtype=np.float32),
        "fwo": np.ascontiguousarray(inputs["L0_ffn_wo"], dtype=np.float32),
        "rw": np.ascontiguousarray(inputs["L0_ffn_router_w"], dtype=np.float32),
        "fc1": np.ascontiguousarray(inputs["L0_ffn_exp_fc1"], dtype=np.float32),
        "fc2": np.ascontiguousarray(inputs["L0_ffn_exp_fc2"], dtype=np.float32),
    }


def kernel(**inputs):
    x = np.ascontiguousarray(inputs["L0_attn_x"], dtype=np.float32)
    B, T, d = x.shape
    assert d == D and B % N_CORES == 0
    bpc = B // N_CORES
    wmap = _host_weights(inputs)
    prog = Prog(T=T, BPC=bpc)
    nc = prog.build()
    in_maps = []
    for c in range(N_CORES):
        m = {"x": x[c * bpc:(c + 1) * bpc]}
        m.update(wmap)
        in_maps.append(m)
    res = run_bass_kernel_spmd(nc, in_maps, core_ids=list(range(N_CORES)))
    return np.concatenate([r["y"] for r in res.results], axis=0)
